# version 06
# baseline (speedup 1.0000x reference)
import contextlib
import os
import traceback
import numpy as np
import ml_dtypes
import concourse.bass as bass
import concourse.mybir as mybir
from concourse.bass_utils import run_bass_kernel_spmd

F32 = mybir.dt.float32
BF16 = mybir.dt.bfloat16
I32 = mybir.dt.int32
AF = mybir.ActivationFunctionType
ALU = mybir.AluOpType
AX = mybir.AxisListType
BF = ml_dtypes.bfloat16

NEG = -1.0e30
T = 2048
D = 1024
NB = 2
N_CORE = 8
ts = bass.ts


ENGS = ("sync", "gpsimd", "vector", "scalar", "tensor")

SELF_SEM = {"tensor": "tS", "vector": "vS", "scalar": "sS", "gpsimd": "gS"}
SELF_WAIT = {"vector", "scalar", "gpsimd"}


class Eng:
    def __init__(self, p, name):
        self.p = p
        self.name = name

    def __getattr__(self, op):
        p, name = self.p, self.name
        ssem = SELF_SEM.get(name)

        def f(*args, **kw):
            if name in SELF_WAIT:
                prior = p.cnt[ssem]
                self.wait(ssem, prior)
            v = p.inc(ssem)
            s = p.sems[ssem]
            if p.debug_lines:
                fr = traceback.extract_stack(limit=2)[0]
                lbl = f"{op}@{fr.lineno}"
                def thunk(e, lbl=lbl, op=op, args=args, kw=kw):
                    with p.nc.named_scope(lbl):
                        getattr(e, op)(*args, **kw).then_inc(s, 1)
                p.q[name].append(thunk)
            else:
                p.q[name].append(
                    lambda e: getattr(e, op)(*args, **kw).then_inc(s, 1))
            return v
        return f

    def dma(self, dst, src, sem):
        p, name = self.p, self.name
        s = p.sem(sem)
        v = p.inc(sem, 16)
        p.q[name].append(lambda e: e.dma_start(dst, src).then_inc(s, 16))
        return v

    def wait(self, sem, v):
        p, name = self.p, self.name
        if v <= p.waited[name].get(sem, 0):
            return
        p.waited[name][sem] = v
        s = p.sems[sem]
        p.q[name].append(lambda e: e.wait_ge(s, v))


class P:
    def __init__(self, nc, es):
        self.nc = nc
        self.es = es
        self.q = {k: [] for k in ENGS}
        self.sems = {}
        self.cnt = {}
        self.waited = {k: {} for k in ENGS}
        self.debug_lines = bool(os.environ.get("MK_DEBUG_LINES"))
        self.sync = Eng(self, "sync")
        self.gps = Eng(self, "gpsimd")
        self.vec = Eng(self, "vector")
        self.sca = Eng(self, "scalar")
        self.ten = Eng(self, "tensor")
        self.engmap = {"sync": self.sync, "gpsimd": self.gps, "vector": self.vec,
                       "scalar": self.sca, "tensor": self.ten}
        self.nbar = 0
        for nm in ("ld", "ldg", "st", "tS", "vS", "sS", "gS", "bar"):
            self.sem(nm)

    def sem(self, nm):
        if nm not in self.sems:
            self.sems[nm] = self.es.enter_context(self.nc.semaphore(nm))
            self.cnt[nm] = 0
        return self.sems[nm]

    def inc(self, sem, n=1):
        self.cnt[sem] += n
        return self.cnt[sem]

    def val(self, sem):
        return self.cnt[sem]

    def barrier(self):
        self.nbar += 1
        tgt = 5 * self.nbar
        s = self.sems["bar"]
        allsems = [k for k in self.sems if k != "bar"]
        for nm in ENGS:
            eng = self.engmap[nm]
            for sm in allsems:
                eng.wait(sm, self.cnt[sm])
            self.q[nm].append(lambda e, s=s: e.nop().then_inc(s, 1))
        for nm in ENGS:
            self.q[nm].append(lambda e, s=s, tgt=tgt: e.wait_ge(s, tgt))

    def sb(self, es, name, shape, dtype):
        self._uid = getattr(self, "_uid", 0) + 1
        return es.enter_context(
            self.nc.sbuf_tensor(f"{name}_u{self._uid}", shape, dtype))

    def ps(self, es, name, shape, dtype=F32):
        self._uid = getattr(self, "_uid", 0) + 1
        return es.enter_context(
            self.nc.psum_tensor(f"{name}_u{self._uid}", shape, dtype))

    def finalize(self):
        with self.nc.Block() as block:
            @block.sync
            def _(e):
                for f in self.q["sync"]:
                    f(e)

            @block.gpsimd
            def _(e):
                for f in self.q["gpsimd"]:
                    f(e)

            @block.vector
            def _(e):
                for f in self.q["vector"]:
                    f(e)

            @block.scalar
            def _(e):
                for f in self.q["scalar"]:
                    f(e)

            @block.tensor
            def _(e):
                for f in self.q["tensor"]:
                    f(e)



def _consts():
    c = {}
    c["identB"] = np.eye(128, dtype=np.float32).astype(BF)
    c["identF"] = np.eye(128, dtype=np.float32)
    i = np.arange(128)
    c["maskC"] = np.where(i[:, None] >= i[None, :], 0.0, NEG).astype(np.float32)
    c["maskA"] = np.where(i[:, None] <= i[None, :], 0.0, NEG).astype(np.float32)
    c["ones64"] = np.ones((64, 128), dtype=np.float32).astype(BF)
    c["ones128"] = np.ones((128, 128), dtype=np.float32).astype(BF)
    g = (i[:, None] // 64 == i[None, :] // 64).astype(np.float32)
    c["G64"] = g.astype(BF)
    p64 = np.zeros((128, 128), dtype=np.float32)
    for blk in range(2):
        o = blk * 64
        for m in range(32):
            p64[o + m + 32, o + m] = -1.0
        for m in range(32, 64):
            p64[o + m - 32, o + m] = 1.0
    c["P64"] = p64.astype(BF)
    selrep = np.zeros((128, 16 * 128), dtype=np.float32)
    selsum = np.zeros((128, 16 * 128), dtype=np.float32)
    for ii in range(16):
        for m in range(128):
            selrep[8 * ii + m // 16, ii * 128 + m] = 1.0
        for k in range(128):
            selsum[k, ii * 128 + 8 * ii + k // 16] = 1.0
    c["selrep"] = selrep.astype(BF)
    c["selsum"] = selsum.astype(BF)
    seln = np.zeros((16, 128), dtype=np.float32)
    for m in range(128):
        seln[m % 16, m] = 1.0
    c["seln"] = seln.astype(BF)
    DH, WIN = 64, 512
    half = np.arange(0, DH, 2, dtype=np.float64)
    inv_freq = 1.0 / (10000.0 ** (half / DH))
    t = np.arange(T, dtype=np.float64)
    freqs = t[:, None] * inv_freq[None, :]
    freqs = np.concatenate([freqs, freqs], axis=-1)
    base = (half + 0.4 * DH) / (1.4 * DH)
    power = (t - T // 2) / float(WIN // 2)
    scale = base[None, :] ** power[:, None]
    scale = np.concatenate([scale, scale], axis=-1)
    cos, sin = np.cos(freqs), np.sin(freqs)

    def fm(x):
        xt = np.ascontiguousarray(x.T.astype(np.float32))
        return np.concatenate([xt, xt], axis=0)
    c["CSq8"] = fm(cos * scale * 8.0)
    c["SSq8"] = fm(sin * scale * 8.0)
    c["CSk"] = fm(cos / scale)
    c["SSk"] = fm(sin / scale)
    return c


def _col_tiles(v, np_dtype=np.float32):
    n = v.shape[0]
    nt = (n + 127) // 128
    out = np.zeros((128, nt), dtype=np_dtype)
    for j in range(nt):
        seg = v[j * 128:(j + 1) * 128]
        out[:len(seg), j] = seg
    return np.ascontiguousarray(out)


def _prep_weights(inp):
    w = {}
    cT = lambda a: np.ascontiguousarray(np.asarray(a).T)
    w["wqT"] = cT(inp["L0_attn_W_q"])
    w["wkT"] = cT(inp["L0_attn_W_k"])
    w["wvT"] = cT(inp["L0_attn_W_v"])
    w["woT"] = cT(inp["L0_attn_W_o"])
    w["lq1T"] = cT(inp["L0_attn_lambda_q1"])
    w["lk1T"] = cT(inp["L0_attn_lambda_k1"])
    w["lq2T"] = cT(inp["L0_attn_lambda_q2"])
    w["lk2T"] = cT(inp["L0_attn_lambda_k2"])
    w["rms"] = np.ascontiguousarray(
        np.asarray(inp["L0_attn_rms_scale"], dtype=np.float32).reshape(128, 1))
    w["wgT"] = cT(inp["L0_ffn_w_gate"])
    w["wuT"] = cT(inp["L0_ffn_w_up"])
    w["wdT"] = cT(inp["L0_ffn_w_down"])
    w["inpT"] = cT(inp["L1_attn_in_proj_w"])
    w["outpT"] = cT(inp["L1_attn_out_proj_w"])
    for sfx, cw, cb, xp, dtw, dtb, alog, dv in (
        ("f", "L1_attn_conv1d_w", "L1_attn_conv1d_bias", "L1_attn_x_proj_w",
         "L1_attn_dt_proj_w", "L1_attn_dt_proj_bias", "L1_attn_A_log", "L1_attn_D"),
        ("b", "L1_attn_conv1d_b_w", "L1_attn_conv1d_b_bias", "L1_attn_x_proj_b_w",
         "L1_attn_dt_proj_b_w", "L1_attn_dt_proj_b_bias", "L1_attn_A_b_log",
         "L1_attn_D_b"),
    ):
        cwa = np.asarray(inp[cw]).reshape(2048, 4)
        w["convw_" + sfx] = np.ascontiguousarray(
            cwa.reshape(16, 128, 4).transpose(1, 0, 2))
        w["convb_" + sfx] = _col_tiles(np.asarray(inp[cb]))
        xpa = np.asarray(inp[xp])
        perm = np.zeros((128, 2048), dtype=np.float32)
        perm[0:16] = xpa[64:80]
        perm[32:48] = xpa[80:96]
        perm[64:128] = xpa[0:64]
        w["xprojT_" + sfx] = cT(perm)
        w["dtpT_" + sfx] = cT(np.asarray(inp[dtw]))
        w["dtb_" + sfx] = _col_tiles(np.asarray(inp[dtb]))
        al = np.asarray(inp[alog])
        ac = np.zeros((128, 256), dtype=np.float32)
        m = np.arange(128)
        for j in range(16):
            for i2 in range(16):
                ac[:, j * 16 + i2] = al[128 * j + 8 * i2 + m // 16, m % 16]
        w["alog_" + sfx] = ac
        w["D_" + sfx] = _col_tiles(np.asarray(inp[dv]))
    w["wqkv"] = np.ascontiguousarray(np.asarray(inp["L1_ffn_wqkv"]))
    w["wo1"] = np.ascontiguousarray(np.asarray(inp["L1_ffn_wo"]))
    w["w1"] = np.ascontiguousarray(np.asarray(inp["L1_ffn_w1"]))
    w["w2"] = np.ascontiguousarray(np.asarray(inp["L1_ffn_w2"]))
    b1 = np.asarray(inp["L1_ffn_b1"])
    w["b1a"] = np.stack([_col_tiles(b1[l, :2730]) for l in range(2)])
    w["b1g"] = np.stack([_col_tiles(b1[l, 2730:]) for l in range(2)])
    w["b2"] = np.stack([_col_tiles(np.asarray(inp["L1_ffn_b2"])[l])
                        for l in range(2)])
    w["lng"] = np.stack([_col_tiles(np.asarray(inp["L1_ffn_ln_g"])[l])
                         for l in range(2)])
    w["lnb"] = np.stack([_col_tiles(np.asarray(inp["L1_ffn_ln_b"])[l])
                         for l in range(2)])
    qs = np.asarray(inp["L1_ffn_q_scale"])
    ks = np.asarray(inp["L1_ffn_k_scale"])
    w["qsc"] = np.ascontiguousarray(
        np.concatenate([qs, qs], axis=1).reshape(2, 128, 1))
    w["ksc"] = np.ascontiguousarray(
        np.concatenate([ks, ks], axis=1).reshape(2, 128, 1))
    return w



def emit_ln(p, es, xsrc_dram, b, xnT, xnT_h=None, identB=None):
    xt = [p.sb(es, f"lnx{i}", [128, D], F32) for i in range(2)]
    sq_v = p.sb(es, "lnsqv", [128, D], BF16)
    sq_s = p.sb(es, "lnsqs", [128, D], BF16)
    st = {nm: p.sb(es, "lnst_" + nm, [128, 1], F32)
          for nm in ("sum", "ssq", "mean", "e2", "var", "std", "rstd")}
    xn_nat = p.sb(es, "lnxn", [128, D], BF16)
    pt_t = [p.ps(es, f"lnpt{i}", [128, 128], BF16) for i in range(2)]
    vdone = [0] * 16
    sdone = [0] * 16
    for tt in range(16):
        buf = xt[tt % 2]
        slot = "lnx" + ("A" if tt % 2 == 0 else "B")
        if tt >= 2:
            p.sync.wait("vS", vdone[tt - 2])
            p.sync.wait("sS", sdone[tt - 2])
        lv = p.sync.dma(buf[:], xsrc_dram[b, ts(tt, 128), :], slot)
        p.vec.wait(slot, lv)
        p.sca.wait(slot, lv)
        p.vec.tensor_scalar(sq_v[:], buf[:], 1.0, 0.0, ALU.mult, ALU.add,
                            accum_out=st["sum"][:])
        sv = p.sca.activation(sq_s[:], buf[:], AF.Square, accum_out=st["ssq"][:])
        sdone[tt] = sv
        p.vec.wait("sS", sv)
        p.vec.tensor_scalar(st["mean"][:], st["sum"][:], 1.0 / D, None, ALU.mult)
        p.vec.tensor_scalar(st["e2"][:], st["ssq"][:], 1.0 / D, None, ALU.mult)
        p.vec.tensor_tensor(st["var"][:], st["mean"][:], st["mean"][:], ALU.mult)
        p.vec.tensor_tensor(st["var"][:], st["e2"][:], st["var"][:], ALU.subtract)
        vv = p.vec.tensor_scalar(st["var"][:], st["var"][:], 1e-5, None, ALU.add)
        p.sca.wait("vS", vv)
        sv = p.sca.activation(st["std"][:], st["var"][:], AF.Sqrt)
        p.vec.wait("sS", sv)
        p.vec.reciprocal(st["rstd"][:], st["std"][:])
        p.vec.wait("tS", p.val("tS"))
        vv = p.vec.tensor_scalar(xn_nat[:], buf[:], st["mean"][:], st["rstd"][:],
                                 ALU.subtract, ALU.mult)
        vdone[tt] = vv
        p.ten.wait("vS", vv)
        for dt in range(8):
            pb = pt_t[dt % 2]
            if dt >= 2:
                p.ten.wait("sS", p.val("sS") - 1)
            tv = p.ten.transpose(pb[:], xn_nat[:, ts(dt, 128)], identB[:])
            p.sca.wait("tS", tv)
            sv = p.sca.copy(xnT[:, dt * T + tt * 128:dt * T + tt * 128 + 128],
                            pb[:])
        if xnT_h is not None:
            p.sync.wait("sS", p.val("sS"))
            src3 = xnT.rearrange("p (a t) -> p a t", a=8)[:, :,
                                                          tt * 128:(tt + 1) * 128]
            p.sync.dma(xnT_h[b, tt].rearrange("a p t -> p a t"), src3, "st")



def build_l0(p, io):
    nc = p.nc
    x, x1 = io["x"], io["x1"]
    xnT_h, attn_h = io["xnT_h"], io["attn_h"]
    es_p = p.es

    identB = p.sb(es_p, "identB", [128, 128], BF16)
    maskC = p.sb(es_p, "maskC", [128, 128], F32)
    rms02 = p.sb(es_p, "rms02", [128, 1], F32)
    lam_bc = p.sb(es_p, "lam_bc", [128, 16], F32)

    p.sync.dma(identB[:], io["identB"], "ld")
    p.sync.dma(maskC[:], io["maskC"], "ld")

    with contextlib.ExitStack() as es0:
        lv4 = [p.sb(es0, f"lv{i}", [64, 16], F32) for i in range(4)]
        ones64 = p.sb(es0, "ones64", [64, 128], BF16)
        prod = p.sb(es0, "prod", [64, 16], BF16)
        pl = p.ps(es0, "pl", [128, 16], F32)
        e1 = p.sb(es0, "e1", [128, 16], F32)
        e2 = p.sb(es0, "e2", [128, 16], F32)
        rmsr = p.sb(es0, "rmsr", [128, 1], F32)
        for i, nm in enumerate(("lq1T", "lk1T", "lq2T", "lk2T")):
            p.sync.dma(lv4[i][:], io[nm], "ld")
        p.sync.dma(rmsr[:], io["rms"], "ld")
        p.gps.dma(ones64[:], io["ones64"], "ldg")
        p.vec.wait("ld", p.val("ld"))
        p.vec.wait("ldg", p.val("ldg"))
        p.vec.tensor_scalar(rms02[:], rmsr[:], 0.2, None, ALU.mult)
        for k in range(2):
            a, bb = (0, 1) if k == 0 else (2, 3)
            vv = p.vec.tensor_tensor(prod[:], lv4[a][:], lv4[bb][:], ALU.mult)
            p.ten.wait("vS", vv)
            tv = p.ten.matmul(pl[:], ones64[:], prod[:], start=True, stop=True)
            p.sca.wait("tS", tv)
            sv = p.sca.activation((e1 if k == 0 else e2)[:], pl[:], AF.Exp)
            p.vec.wait("sS", sv)
        p.vec.tensor_tensor(lam_bc[:], e1[:], e2[:], ALU.subtract)
        p.vec.tensor_scalar(lam_bc[:], lam_bc[:], 0.8, None, ALU.add)
        p.barrier()

    for b in range(NB):
        with contextlib.ExitStack() as es_b:
            xnT = p.sb(es_b, "xnT", [128, 8 * T], BF16)
            acc = p.sb(es_b, "l0acc", [128, 16 * 1024], F32)
            with contextlib.ExitStack() as es_ln:
                emit_ln(p, es_ln, x, b, xnT, xnT_h=xnT_h, identB=identB)
                p.barrier()
            with contextlib.ExitStack() as es_a:
                _l0_attn(p, es_a, io, b, xnT, acc, identB, maskC, rms02, lam_bc)
            p.sync.wait("vS", p.val("vS"))
            p.sync.dma(attn_h[b].rearrange("(a p) f -> p a f", p=128),
                       acc.rearrange("p (a f) -> p a f", a=16), "st")
            p.sync.wait("st", p.val("st"))
            p.barrier()
        with contextlib.ExitStack() as es_f:
            _l0_ffn(p, es_f, io, b, x, x1, xnT_h, attn_h)
            p.barrier()
    p.sync.wait("st", p.val("st"))


def _l0_attn(p, es, io, b, xnT, acc, identB, maskC, rms02, lam_bc):
    GH = 2
    wq_g = p.sb(es, "wq_g", [128, 8 * 128 * GH], BF16)
    wk_g = p.sb(es, "wk_g", [128, 8 * 128 * GH], BF16)
    wv_g = p.sb(es, "wv_g", [128, 8 * 128 * GH], BF16)
    wo_g = p.sb(es, "wo_g", [128, GH * 1024], BF16)
    qT_g = p.sb(es, "qT_g", [128, GH * T], BF16)
    kT_g = p.sb(es, "kT_g", [128, GH * T], BF16)
    v_g = p.sb(es, "v_g", [128, 16 * 128 * GH], BF16)
    ps_s1 = p.ps(es, "ps_s1", [128, 1024], F32)
    ps_s2 = p.ps(es, "ps_s2", [128, 1024], F32)
    ps_o = p.ps(es, "ps_o", [128, 128], F32)

    pt_t = [p.ps(es, f"apt{i}", [128, 128], BF16) for i in range(2)]
    oT_all = p.sb(es, "oT_all", [128, GH * 16 * 128], BF16)
    wo_mm_last = [0]
    E1 = [p.sb(es, f"E1_{i}", [128, T], BF16) for i in range(2)]
    E2 = [p.sb(es, f"E2_{i}", [128, T], BF16) for i in range(2)]
    attn_sb = [p.sb(es, f"attn{i}", [128, T], BF16) for i in range(2)]
    attnT = [p.sb(es, f"attnT{i}", [128, T], BF16) for i in range(2)]
    atd = [0, 0]
    onat = [p.sb(es, f"onat{i}", [128, 128], BF16) for i in range(2)]
    sq128 = p.sb(es, "sq128", [128, 128], BF16)
    st2 = {}
    for nm in ("m", "nm", "rr", "rri"):
        st2[nm] = [p.sb(es, f"at_{nm}{i}", [128, 2], F32) for i in range(2)]
    st2["rp"] = [p.sb(es, f"at_rp{i}", [128, 4], F32) for i in range(2)]
    for nm in ("c2", "ssq", "t0", "s0", "f"):
        st2[nm] = [p.sb(es, f"at_{nm}{i}", [128, 1], F32) for i in range(2)]

    for g in range(16 // GH):
        p.gps.wait("tS", p.val("tS"))
        W = 128 * GH
        for wdst, wsrc in ((wq_g, "wqT"), (wk_g, "wkT"), (wv_g, "wvT")):
            p.gps.dma(wdst.rearrange("p (a w) -> p a w", a=8),
                      io[wsrc].rearrange("(a p) f -> p a f", p=128)[
                          :, :, g * W:(g + 1) * W], "ldg")
        p.gps.dma(wo_g.rearrange("p (a f) -> p a f", a=GH),
                  io["woT"].rearrange("(a p) f -> p a f", p=128)[
                      :, GH * g:GH * (g + 1), :], "ldg")
        lgv = p.val("ldg")
        p.vec.wait("ldg", lgv)
        p.vec.wait("tS", p.val("tS"))
        vv = 0
        for hh in range(GH):
            vv = p.vec.tensor_scalar(wo_g[:, ts(hh, 1024)],
                                     wo_g[:, ts(hh, 1024)], rms02[:, 0:1], None,
                                     ALU.mult)
        p.ten.wait("vS", vv)
        p.ten.wait("ldg", lgv)
        p.ten.wait("sS", p.val("sS"))
        ps_qkv = ps_s1
        for hh in range(GH):
            for (wsb, dst) in ((wq_g, qT_g), (wk_g, kT_g)):
                for ch in range(4):
                    for dt in range(8):
                        tv = p.ten.matmul(
                            ps_qkv[:, 0:512],
                            wsb[:, dt * 128 * GH + hh * 128:
                                dt * 128 * GH + hh * 128 + 128],
                            xnT[:, dt * T + ch * 512:dt * T + (ch + 1) * 512],
                            start=(dt == 0), stop=(dt == 7))
                    p.sca.wait("tS", tv)
                    sv = p.sca.copy(
                        dst[:, hh * T + ch * 512:hh * T + (ch + 1) * 512],
                        ps_qkv[:, 0:512])
                    p.ten.wait("sS", sv)
        for tt in range(16):
            for dt in range(8):
                tv = p.ten.matmul(ps_qkv[:, 0:128 * GH],
                                  xnT[:, dt * T + tt * 128:dt * T + tt * 128 + 128],
                                  wv_g[:, ts(dt, 128 * GH)],
                                  start=(dt == 0), stop=(dt == 7))
            p.sca.wait("tS", tv)
            sv = p.sca.copy(v_g[:, ts(tt, 128 * GH)], ps_qkv[:, 0:128 * GH])
            p.ten.wait("sS", sv)
        iters = [(tt, hh) for tt in range(16) for hh in range(GH)]

        def stage_S(it, mi, kh):
            tt, hh = iters[it]
            par = it % 2
            base = 64 * mi
            pss = ps_s1 if mi == 0 else ps_s2
            kspan = (tt + 1) * 128
            lo = kh * 1024
            hi = min(kspan, lo + 1024)
            p.ten.wait("sS", st["exp" + str(mi)][par] if kh_first(it) != kh
                       else st["exp" + str(mi)][1 - par])
            for c in range(lo, hi, 512):
                cn = min(512, hi - c)
                tv = p.ten.matmul(
                    pss[:, c - lo:c - lo + cn],
                    qT_g[base:base + 64,
                         hh * T + tt * 128:hh * T + tt * 128 + 128],
                    kT_g[base:base + 64, hh * T + c:hh * T + c + cn],
                    start=True, stop=True)
            st["smm" + str(mi)][par] = tv

        def kh_first(it):
            tt, hh = iters[it]
            return tt // 8

        def vec_max(it, mi):
            tt, hh = iters[it]
            par = it % 2
            pss = ps_s1 if mi == 0 else ps_s2
            kspan = (tt + 1) * 128
            dlo = kspan - 128 - kh_first(it) * 1024
            p.vec.wait("tS", st["smm" + str(mi)][par])
            p.vec.tensor_tensor(pss[:, dlo:dlo + 128],
                                pss[:, dlo:dlo + 128], maskC[:], ALU.add)
            p.vec.reduce_max(st2["m"][par][:, mi:mi + 1],
                             pss[:, dlo:dlo + 128], AX.X)
            st["negm" + str(mi)][par] = p.vec.tensor_scalar(
                st2["nm"][par][:, mi:mi + 1], st2["m"][par][:, mi:mi + 1],
                -0.125, None, ALU.mult)

        def sca_exp(it, mi, kh):
            tt, hh = iters[it]
            par = it % 2
            pss = ps_s1 if mi == 0 else ps_s2
            kspan = (tt + 1) * 128
            lo = kh * 1024
            hi = min(kspan, lo + 1024)
            Em = (E1 if mi == 0 else E2)[par]
            p.sca.wait("tS", st["smm" + str(mi)][par])
            p.sca.wait("vS", st["negm" + str(mi)][par])
            if kh == kh_first(it):
                p.sca.wait("vS", st["comb"][par])
            sv = p.sca.activation(Em[:, lo:hi], pss[:, 0:hi - lo], AF.Exp,
                                  bias=st2["nm"][par][:, mi:mi + 1], scale=0.125,
                                  accum_out=st2["rp"][par][:, 2 * mi + kh:
                                                           2 * mi + kh + 1])
            st["exp" + str(mi)][par] = sv

        def vec_combine(it):
            tt, hh = iters[it]
            par = it % 2
            hfull = hh + GH * g
            kspan = (tt + 1) * 128
            p.vec.wait("sS", st["exp1"][par])
            tt_ = iters[it][0]
            if tt_ // 8 == 0:
                p.vec.tensor_copy(st2["rr"][par][:],
                                  st2["rp"][par][:, 0:4:2])
            else:
                p.vec.tensor_tensor(st2["rr"][par][:],
                                    st2["rp"][par][:, 0:4:2],
                                    st2["rp"][par][:, 1:4:2], ALU.add)
            p.vec.reciprocal(st2["rri"][par][:], st2["rr"][par][:])
            p.vec.tensor_scalar(st2["c2"][par][:], st2["rri"][par][:, 1:2],
                                lam_bc[:, hfull:hfull + 1], -1.0, ALU.mult,
                                ALU.mult)
            p.vec.wait("tS", st["trans"][par])
            p.vec.tensor_scalar(attn_sb[par][:, 0:kspan], E1[par][:, 0:kspan],
                                st2["rri"][par][:, 0:1], None, ALU.mult)
            st["comb"][par] = p.vec.scalar_tensor_tensor(
                attn_sb[par][:, 0:kspan], E2[par][:, 0:kspan],
                st2["c2"][par][:, 0:1], attn_sb[par][:, 0:kspan], ALU.mult,
                ALU.add)

        def ten_trans_pv(it):
            tt, hh = iters[it]
            par = it % 2
            p.ten.wait("vS", st["comb"][par])
            p.ten.wait("vS", st["onat"][1 - par])
            for kt in range(tt + 1):
                pb = pt_t[kt % 2]
                if kt % 2 == 0:
                    p.ten.wait("sS", st["tevac"][0])
                else:
                    p.ten.wait("vS", st["tevac"][1])
                tv = p.ten.transpose(pb[:], attn_sb[par][:, ts(kt, 128)],
                                     identB[:])
                if kt % 2 == 0:
                    p.sca.wait("tS", tv)
                    p.sca.wait("tS", st["pv"][1 - par])
                    st["tevac"][0] = p.sca.copy(attnT[par][:, ts(kt, 128)],
                                                pb[:])
                else:
                    p.vec.wait("tS", tv)
                    p.vec.wait("tS", st["pv"][1 - par])
                    st["tevac"][1] = p.vec.tensor_copy(
                        attnT[par][:, ts(kt, 128)], pb[:])
            p.ten.wait("sS", st["tevac"][0])
            p.ten.wait("vS", st["tevac"][1])
            for kt in range(tt + 1):
                tv = p.ten.matmul(ps_o[:], attnT[par][:, ts(kt, 128)],
                                  v_g[:, kt * 128 * GH + hh * 128:
                                      kt * 128 * GH + hh * 128 + 128],
                                  start=(kt == 0), stop=(kt == tt),
                                  skip_group_check=True)
            st["pv"][par] = tv
            p.sca.wait("tS", tv)
            p.sca.wait("vS", st["onat"][par])
            sv = p.sca.activation(sq128[:], ps_o[:], AF.Square,
                                  accum_out=st2["ssq"][par][:])
            p.vec.wait("sS", sv)
            vv = p.vec.tensor_scalar(st2["t0"][par][:], st2["ssq"][par][:],
                                     2.0 ** -40 / 128, 1e-5 * 2.0 ** -40,
                                     ALU.mult, ALU.add)
            p.sca.wait("vS", vv)
            p.sca.activation(st2["s0"][par][:], st2["t0"][par][:], AF.Ln)
            sv = p.sca.activation(st2["f"][par][:], st2["s0"][par][:], AF.Exp,
                                  scale=-0.5)
            p.vec.wait("sS", sv)
            vv = p.vec.tensor_scalar(onat[par][:], ps_o[:],
                                     st2["f"][par][:, 0:1], 2.0 ** -20,
                                     ALU.mult, ALU.mult)
            st["onat"][par] = vv

        def ten_wo(it):
            tt, hh = iters[it]
            par = it % 2
            p.ten.wait("vS", st["onat"][par])
            p.ten.wait("sS", st["tevac"][0])
            tv = p.ten.transpose(pt_t[0][:], onat[par][:], identB[:])
            p.sca.wait("tS", tv)
            p.sca.wait("tS", wo_mm_last[0])
            sv = p.sca.copy(oT_all[:, (hh * 16 + tt) * 128:
                                   (hh * 16 + tt) * 128 + 128], pt_t[0][:])
            st["tevac"][0] = sv

        def wo_phase():
            ps_wo = ps_s1
            p.ten.wait("sS", p.val("sS"))
            vv = 0
            for tt in range(16):
                for half in range(2):
                    for hh in range(GH):
                        tv = p.ten.matmul(
                            ps_wo[:, ts(half, 512)],
                            oT_all[:, (hh * 16 + tt) * 128:
                                   (hh * 16 + tt) * 128 + 128],
                            wo_g[:, hh * 1024 + half * 512:
                                 hh * 1024 + (half + 1) * 512],
                            start=(hh == 0), stop=(hh == GH - 1),
                            skip_group_check=True)
                p.vec.wait("tS", tv)
                for half in range(2):
                    dst = acc[:, tt * 1024 + half * 512:
                              tt * 1024 + (half + 1) * 512]
                    if g == 0:
                        vv = p.vec.tensor_copy(dst, ps_wo[:, ts(half, 512)])
                    else:
                        vv = p.vec.tensor_tensor(dst, ps_wo[:, ts(half, 512)],
                                                 dst, ALU.add)
                p.ten.wait("vS", vv)
            wo_mm_last[0] = p.val("tS")

        st = {nm: [0, 0] for nm in
              ("smm0", "smm1", "negm0", "negm1", "exp0", "exp1", "comb",
               "trans", "tevac", "pv", "onat")}
        nit = len(iters)
        for it in range(nit):
            khf = kh_first(it)
            stage_S(it, 0, khf)
            vec_max(it, 0)
            sca_exp(it, 0, khf)
            stage_S(it, 1, khf)
            vec_max(it, 1)
            sca_exp(it, 1, khf)
            if khf == 1:
                stage_S(it, 0, 0)
                sca_exp(it, 0, 0)
                stage_S(it, 1, 0)
                sca_exp(it, 1, 0)
            if it >= 1:
                ten_trans_pv(it - 1)
                st["trans"][(it - 1) % 2] = st["pv"][(it - 1) % 2]
                ten_wo(it - 1)
            vec_combine(it)
        ten_trans_pv(nit - 1)
        st["trans"][(nit - 1) % 2] = st["pv"][(nit - 1) % 2]
        ten_wo(nit - 1)
        wo_phase()


def _l0_ffn(p, es, io, b, x, x1, xnT_h, attn_h):
    wg_sb = p.sb(es, "wg_sb", [128, 8 * 3072], BF16)
    wu_sb = p.sb(es, "wu_sb", [128, 8 * 3072], BF16)
    wd_sb = p.sb(es, "wd_sb", [128, 24 * 1024], BF16)
    xnt_c = p.sb(es, "xnt_c", [128, 8 * 512], BF16)
    h_full = p.sb(es, "h_full", [128, 24 * 512], BF16)
    ps_g2 = [p.ps(es, f"ps_g{i}", [128, 512], F32) for i in range(2)]
    ps_u2 = [p.ps(es, f"ps_u{i}", [128, 512], F32) for i in range(2)]
    ps_d = [p.ps(es, f"ps_d{i}", [128, 512], F32) for i in range(4)]
    g_sb2 = [p.sb(es, f"g_sb{i}", [128, 512], BF16) for i in range(2)]
    u_sb2 = [p.sb(es, f"u_sb{i}", [128, 512], BF16) for i in range(2)]
    at2 = [p.sb(es, f"at2_{i}", [128, 1024], F32) for i in range(2)]
    x2b = [p.sb(es, f"x2b_{i}", [128, 1024], F32) for i in range(2)]
    o_t = p.sb(es, "o_t", [128, 1024], F32)

    p.gps.dma(wg_sb.rearrange("p (a f) -> p a f", a=8),
              io["wgT"].rearrange("(a p) f -> p a f", p=128), "ldg")
    p.gps.dma(wu_sb.rearrange("p (a f) -> p a f", a=8),
              io["wuT"].rearrange("(a p) f -> p a f", p=128), "ldg")
    p.gps.dma(wd_sb.rearrange("p (a f) -> p a f", a=24),
              io["wdT"].rearrange("(a p) f -> p a f", p=128), "ldg")
    wv_ = p.val("ldg")
    p.ten.wait("ldg", wv_)
    xat_used = [0, 0]
    for tc in range(4):
        p.sync.wait("tS", p.val("tS"))
        p.vec.wait("tS", p.val("tS"))
        for tti in range(4):
            tt = 4 * tc + tti
            lvx = p.sync.dma(
                xnt_c.rearrange("p (a t) -> p a t", a=8)[:, :,
                                                         tti * 128:(tti + 1) * 128],
                xnT_h[b, tt].rearrange("a p t -> p a t"), "ldf")
        p.ten.wait("ldf", lvx)
        guev = [0, 0]
        hfv = [0, 0]
        for ht in range(24):
            sl = ht % 2
            p.ten.wait("sS", guev[sl])
            for dt in range(8):
                p.ten.matmul(ps_g2[sl][:],
                             wg_sb[:, dt * 3072 + ht * 128:
                                   dt * 3072 + ht * 128 + 128],
                             xnt_c[:, ts(dt, 512)],
                             start=(dt == 0), stop=(dt == 7),
                             skip_group_check=True)
            for dt in range(8):
                tv = p.ten.matmul(ps_u2[sl][:],
                                  wu_sb[:, dt * 3072 + ht * 128:
                                        dt * 3072 + ht * 128 + 128],
                                  xnt_c[:, ts(dt, 512)],
                                  start=(dt == 0), stop=(dt == 7),
                                  skip_group_check=True)
            p.sca.wait("tS", tv)
            p.sca.wait("vS", hfv[sl])
            p.sca.activation(g_sb2[sl][:], ps_g2[sl][:], AF.Silu,
                             scale=1.0 / 127.0)
            guev[sl] = p.sca.activation(u_sb2[sl][:], ps_u2[sl][:], AF.Copy,
                                        scale=1.0 / 16129.0)
            p.vec.wait("sS", guev[sl])
            hfv[sl] = p.vec.tensor_tensor(h_full[:, ts(ht, 512)], g_sb2[sl][:],
                                          u_sb2[sl][:], ALU.mult)
        p.ten.wait("vS", p.val("vS"))
        for st4 in range(4):
            sl = st4 % 2
            tt = 4 * tc + st4
            sem = "ldfA" if sl == 0 else "ldfB"
            p.sync.wait("vS", xat_used[sl])
            p.sync.dma(at2[sl][:], attn_h[b, ts(tt, 128), :], sem)
            lv2 = p.sync.dma(x2b[sl][:], x[b, ts(tt, 128), :], sem)
            p.vec.wait("st", p.val("st"))
            p.vec.wait(sem, lv2)
            for half in range(2):
                pd = ps_d[st4]
                for ht in range(24):
                    tv = p.ten.matmul(pd[:], h_full[:, ht * 512 + st4 * 128:
                                                    ht * 512 + st4 * 128 + 128],
                                      wd_sb[:, ht * 1024 + half * 512:
                                            ht * 1024 + (half + 1) * 512],
                                      start=(ht == 0), stop=(ht == 23))
                p.vec.wait("tS", tv)
                dst = o_t[:, half * 512:(half + 1) * 512]
                p.vec.tensor_tensor(dst, pd[:],
                                    x2b[sl][:, half * 512:(half + 1) * 512],
                                    ALU.add)
                vv = p.vec.tensor_tensor(dst,
                                         at2[sl][:, half * 512:(half + 1) * 512],
                                         dst, ALU.add)
                p.ten.wait("vS", vv)
            xat_used[sl] = vv
            p.sync.wait("vS", vv)
            p.sync.wait("st", p.val("st"))
            p.sync.dma(x1[b, ts(tt, 128), :], o_t[:], "st")



CH = 512
NCH = T // CH


def build_mamba(p, io):
    nc = p.nc
    x1 = io["x1"]
    xn1T_h = io["xn1T_h"]
    x_h = io["x_h"]
    sz_h = io["sz_h"]
    maccT = io["maccT"]
    es_p = p.es

    identB = p.sb(es_p, "identB", [128, 128], BF16)
    selrep = p.sb(es_p, "selrep", [128, 2048], BF16)
    selsum = p.sb(es_p, "selsum", [128, 2048], BF16)
    seln = p.sb(es_p, "seln", [16, 128], BF16)
    p.sync.dma(identB[:], io["identB"], "ld")
    p.sync.dma(selrep[:], io["selrep"], "ld")
    p.sync.dma(selsum[:], io["selsum"], "ld")
    p.sync.dma(seln[:], io["seln"], "ld")
    p.ten.wait("ld", p.val("ld"))

    for b in range(NB):
        with contextlib.ExitStack() as es_b:
            xn1T = p.sb(es_b, "xn1T", [128, 8 * T], BF16)
            with contextlib.ExitStack() as es_ln:
                emit_ln(p, es_ln, x1, b, xn1T, xnT_h=None, identB=identB)
            p.sync.wait("sS", p.val("sS"))
            for dt in range(8):
                p.sync.dma(xn1T_h[b, dt], xn1T[:, ts(dt, T)], "st")
            p.barrier()
            _mamba_inproj(p, io, b, xn1T)
            p.barrier()
        for dir_ in ("f", "b"):
            with contextlib.ExitStack() as es_d:
                _mamba_dir(p, es_d, io, b, dir_, selrep, selsum, seln)
                p.barrier()
    p.sync.wait("st", p.val("st"))


def _mamba_inproj(p, io, b, xn1T):
    es = contextlib.ExitStack()
    with es:
        wbuf = [p.sb(es, f"ipw{i}", [128, 8 * 128], BF16) for i in range(2)]
        ob = [p.sb(es, f"ipo{i}", [128, 2048], BF16) for i in range(2)]
        psl = [p.ps(es, f"ps_ip{i}", [128, 2048], F32) for i in range(2)]
        wsem = ["ipwA", "ipwB"]
        stsem = ["ipstA", "ipstB"]
        wv = [0, 0]
        sdone = [0, 0]
        stv = [0, 0]
        mmdone = [0, 0]
        for ft in range(32):
            sl = ft % 2
            ps = psl[sl]
            p.gps.wait("tS", mmdone[sl])
            wv[sl] = p.gps.dma(wbuf[sl].rearrange("p (a w) -> p a w", a=8),
                               io["inpT"].rearrange("(a p) f -> p a f", p=128)[
                                   :, :, ts(ft, 128)], wsem[sl])
            p.ten.wait(wsem[sl], wv[sl])
            p.ten.wait("sS", sdone[sl])
            for ch in range(4):
                for dt in range(8):
                    tv = p.ten.matmul(ps[:, ts(ch, 512)], wbuf[sl][:, ts(dt, 128)],
                                      xn1T[:, dt * T + ch * 512:
                                           dt * T + (ch + 1) * 512],
                                      start=(dt == 0), stop=(dt == 7))
            mmdone[sl] = tv
            p.sca.wait("tS", tv)
            p.sca.wait(stsem[sl], stv[sl])
            if ft < 16:
                sv = p.sca.copy(ob[sl][:], ps[:])
                p.sync.wait("sS", sv)
                stv[sl] = p.sync.dma(io["x_h"][b, ft], ob[sl][:], stsem[sl])
            else:
                sv = p.sca.activation(ob[sl][:], ps[:], AF.Silu)
                p.sync.wait("sS", sv)
                stv[sl] = p.sync.dma(io["sz_h"][b, ft - 16], ob[sl][:], stsem[sl])
            sdone[sl] = sv
        p.sync.wait(stsem[0], stv[0])
        p.sync.wait(stsem[1], stv[1])


def _mamba_dir(p, es, io, b, dir_, selrep, selsum, seln):
    sfx = dir_
    rev = (dir_ == "b")
    xprojT = p.sb(es, "xprojT", [128, 16 * 128], BF16)
    dtpT = p.sb(es, "dtpT", [64, 2048], BF16)
    convw = p.sb(es, "convw", [128, 16 * 4], F32)
    convb = p.sb(es, "convb", [128, 16], F32)
    dtb = p.sb(es, "dtb", [128, 16], F32)
    Dv = p.sb(es, "Dv", [128, 16], F32)
    alog = p.sb(es, "alog", [128, 256], F32)
    Acol = p.sb(es, "Acol", [128, 256], F32)
    outpT = p.sb(es, "outpT", [128, 16 * 1024], BF16)
    p.gps.dma(xprojT.rearrange("p (a w) -> p a w", a=16),
              io["xprojT_" + sfx].rearrange("(a p) f -> p a f", p=128), "ldg")
    p.gps.dma(outpT.rearrange("p (a f) -> p a f", a=16),
              io["outpT"].rearrange("(a p) f -> p a f", p=128), "ldg")
    p.gps.dma(dtpT[:], io["dtpT_" + sfx], "ldg")
    p.sync.dma(convw.rearrange("p (a t) -> p a t", a=16),
               io["convw_" + sfx], "ld")
    p.sync.dma(convb[:], io["convb_" + sfx], "ld")
    p.sync.dma(dtb[:], io["dtb_" + sfx], "ld")
    p.sync.dma(Dv[:], io["D_" + sfx], "ld")
    p.sync.dma(alog[:], io["alog_" + sfx], "ld")
    ldv, lgv = p.val("ld"), p.val("ldg")
    p.sca.wait("ld", ldv)
    sv = p.sca.activation(Acol[:], alog[:], AF.Exp)
    p.vec.wait("sS", sv)
    p.vec.tensor_scalar(Acol[:], Acol[:], -1.0, None, ALU.mult)

    state = p.sb(es, "state", [128, 256], F32)
    p.vec.memset(state[:], 0.0)
    halo = p.sb(es, "halo", [128, 16 * 4], BF16)
    p.vec.memset(halo[:], 0.0)

    W = p.sb(es, "Wc", [128, 16 * (CH + 3)], BF16)
    tmp_r = [p.sb(es, f"tmpr{i}", [128, CH + 3], BF16) for i in range(2)]
    u_ch = p.sb(es, "u_ch", [128, 16 * CH], BF16)
    delta_ch = p.sb(es, "delta_ch", [128, 16 * CH], BF16)
    du_ch = p.sb(es, "du_ch", [128, 16 * CH], BF16)
    sz_ch = p.sb(es, "sz_ch", [128, 16 * CH], BF16)
    yg = p.sb(es, "yg", [128, 16 * CH], BF16)
    xdbl_dt = p.sb(es, "xdbl_dt", [64, CH], BF16)
    B_sb = p.sb(es, "B_sb", [16, CH], BF16)
    C_sb = p.sb(es, "C_sb", [16, CH], BF16)
    B_rep = p.sb(es, "B_rep", [128, CH], BF16)
    C_rep = p.sb(es, "C_rep", [128, CH], BF16)
    ftmp = p.sb(es, "ftmp", [128, CH], F32)
    ftmp2 = p.sb(es, "ftmp2", [128, CH], F32)
    ftmp3 = p.sb(es, "ftmp3", [128, CH], F32)
    dA_sb = [p.sb(es, f"dA{i}", [128, CH], F32) for i in range(2)]
    du_r = [p.sb(es, f"dur{i}", [128, CH], BF16) for i in range(2)]
    bb_sb = [p.sb(es, f"bb{i}", [128, CH], BF16) for i in range(2)]
    h_sb = [p.sb(es, f"hsb{i}", [128, CH], BF16) for i in range(2)]
    hC = [p.sb(es, f"hC{i}", [128, CH], BF16) for i in range(2)]
    mo_t = p.sb(es, "mo_t", [128, CH], F32)
    ps_a2 = [p.ps(es, f"ps_a{i}", [128, CH], F32) for i in range(2)]
    ps_b2 = [p.ps(es, f"ps_b{i}", [128, CH], F32) for i in range(2)]
    ps_a = ps_a2[0]
    ps_b = ps_b2[0]
    ps_y = p.ps(es, "ps_y", [128, CH], F32)
    ps_op = p.ps(es, "ps_op", [128, CH], F32)

    p.ten.wait("ldg", lgv)
    p.vec.wait("ld", ldv)

    for c in range(NCH):
        t0 = c * CH if not rev else T - CH * (c + 1)
        p.sync.wait("vS", p.val("vS"))
        if not rev:
            for dt in range(16):
                wsl = W[:, dt * (CH + 3):(dt + 1) * (CH + 3)]
                if c == 0:
                    p.vec.memset(wsl[:, 0:3], 0.0)
                else:
                    p.vec.tensor_copy(wsl[:, 0:3], halo[:, ts(dt, 4)][:, 0:3])
            vh = p.val("vS")
            p.sync.wait("vS", vh)
            lv = 0
            for dt in range(16):
                wsl = W[:, dt * (CH + 3):(dt + 1) * (CH + 3)]
                lv = p.sync.dma(wsl[:, 3:CH + 3],
                                io["x_h"][b, dt, :, t0:t0 + CH], "ldx")
            p.vec.wait("ldx", lv)
        else:
            sl_done = [0, 0]
            for dt in range(16):
                sl = dt % 2
                wsl = W[:, dt * (CH + 3):(dt + 1) * (CH + 3)]
                p.sync.wait("vS", sl_done[sl])
                if c == 0:
                    lv = p.sync.dma(tmp_r[sl][:, 0:CH],
                                    io["x_h"][b, dt, :, t0:t0 + CH],
                                    "ldxA" if sl == 0 else "ldxB")
                    p.vec.wait("ldxA" if sl == 0 else "ldxB", lv)
                    p.vec.memset(tmp_r[sl][:, CH:CH + 3], 0.0)
                else:
                    lv = p.sync.dma(tmp_r[sl][:, 0:CH + 3],
                                    io["x_h"][b, dt, :, t0:t0 + CH + 3],
                                    "ldxA" if sl == 0 else "ldxB")
                    p.vec.wait("ldxA" if sl == 0 else "ldxB", lv)
                sl_done[sl] = p.vec.tensor_copy(wsl[:], tmp_r[sl][:, ::-1])
        for dt in range(16):
            wsl = W[:, dt * (CH + 3):(dt + 1) * (CH + 3)]
            usl = u_ch[:, ts(dt, CH)]
            p.vec.tensor_scalar(ftmp[:], wsl[:, 3:CH + 3],
                                convw[:, dt * 4 + 3:dt * 4 + 4], None, ALU.mult)
            for tap in range(3):
                p.vec.scalar_tensor_tensor(
                    ftmp[:], wsl[:, 2 - tap:CH + 2 - tap],
                    convw[:, dt * 4 + 2 - tap:dt * 4 + 3 - tap], ftmp[:],
                    ALU.mult, ALU.add)
            p.vec.tensor_scalar(usl, ftmp[:], convb[:, dt:dt + 1], None, ALU.add)
            p.vec.tensor_copy(halo[:, ts(dt, 4)][:, 0:3], wsl[:, CH:CH + 3])
        p.ten.wait("vS", p.val("vS"))
        p.ten.wait("sS", p.val("sS"))
        for dt in range(16):
            tv = p.ten.matmul(ps_a[:], xprojT[:, ts(dt, 128)],
                              u_ch[:, ts(dt, CH)], start=(dt == 0),
                              stop=(dt == 15))
        p.sca.wait("tS", tv)
        sv = p.sca.copy(xdbl_dt[:], ps_a[64:128, :])
        sv = p.sca.copy(B_sb[:], ps_a[0:16, :])
        sv = p.sca.copy(C_sb[:], ps_a[32:48, :])
        p.ten.wait("sS", sv)
        tv = p.ten.matmul(ps_b[:], seln[:], B_sb[:], start=True, stop=True)
        p.vec.wait("tS", tv)
        vv = p.vec.tensor_copy(B_rep[:], ps_b[:])
        p.ten.wait("vS", vv)
        tv = p.ten.matmul(ps_b[:], seln[:], C_sb[:], start=True, stop=True)
        p.vec.wait("tS", tv)
        vv = p.vec.tensor_copy(C_rep[:], ps_b[:])
        p.ten.wait("vS", vv)
        for dt in range(16):
            tv = p.ten.matmul(ps_b[:], dtpT[:, ts(dt, 128)], xdbl_dt[:],
                              start=True, stop=True)
            p.vec.wait("tS", tv)
            vv = p.vec.tensor_scalar(ftmp[:], ps_b[:], dtb[:, dt:dt + 1], None,
                                     ALU.add)
            p.ten.wait("vS", vv)
            p.vec.tensor_scalar(ftmp[:], ftmp[:], 1e-5, 1.0, ALU.max, ALU.min)
            p.vec.tensor_scalar(ftmp[:], ftmp[:], dtb[:, dt:dt + 1], None,
                                ALU.add)
            p.sca.wait("vS", p.val("vS"))
            sv = p.sca.activation(ftmp2[:], ftmp[:], AF.Abs)
            sv = p.sca.activation(ftmp2[:], ftmp2[:], AF.Exp, scale=-1.0)
            sv = p.sca.activation(ftmp2[:], ftmp2[:], AF.Ln, bias=1.0)
            sv = p.sca.activation(ftmp3[:], ftmp[:], AF.Relu)
            p.vec.wait("sS", sv)
            dsl = delta_ch[:, ts(dt, CH)]
            usl = u_ch[:, ts(dt, CH)]
            p.vec.tensor_tensor(dsl, ftmp3[:], ftmp2[:], ALU.add)
            p.vec.tensor_scalar(usl, usl, -10.0, 10.0, ALU.max, ALU.min)
            p.vec.tensor_tensor(du_ch[:, ts(dt, CH)], dsl, usl, ALU.mult)
        if not rev:
            p.sync.wait("vS", p.val("vS"))
            for dt in range(16):
                lv = p.sync.dma(sz_ch[:, ts(dt, CH)],
                                io["sz_h"][b, dt, :, t0:t0 + CH], "ldsz")
            p.vec.wait("ldsz", lv)
        else:
            sl_done = [0, 0]
            p.sync.wait("vS", p.val("vS"))
            for dt in range(16):
                sl = dt % 2
                p.sync.wait("vS", sl_done[sl])
                sem = "ldszA" if sl == 0 else "ldszB"
                lv = p.sync.dma(tmp_r[sl][:, 0:CH],
                                io["sz_h"][b, dt, :, t0:t0 + CH], sem)
                p.vec.wait(sem, lv)
                sl_done[sl] = p.vec.tensor_copy(sz_ch[:, ts(dt, CH)],
                                                tmp_r[sl][:, CH - 1::-1])
        p.ten.wait("vS", p.val("vS"))
        expv = [0, 0]
        bbv = [0, 0]
        scanv = [0, 0]
        hCv = [0, 0]
        ymmv = [0, 0]
        replv = [0, 0]
        durv = [0, 0]
        stcv = [0, 0]

        def emit_repl(j, i):
            s = i % 2
            kidx = j * 16 + i
            p.ten.wait("sS", expv[s])
            p.ten.matmul(ps_a2[s][:], selrep[:, ts(i, 128)],
                         delta_ch[:, ts(j, CH)], start=True, stop=True)
            p.ten.wait("sS", durv[s])
            replv[s] = p.ten.matmul(ps_b2[s][:], selrep[:, ts(i, 128)],
                                    du_ch[:, ts(j, CH)], start=True, stop=True)
            p.sca.wait("tS", replv[s] - 1)
            p.sca.wait("vS", scanv[s])
            expv[s] = p.sca.activation(dA_sb[s][:], ps_a2[s][:], AF.Exp,
                                       scale=Acol[:, kidx:kidx + 1])

        def emit_rest(j, i):
            s = i % 2
            kidx = j * 16 + i
            p.sca.wait("tS", replv[s])
            p.sca.wait("vS", bbv[s])
            durv[s] = p.sca.copy(du_r[s][:], ps_b2[s][:])
            p.vec.wait("sS", durv[s])
            bbv[s] = p.vec.tensor_tensor(bb_sb[s][:], du_r[s][:], B_rep[:],
                                         ALU.mult)
            p.vec.wait("sS", expv[s])
            p.vec.wait("sS", stcv[s])
            scanv[s] = p.vec.tensor_tensor_scan(
                h_sb[s][:], dA_sb[s][:], bb_sb[s][:],
                state[:, kidx:kidx + 1], ALU.mult, ALU.add)
            p.sca.wait("vS", scanv[s])
            stcv[s] = p.sca.copy(state[:, kidx:kidx + 1], h_sb[s][:, CH - 1:CH])
            p.vec.wait("tS", ymmv[s])
            hCv[s] = p.vec.tensor_tensor(hC[s][:], h_sb[s][:], C_rep[:],
                                         ALU.mult)
            p.ten.wait("vS", hCv[s])
            ymmv[s] = p.ten.matmul(ps_y[:], selsum[:, ts(i, 128)], hC[s][:],
                                   start=(i == 0), stop=(i == 15),
                                   skip_group_check=True)

        for j in range(16):
            emit_repl(j, 0)
            for i in range(16):
                if i + 1 < 16:
                    emit_repl(j, i + 1)
                emit_rest(j, i)
            p.vec.wait("tS", ymmv[15 % 2])
            p.vec.scalar_tensor_tensor(ftmp[:], u_ch[:, ts(j, CH)],
                                       Dv[:, j:j + 1], ps_y[:], ALU.mult,
                                       ALU.add)
            vv = p.vec.tensor_tensor(yg[:, ts(j, CH)], ftmp[:],
                                     sz_ch[:, ts(j, CH)], ALU.mult)
            p.ten.wait("vS", vv)
        for ft in range(8):
            for j in range(16):
                tv = p.ten.matmul(ps_op[:], outpT[:, j * 1024 + ft * 128:
                                                  j * 1024 + ft * 128 + 128],
                                  yg[:, ts(j, CH)], start=(j == 0),
                                  stop=(j == 15))
            p.vec.wait("tS", tv)
            src_ps = ps_op[:, ::-1] if rev else ps_op[:]
            if dir_ == "f":
                vv = p.vec.tensor_copy(mo_t[:], src_ps)
            else:
                p.sync.wait("mrw", p.val("mrw"))
                lv = p.sync.dma(mo_t[:], io["maccT"][b, ft, :, t0:t0 + CH],
                                "mrw")
                p.vec.wait("mrw", lv)
                vv = p.vec.tensor_tensor(mo_t[:], src_ps, mo_t[:], ALU.add)
            p.ten.wait("vS", vv)
            p.sync.wait("vS", vv)
            sv2 = p.sync.dma(io["maccT"][b, ft, :, t0:t0 + CH], mo_t[:], "mrw")
            p.vec.wait("mrw", sv2)
    p.sync.wait("mrw", p.val("mrw"))



def build_l1ffn(p, io):
    nc = p.nc
    es_p = p.es
    identB = p.sb(es_p, "identB", [128, 128], BF16)
    identF = p.sb(es_p, "identF", [128, 128], F32)
    maskC = p.sb(es_p, "maskC", [128, 128], F32)
    maskA = p.sb(es_p, "maskA", [128, 128], F32)
    G64 = p.sb(es_p, "G64", [128, 128], BF16)
    P64 = p.sb(es_p, "P64", [128, 128], BF16)
    ones128 = p.sb(es_p, "ones128", [128, 128], BF16)
    tab = {}
    for nm in ("CSq8", "SSq8", "CSk", "SSk"):
        tab[nm] = p.sb(es_p, nm, [128, T], BF16)
        p.gps.dma(tab[nm][:], io[nm], "ldtab")
    for nm, t_ in (("identB", identB), ("identF", identF), ("maskC", maskC),
                   ("maskA", maskA), ("G64", G64), ("P64", P64),
                   ("ones128", ones128)):
        p.sync.dma(t_[:], io[nm], "ld")
    for l in range(2):
        for nm in ("qsc", "ksc", "lng", "lnb", "b1a", "b1g", "b2"):
            b1col(p, io, nm, l, 0, 1)
    for e in (p.vec, p.sca, p.ten):
        e.wait("ld", p.val("ld"))
        e.wait("ldtab", p.val("ldtab"))

    for b in range(NB):
        with contextlib.ExitStack() as es_b:
            x_br = p.sb(es_b, "x_br", [128, 8 * T], F32)
            xtmp = p.sb(es_b, "xtmp", [128, T], BF16)
            for dt in range(8):
                lv = p.sync.dma(xtmp[:], io["xn1T_h"][b, dt], "ldf")
                p.vec.wait("ldf", lv)
                vv = p.vec.tensor_copy(x_br[:, ts(dt, T)], xtmp[:])
                p.sync.wait("vS", vv)
            for l in range(2):
                _l1f_attn(p, io, b, l, x_br, identB, maskC, maskA, G64, P64,
                          tab)
                p.barrier()
                _l1f_ffn(p, io, b, l, x_br, ones128)
                p.barrier()
            _l1f_out(p, es_b, io, b, x_br, identF)
            p.barrier()
    p.sync.wait("st", p.val("st"))


def _shift_cast(p, x_br, hb, dt):
    if dt < 4:
        return p.vec.tensor_copy(hb[:, ts(dt, T)], x_br[:, ts(dt, T)])
    p.vec.memset(hb[:, dt * T:dt * T + 1], 0.0)
    return p.vec.tensor_copy(hb[:, dt * T + 1:(dt + 1) * T],
                             x_br[:, dt * T:(dt + 1) * T - 1])


def _l1f_attn(p, io, b, l, x_br, identB, maskC, maskA, G64, P64, tab):
    es = contextlib.ExitStack()
    with es:
        hb = p.sb(es, "hb", [128, 8 * T], BF16)
        o_sb = p.sb(es, "o_sb", [128, 4 * T], BF16)
        wq_h = p.sb(es, "wq_h", [128, 8 * 64], BF16)
        wk_h = p.sb(es, "wk_h", [128, 8 * 64], BF16)
        wv_h = p.sb(es, "wv_h", [128, 8 * 64], BF16)
        wo_sb = p.sb(es, "wo_sb", [128, 4 * 1024], BF16)
        qT = p.sb(es, "qT", [64, T], BF16)
        kT = p.sb(es, "kT", [64, T], BF16)
        qrot = p.sb(es, "qrot", [64, T], BF16)
        krot = p.sb(es, "krot", [64, T], BF16)
        v_sb = p.sb(es, "v_sb", [128, 16 * 64], BF16)
        sq_h = p.sb(es, "sq_h", [64, T], BF16)
        f32a = p.sb(es, "f32a", [64, 512], F32)
        f32b = p.sb(es, "f32b", [64, 512], F32)
        E_sb = p.sb(es, "E_sb", [128, 640], BF16)
        attn_sb = p.sb(es, "attn_sb", [128, 640], BF16)
        attnT = p.sb(es, "attnT", [128, 640], BF16)
        stq = {nm: p.sb(es, "l1_" + nm, [128, 1], F32)
               for nm in ("m", "nm", "r", "ri")}
        ps_s = p.ps(es, "ps_s", [128, 1024], F32)
        ps_qf2 = [p.ps(es, f"ps_qf{i}", [128, 512], F32) for i in range(2)]
        ps_qf = ps_qf2[0]
        ps_q = ps_qf[0:64, :]
        ps_of = p.ps(es, "ps_of", [128, 128], F32)
        ps_o = ps_of[0:64, :]
        ps_wo = p.ps(es, "ps_wo", [128, 512], F32)
        pt_t = [p.ps(es, f"l1pt{i}", [128, 128], BF16) for i in range(2)]

        p.vec.wait("tS", p.val("tS"))
        for dt in range(8):
            vv = _shift_cast(p, x_br, hb, dt)
        p.ten.wait("vS", vv)
        p.gps.dma(wo_sb.rearrange("p (a f) -> p a f", a=4),
                  io["wo1"][l].rearrange("(a p) f -> p a f", p=128), "ldg")
        for h in range(8):
            p.gps.wait("tS", p.val("tS"))
            wq3 = io["wqkv"][l].rearrange("(a p) f -> p a f", p=128)
            for wdst, off in ((wq_h, 0), (wk_h, 512), (wv_h, 1024)):
                p.gps.dma(wdst.rearrange("p (a w) -> p a w", a=8),
                          wq3[:, :, off + h * 64:off + h * 64 + 64], "ldg")
            lgv = p.val("ldg")
            p.ten.wait("ldg", lgv)
            p.ten.wait("sS", p.val("sS"))
            qkvev = [0, 0]
            for ci, (wsb, dst, ch) in enumerate(
                    [(w_, d_, c_) for (w_, d_) in ((wq_h, qT), (wk_h, kT))
                     for c_ in range(4)]):
                sl = ci % 2
                p.ten.wait("sS", qkvev[sl])
                for dt in range(8):
                    tv = p.ten.matmul(ps_qf2[sl][0:64, :], wsb[:, ts(dt, 64)],
                                      hb[:, dt * T + ch * 512:
                                         dt * T + (ch + 1) * 512],
                                      start=(dt == 0), stop=(dt == 7))
                p.sca.wait("tS", tv)
                qkvev[sl] = p.sca.copy(dst[:, ts(ch, 512)], ps_qf2[sl][0:64, :])
            for tt in range(16):
                sl = tt % 2
                p.ten.wait("sS", qkvev[sl])
                for dt in range(8):
                    tv = p.ten.matmul(ps_qf2[sl][:, 0:64],
                                      hb[:, dt * T + tt * 128:
                                         dt * T + tt * 128 + 128],
                                      wv_h[:, ts(dt, 64)],
                                      start=(dt == 0), stop=(dt == 7))
                p.sca.wait("tS", tv)
                qkvev[sl] = p.sca.copy(v_sb[:, ts(tt, 64)], ps_qf2[sl][:, 0:64])
            p.ten.wait("sS", p.val("sS"))
            for (src, dst, scol, cs, ss) in (
                    (qT, qrot, "qsc", "CSq8", "SSq8"),
                    (kT, krot, "ksc", "CSk", "SSk")):
                p.sca.wait("tS", p.val("tS"))
                sv = p.sca.activation(sq_h[:], src[:], AF.Square)
                p.ten.wait("sS", sv)
                for ch in range(4):
                    tv = p.ten.matmul(ps_q[:], G64[0:64, 0:64],
                                      sq_h[:, ts(ch, 512)],
                                      start=True, stop=True)
                    p.vec.wait("tS", tv)
                    p.vec.tensor_scalar(f32a[:], ps_q[:], 1e-24, 3e38,
                                        ALU.max, ALU.min)
                    vv = p.vec.tensor_scalar(f32a[:], f32a[:], 1.0, None,
                                             ALU.mult)
                    p.sca.wait("vS", vv)
                    sv = p.sca.activation(f32a[:], f32a[:], AF.Sqrt)
                    p.vec.wait("sS", sv)
                    p.vec.reciprocal(f32b[:], f32a[:])
                    vv = p.vec.scalar_tensor_tensor(
                        sq_h[:, ts(ch, 512)], src[:, ts(ch, 512)],
                        b1col(p, io, scol, l, 0, 64), f32b[:], ALU.mult,
                        ALU.mult)
                    p.ten.wait("vS", vv)
                    tv = p.ten.matmul(ps_q[:], P64[0:64, 0:64],
                                      sq_h[:, ts(ch, 512)], start=True,
                                      stop=True)
                    p.vec.wait("tS", tv)
                    p.vec.tensor_tensor(f32a[:], sq_h[:, ts(ch, 512)],
                                        tab[cs][0:64, ts(ch, 512)], ALU.mult)
                    p.vec.tensor_tensor(f32b[:], ps_q[:],
                                        tab[ss][0:64, ts(ch, 512)], ALU.mult)
                    vv = p.vec.tensor_tensor(dst[:, ts(ch, 512)], f32a[:],
                                             f32b[:], ALU.add)
                    p.ten.wait("vS", vv)
                p.sca.wait("vS", p.val("vS"))
            for tt in range(16):
                kt0 = max(0, tt - 4)
                nkt = tt + 1 - kt0
                span = nkt * 128
                tv = p.ten.matmul(ps_s[:, 0:min(span, 512)],
                                  qrot[:, ts(tt, 128)],
                                  krot[:, kt0 * 128:kt0 * 128 + min(span, 512)],
                                  start=True, stop=True)
                if span > 512:
                    tv = p.ten.matmul(ps_s[:, 512:span],
                                      qrot[:, ts(tt, 128)],
                                      krot[:, kt0 * 128 + 512:kt0 * 128 + span],
                                      start=True, stop=True)
                p.vec.wait("tS", tv)
                if tt >= 4:
                    p.vec.tensor_tensor(ps_s[:, 0:128], ps_s[:, 0:128],
                                        maskA[:], ALU.add)
                p.vec.tensor_tensor(ps_s[:, span - 128:span],
                                    ps_s[:, span - 128:span], maskC[:], ALU.add)
                p.vec.reduce_max(stq["m"][:], ps_s[:, 0:span], AX.X)
                vv = p.vec.tensor_scalar(stq["nm"][:], stq["m"][:], -1.0, None,
                                         ALU.mult)
                p.sca.wait("vS", vv)
                sv = p.sca.activation(E_sb[:, 0:span], ps_s[:, 0:span], AF.Exp,
                                      bias=stq["nm"][:, 0:1],
                                      accum_out=stq["r"][:])
                p.ten.wait("sS", sv)
                p.vec.wait("sS", sv)
                p.vec.reciprocal(stq["ri"][:], stq["r"][:])
                p.vec.wait("tS", p.val("tS"))
                vv = p.vec.tensor_scalar(attn_sb[:, 0:span], E_sb[:, 0:span],
                                         stq["ri"][:, 0:1], None, ALU.mult)
                p.ten.wait("vS", vv)
                p.ten.wait("sS", p.val("sS"))
                for kt in range(nkt):
                    pb = pt_t[kt % 2]
                    if kt >= 2:
                        p.ten.wait("sS", p.val("sS") - 1)
                    tv = p.ten.transpose(pb[:], attn_sb[:, ts(kt, 128)],
                                         identB[:])
                    p.sca.wait("tS", tv)
                    sv = p.sca.copy(attnT[:, ts(kt, 128)], pb[:])
                p.ten.wait("sS", p.val("sS"))
                for kt in range(nkt):
                    tv = p.ten.matmul(ps_o[:],
                                      v_sb[:, (kt0 + kt) * 64:(kt0 + kt + 1) * 64],
                                      attnT[:, ts(kt, 128)],
                                      start=(kt == 0), stop=(kt == nkt - 1))
                p.sca.wait("tS", tv)
                sv = p.sca.copy(
                    o_sb[(h % 2) * 64:(h % 2) * 64 + 64,
                         (h // 2) * T + tt * 128:(h // 2) * T + tt * 128 + 128],
                    ps_o[:])
                p.ten.wait("sS", sv)
        p.ten.wait("ldg", p.val("ldg"))
        p.ten.wait("sS", p.val("sS"))
        for ft in range(8):
            for ch in range(4):
                for et in range(4):
                    tv = p.ten.matmul(
                        ps_wo[:],
                        wo_sb[:, et * 1024 + ft * 128:et * 1024 + ft * 128 + 128],
                        o_sb[:, et * T + ch * 512:et * T + (ch + 1) * 512],
                        start=(et == 0), stop=(et == 3))
                p.vec.wait("tS", tv)
                dst = x_br[:, ft * T + ch * 512:ft * T + (ch + 1) * 512]
                vv = p.vec.tensor_tensor(dst, ps_wo[:], dst, ALU.add)
                p.ten.wait("vS", vv)


def _l1f_ffn(p, io, b, l, x_br, ones128):
    es = contextlib.ExitStack()
    HID = 2730
    PAIRS = [(j * 128, min(128, HID - j * 128)) for j in range((HID + 127) // 128)]
    with es:
        xn2 = p.sb(es, "xn2", [128, 8 * T], BF16)
        with contextlib.ExitStack() as es2:
            hb2 = p.sb(es2, "hb2", [128, 8 * T], BF16)
            hsq = p.sb(es2, "hsq", [128, 8 * T], BF16)
            m_bc = p.sb(es2, "m_bc", [128, 512], F32)
            r_bc = p.sb(es2, "r_bc", [128, 512], F32)
            fa = p.sb(es2, "fa", [128, 512], F32)
            ps_1 = p.ps(es2, "ps_1", [128, 512], F32)
            ps_2 = p.ps(es2, "ps_2", [128, 512], F32)
            p.vec.wait("tS", p.val("tS"))
            for dt in range(8):
                vv = _shift_cast(p, x_br, hb2, dt)
                p.sca.wait("vS", vv)
                sv = p.sca.activation(hsq[:, ts(dt, T)], hb2[:, ts(dt, T)],
                                      AF.Square)
            p.ten.wait("sS", p.val("sS"))
            p.ten.wait("vS", p.val("vS"))
            for ch in range(4):
                for dt in range(8):
                    p.ten.matmul(ps_1[:], ones128[:],
                                 hb2[:, dt * T + ch * 512:dt * T + (ch + 1) * 512],
                                 start=(dt == 0), stop=(dt == 7),
                                 skip_group_check=True)
                for dt in range(8):
                    tv = p.ten.matmul(ps_2[:], ones128[:],
                                      hsq[:, dt * T + ch * 512:
                                          dt * T + (ch + 1) * 512],
                                      start=(dt == 0), stop=(dt == 7),
                                      skip_group_check=True)
                p.vec.wait("tS", tv)
                p.vec.tensor_scalar(m_bc[:], ps_1[:], 1.0 / D, None, ALU.mult)
                p.vec.tensor_scalar(r_bc[:], ps_2[:], 1.0 / D, None, ALU.mult)
                p.vec.tensor_tensor(fa[:], m_bc[:], m_bc[:], ALU.mult)
                p.vec.tensor_tensor(fa[:], r_bc[:], fa[:], ALU.subtract)
                vv = p.vec.tensor_scalar(fa[:], fa[:], 1e-5, None, ALU.add)
                p.ten.wait("vS", vv)
                p.sca.wait("vS", vv)
                sv = p.sca.activation(fa[:], fa[:], AF.Sqrt)
                p.vec.wait("sS", sv)
                p.vec.reciprocal(r_bc[:], fa[:])
                for dt in range(8):
                    sl = slice(dt * T + ch * 512, dt * T + (ch + 1) * 512)
                    p.vec.tensor_tensor(fa[:], hb2[:, sl], m_bc[:], ALU.subtract)
                    p.vec.tensor_tensor(fa[:], fa[:], r_bc[:], ALU.mult)
                    vv = p.vec.tensor_scalar(
                        xn2[:, sl], fa[:], b1col(p, io, "lng", l, dt, 128),
                        b1col(p, io, "lnb", l, dt, 128), ALU.mult, ALU.add)
        p.barrier()
        with contextlib.ExitStack() as es3:
            hh_sb = p.sb(es3, "hh_sb", [128, len(PAIRS) * 512], BF16)
            w1a = [p.sb(es3, f"w1a{i}", [128, 8 * 128], BF16) for i in range(2)]
            w1g = [p.sb(es3, f"w1g{i}", [128, 8 * 128], BF16) for i in range(2)]
            w2_sb = p.sb(es3, "w2_sb", [128, len(PAIRS) * 1024], BF16)
            ha = p.sb(es3, "ha", [128, 512], BF16)
            hg = p.sb(es3, "hg", [128, 512], BF16)
            ps_a = p.ps(es3, "ps_a", [128, 512], F32)
            ps_g = p.ps(es3, "ps_g", [128, 512], F32)
            ps_w2 = p.ps(es3, "ps_w2", [128, 512], F32)
            for j, (off, sz) in enumerate(PAIRS):
                p.gps.dma(w2_sb[0:sz, ts(j, 1024)],
                          io["w2"][l, off:off + sz, :], "ldg")
            w1sem = ["w1s0", "w1s1"]
            w1lv = [0, 0]
            w1used = [0, 0]
            npair = len(PAIRS)

            w13 = io["w1"][l].rearrange("(a p) f -> p a f", p=128)

            def w1_prefetch(ch, jj):
                sl = jj % 2
                off, sz = PAIRS[jj]
                p.gps.wait("tS", w1used[sl])
                p.gps.dma(w1a[sl].rearrange("p (a w) -> p a w", a=8)[:, :, 0:sz],
                          w13[:, :, off:off + sz], w1sem[sl])
                w1lv[sl] = p.gps.dma(
                    w1g[sl].rearrange("p (a w) -> p a w", a=8)[:, :, 0:sz],
                    w13[:, :, HID + off:HID + off + sz], w1sem[sl])

            evacv = [0, 0]
            for ch in range(4):
                if ch == 0:
                    w1_prefetch(0, 0)
                for j, (off, sz) in enumerate(PAIRS):
                    sl = j % 2
                    nj = j + 1
                    p.ten.wait(w1sem[sl], w1lv[sl])
                    p.ten.wait("sS", p.val("sS"))
                    p.ten.wait("vS", p.val("vS"))
                    for dt in range(8):
                        p.ten.matmul(ps_a[0:sz, :],
                                     w1a[sl][:, ts(dt, 128)][:, 0:sz],
                                     xn2[:, dt * T + ch * 512:
                                         dt * T + (ch + 1) * 512],
                                     start=(dt == 0), stop=(dt == 7),
                                     skip_group_check=True)
                    for dt in range(8):
                        tv = p.ten.matmul(ps_g[0:sz, :],
                                          w1g[sl][:, ts(dt, 128)][:, 0:sz],
                                          xn2[:, dt * T + ch * 512:
                                              dt * T + (ch + 1) * 512],
                                          start=(dt == 0), stop=(dt == 7),
                                          skip_group_check=True)
                    w1used[sl] = tv
                    if nj < npair:
                        w1_prefetch(ch, nj)
                    elif ch < 3:
                        w1_prefetch(ch + 1, 0)
                    p.vec.wait("tS", tv)
                    vv = p.vec.tensor_scalar(ha[0:sz, :], ps_a[0:sz, :],
                                             b1col(p, io, "b1a", l, j, sz),
                                             None, ALU.add)
                    p.sca.wait("tS", tv)
                    sv = p.sca.activation(hg[0:sz, :], ps_g[0:sz, :], AF.Gelu,
                                          bias=b1col(p, io, "b1g", l, j, sz))
                    p.vec.wait("sS", sv)
                    vv = p.vec.tensor_tensor(hh_sb[0:sz, ts(j, 512)],
                                             ha[0:sz, :], hg[0:sz, :], ALU.mult)
                    p.ten.wait("vS", vv)
                p.ten.wait("ldg", p.val("ldg"))
                for ft in range(8):
                    for j, (off, sz) in enumerate(PAIRS):
                        tv = p.ten.matmul(
                            ps_w2[:], w2_sb[0:sz, j * 1024 + ft * 128:
                                            j * 1024 + ft * 128 + 128],
                            hh_sb[0:sz, ts(j, 512)],
                            start=(j == 0), stop=(j == len(PAIRS) - 1))
                    p.vec.wait("tS", tv)
                    dst = x_br[:, ft * T + ch * 512:ft * T + (ch + 1) * 512]
                    vv = p.vec.scalar_tensor_tensor(
                        dst, ps_w2[:], b1col(p, io, "b2", l, ft, 128), dst,
                        ALU.add, ALU.add)
                    p.ten.wait("vS", vv)


def b1col(p, io, name, l, j, sz):
    key = ("_b1", name, l)
    cache = p.__dict__.setdefault("_colcache", {})
    if key not in cache:
        nt = io[name].shape[2]
        t = p.sb(p.es, f"col_{name}{l}", [128, nt], F32)
        p.sync.dma(t[:], io[name][l], "ld")
        cache[key] = t
    return cache[key][0:sz, j:j + 1]


def _l1f_out(p, es_b, io, b, x_br, identF):
    with contextlib.ExitStack() as es:
        mt = p.sb(es, "mt", [128, T], F32)
        out_t = p.sb(es, "out_t", [128, D], F32)
        x1_t = p.sb(es, "x1_t", [128, D], F32)
        ps_f = [p.ps(es, f"opf{i}", [128, 128], F32) for i in range(2)]
        for dt in range(8):
            p.sync.wait("vS", p.val("vS"))
            lv = p.sync.dma(mt[:], io["maccT"][b, dt], "ldf")
            p.vec.wait("ldf", lv)
            vv = p.vec.tensor_tensor(x_br[:, ts(dt, T)], mt[:],
                                     x_br[:, ts(dt, T)], ALU.add)
        p.ten.wait("vS", p.val("vS"))
        for tt in range(16):
            p.sync.wait("st", p.val("st"))
            lv = p.sync.dma(x1_t[:], io["x1"][b, ts(tt, 128), :], "ldx1")
            p.ten.wait("sS", p.val("sS"))
            p.ten.wait("vS", p.val("vS"))
            p.ten.wait("st", p.val("st"))
            for dt in range(8):
                pb = ps_f[dt % 2]
                if dt >= 2:
                    p.ten.wait("sS", p.val("sS") - 1)
                tv = p.ten.transpose(pb[:], x_br[:, dt * T + tt * 128:
                                                 dt * T + tt * 128 + 128],
                                     identF[:])
                p.sca.wait("tS", tv)
                sv = p.sca.copy(out_t[:, ts(dt, 128)], pb[:])
            p.vec.wait("sS", p.val("sS"))
            p.vec.wait("ldx1", lv)
            vv = p.vec.tensor_tensor(out_t[:], x1_t[:], out_t[:], ALU.add)
            p.sync.wait("vS", vv)
            p.sync.dma(io["x2"][b, ts(tt, 128), :], out_t[:], "st")




_L0_NAMES = ["wqT", "wkT", "wvT", "woT", "lq1T", "lk1T", "lq2T", "lk2T", "rms",
             "wgT", "wuT", "wdT", "identB", "maskC", "ones64"]


def _declare(nc, io, cw):
    for nm, arr in cw.items():
        dt = {np.dtype(np.float32): F32, np.dtype(np.int32): I32,
              np.dtype(BF): BF16}[arr.dtype]
        io[nm] = nc.dram_tensor(nm, list(arr.shape), dt, kind="ExternalInput").ap()


def run_l0(pool, xfull):
    cw = {nm: pool[nm] for nm in _L0_NAMES}
    with contextlib.ExitStack() as es:
        nc = bass.Bass("TRN2", target_bir_lowering=False)
        io = {}
        io["x"] = nc.dram_tensor("x", [NB, T, D], F32, kind="ExternalInput").ap()
        io["x1"] = nc.dram_tensor("x1", [NB, T, D], F32,
                                  kind="ExternalOutput").ap()
        io["xnT_h"] = nc.dram_tensor("xnT_h", [NB, 16, 8, 128, 128], BF16,
                                     kind="Internal").ap()
        io["attn_h"] = nc.dram_tensor("attn_h", [NB, T, D], F32,
                                      kind="Internal").ap()
        _declare(nc, io, cw)
        p = P(nc, es)
        build_l0(p, io)
        p.finalize()
        in_maps = []
        for c in range(N_CORE):
            m = dict(cw)
            m["x"] = np.ascontiguousarray(xfull[2 * c:2 * c + 2])
            in_maps.append(m)
        res = run_bass_kernel_spmd(nc, in_maps, core_ids=list(range(N_CORE)))
    return np.concatenate([r["x1"] for r in res.results], axis=0)


_MB_NAMES = (["inpT", "outpT", "identB", "selrep", "selsum", "seln"] +
             [f"{nm}_{s}" for s in ("f", "b")
              for nm in ("xprojT", "dtpT", "convw", "convb", "dtb", "alog", "D")])


def run_mamba(pool, x1full):
    cw = {nm: pool[nm] for nm in _MB_NAMES}
    with contextlib.ExitStack() as es:
        nc = bass.Bass("TRN2", target_bir_lowering=False)
        io = {}
        io["x1"] = nc.dram_tensor("x1", [NB, T, D], F32,
                                  kind="ExternalInput").ap()
        io["xn1T_h"] = nc.dram_tensor("xn1T_h", [NB, 8, 128, T], BF16,
                                      kind="ExternalOutput").ap()
        io["maccT"] = nc.dram_tensor("maccT", [NB, 8, 128, T], F32,
                                     kind="ExternalOutput").ap()
        io["x_h"] = nc.dram_tensor("x_h", [NB, 16, 128, T], BF16,
                                   kind="Internal").ap()
        io["sz_h"] = nc.dram_tensor("sz_h", [NB, 16, 128, T], BF16,
                                    kind="Internal").ap()
        _declare(nc, io, cw)
        p = P(nc, es)
        build_mamba(p, io)
        p.finalize()
        in_maps = []
        for c in range(N_CORE):
            m = dict(cw)
            m["x1"] = np.ascontiguousarray(x1full[2 * c:2 * c + 2])
            in_maps.append(m)
        res = run_bass_kernel_spmd(nc, in_maps, core_ids=list(range(N_CORE)))
    xn1T = np.concatenate([r["xn1T_h"] for r in res.results], axis=0)
    macc = np.concatenate([r["maccT"] for r in res.results], axis=0)
    return xn1T, macc


_L1_NAMES = ["wqkv", "wo1", "w1", "w2", "b1a", "b1g", "b2", "lng", "lnb",
             "qsc", "ksc", "identB", "identF", "maskC", "maskA", "G64", "P64",
             "ones128", "CSq8", "SSq8", "CSk", "SSk"]


def run_l1ffn(pool, x1full, xn1Tfull, maccfull):
    cw = {nm: pool[nm] for nm in _L1_NAMES}
    with contextlib.ExitStack() as es:
        nc = bass.Bass("TRN2", target_bir_lowering=False)
        io = {}
        io["x1"] = nc.dram_tensor("x1", [NB, T, D], F32,
                                  kind="ExternalInput").ap()
        io["xn1T_h"] = nc.dram_tensor("xn1T_h", [NB, 8, 128, T], BF16,
                                      kind="ExternalInput").ap()
        io["maccT"] = nc.dram_tensor("maccT", [NB, 8, 128, T], F32,
                                     kind="ExternalInput").ap()
        io["x2"] = nc.dram_tensor("x2", [NB, T, D], F32,
                                  kind="ExternalOutput").ap()
        _declare(nc, io, cw)
        p = P(nc, es)
        build_l1ffn(p, io)
        p.finalize()
        in_maps = []
        for c in range(N_CORE):
            m = dict(cw)
            m["x1"] = np.ascontiguousarray(x1full[2 * c:2 * c + 2])
            m["xn1T_h"] = np.ascontiguousarray(xn1Tfull[2 * c:2 * c + 2])
            m["maccT"] = np.ascontiguousarray(maccfull[2 * c:2 * c + 2])
            in_maps.append(m)
        res = run_bass_kernel_spmd(nc, in_maps, core_ids=list(range(N_CORE)))
    return np.concatenate([r["x2"] for r in res.results], axis=0)


def run_fused(pool, xfull):
    names = sorted(set(_L0_NAMES + list(_MB_NAMES) + _L1_NAMES))
    cw = {nm: pool[nm] for nm in names}
    with contextlib.ExitStack() as es:
        nc = bass.Bass("TRN2", target_bir_lowering=False)
        io = {}
        io["x"] = nc.dram_tensor("x", [NB, T, D], F32, kind="ExternalInput").ap()
        io["x2"] = nc.dram_tensor("x2", [NB, T, D], F32,
                                  kind="ExternalOutput").ap()
        for nm, shape, dt_ in (
                ("x1", [NB, T, D], F32),
                ("xnT_h", [NB, 16, 8, 128, 128], BF16),
                ("attn_h", [NB, T, D], F32),
                ("xn1T_h", [NB, 8, 128, T], BF16),
                ("maccT", [NB, 8, 128, T], F32),
                ("x_h", [NB, 16, 128, T], BF16),
                ("sz_h", [NB, 16, 128, T], BF16)):
            io[nm] = nc.dram_tensor(nm, shape, dt_, kind="Internal").ap()
        _declare(nc, io, cw)
        p = P(nc, es)
        build_l0(p, io)
        p.barrier()
        build_mamba(p, io)
        p.barrier()
        build_l1ffn(p, io)
        p.finalize()
        in_maps = []
        for c in range(N_CORE):
            m = dict(cw)
            m["x"] = np.ascontiguousarray(xfull[2 * c:2 * c + 2])
            in_maps.append(m)
        res = run_bass_kernel_spmd(nc, in_maps, core_ids=list(range(N_CORE)))
    return np.concatenate([r["x2"] for r in res.results], axis=0)


def kernel(**inputs):
    consts = _consts()
    w = _prep_weights(inputs)
    pool = {**w, **consts}
    xfull = np.asarray(inputs["L0_attn_X"], dtype=np.float32)
    return run_fused(pool, xfull)


# version 07
# speedup vs baseline: 1.0170x; 1.0170x over previous
import contextlib
import os
import traceback
import numpy as np
import ml_dtypes
import concourse.bass as bass
import concourse.mybir as mybir
from concourse.bass_utils import run_bass_kernel_spmd

F32 = mybir.dt.float32
BF16 = mybir.dt.bfloat16
I32 = mybir.dt.int32
AF = mybir.ActivationFunctionType
ALU = mybir.AluOpType
AX = mybir.AxisListType
BF = ml_dtypes.bfloat16

NEG = -1.0e30
T = 2048
D = 1024
NB = 2
N_CORE = 8
ts = bass.ts


ENGS = ("sync", "gpsimd", "vector", "scalar", "tensor")

SELF_SEM = {"tensor": "tS", "vector": "vS", "scalar": "sS", "gpsimd": "gS"}
SELF_WAIT = {"vector", "scalar", "gpsimd"}


class Eng:
    def __init__(self, p, name):
        self.p = p
        self.name = name

    def __getattr__(self, op):
        p, name = self.p, self.name
        ssem = SELF_SEM.get(name)

        def f(*args, **kw):
            if name in SELF_WAIT:
                prior = p.cnt[ssem]
                self.wait(ssem, prior)
            v = p.inc(ssem)
            s = p.sems[ssem]
            if p.debug_lines:
                fr = traceback.extract_stack(limit=2)[0]
                lbl = f"{op}@{fr.lineno}"
                def thunk(e, lbl=lbl, op=op, args=args, kw=kw):
                    with p.nc.named_scope(lbl):
                        getattr(e, op)(*args, **kw).then_inc(s, 1)
                p.q[name].append(thunk)
            else:
                p.q[name].append(
                    lambda e: getattr(e, op)(*args, **kw).then_inc(s, 1))
            return v
        return f

    def dma(self, dst, src, sem):
        p, name = self.p, self.name
        s = p.sem(sem)
        v = p.inc(sem, 16)
        p.q[name].append(lambda e: e.dma_start(dst, src).then_inc(s, 16))
        return v

    def wait(self, sem, v):
        p, name = self.p, self.name
        if v <= p.waited[name].get(sem, 0):
            return
        p.waited[name][sem] = v
        s = p.sems[sem]
        p.q[name].append(lambda e: e.wait_ge(s, v))


class P:
    def __init__(self, nc, es):
        self.nc = nc
        self.es = es
        self.q = {k: [] for k in ENGS}
        self.sems = {}
        self.cnt = {}
        self.waited = {k: {} for k in ENGS}
        self.debug_lines = bool(os.environ.get("MK_DEBUG_LINES"))
        self.sync = Eng(self, "sync")
        self.gps = Eng(self, "gpsimd")
        self.vec = Eng(self, "vector")
        self.sca = Eng(self, "scalar")
        self.ten = Eng(self, "tensor")
        self.engmap = {"sync": self.sync, "gpsimd": self.gps, "vector": self.vec,
                       "scalar": self.sca, "tensor": self.ten}
        self.nbar = 0
        for nm in ("ld", "ldg", "st", "tS", "vS", "sS", "gS", "bar"):
            self.sem(nm)

    def sem(self, nm):
        if nm not in self.sems:
            self.sems[nm] = self.es.enter_context(self.nc.semaphore(nm))
            self.cnt[nm] = 0
        return self.sems[nm]

    def inc(self, sem, n=1):
        self.cnt[sem] += n
        return self.cnt[sem]

    def val(self, sem):
        return self.cnt[sem]

    def barrier(self):
        self.nbar += 1
        tgt = 5 * self.nbar
        s = self.sems["bar"]
        allsems = [k for k in self.sems if k != "bar"]
        for nm in ENGS:
            eng = self.engmap[nm]
            for sm in allsems:
                eng.wait(sm, self.cnt[sm])
            self.q[nm].append(lambda e, s=s: e.nop().then_inc(s, 1))
        for nm in ENGS:
            self.q[nm].append(lambda e, s=s, tgt=tgt: e.wait_ge(s, tgt))

    def sb(self, es, name, shape, dtype):
        self._uid = getattr(self, "_uid", 0) + 1
        return es.enter_context(
            self.nc.sbuf_tensor(f"{name}_u{self._uid}", shape, dtype))

    def ps(self, es, name, shape, dtype=F32):
        self._uid = getattr(self, "_uid", 0) + 1
        return es.enter_context(
            self.nc.psum_tensor(f"{name}_u{self._uid}", shape, dtype))

    def finalize(self):
        with self.nc.Block() as block:
            @block.sync
            def _(e):
                for f in self.q["sync"]:
                    f(e)

            @block.gpsimd
            def _(e):
                for f in self.q["gpsimd"]:
                    f(e)

            @block.vector
            def _(e):
                for f in self.q["vector"]:
                    f(e)

            @block.scalar
            def _(e):
                for f in self.q["scalar"]:
                    f(e)

            @block.tensor
            def _(e):
                for f in self.q["tensor"]:
                    f(e)



def _consts():
    c = {}
    c["identB"] = np.eye(128, dtype=np.float32).astype(BF)
    c["identF"] = np.eye(128, dtype=np.float32)
    i = np.arange(128)
    c["maskC"] = np.where(i[:, None] >= i[None, :], 0.0, NEG).astype(np.float32)
    c["maskA"] = np.where(i[:, None] <= i[None, :], 0.0, NEG).astype(np.float32)
    c["ones64"] = np.ones((64, 128), dtype=np.float32).astype(BF)
    c["ones128"] = np.ones((128, 128), dtype=np.float32).astype(BF)
    g = (i[:, None] // 64 == i[None, :] // 64).astype(np.float32)
    c["G64"] = g.astype(BF)
    p64 = np.zeros((128, 128), dtype=np.float32)
    for blk in range(2):
        o = blk * 64
        for m in range(32):
            p64[o + m + 32, o + m] = -1.0
        for m in range(32, 64):
            p64[o + m - 32, o + m] = 1.0
    c["P64"] = p64.astype(BF)
    selrep = np.zeros((128, 16 * 128), dtype=np.float32)
    selsum = np.zeros((128, 16 * 128), dtype=np.float32)
    for ii in range(16):
        for m in range(128):
            selrep[8 * ii + m // 16, ii * 128 + m] = 1.0
        for k in range(128):
            selsum[k, ii * 128 + 8 * ii + k // 16] = 1.0
    c["selrep"] = selrep.astype(BF)
    c["selsum"] = selsum.astype(BF)
    seln = np.zeros((16, 128), dtype=np.float32)
    for m in range(128):
        seln[m % 16, m] = 1.0
    c["seln"] = seln.astype(BF)
    DH, WIN = 64, 512
    half = np.arange(0, DH, 2, dtype=np.float64)
    inv_freq = 1.0 / (10000.0 ** (half / DH))
    t = np.arange(T, dtype=np.float64)
    freqs = t[:, None] * inv_freq[None, :]
    freqs = np.concatenate([freqs, freqs], axis=-1)
    base = (half + 0.4 * DH) / (1.4 * DH)
    power = (t - T // 2) / float(WIN // 2)
    scale = base[None, :] ** power[:, None]
    scale = np.concatenate([scale, scale], axis=-1)
    cos, sin = np.cos(freqs), np.sin(freqs)

    def fm(x):
        xt = np.ascontiguousarray(x.T.astype(np.float32))
        return np.concatenate([xt, xt], axis=0)
    c["CSq8"] = fm(cos * scale * 8.0)
    c["SSq8"] = fm(sin * scale * 8.0)
    c["CSk"] = fm(cos / scale)
    c["SSk"] = fm(sin / scale)
    return c


def _col_tiles(v, np_dtype=np.float32):
    n = v.shape[0]
    nt = (n + 127) // 128
    out = np.zeros((128, nt), dtype=np_dtype)
    for j in range(nt):
        seg = v[j * 128:(j + 1) * 128]
        out[:len(seg), j] = seg
    return np.ascontiguousarray(out)


def _prep_weights(inp):
    w = {}
    cT = lambda a: np.ascontiguousarray(np.asarray(a).T)
    w["wqT"] = cT(inp["L0_attn_W_q"])
    w["wkT"] = cT(inp["L0_attn_W_k"])
    w["wvT"] = cT(inp["L0_attn_W_v"])
    w["woT"] = cT(inp["L0_attn_W_o"])
    w["lq1T"] = cT(inp["L0_attn_lambda_q1"])
    w["lk1T"] = cT(inp["L0_attn_lambda_k1"])
    w["lq2T"] = cT(inp["L0_attn_lambda_q2"])
    w["lk2T"] = cT(inp["L0_attn_lambda_k2"])
    w["rms"] = np.ascontiguousarray(
        np.asarray(inp["L0_attn_rms_scale"], dtype=np.float32).reshape(128, 1))
    w["wgT"] = cT(inp["L0_ffn_w_gate"])
    w["wuT"] = cT(inp["L0_ffn_w_up"])
    w["wdT"] = cT(inp["L0_ffn_w_down"])
    w["inpT"] = cT(inp["L1_attn_in_proj_w"])
    w["outpT"] = cT(inp["L1_attn_out_proj_w"])
    for sfx, cw, cb, xp, dtw, dtb, alog, dv in (
        ("f", "L1_attn_conv1d_w", "L1_attn_conv1d_bias", "L1_attn_x_proj_w",
         "L1_attn_dt_proj_w", "L1_attn_dt_proj_bias", "L1_attn_A_log", "L1_attn_D"),
        ("b", "L1_attn_conv1d_b_w", "L1_attn_conv1d_b_bias", "L1_attn_x_proj_b_w",
         "L1_attn_dt_proj_b_w", "L1_attn_dt_proj_b_bias", "L1_attn_A_b_log",
         "L1_attn_D_b"),
    ):
        cwa = np.asarray(inp[cw]).reshape(2048, 4)
        w["convw_" + sfx] = np.ascontiguousarray(
            cwa.reshape(16, 128, 4).transpose(1, 0, 2))
        w["convb_" + sfx] = _col_tiles(np.asarray(inp[cb]))
        xpa = np.asarray(inp[xp])
        perm = np.zeros((128, 2048), dtype=np.float32)
        perm[0:16] = xpa[64:80]
        perm[32:48] = xpa[80:96]
        perm[64:128] = xpa[0:64]
        w["xprojT_" + sfx] = cT(perm)
        w["dtpT_" + sfx] = cT(np.asarray(inp[dtw]))
        w["dtb_" + sfx] = _col_tiles(np.asarray(inp[dtb]))
        al = np.asarray(inp[alog])
        ac = np.zeros((128, 256), dtype=np.float32)
        m = np.arange(128)
        for j in range(16):
            for i2 in range(16):
                ac[:, j * 16 + i2] = al[128 * j + 8 * i2 + m // 16, m % 16]
        w["alog_" + sfx] = ac
        w["D_" + sfx] = _col_tiles(np.asarray(inp[dv]))
    w["wqkv"] = np.ascontiguousarray(np.asarray(inp["L1_ffn_wqkv"]))
    w["wo1"] = np.ascontiguousarray(np.asarray(inp["L1_ffn_wo"]))
    w["w1"] = np.ascontiguousarray(np.asarray(inp["L1_ffn_w1"]))
    w["w2"] = np.ascontiguousarray(np.asarray(inp["L1_ffn_w2"]))
    b1 = np.asarray(inp["L1_ffn_b1"])
    w["b1a"] = np.stack([_col_tiles(b1[l, :2730]) for l in range(2)])
    w["b1g"] = np.stack([_col_tiles(b1[l, 2730:]) for l in range(2)])
    w["b2"] = np.stack([_col_tiles(np.asarray(inp["L1_ffn_b2"])[l])
                        for l in range(2)])
    w["lng"] = np.stack([_col_tiles(np.asarray(inp["L1_ffn_ln_g"])[l])
                         for l in range(2)])
    w["lnb"] = np.stack([_col_tiles(np.asarray(inp["L1_ffn_ln_b"])[l])
                         for l in range(2)])
    qs = np.asarray(inp["L1_ffn_q_scale"])
    ks = np.asarray(inp["L1_ffn_k_scale"])
    w["qsc"] = np.ascontiguousarray(
        np.concatenate([qs, qs], axis=1).reshape(2, 128, 1))
    w["ksc"] = np.ascontiguousarray(
        np.concatenate([ks, ks], axis=1).reshape(2, 128, 1))
    return w



def emit_ln(p, es, xsrc_dram, b, xnT, xnT_h=None, identB=None):
    xt = [p.sb(es, f"lnx{i}", [128, D], F32) for i in range(2)]
    sq_v = p.sb(es, "lnsqv", [128, D], BF16)
    sq_s = p.sb(es, "lnsqs", [128, D], BF16)
    st = {nm: p.sb(es, "lnst_" + nm, [128, 1], F32)
          for nm in ("sum", "ssq", "mean", "e2", "var", "std", "rstd")}
    xn_nat = p.sb(es, "lnxn", [128, D], BF16)
    pt_t = [p.ps(es, f"lnpt{i}", [128, 128], BF16) for i in range(2)]
    vdone = [0] * 16
    sdone = [0] * 16
    for tt in range(16):
        buf = xt[tt % 2]
        slot = "lnx" + ("A" if tt % 2 == 0 else "B")
        if tt >= 2:
            p.sync.wait("vS", vdone[tt - 2])
            p.sync.wait("sS", sdone[tt - 2])
        lv = p.sync.dma(buf[:], xsrc_dram[b, ts(tt, 128), :], slot)
        p.vec.wait(slot, lv)
        p.sca.wait(slot, lv)
        p.vec.tensor_scalar(sq_v[:], buf[:], 1.0, 0.0, ALU.mult, ALU.add,
                            accum_out=st["sum"][:])
        sv = p.sca.activation(sq_s[:], buf[:], AF.Square, accum_out=st["ssq"][:])
        sdone[tt] = sv
        p.vec.wait("sS", sv)
        p.vec.tensor_scalar(st["mean"][:], st["sum"][:], 1.0 / D, None, ALU.mult)
        p.vec.tensor_scalar(st["e2"][:], st["ssq"][:], 1.0 / D, None, ALU.mult)
        p.vec.tensor_tensor(st["var"][:], st["mean"][:], st["mean"][:], ALU.mult)
        p.vec.tensor_tensor(st["var"][:], st["e2"][:], st["var"][:], ALU.subtract)
        vv = p.vec.tensor_scalar(st["var"][:], st["var"][:], 1e-5, None, ALU.add)
        p.sca.wait("vS", vv)
        sv = p.sca.activation(st["std"][:], st["var"][:], AF.Sqrt)
        p.vec.wait("sS", sv)
        p.vec.reciprocal(st["rstd"][:], st["std"][:])
        p.vec.wait("tS", p.val("tS"))
        vv = p.vec.tensor_scalar(xn_nat[:], buf[:], st["mean"][:], st["rstd"][:],
                                 ALU.subtract, ALU.mult)
        vdone[tt] = vv
        p.ten.wait("vS", vv)
        for dt in range(8):
            pb = pt_t[dt % 2]
            if dt >= 2:
                p.ten.wait("sS", p.val("sS") - 1)
            tv = p.ten.transpose(pb[:], xn_nat[:, ts(dt, 128)], identB[:])
            p.sca.wait("tS", tv)
            sv = p.sca.copy(xnT[:, dt * T + tt * 128:dt * T + tt * 128 + 128],
                            pb[:])
        if xnT_h is not None:
            p.sync.wait("sS", p.val("sS"))
            src3 = xnT.rearrange("p (a t) -> p a t", a=8)[:, :,
                                                          tt * 128:(tt + 1) * 128]
            p.sync.dma(xnT_h[b, tt].rearrange("a p t -> p a t"), src3, "st")



def build_l0(p, io):
    nc = p.nc
    x, x1 = io["x"], io["x1"]
    xnT_h, attn_h = io["xnT_h"], io["attn_h"]
    es_p = p.es

    identB = p.sb(es_p, "identB", [128, 128], BF16)
    maskC = p.sb(es_p, "maskC", [128, 128], F32)
    rms02 = p.sb(es_p, "rms02", [128, 1], F32)
    lam_bc = p.sb(es_p, "lam_bc", [128, 16], F32)

    p.sync.dma(identB[:], io["identB"], "ld")
    p.sync.dma(maskC[:], io["maskC"], "ld")

    with contextlib.ExitStack() as es0:
        lv4 = [p.sb(es0, f"lv{i}", [64, 16], F32) for i in range(4)]
        ones64 = p.sb(es0, "ones64", [64, 128], BF16)
        prod = p.sb(es0, "prod", [64, 16], BF16)
        pl = p.ps(es0, "pl", [128, 16], F32)
        e1 = p.sb(es0, "e1", [128, 16], F32)
        e2 = p.sb(es0, "e2", [128, 16], F32)
        rmsr = p.sb(es0, "rmsr", [128, 1], F32)
        for i, nm in enumerate(("lq1T", "lk1T", "lq2T", "lk2T")):
            p.sync.dma(lv4[i][:], io[nm], "ld")
        p.sync.dma(rmsr[:], io["rms"], "ld")
        p.gps.dma(ones64[:], io["ones64"], "ldg")
        p.vec.wait("ld", p.val("ld"))
        p.vec.wait("ldg", p.val("ldg"))
        p.vec.tensor_scalar(rms02[:], rmsr[:], 0.2, None, ALU.mult)
        for k in range(2):
            a, bb = (0, 1) if k == 0 else (2, 3)
            vv = p.vec.tensor_tensor(prod[:], lv4[a][:], lv4[bb][:], ALU.mult)
            p.ten.wait("vS", vv)
            tv = p.ten.matmul(pl[:], ones64[:], prod[:], start=True, stop=True)
            p.sca.wait("tS", tv)
            sv = p.sca.activation((e1 if k == 0 else e2)[:], pl[:], AF.Exp)
            p.vec.wait("sS", sv)
        p.vec.tensor_tensor(lam_bc[:], e1[:], e2[:], ALU.subtract)
        p.vec.tensor_scalar(lam_bc[:], lam_bc[:], 0.8, None, ALU.add)
        p.barrier()

    for b in range(NB):
        with contextlib.ExitStack() as es_b:
            xnT = p.sb(es_b, "xnT", [128, 8 * T], BF16)
            acc = p.sb(es_b, "l0acc", [128, 16 * 1024], F32)
            with contextlib.ExitStack() as es_ln:
                emit_ln(p, es_ln, x, b, xnT, xnT_h=xnT_h, identB=identB)
                p.barrier()
            with contextlib.ExitStack() as es_a:
                _l0_attn(p, es_a, io, b, xnT, acc, identB, maskC, rms02, lam_bc)
            p.sync.wait("vS", p.val("vS"))
            p.sync.dma(attn_h[b].rearrange("(a p) f -> p a f", p=128),
                       acc.rearrange("p (a f) -> p a f", a=16), "st")
            p.sync.wait("st", p.val("st"))
            p.barrier()
        with contextlib.ExitStack() as es_f:
            _l0_ffn(p, es_f, io, b, x, x1, xnT_h, attn_h)
            p.barrier()
    p.sync.wait("st", p.val("st"))


def _l0_attn(p, es, io, b, xnT, acc, identB, maskC, rms02, lam_bc):
    GH = 2
    wq_g = p.sb(es, "wq_g", [128, 8 * 128 * GH], BF16)
    wk_g = p.sb(es, "wk_g", [128, 8 * 128 * GH], BF16)
    wv_g = p.sb(es, "wv_g", [128, 8 * 128 * GH], BF16)
    wo_g = p.sb(es, "wo_g", [128, GH * 1024], BF16)
    qT_g = p.sb(es, "qT_g", [128, GH * T], BF16)
    kT_g = p.sb(es, "kT_g", [128, GH * T], BF16)
    v_g = p.sb(es, "v_g", [128, 16 * 128 * GH], BF16)
    ps_s1 = p.ps(es, "ps_s1", [128, 1024], F32)
    ps_s2 = p.ps(es, "ps_s2", [128, 1024], F32)
    ps_o = p.ps(es, "ps_o", [128, 128], F32)

    pt_t = [p.ps(es, f"apt{i}", [128, 128], BF16) for i in range(2)]
    oT_all = p.sb(es, "oT_all", [128, GH * 16 * 128], BF16)
    wo_mm_last = [0]
    E1 = [p.sb(es, f"E1_{i}", [128, T], BF16) for i in range(2)]
    E2 = [p.sb(es, f"E2_{i}", [128, T], BF16) for i in range(2)]
    attn_sb = [p.sb(es, f"attn{i}", [128, T], BF16) for i in range(2)]
    attnT = [p.sb(es, f"attnT{i}", [128, T], BF16) for i in range(2)]
    atd = [0, 0]
    onat = [p.sb(es, f"onat{i}", [128, 128], BF16) for i in range(2)]
    sq128 = p.sb(es, "sq128", [128, 128], BF16)
    st2 = {}
    for nm in ("m", "nm", "rr", "rri"):
        st2[nm] = [p.sb(es, f"at_{nm}{i}", [128, 2], F32) for i in range(2)]
    st2["rp"] = [p.sb(es, f"at_rp{i}", [128, 4], F32) for i in range(2)]
    for nm in ("c2", "ssq", "t0", "s0", "f"):
        st2[nm] = [p.sb(es, f"at_{nm}{i}", [128, 1], F32) for i in range(2)]

    for g in range(16 // GH):
        p.gps.wait("tS", p.val("tS"))
        W = 128 * GH
        for wdst, wsrc in ((wq_g, "wqT"), (wk_g, "wkT"), (wv_g, "wvT")):
            p.gps.dma(wdst.rearrange("p (a w) -> p a w", a=8),
                      io[wsrc].rearrange("(a p) f -> p a f", p=128)[
                          :, :, g * W:(g + 1) * W], "ldg")
        p.gps.dma(wo_g.rearrange("p (a f) -> p a f", a=GH),
                  io["woT"].rearrange("(a p) f -> p a f", p=128)[
                      :, GH * g:GH * (g + 1), :], "ldg")
        lgv = p.val("ldg")
        p.vec.wait("ldg", lgv)
        p.vec.wait("tS", p.val("tS"))
        vv = 0
        for hh in range(GH):
            vv = p.vec.tensor_scalar(wo_g[:, ts(hh, 1024)],
                                     wo_g[:, ts(hh, 1024)], rms02[:, 0:1], None,
                                     ALU.mult)
        p.ten.wait("vS", vv)
        p.ten.wait("ldg", lgv)
        p.ten.wait("sS", p.val("sS"))
        ps_qkv = ps_s1
        for hh in range(GH):
            for (wsb, dst) in ((wq_g, qT_g), (wk_g, kT_g)):
                for ch in range(4):
                    for dt in range(8):
                        tv = p.ten.matmul(
                            ps_qkv[:, 0:512],
                            wsb[:, dt * 128 * GH + hh * 128:
                                dt * 128 * GH + hh * 128 + 128],
                            xnT[:, dt * T + ch * 512:dt * T + (ch + 1) * 512],
                            start=(dt == 0), stop=(dt == 7))
                    p.sca.wait("tS", tv)
                    sv = p.sca.copy(
                        dst[:, hh * T + ch * 512:hh * T + (ch + 1) * 512],
                        ps_qkv[:, 0:512])
                    p.ten.wait("sS", sv)
        for tt in range(16):
            for dt in range(8):
                tv = p.ten.matmul(ps_qkv[:, 0:128 * GH],
                                  xnT[:, dt * T + tt * 128:dt * T + tt * 128 + 128],
                                  wv_g[:, ts(dt, 128 * GH)],
                                  start=(dt == 0), stop=(dt == 7))
            p.sca.wait("tS", tv)
            sv = p.sca.copy(v_g[:, ts(tt, 128 * GH)], ps_qkv[:, 0:128 * GH])
            p.ten.wait("sS", sv)
        iters = [(tt, hh) for tt in range(16) for hh in range(GH)]

        def stage_S(it, mi, kh):
            tt, hh = iters[it]
            par = it % 2
            base = 64 * mi
            pss = ps_s1 if mi == 0 else ps_s2
            kspan = (tt + 1) * 128
            lo = kh * 1024
            hi = min(kspan, lo + 1024)
            p.ten.wait("sS", st["exp" + str(mi)][par] if kh_first(it) != kh
                       else st["exp" + str(mi)][1 - par])
            for c in range(lo, hi, 512):
                cn = min(512, hi - c)
                tv = p.ten.matmul(
                    pss[:, c - lo:c - lo + cn],
                    qT_g[base:base + 64,
                         hh * T + tt * 128:hh * T + tt * 128 + 128],
                    kT_g[base:base + 64, hh * T + c:hh * T + c + cn],
                    start=True, stop=True)
            st["smm" + str(mi)][par] = tv

        def kh_first(it):
            tt, hh = iters[it]
            return tt // 8

        def vec_max(it, mi):
            tt, hh = iters[it]
            par = it % 2
            pss = ps_s1 if mi == 0 else ps_s2
            kspan = (tt + 1) * 128
            dlo = kspan - 128 - kh_first(it) * 1024
            p.vec.wait("tS", st["smm" + str(mi)][par])
            p.vec.tensor_tensor(pss[:, dlo:dlo + 128],
                                pss[:, dlo:dlo + 128], maskC[:], ALU.add)
            p.vec.reduce_max(st2["m"][par][:, mi:mi + 1],
                             pss[:, dlo:dlo + 128], AX.X)
            st["negm" + str(mi)][par] = p.vec.tensor_scalar(
                st2["nm"][par][:, mi:mi + 1], st2["m"][par][:, mi:mi + 1],
                -0.125, None, ALU.mult)

        def sca_exp(it, mi, kh):
            tt, hh = iters[it]
            par = it % 2
            pss = ps_s1 if mi == 0 else ps_s2
            kspan = (tt + 1) * 128
            lo = kh * 1024
            hi = min(kspan, lo + 1024)
            Em = (E1 if mi == 0 else E2)[par]
            p.sca.wait("tS", st["smm" + str(mi)][par])
            p.sca.wait("vS", st["negm" + str(mi)][par])
            if kh == kh_first(it):
                p.sca.wait("vS", st["comb"][par])
            sv = p.sca.activation(Em[:, lo:hi], pss[:, 0:hi - lo], AF.Exp,
                                  bias=st2["nm"][par][:, mi:mi + 1], scale=0.125,
                                  accum_out=st2["rp"][par][:, 2 * mi + kh:
                                                           2 * mi + kh + 1])
            st["exp" + str(mi)][par] = sv

        def vec_combine(it):
            tt, hh = iters[it]
            par = it % 2
            hfull = hh + GH * g
            kspan = (tt + 1) * 128
            p.vec.wait("sS", st["exp1"][par])
            tt_ = iters[it][0]
            if tt_ // 8 == 0:
                p.vec.tensor_copy(st2["rr"][par][:],
                                  st2["rp"][par][:, 0:4:2])
            else:
                p.vec.tensor_tensor(st2["rr"][par][:],
                                    st2["rp"][par][:, 0:4:2],
                                    st2["rp"][par][:, 1:4:2], ALU.add)
            p.vec.reciprocal(st2["rri"][par][:], st2["rr"][par][:])
            p.vec.tensor_scalar(st2["c2"][par][:], st2["rri"][par][:, 1:2],
                                lam_bc[:, hfull:hfull + 1], -1.0, ALU.mult,
                                ALU.mult)
            p.vec.wait("tS", st["trans"][par])
            p.vec.tensor_scalar(attn_sb[par][:, 0:kspan], E1[par][:, 0:kspan],
                                st2["rri"][par][:, 0:1], None, ALU.mult)
            st["comb"][par] = p.vec.scalar_tensor_tensor(
                attn_sb[par][:, 0:kspan], E2[par][:, 0:kspan],
                st2["c2"][par][:, 0:1], attn_sb[par][:, 0:kspan], ALU.mult,
                ALU.add)

        def ten_trans_pv(it):
            tt, hh = iters[it]
            par = it % 2
            p.ten.wait("vS", st["comb"][par])
            p.ten.wait("vS", st["onat"][1 - par])
            for kt in range(tt + 1):
                pb = pt_t[kt % 2]
                if kt % 2 == 0:
                    p.ten.wait("sS", st["tevac"][0])
                else:
                    p.ten.wait("vS", st["tevac"][1])
                tv = p.ten.transpose(pb[:], attn_sb[par][:, ts(kt, 128)],
                                     identB[:])
                if kt % 2 == 0:
                    p.sca.wait("tS", tv)
                    p.sca.wait("tS", st["pv"][1 - par])
                    st["tevac"][0] = p.sca.copy(attnT[par][:, ts(kt, 128)],
                                                pb[:])
                else:
                    p.vec.wait("tS", tv)
                    p.vec.wait("tS", st["pv"][1 - par])
                    st["tevac"][1] = p.vec.tensor_copy(
                        attnT[par][:, ts(kt, 128)], pb[:])
            p.ten.wait("sS", st["tevac"][0])
            p.ten.wait("vS", st["tevac"][1])
            for kt in range(tt + 1):
                tv = p.ten.matmul(ps_o[:], attnT[par][:, ts(kt, 128)],
                                  v_g[:, kt * 128 * GH + hh * 128:
                                      kt * 128 * GH + hh * 128 + 128],
                                  start=(kt == 0), stop=(kt == tt),
                                  skip_group_check=True)
            st["pv"][par] = tv
            p.sca.wait("tS", tv)
            p.sca.wait("vS", st["onat"][par])
            sv = p.sca.activation(sq128[:], ps_o[:], AF.Square,
                                  accum_out=st2["ssq"][par][:])
            p.vec.wait("sS", sv)
            vv = p.vec.tensor_scalar(st2["t0"][par][:], st2["ssq"][par][:],
                                     2.0 ** -40 / 128, 1e-5 * 2.0 ** -40,
                                     ALU.mult, ALU.add)
            p.sca.wait("vS", vv)
            p.sca.activation(st2["s0"][par][:], st2["t0"][par][:], AF.Ln)
            sv = p.sca.activation(st2["f"][par][:], st2["s0"][par][:], AF.Exp,
                                  scale=-0.5)
            p.vec.wait("sS", sv)
            vv = p.vec.tensor_scalar(onat[par][:], ps_o[:],
                                     st2["f"][par][:, 0:1], 2.0 ** -20,
                                     ALU.mult, ALU.mult)
            st["onat"][par] = vv

        def ten_wo(it):
            tt, hh = iters[it]
            par = it % 2
            p.ten.wait("vS", st["onat"][par])
            p.ten.wait("sS", st["tevac"][0])
            tv = p.ten.transpose(pt_t[0][:], onat[par][:], identB[:])
            p.sca.wait("tS", tv)
            p.sca.wait("tS", wo_mm_last[0])
            sv = p.sca.copy(oT_all[:, (hh * 16 + tt) * 128:
                                   (hh * 16 + tt) * 128 + 128], pt_t[0][:])
            st["tevac"][0] = sv

        def wo_phase():
            ps_wo = ps_s1
            p.ten.wait("sS", p.val("sS"))
            vv = 0
            for tt in range(16):
                for half in range(2):
                    for hh in range(GH):
                        tv = p.ten.matmul(
                            ps_wo[:, ts(half, 512)],
                            oT_all[:, (hh * 16 + tt) * 128:
                                   (hh * 16 + tt) * 128 + 128],
                            wo_g[:, hh * 1024 + half * 512:
                                 hh * 1024 + (half + 1) * 512],
                            start=(hh == 0), stop=(hh == GH - 1),
                            skip_group_check=True)
                p.vec.wait("tS", tv)
                for half in range(2):
                    dst = acc[:, tt * 1024 + half * 512:
                              tt * 1024 + (half + 1) * 512]
                    if g == 0:
                        vv = p.vec.tensor_copy(dst, ps_wo[:, ts(half, 512)])
                    else:
                        vv = p.vec.tensor_tensor(dst, ps_wo[:, ts(half, 512)],
                                                 dst, ALU.add)
                p.ten.wait("vS", vv)
            wo_mm_last[0] = p.val("tS")

        st = {nm: [0, 0] for nm in
              ("smm0", "smm1", "negm0", "negm1", "exp0", "exp1", "comb",
               "trans", "tevac", "pv", "onat")}
        nit = len(iters)
        for it in range(nit):
            khf = kh_first(it)
            stage_S(it, 0, khf)
            vec_max(it, 0)
            sca_exp(it, 0, khf)
            stage_S(it, 1, khf)
            vec_max(it, 1)
            sca_exp(it, 1, khf)
            if khf == 1:
                stage_S(it, 0, 0)
                sca_exp(it, 0, 0)
                stage_S(it, 1, 0)
                sca_exp(it, 1, 0)
            if it >= 1:
                ten_trans_pv(it - 1)
                st["trans"][(it - 1) % 2] = st["pv"][(it - 1) % 2]
                ten_wo(it - 1)
            vec_combine(it)
        ten_trans_pv(nit - 1)
        st["trans"][(nit - 1) % 2] = st["pv"][(nit - 1) % 2]
        ten_wo(nit - 1)
        wo_phase()


def _l0_ffn(p, es, io, b, x, x1, xnT_h, attn_h):
    wg_sb = p.sb(es, "wg_sb", [128, 8 * 3072], BF16)
    wu_sb = p.sb(es, "wu_sb", [128, 8 * 3072], BF16)
    wd_sb = p.sb(es, "wd_sb", [128, 24 * 1024], BF16)
    xnt_c = p.sb(es, "xnt_c", [128, 8 * 512], BF16)
    h_full = p.sb(es, "h_full", [128, 24 * 512], BF16)
    ps_g2 = [p.ps(es, f"ps_g{i}", [128, 512], F32) for i in range(2)]
    ps_u2 = [p.ps(es, f"ps_u{i}", [128, 512], F32) for i in range(2)]
    ps_d = [p.ps(es, f"ps_d{i}", [128, 512], F32) for i in range(4)]
    g_sb2 = [p.sb(es, f"g_sb{i}", [128, 512], BF16) for i in range(2)]
    u_sb2 = [p.sb(es, f"u_sb{i}", [128, 512], BF16) for i in range(2)]
    at2 = [p.sb(es, f"at2_{i}", [128, 1024], F32) for i in range(2)]
    x2b = [p.sb(es, f"x2b_{i}", [128, 1024], F32) for i in range(2)]
    o_t = p.sb(es, "o_t", [128, 1024], F32)

    p.gps.dma(wg_sb.rearrange("p (a f) -> p a f", a=8),
              io["wgT"].rearrange("(a p) f -> p a f", p=128), "ldg")
    p.gps.dma(wu_sb.rearrange("p (a f) -> p a f", a=8),
              io["wuT"].rearrange("(a p) f -> p a f", p=128), "ldg")
    p.gps.dma(wd_sb.rearrange("p (a f) -> p a f", a=24),
              io["wdT"].rearrange("(a p) f -> p a f", p=128), "ldg")
    wv_ = p.val("ldg")
    p.ten.wait("ldg", wv_)
    xat_used = [0, 0]
    for tc in range(4):
        p.sync.wait("tS", p.val("tS"))
        p.vec.wait("tS", p.val("tS"))
        for tti in range(4):
            tt = 4 * tc + tti
            lvx = p.sync.dma(
                xnt_c.rearrange("p (a t) -> p a t", a=8)[:, :,
                                                         tti * 128:(tti + 1) * 128],
                xnT_h[b, tt].rearrange("a p t -> p a t"), "ldf")
        p.ten.wait("ldf", lvx)
        guev = [0, 0]
        hfv = [0, 0]
        for ht in range(24):
            sl = ht % 2
            p.ten.wait("sS", guev[sl])
            for dt in range(8):
                p.ten.matmul(ps_g2[sl][:],
                             wg_sb[:, dt * 3072 + ht * 128:
                                   dt * 3072 + ht * 128 + 128],
                             xnt_c[:, ts(dt, 512)],
                             start=(dt == 0), stop=(dt == 7),
                             skip_group_check=True)
            for dt in range(8):
                tv = p.ten.matmul(ps_u2[sl][:],
                                  wu_sb[:, dt * 3072 + ht * 128:
                                        dt * 3072 + ht * 128 + 128],
                                  xnt_c[:, ts(dt, 512)],
                                  start=(dt == 0), stop=(dt == 7),
                                  skip_group_check=True)
            p.sca.wait("tS", tv)
            p.sca.wait("vS", hfv[sl])
            p.sca.activation(g_sb2[sl][:], ps_g2[sl][:], AF.Silu,
                             scale=1.0 / 127.0)
            guev[sl] = p.sca.activation(u_sb2[sl][:], ps_u2[sl][:], AF.Copy,
                                        scale=1.0 / 16129.0)
            p.vec.wait("sS", guev[sl])
            hfv[sl] = p.vec.tensor_tensor(h_full[:, ts(ht, 512)], g_sb2[sl][:],
                                          u_sb2[sl][:], ALU.mult)
        p.ten.wait("vS", p.val("vS"))
        for st4 in range(4):
            sl = st4 % 2
            tt = 4 * tc + st4
            sem = "ldfA" if sl == 0 else "ldfB"
            p.sync.wait("vS", xat_used[sl])
            p.sync.dma(at2[sl][:], attn_h[b, ts(tt, 128), :], sem)
            lv2 = p.sync.dma(x2b[sl][:], x[b, ts(tt, 128), :], sem)
            p.vec.wait("st", p.val("st"))
            p.vec.wait(sem, lv2)
            for half in range(2):
                pd = ps_d[st4]
                for ht in range(24):
                    tv = p.ten.matmul(pd[:], h_full[:, ht * 512 + st4 * 128:
                                                    ht * 512 + st4 * 128 + 128],
                                      wd_sb[:, ht * 1024 + half * 512:
                                            ht * 1024 + (half + 1) * 512],
                                      start=(ht == 0), stop=(ht == 23))
                p.vec.wait("tS", tv)
                dst = o_t[:, half * 512:(half + 1) * 512]
                p.vec.tensor_tensor(dst, pd[:],
                                    x2b[sl][:, half * 512:(half + 1) * 512],
                                    ALU.add)
                vv = p.vec.tensor_tensor(dst,
                                         at2[sl][:, half * 512:(half + 1) * 512],
                                         dst, ALU.add)
                p.ten.wait("vS", vv)
            xat_used[sl] = vv
            p.sync.wait("vS", vv)
            p.sync.wait("st", p.val("st"))
            p.sync.dma(x1[b, ts(tt, 128), :], o_t[:], "st")



CH = 512
NCH = T // CH


def build_mamba(p, io):
    nc = p.nc
    x1 = io["x1"]
    xn1T_h = io["xn1T_h"]
    x_h = io["x_h"]
    sz_h = io["sz_h"]
    maccT = io["maccT"]
    es_p = p.es

    identB = p.sb(es_p, "identB", [128, 128], BF16)
    selrep = p.sb(es_p, "selrep", [128, 2048], BF16)
    selsum = p.sb(es_p, "selsum", [128, 2048], BF16)
    seln = p.sb(es_p, "seln", [16, 128], BF16)
    p.sync.dma(identB[:], io["identB"], "ld")
    p.sync.dma(selrep[:], io["selrep"], "ld")
    p.sync.dma(selsum[:], io["selsum"], "ld")
    p.sync.dma(seln[:], io["seln"], "ld")
    p.ten.wait("ld", p.val("ld"))

    for b in range(NB):
        with contextlib.ExitStack() as es_b:
            xn1T = p.sb(es_b, "xn1T", [128, 8 * T], BF16)
            with contextlib.ExitStack() as es_ln:
                emit_ln(p, es_ln, x1, b, xn1T, xnT_h=None, identB=identB)
            p.sync.wait("sS", p.val("sS"))
            for dt in range(8):
                p.sync.dma(xn1T_h[b, dt], xn1T[:, ts(dt, T)], "st")
            p.barrier()
            _mamba_inproj(p, io, b, xn1T)
            p.barrier()
        for dir_ in ("f", "b"):
            with contextlib.ExitStack() as es_d:
                _mamba_dir(p, es_d, io, b, dir_, selrep, selsum, seln)
                p.barrier()
    p.sync.wait("st", p.val("st"))


def _mamba_inproj(p, io, b, xn1T):
    es = contextlib.ExitStack()
    with es:
        wbuf = [p.sb(es, f"ipw{i}", [128, 8 * 128], BF16) for i in range(2)]
        ob = [p.sb(es, f"ipo{i}", [128, 2048], BF16) for i in range(2)]
        psl = [p.ps(es, f"ps_ip{i}", [128, 2048], F32) for i in range(2)]
        wsem = ["ipwA", "ipwB"]
        stsem = ["ipstA", "ipstB"]
        wv = [0, 0]
        sdone = [0, 0]
        stv = [0, 0]
        mmdone = [0, 0]
        for ft in range(32):
            sl = ft % 2
            ps = psl[sl]
            p.gps.wait("tS", mmdone[sl])
            wv[sl] = p.gps.dma(wbuf[sl].rearrange("p (a w) -> p a w", a=8),
                               io["inpT"].rearrange("(a p) f -> p a f", p=128)[
                                   :, :, ts(ft, 128)], wsem[sl])
            p.ten.wait(wsem[sl], wv[sl])
            p.ten.wait("sS", sdone[sl])
            for ch in range(4):
                for dt in range(8):
                    tv = p.ten.matmul(ps[:, ts(ch, 512)], wbuf[sl][:, ts(dt, 128)],
                                      xn1T[:, dt * T + ch * 512:
                                           dt * T + (ch + 1) * 512],
                                      start=(dt == 0), stop=(dt == 7))
            mmdone[sl] = tv
            p.sca.wait("tS", tv)
            p.sca.wait(stsem[sl], stv[sl])
            if ft < 16:
                sv = p.sca.copy(ob[sl][:], ps[:])
                p.sync.wait("sS", sv)
                stv[sl] = p.sync.dma(io["x_h"][b, ft], ob[sl][:], stsem[sl])
            else:
                sv = p.sca.activation(ob[sl][:], ps[:], AF.Silu)
                p.sync.wait("sS", sv)
                stv[sl] = p.sync.dma(io["sz_h"][b, ft - 16], ob[sl][:], stsem[sl])
            sdone[sl] = sv
        p.sync.wait(stsem[0], stv[0])
        p.sync.wait(stsem[1], stv[1])


def _mamba_dir(p, es, io, b, dir_, selrep, selsum, seln):
    sfx = dir_
    rev = (dir_ == "b")
    xprojT = p.sb(es, "xprojT", [128, 16 * 128], BF16)
    dtpT = p.sb(es, "dtpT", [64, 2048], BF16)
    convw = p.sb(es, "convw", [128, 16 * 4], F32)
    convb = p.sb(es, "convb", [128, 16], F32)
    dtb = p.sb(es, "dtb", [128, 16], F32)
    Dv = p.sb(es, "Dv", [128, 16], F32)
    alog = p.sb(es, "alog", [128, 256], F32)
    Acol = p.sb(es, "Acol", [128, 256], F32)
    outpT = p.sb(es, "outpT", [128, 16 * 1024], BF16)
    p.gps.dma(xprojT.rearrange("p (a w) -> p a w", a=16),
              io["xprojT_" + sfx].rearrange("(a p) f -> p a f", p=128), "ldg")
    p.gps.dma(outpT.rearrange("p (a f) -> p a f", a=16),
              io["outpT"].rearrange("(a p) f -> p a f", p=128), "ldg")
    p.gps.dma(dtpT[:], io["dtpT_" + sfx], "ldg")
    p.sync.dma(convw.rearrange("p (a t) -> p a t", a=16),
               io["convw_" + sfx], "ld")
    p.sync.dma(convb[:], io["convb_" + sfx], "ld")
    p.sync.dma(dtb[:], io["dtb_" + sfx], "ld")
    p.sync.dma(Dv[:], io["D_" + sfx], "ld")
    p.sync.dma(alog[:], io["alog_" + sfx], "ld")
    ldv, lgv = p.val("ld"), p.val("ldg")
    p.sca.wait("ld", ldv)
    sv = p.sca.activation(Acol[:], alog[:], AF.Exp)
    p.vec.wait("sS", sv)
    p.vec.tensor_scalar(Acol[:], Acol[:], -1.0, None, ALU.mult)

    state = p.sb(es, "state", [128, 256], F32)
    p.vec.memset(state[:], 0.0)
    halo = p.sb(es, "halo", [128, 16 * 4], BF16)
    p.vec.memset(halo[:], 0.0)

    W = p.sb(es, "Wc", [128, 16 * (CH + 3)], BF16)
    tmp_r = [p.sb(es, f"tmpr{i}", [128, CH + 3], BF16) for i in range(2)]
    u_ch = p.sb(es, "u_ch", [128, 16 * CH], BF16)
    delta_ch = p.sb(es, "delta_ch", [128, 16 * CH], BF16)
    du_ch = p.sb(es, "du_ch", [128, 16 * CH], BF16)
    sz_ch = p.sb(es, "sz_ch", [128, 16 * CH], BF16)
    yg = p.sb(es, "yg", [128, 16 * CH], BF16)
    xdbl_dt = p.sb(es, "xdbl_dt", [64, CH], BF16)
    B_sb = p.sb(es, "B_sb", [16, CH], BF16)
    C_sb = p.sb(es, "C_sb", [16, CH], BF16)
    B_rep = p.sb(es, "B_rep", [128, CH], BF16)
    C_rep = p.sb(es, "C_rep", [128, CH], BF16)
    ftmp = p.sb(es, "ftmp", [128, CH], F32)
    ftmp2 = p.sb(es, "ftmp2", [128, CH], F32)
    ftmp3 = p.sb(es, "ftmp3", [128, CH], F32)
    dA_sb = [p.sb(es, f"dA{i}", [128, CH], F32) for i in range(2)]
    bb_sb = [p.sb(es, f"bb{i}", [128, CH], F32) for i in range(2)]
    h_sb = [p.sb(es, f"hsb{i}", [128, CH], BF16) for i in range(2)]
    hC = [p.sb(es, f"hC{i}", [128, CH], BF16) for i in range(2)]
    mo_t = p.sb(es, "mo_t", [128, CH], F32)
    ps_a2 = [p.ps(es, f"ps_a{i}", [128, CH], F32) for i in range(2)]
    ps_b2 = [p.ps(es, f"ps_b{i}", [128, CH], F32) for i in range(2)]
    ps_a = ps_a2[0]
    ps_b = ps_b2[0]
    ps_y = p.ps(es, "ps_y", [128, CH], F32)
    ps_op = p.ps(es, "ps_op", [128, CH], F32)

    p.ten.wait("ldg", lgv)
    p.vec.wait("ld", ldv)

    for c in range(NCH):
        t0 = c * CH if not rev else T - CH * (c + 1)
        p.sync.wait("vS", p.val("vS"))
        if not rev:
            for dt in range(16):
                wsl = W[:, dt * (CH + 3):(dt + 1) * (CH + 3)]
                if c == 0:
                    p.vec.memset(wsl[:, 0:3], 0.0)
                else:
                    p.vec.tensor_copy(wsl[:, 0:3], halo[:, ts(dt, 4)][:, 0:3])
            vh = p.val("vS")
            p.sync.wait("vS", vh)
            lv = 0
            for dt in range(16):
                wsl = W[:, dt * (CH + 3):(dt + 1) * (CH + 3)]
                lv = p.sync.dma(wsl[:, 3:CH + 3],
                                io["x_h"][b, dt, :, t0:t0 + CH], "ldx")
            p.vec.wait("ldx", lv)
        else:
            sl_done = [0, 0]
            for dt in range(16):
                sl = dt % 2
                wsl = W[:, dt * (CH + 3):(dt + 1) * (CH + 3)]
                p.sync.wait("vS", sl_done[sl])
                if c == 0:
                    lv = p.sync.dma(tmp_r[sl][:, 0:CH],
                                    io["x_h"][b, dt, :, t0:t0 + CH],
                                    "ldxA" if sl == 0 else "ldxB")
                    p.vec.wait("ldxA" if sl == 0 else "ldxB", lv)
                    p.vec.memset(tmp_r[sl][:, CH:CH + 3], 0.0)
                else:
                    lv = p.sync.dma(tmp_r[sl][:, 0:CH + 3],
                                    io["x_h"][b, dt, :, t0:t0 + CH + 3],
                                    "ldxA" if sl == 0 else "ldxB")
                    p.vec.wait("ldxA" if sl == 0 else "ldxB", lv)
                sl_done[sl] = p.vec.tensor_copy(wsl[:], tmp_r[sl][:, ::-1])
        for dt in range(16):
            wsl = W[:, dt * (CH + 3):(dt + 1) * (CH + 3)]
            usl = u_ch[:, ts(dt, CH)]
            p.vec.tensor_scalar(ftmp[:], wsl[:, 3:CH + 3],
                                convw[:, dt * 4 + 3:dt * 4 + 4], None, ALU.mult)
            for tap in range(3):
                p.vec.scalar_tensor_tensor(
                    ftmp[:], wsl[:, 2 - tap:CH + 2 - tap],
                    convw[:, dt * 4 + 2 - tap:dt * 4 + 3 - tap], ftmp[:],
                    ALU.mult, ALU.add)
            p.vec.tensor_scalar(usl, ftmp[:], convb[:, dt:dt + 1], None, ALU.add)
            p.vec.tensor_copy(halo[:, ts(dt, 4)][:, 0:3], wsl[:, CH:CH + 3])
        p.ten.wait("vS", p.val("vS"))
        p.ten.wait("sS", p.val("sS"))
        for dt in range(16):
            tv = p.ten.matmul(ps_a[:], xprojT[:, ts(dt, 128)],
                              u_ch[:, ts(dt, CH)], start=(dt == 0),
                              stop=(dt == 15))
        p.sca.wait("tS", tv)
        sv = p.sca.copy(xdbl_dt[:], ps_a[64:128, :])
        sv = p.sca.copy(B_sb[:], ps_a[0:16, :])
        sv = p.sca.copy(C_sb[:], ps_a[32:48, :])
        p.ten.wait("sS", sv)
        tv = p.ten.matmul(ps_b[:], seln[:], B_sb[:], start=True, stop=True)
        p.vec.wait("tS", tv)
        vv = p.vec.tensor_copy(B_rep[:], ps_b[:])
        p.ten.wait("vS", vv)
        tv = p.ten.matmul(ps_b[:], seln[:], C_sb[:], start=True, stop=True)
        p.vec.wait("tS", tv)
        vv = p.vec.tensor_copy(C_rep[:], ps_b[:])
        p.ten.wait("vS", vv)
        for dt in range(16):
            tv = p.ten.matmul(ps_b[:], dtpT[:, ts(dt, 128)], xdbl_dt[:],
                              start=True, stop=True)
            p.vec.wait("tS", tv)
            vv = p.vec.tensor_scalar(ftmp[:], ps_b[:], dtb[:, dt:dt + 1], None,
                                     ALU.add)
            p.ten.wait("vS", vv)
            p.vec.tensor_scalar(ftmp[:], ftmp[:], 1e-5, 1.0, ALU.max, ALU.min)
            p.vec.tensor_scalar(ftmp[:], ftmp[:], dtb[:, dt:dt + 1], None,
                                ALU.add)
            p.sca.wait("vS", p.val("vS"))
            sv = p.sca.activation(ftmp2[:], ftmp[:], AF.Abs)
            sv = p.sca.activation(ftmp2[:], ftmp2[:], AF.Exp, scale=-1.0)
            sv = p.sca.activation(ftmp2[:], ftmp2[:], AF.Ln, bias=1.0)
            sv = p.sca.activation(ftmp3[:], ftmp[:], AF.Relu)
            p.vec.wait("sS", sv)
            dsl = delta_ch[:, ts(dt, CH)]
            usl = u_ch[:, ts(dt, CH)]
            p.vec.tensor_tensor(dsl, ftmp3[:], ftmp2[:], ALU.add)
            p.vec.tensor_scalar(usl, usl, -10.0, 10.0, ALU.max, ALU.min)
            p.vec.tensor_tensor(du_ch[:, ts(dt, CH)], dsl, usl, ALU.mult)
        if not rev:
            p.sync.wait("vS", p.val("vS"))
            for dt in range(16):
                lv = p.sync.dma(sz_ch[:, ts(dt, CH)],
                                io["sz_h"][b, dt, :, t0:t0 + CH], "ldsz")
            p.vec.wait("ldsz", lv)
        else:
            sl_done = [0, 0]
            p.sync.wait("vS", p.val("vS"))
            for dt in range(16):
                sl = dt % 2
                p.sync.wait("vS", sl_done[sl])
                sem = "ldszA" if sl == 0 else "ldszB"
                lv = p.sync.dma(tmp_r[sl][:, 0:CH],
                                io["sz_h"][b, dt, :, t0:t0 + CH], sem)
                p.vec.wait(sem, lv)
                sl_done[sl] = p.vec.tensor_copy(sz_ch[:, ts(dt, CH)],
                                                tmp_r[sl][:, CH - 1::-1])
        p.ten.wait("vS", p.val("vS"))
        expv = [0, 0]
        bbv = [0, 0]
        scanv = [0, 0]
        hCv = [0, 0]
        ymmv = [0, 0]
        replv = [0, 0]
        durv = [0, 0]
        stcv = [0, 0]

        def emit_repl(j, i):
            s = i % 2
            kidx = j * 16 + i
            p.ten.wait("sS", expv[s])
            p.ten.matmul(ps_a2[s][:], selrep[:, ts(i, 128)],
                         delta_ch[:, ts(j, CH)], start=True, stop=True)
            p.ten.wait("vS", bbv[s])
            replv[s] = p.ten.matmul(ps_b2[s][:], selrep[:, ts(i, 128)],
                                    du_ch[:, ts(j, CH)], start=True, stop=True)
            p.sca.wait("tS", replv[s] - 1)
            p.sca.wait("vS", scanv[s])
            expv[s] = p.sca.activation(dA_sb[s][:], ps_a2[s][:], AF.Exp,
                                       scale=Acol[:, kidx:kidx + 1])

        def emit_rest(j, i):
            s = i % 2
            kidx = j * 16 + i
            p.vec.wait("tS", replv[s])
            bbv[s] = p.vec.tensor_tensor(bb_sb[s][:], ps_b2[s][:], B_rep[:],
                                         ALU.mult)
            p.vec.wait("sS", expv[s])
            scanv[s] = p.vec.tensor_tensor_scan(
                h_sb[s][:], dA_sb[s][:], bb_sb[s][:],
                state[:, kidx:kidx + 1], ALU.mult, ALU.add)
            p.vec.tensor_copy(state[:, kidx:kidx + 1], h_sb[s][:, CH - 1:CH])
            p.vec.wait("tS", ymmv[s])
            hCv[s] = p.vec.tensor_tensor(hC[s][:], h_sb[s][:], C_rep[:],
                                         ALU.mult)
            p.ten.wait("vS", hCv[s])
            ymmv[s] = p.ten.matmul(ps_y[:], selsum[:, ts(i, 128)], hC[s][:],
                                   start=(i == 0), stop=(i == 15),
                                   skip_group_check=True)

        for j in range(16):
            emit_repl(j, 0)
            for i in range(16):
                if i + 1 < 16:
                    emit_repl(j, i + 1)
                emit_rest(j, i)
            p.vec.wait("tS", ymmv[15 % 2])
            p.vec.scalar_tensor_tensor(ftmp[:], u_ch[:, ts(j, CH)],
                                       Dv[:, j:j + 1], ps_y[:], ALU.mult,
                                       ALU.add)
            vv = p.vec.tensor_tensor(yg[:, ts(j, CH)], ftmp[:],
                                     sz_ch[:, ts(j, CH)], ALU.mult)
            p.ten.wait("vS", vv)
        for ft in range(8):
            for j in range(16):
                tv = p.ten.matmul(ps_op[:], outpT[:, j * 1024 + ft * 128:
                                                  j * 1024 + ft * 128 + 128],
                                  yg[:, ts(j, CH)], start=(j == 0),
                                  stop=(j == 15))
            p.vec.wait("tS", tv)
            src_ps = ps_op[:, ::-1] if rev else ps_op[:]
            if dir_ == "f":
                vv = p.vec.tensor_copy(mo_t[:], src_ps)
            else:
                p.sync.wait("mrw", p.val("mrw"))
                lv = p.sync.dma(mo_t[:], io["maccT"][b, ft, :, t0:t0 + CH],
                                "mrw")
                p.vec.wait("mrw", lv)
                vv = p.vec.tensor_tensor(mo_t[:], src_ps, mo_t[:], ALU.add)
            p.ten.wait("vS", vv)
            p.sync.wait("vS", vv)
            sv2 = p.sync.dma(io["maccT"][b, ft, :, t0:t0 + CH], mo_t[:], "mrw")
            p.vec.wait("mrw", sv2)
    p.sync.wait("mrw", p.val("mrw"))



def build_l1ffn(p, io):
    nc = p.nc
    es_p = p.es
    identB = p.sb(es_p, "identB", [128, 128], BF16)
    identF = p.sb(es_p, "identF", [128, 128], F32)
    maskC = p.sb(es_p, "maskC", [128, 128], F32)
    maskA = p.sb(es_p, "maskA", [128, 128], F32)
    G64 = p.sb(es_p, "G64", [128, 128], BF16)
    P64 = p.sb(es_p, "P64", [128, 128], BF16)
    ones128 = p.sb(es_p, "ones128", [128, 128], BF16)
    tab = {}
    for nm in ("CSq8", "SSq8", "CSk", "SSk"):
        tab[nm] = p.sb(es_p, nm, [128, T], BF16)
        p.gps.dma(tab[nm][:], io[nm], "ldtab")
    for nm, t_ in (("identB", identB), ("identF", identF), ("maskC", maskC),
                   ("maskA", maskA), ("G64", G64), ("P64", P64),
                   ("ones128", ones128)):
        p.sync.dma(t_[:], io[nm], "ld")
    for l in range(2):
        for nm in ("qsc", "ksc", "lng", "lnb", "b1a", "b1g", "b2"):
            b1col(p, io, nm, l, 0, 1)
    for e in (p.vec, p.sca, p.ten):
        e.wait("ld", p.val("ld"))
        e.wait("ldtab", p.val("ldtab"))

    for b in range(NB):
        with contextlib.ExitStack() as es_b:
            x_br = p.sb(es_b, "x_br", [128, 8 * T], F32)
            xtmp = p.sb(es_b, "xtmp", [128, T], BF16)
            for dt in range(8):
                lv = p.sync.dma(xtmp[:], io["xn1T_h"][b, dt], "ldf")
                p.vec.wait("ldf", lv)
                vv = p.vec.tensor_copy(x_br[:, ts(dt, T)], xtmp[:])
                p.sync.wait("vS", vv)
            for l in range(2):
                _l1f_attn(p, io, b, l, x_br, identB, maskC, maskA, G64, P64,
                          tab)
                p.barrier()
                _l1f_ffn(p, io, b, l, x_br, ones128)
                p.barrier()
            _l1f_out(p, es_b, io, b, x_br, identF)
            p.barrier()
    p.sync.wait("st", p.val("st"))


def _shift_cast(p, x_br, hb, dt):
    if dt < 4:
        return p.vec.tensor_copy(hb[:, ts(dt, T)], x_br[:, ts(dt, T)])
    p.vec.memset(hb[:, dt * T:dt * T + 1], 0.0)
    return p.vec.tensor_copy(hb[:, dt * T + 1:(dt + 1) * T],
                             x_br[:, dt * T:(dt + 1) * T - 1])


def _l1f_attn(p, io, b, l, x_br, identB, maskC, maskA, G64, P64, tab):
    es = contextlib.ExitStack()
    with es:
        hb = p.sb(es, "hb", [128, 8 * T], BF16)
        o_sb = p.sb(es, "o_sb", [128, 4 * T], BF16)
        wq_h = p.sb(es, "wq_h", [128, 8 * 64], BF16)
        wk_h = p.sb(es, "wk_h", [128, 8 * 64], BF16)
        wv_h = p.sb(es, "wv_h", [128, 8 * 64], BF16)
        wo_sb = p.sb(es, "wo_sb", [128, 4 * 1024], BF16)
        qT = p.sb(es, "qT", [64, T], BF16)
        kT = p.sb(es, "kT", [64, T], BF16)
        qrot = p.sb(es, "qrot", [64, T], BF16)
        krot = p.sb(es, "krot", [64, T], BF16)
        v_sb = p.sb(es, "v_sb", [128, 16 * 64], BF16)
        sq_h = p.sb(es, "sq_h", [64, T], BF16)
        f32a = p.sb(es, "f32a", [64, 512], F32)
        f32b = p.sb(es, "f32b", [64, 512], F32)
        E_sb = p.sb(es, "E_sb", [128, 640], BF16)
        attn_sb = p.sb(es, "attn_sb", [128, 640], BF16)
        attnT = p.sb(es, "attnT", [128, 640], BF16)
        stq = {nm: p.sb(es, "l1_" + nm, [128, 1], F32)
               for nm in ("m", "nm", "r", "ri")}
        ps_s = p.ps(es, "ps_s", [128, 1024], F32)
        ps_qf2 = [p.ps(es, f"ps_qf{i}", [128, 512], F32) for i in range(2)]
        ps_qf = ps_qf2[0]
        ps_q = ps_qf[0:64, :]
        ps_of = p.ps(es, "ps_of", [128, 128], F32)
        ps_o = ps_of[0:64, :]
        ps_wo = p.ps(es, "ps_wo", [128, 512], F32)
        pt_t = [p.ps(es, f"l1pt{i}", [128, 128], BF16) for i in range(2)]

        p.vec.wait("tS", p.val("tS"))
        for dt in range(8):
            vv = _shift_cast(p, x_br, hb, dt)
        p.ten.wait("vS", vv)
        p.gps.dma(wo_sb.rearrange("p (a f) -> p a f", a=4),
                  io["wo1"][l].rearrange("(a p) f -> p a f", p=128), "ldg")
        for h in range(8):
            p.gps.wait("tS", p.val("tS"))
            wq3 = io["wqkv"][l].rearrange("(a p) f -> p a f", p=128)
            for wdst, off in ((wq_h, 0), (wk_h, 512), (wv_h, 1024)):
                p.gps.dma(wdst.rearrange("p (a w) -> p a w", a=8),
                          wq3[:, :, off + h * 64:off + h * 64 + 64], "ldg")
            lgv = p.val("ldg")
            p.ten.wait("ldg", lgv)
            p.ten.wait("sS", p.val("sS"))
            qkvev = [0, 0]
            for ci, (wsb, dst, ch) in enumerate(
                    [(w_, d_, c_) for (w_, d_) in ((wq_h, qT), (wk_h, kT))
                     for c_ in range(4)]):
                sl = ci % 2
                p.ten.wait("sS", qkvev[sl])
                for dt in range(8):
                    tv = p.ten.matmul(ps_qf2[sl][0:64, :], wsb[:, ts(dt, 64)],
                                      hb[:, dt * T + ch * 512:
                                         dt * T + (ch + 1) * 512],
                                      start=(dt == 0), stop=(dt == 7))
                p.sca.wait("tS", tv)
                qkvev[sl] = p.sca.copy(dst[:, ts(ch, 512)], ps_qf2[sl][0:64, :])
            for tt in range(16):
                sl = tt % 2
                p.ten.wait("sS", qkvev[sl])
                for dt in range(8):
                    tv = p.ten.matmul(ps_qf2[sl][:, 0:64],
                                      hb[:, dt * T + tt * 128:
                                         dt * T + tt * 128 + 128],
                                      wv_h[:, ts(dt, 64)],
                                      start=(dt == 0), stop=(dt == 7))
                p.sca.wait("tS", tv)
                qkvev[sl] = p.sca.copy(v_sb[:, ts(tt, 64)], ps_qf2[sl][:, 0:64])
            p.ten.wait("sS", p.val("sS"))
            for (src, dst, scol, cs, ss) in (
                    (qT, qrot, "qsc", "CSq8", "SSq8"),
                    (kT, krot, "ksc", "CSk", "SSk")):
                p.sca.wait("tS", p.val("tS"))
                sv = p.sca.activation(sq_h[:], src[:], AF.Square)
                p.ten.wait("sS", sv)
                for ch in range(4):
                    tv = p.ten.matmul(ps_q[:], G64[0:64, 0:64],
                                      sq_h[:, ts(ch, 512)],
                                      start=True, stop=True)
                    p.vec.wait("tS", tv)
                    p.vec.tensor_scalar(f32a[:], ps_q[:], 1e-24, 3e38,
                                        ALU.max, ALU.min)
                    vv = p.vec.tensor_scalar(f32a[:], f32a[:], 1.0, None,
                                             ALU.mult)
                    p.sca.wait("vS", vv)
                    sv = p.sca.activation(f32a[:], f32a[:], AF.Sqrt)
                    p.vec.wait("sS", sv)
                    p.vec.reciprocal(f32b[:], f32a[:])
                    vv = p.vec.scalar_tensor_tensor(
                        sq_h[:, ts(ch, 512)], src[:, ts(ch, 512)],
                        b1col(p, io, scol, l, 0, 64), f32b[:], ALU.mult,
                        ALU.mult)
                    p.ten.wait("vS", vv)
                    tv = p.ten.matmul(ps_q[:], P64[0:64, 0:64],
                                      sq_h[:, ts(ch, 512)], start=True,
                                      stop=True)
                    p.vec.wait("tS", tv)
                    p.vec.tensor_tensor(f32a[:], sq_h[:, ts(ch, 512)],
                                        tab[cs][0:64, ts(ch, 512)], ALU.mult)
                    p.vec.tensor_tensor(f32b[:], ps_q[:],
                                        tab[ss][0:64, ts(ch, 512)], ALU.mult)
                    vv = p.vec.tensor_tensor(dst[:, ts(ch, 512)], f32a[:],
                                             f32b[:], ALU.add)
                    p.ten.wait("vS", vv)
                p.sca.wait("vS", p.val("vS"))
            for tt in range(16):
                kt0 = max(0, tt - 4)
                nkt = tt + 1 - kt0
                span = nkt * 128
                tv = p.ten.matmul(ps_s[:, 0:min(span, 512)],
                                  qrot[:, ts(tt, 128)],
                                  krot[:, kt0 * 128:kt0 * 128 + min(span, 512)],
                                  start=True, stop=True)
                if span > 512:
                    tv = p.ten.matmul(ps_s[:, 512:span],
                                      qrot[:, ts(tt, 128)],
                                      krot[:, kt0 * 128 + 512:kt0 * 128 + span],
                                      start=True, stop=True)
                p.vec.wait("tS", tv)
                if tt >= 4:
                    p.vec.tensor_tensor(ps_s[:, 0:128], ps_s[:, 0:128],
                                        maskA[:], ALU.add)
                p.vec.tensor_tensor(ps_s[:, span - 128:span],
                                    ps_s[:, span - 128:span], maskC[:], ALU.add)
                p.vec.reduce_max(stq["m"][:], ps_s[:, 0:span], AX.X)
                vv = p.vec.tensor_scalar(stq["nm"][:], stq["m"][:], -1.0, None,
                                         ALU.mult)
                p.sca.wait("vS", vv)
                sv = p.sca.activation(E_sb[:, 0:span], ps_s[:, 0:span], AF.Exp,
                                      bias=stq["nm"][:, 0:1],
                                      accum_out=stq["r"][:])
                p.ten.wait("sS", sv)
                p.vec.wait("sS", sv)
                p.vec.reciprocal(stq["ri"][:], stq["r"][:])
                p.vec.wait("tS", p.val("tS"))
                vv = p.vec.tensor_scalar(attn_sb[:, 0:span], E_sb[:, 0:span],
                                         stq["ri"][:, 0:1], None, ALU.mult)
                p.ten.wait("vS", vv)
                p.ten.wait("sS", p.val("sS"))
                for kt in range(nkt):
                    pb = pt_t[kt % 2]
                    if kt >= 2:
                        p.ten.wait("sS", p.val("sS") - 1)
                    tv = p.ten.transpose(pb[:], attn_sb[:, ts(kt, 128)],
                                         identB[:])
                    p.sca.wait("tS", tv)
                    sv = p.sca.copy(attnT[:, ts(kt, 128)], pb[:])
                p.ten.wait("sS", p.val("sS"))
                for kt in range(nkt):
                    tv = p.ten.matmul(ps_o[:],
                                      v_sb[:, (kt0 + kt) * 64:(kt0 + kt + 1) * 64],
                                      attnT[:, ts(kt, 128)],
                                      start=(kt == 0), stop=(kt == nkt - 1))
                p.sca.wait("tS", tv)
                sv = p.sca.copy(
                    o_sb[(h % 2) * 64:(h % 2) * 64 + 64,
                         (h // 2) * T + tt * 128:(h // 2) * T + tt * 128 + 128],
                    ps_o[:])
                p.ten.wait("sS", sv)
        p.ten.wait("ldg", p.val("ldg"))
        p.ten.wait("sS", p.val("sS"))
        for ft in range(8):
            for ch in range(4):
                for et in range(4):
                    tv = p.ten.matmul(
                        ps_wo[:],
                        wo_sb[:, et * 1024 + ft * 128:et * 1024 + ft * 128 + 128],
                        o_sb[:, et * T + ch * 512:et * T + (ch + 1) * 512],
                        start=(et == 0), stop=(et == 3))
                p.vec.wait("tS", tv)
                dst = x_br[:, ft * T + ch * 512:ft * T + (ch + 1) * 512]
                vv = p.vec.tensor_tensor(dst, ps_wo[:], dst, ALU.add)
                p.ten.wait("vS", vv)


def _l1f_ffn(p, io, b, l, x_br, ones128):
    es = contextlib.ExitStack()
    HID = 2730
    PAIRS = [(j * 128, min(128, HID - j * 128)) for j in range((HID + 127) // 128)]
    with es:
        xn2 = p.sb(es, "xn2", [128, 8 * T], BF16)
        with contextlib.ExitStack() as es2:
            hb2 = p.sb(es2, "hb2", [128, 8 * T], BF16)
            hsq = p.sb(es2, "hsq", [128, 8 * T], BF16)
            m_bc = p.sb(es2, "m_bc", [128, 512], F32)
            r_bc = p.sb(es2, "r_bc", [128, 512], F32)
            fa = p.sb(es2, "fa", [128, 512], F32)
            ps_1 = p.ps(es2, "ps_1", [128, 512], F32)
            ps_2 = p.ps(es2, "ps_2", [128, 512], F32)
            p.vec.wait("tS", p.val("tS"))
            for dt in range(8):
                vv = _shift_cast(p, x_br, hb2, dt)
                p.sca.wait("vS", vv)
                sv = p.sca.activation(hsq[:, ts(dt, T)], hb2[:, ts(dt, T)],
                                      AF.Square)
            p.ten.wait("sS", p.val("sS"))
            p.ten.wait("vS", p.val("vS"))
            for ch in range(4):
                for dt in range(8):
                    p.ten.matmul(ps_1[:], ones128[:],
                                 hb2[:, dt * T + ch * 512:dt * T + (ch + 1) * 512],
                                 start=(dt == 0), stop=(dt == 7),
                                 skip_group_check=True)
                for dt in range(8):
                    tv = p.ten.matmul(ps_2[:], ones128[:],
                                      hsq[:, dt * T + ch * 512:
                                          dt * T + (ch + 1) * 512],
                                      start=(dt == 0), stop=(dt == 7),
                                      skip_group_check=True)
                p.vec.wait("tS", tv)
                p.vec.tensor_scalar(m_bc[:], ps_1[:], 1.0 / D, None, ALU.mult)
                p.vec.tensor_scalar(r_bc[:], ps_2[:], 1.0 / D, None, ALU.mult)
                p.vec.tensor_tensor(fa[:], m_bc[:], m_bc[:], ALU.mult)
                p.vec.tensor_tensor(fa[:], r_bc[:], fa[:], ALU.subtract)
                vv = p.vec.tensor_scalar(fa[:], fa[:], 1e-5, None, ALU.add)
                p.ten.wait("vS", vv)
                p.sca.wait("vS", vv)
                sv = p.sca.activation(fa[:], fa[:], AF.Sqrt)
                p.vec.wait("sS", sv)
                p.vec.reciprocal(r_bc[:], fa[:])
                for dt in range(8):
                    sl = slice(dt * T + ch * 512, dt * T + (ch + 1) * 512)
                    p.vec.tensor_tensor(fa[:], hb2[:, sl], m_bc[:], ALU.subtract)
                    p.vec.tensor_tensor(fa[:], fa[:], r_bc[:], ALU.mult)
                    vv = p.vec.tensor_scalar(
                        xn2[:, sl], fa[:], b1col(p, io, "lng", l, dt, 128),
                        b1col(p, io, "lnb", l, dt, 128), ALU.mult, ALU.add)
        p.barrier()
        with contextlib.ExitStack() as es3:
            hh_sb = p.sb(es3, "hh_sb", [128, len(PAIRS) * 512], BF16)
            w1a = [p.sb(es3, f"w1a{i}", [128, 8 * 128], BF16) for i in range(2)]
            w1g = [p.sb(es3, f"w1g{i}", [128, 8 * 128], BF16) for i in range(2)]
            w2_sb = p.sb(es3, "w2_sb", [128, len(PAIRS) * 1024], BF16)
            ha = p.sb(es3, "ha", [128, 512], BF16)
            hg = p.sb(es3, "hg", [128, 512], BF16)
            ps_a = p.ps(es3, "ps_a", [128, 512], F32)
            ps_g = p.ps(es3, "ps_g", [128, 512], F32)
            ps_w2 = p.ps(es3, "ps_w2", [128, 512], F32)
            for j, (off, sz) in enumerate(PAIRS):
                p.gps.dma(w2_sb[0:sz, ts(j, 1024)],
                          io["w2"][l, off:off + sz, :], "ldg")
            w1sem = ["w1s0", "w1s1"]
            w1lv = [0, 0]
            w1used = [0, 0]
            npair = len(PAIRS)

            w13 = io["w1"][l].rearrange("(a p) f -> p a f", p=128)

            def w1_prefetch(ch, jj):
                sl = jj % 2
                off, sz = PAIRS[jj]
                p.gps.wait("tS", w1used[sl])
                p.gps.dma(w1a[sl].rearrange("p (a w) -> p a w", a=8)[:, :, 0:sz],
                          w13[:, :, off:off + sz], w1sem[sl])
                w1lv[sl] = p.gps.dma(
                    w1g[sl].rearrange("p (a w) -> p a w", a=8)[:, :, 0:sz],
                    w13[:, :, HID + off:HID + off + sz], w1sem[sl])

            evacv = [0, 0]
            for ch in range(4):
                if ch == 0:
                    w1_prefetch(0, 0)
                for j, (off, sz) in enumerate(PAIRS):
                    sl = j % 2
                    nj = j + 1
                    p.ten.wait(w1sem[sl], w1lv[sl])
                    p.ten.wait("sS", p.val("sS"))
                    p.ten.wait("vS", p.val("vS"))
                    for dt in range(8):
                        p.ten.matmul(ps_a[0:sz, :],
                                     w1a[sl][:, ts(dt, 128)][:, 0:sz],
                                     xn2[:, dt * T + ch * 512:
                                         dt * T + (ch + 1) * 512],
                                     start=(dt == 0), stop=(dt == 7),
                                     skip_group_check=True)
                    for dt in range(8):
                        tv = p.ten.matmul(ps_g[0:sz, :],
                                          w1g[sl][:, ts(dt, 128)][:, 0:sz],
                                          xn2[:, dt * T + ch * 512:
                                              dt * T + (ch + 1) * 512],
                                          start=(dt == 0), stop=(dt == 7),
                                          skip_group_check=True)
                    w1used[sl] = tv
                    if nj < npair:
                        w1_prefetch(ch, nj)
                    elif ch < 3:
                        w1_prefetch(ch + 1, 0)
                    p.vec.wait("tS", tv)
                    vv = p.vec.tensor_scalar(ha[0:sz, :], ps_a[0:sz, :],
                                             b1col(p, io, "b1a", l, j, sz),
                                             None, ALU.add)
                    p.sca.wait("tS", tv)
                    sv = p.sca.activation(hg[0:sz, :], ps_g[0:sz, :], AF.Gelu,
                                          bias=b1col(p, io, "b1g", l, j, sz))
                    p.vec.wait("sS", sv)
                    vv = p.vec.tensor_tensor(hh_sb[0:sz, ts(j, 512)],
                                             ha[0:sz, :], hg[0:sz, :], ALU.mult)
                    p.ten.wait("vS", vv)
                p.ten.wait("ldg", p.val("ldg"))
                for ft in range(8):
                    for j, (off, sz) in enumerate(PAIRS):
                        tv = p.ten.matmul(
                            ps_w2[:], w2_sb[0:sz, j * 1024 + ft * 128:
                                            j * 1024 + ft * 128 + 128],
                            hh_sb[0:sz, ts(j, 512)],
                            start=(j == 0), stop=(j == len(PAIRS) - 1))
                    p.vec.wait("tS", tv)
                    dst = x_br[:, ft * T + ch * 512:ft * T + (ch + 1) * 512]
                    vv = p.vec.scalar_tensor_tensor(
                        dst, ps_w2[:], b1col(p, io, "b2", l, ft, 128), dst,
                        ALU.add, ALU.add)
                    p.ten.wait("vS", vv)


def b1col(p, io, name, l, j, sz):
    key = ("_b1", name, l)
    cache = p.__dict__.setdefault("_colcache", {})
    if key not in cache:
        nt = io[name].shape[2]
        t = p.sb(p.es, f"col_{name}{l}", [128, nt], F32)
        p.sync.dma(t[:], io[name][l], "ld")
        cache[key] = t
    return cache[key][0:sz, j:j + 1]


def _l1f_out(p, es_b, io, b, x_br, identF):
    with contextlib.ExitStack() as es:
        mt = p.sb(es, "mt", [128, T], F32)
        out_t = p.sb(es, "out_t", [128, D], F32)
        x1_t = p.sb(es, "x1_t", [128, D], F32)
        ps_f = [p.ps(es, f"opf{i}", [128, 128], F32) for i in range(2)]
        for dt in range(8):
            p.sync.wait("vS", p.val("vS"))
            lv = p.sync.dma(mt[:], io["maccT"][b, dt], "ldf")
            p.vec.wait("ldf", lv)
            vv = p.vec.tensor_tensor(x_br[:, ts(dt, T)], mt[:],
                                     x_br[:, ts(dt, T)], ALU.add)
        p.ten.wait("vS", p.val("vS"))
        for tt in range(16):
            p.sync.wait("st", p.val("st"))
            lv = p.sync.dma(x1_t[:], io["x1"][b, ts(tt, 128), :], "ldx1")
            p.ten.wait("sS", p.val("sS"))
            p.ten.wait("vS", p.val("vS"))
            p.ten.wait("st", p.val("st"))
            for dt in range(8):
                pb = ps_f[dt % 2]
                if dt >= 2:
                    p.ten.wait("sS", p.val("sS") - 1)
                tv = p.ten.transpose(pb[:], x_br[:, dt * T + tt * 128:
                                                 dt * T + tt * 128 + 128],
                                     identF[:])
                p.sca.wait("tS", tv)
                sv = p.sca.copy(out_t[:, ts(dt, 128)], pb[:])
            p.vec.wait("sS", p.val("sS"))
            p.vec.wait("ldx1", lv)
            vv = p.vec.tensor_tensor(out_t[:], x1_t[:], out_t[:], ALU.add)
            p.sync.wait("vS", vv)
            p.sync.dma(io["x2"][b, ts(tt, 128), :], out_t[:], "st")




_L0_NAMES = ["wqT", "wkT", "wvT", "woT", "lq1T", "lk1T", "lq2T", "lk2T", "rms",
             "wgT", "wuT", "wdT", "identB", "maskC", "ones64"]


def _declare(nc, io, cw):
    for nm, arr in cw.items():
        dt = {np.dtype(np.float32): F32, np.dtype(np.int32): I32,
              np.dtype(BF): BF16}[arr.dtype]
        io[nm] = nc.dram_tensor(nm, list(arr.shape), dt, kind="ExternalInput").ap()


def run_l0(pool, xfull):
    cw = {nm: pool[nm] for nm in _L0_NAMES}
    with contextlib.ExitStack() as es:
        nc = bass.Bass("TRN2", target_bir_lowering=False)
        io = {}
        io["x"] = nc.dram_tensor("x", [NB, T, D], F32, kind="ExternalInput").ap()
        io["x1"] = nc.dram_tensor("x1", [NB, T, D], F32,
                                  kind="ExternalOutput").ap()
        io["xnT_h"] = nc.dram_tensor("xnT_h", [NB, 16, 8, 128, 128], BF16,
                                     kind="Internal").ap()
        io["attn_h"] = nc.dram_tensor("attn_h", [NB, T, D], F32,
                                      kind="Internal").ap()
        _declare(nc, io, cw)
        p = P(nc, es)
        build_l0(p, io)
        p.finalize()
        in_maps = []
        for c in range(N_CORE):
            m = dict(cw)
            m["x"] = np.ascontiguousarray(xfull[2 * c:2 * c + 2])
            in_maps.append(m)
        res = run_bass_kernel_spmd(nc, in_maps, core_ids=list(range(N_CORE)))
    return np.concatenate([r["x1"] for r in res.results], axis=0)


_MB_NAMES = (["inpT", "outpT", "identB", "selrep", "selsum", "seln"] +
             [f"{nm}_{s}" for s in ("f", "b")
              for nm in ("xprojT", "dtpT", "convw", "convb", "dtb", "alog", "D")])


def run_mamba(pool, x1full):
    cw = {nm: pool[nm] for nm in _MB_NAMES}
    with contextlib.ExitStack() as es:
        nc = bass.Bass("TRN2", target_bir_lowering=False)
        io = {}
        io["x1"] = nc.dram_tensor("x1", [NB, T, D], F32,
                                  kind="ExternalInput").ap()
        io["xn1T_h"] = nc.dram_tensor("xn1T_h", [NB, 8, 128, T], BF16,
                                      kind="ExternalOutput").ap()
        io["maccT"] = nc.dram_tensor("maccT", [NB, 8, 128, T], F32,
                                     kind="ExternalOutput").ap()
        io["x_h"] = nc.dram_tensor("x_h", [NB, 16, 128, T], BF16,
                                   kind="Internal").ap()
        io["sz_h"] = nc.dram_tensor("sz_h", [NB, 16, 128, T], BF16,
                                    kind="Internal").ap()
        _declare(nc, io, cw)
        p = P(nc, es)
        build_mamba(p, io)
        p.finalize()
        in_maps = []
        for c in range(N_CORE):
            m = dict(cw)
            m["x1"] = np.ascontiguousarray(x1full[2 * c:2 * c + 2])
            in_maps.append(m)
        res = run_bass_kernel_spmd(nc, in_maps, core_ids=list(range(N_CORE)))
    xn1T = np.concatenate([r["xn1T_h"] for r in res.results], axis=0)
    macc = np.concatenate([r["maccT"] for r in res.results], axis=0)
    return xn1T, macc


_L1_NAMES = ["wqkv", "wo1", "w1", "w2", "b1a", "b1g", "b2", "lng", "lnb",
             "qsc", "ksc", "identB", "identF", "maskC", "maskA", "G64", "P64",
             "ones128", "CSq8", "SSq8", "CSk", "SSk"]


def run_l1ffn(pool, x1full, xn1Tfull, maccfull):
    cw = {nm: pool[nm] for nm in _L1_NAMES}
    with contextlib.ExitStack() as es:
        nc = bass.Bass("TRN2", target_bir_lowering=False)
        io = {}
        io["x1"] = nc.dram_tensor("x1", [NB, T, D], F32,
                                  kind="ExternalInput").ap()
        io["xn1T_h"] = nc.dram_tensor("xn1T_h", [NB, 8, 128, T], BF16,
                                      kind="ExternalInput").ap()
        io["maccT"] = nc.dram_tensor("maccT", [NB, 8, 128, T], F32,
                                     kind="ExternalInput").ap()
        io["x2"] = nc.dram_tensor("x2", [NB, T, D], F32,
                                  kind="ExternalOutput").ap()
        _declare(nc, io, cw)
        p = P(nc, es)
        build_l1ffn(p, io)
        p.finalize()
        in_maps = []
        for c in range(N_CORE):
            m = dict(cw)
            m["x1"] = np.ascontiguousarray(x1full[2 * c:2 * c + 2])
            m["xn1T_h"] = np.ascontiguousarray(xn1Tfull[2 * c:2 * c + 2])
            m["maccT"] = np.ascontiguousarray(maccfull[2 * c:2 * c + 2])
            in_maps.append(m)
        res = run_bass_kernel_spmd(nc, in_maps, core_ids=list(range(N_CORE)))
    return np.concatenate([r["x2"] for r in res.results], axis=0)


def run_fused(pool, xfull):
    names = sorted(set(_L0_NAMES + list(_MB_NAMES) + _L1_NAMES))
    cw = {nm: pool[nm] for nm in names}
    with contextlib.ExitStack() as es:
        nc = bass.Bass("TRN2", target_bir_lowering=False)
        io = {}
        io["x"] = nc.dram_tensor("x", [NB, T, D], F32, kind="ExternalInput").ap()
        io["x2"] = nc.dram_tensor("x2", [NB, T, D], F32,
                                  kind="ExternalOutput").ap()
        for nm, shape, dt_ in (
                ("x1", [NB, T, D], F32),
                ("xnT_h", [NB, 16, 8, 128, 128], BF16),
                ("attn_h", [NB, T, D], F32),
                ("xn1T_h", [NB, 8, 128, T], BF16),
                ("maccT", [NB, 8, 128, T], F32),
                ("x_h", [NB, 16, 128, T], BF16),
                ("sz_h", [NB, 16, 128, T], BF16)):
            io[nm] = nc.dram_tensor(nm, shape, dt_, kind="Internal").ap()
        _declare(nc, io, cw)
        p = P(nc, es)
        build_l0(p, io)
        p.barrier()
        build_mamba(p, io)
        p.barrier()
        build_l1ffn(p, io)
        p.finalize()
        in_maps = []
        for c in range(N_CORE):
            m = dict(cw)
            m["x"] = np.ascontiguousarray(xfull[2 * c:2 * c + 2])
            in_maps.append(m)
        res = run_bass_kernel_spmd(nc, in_maps, core_ids=list(range(N_CORE)))
    return np.concatenate([r["x2"] for r in res.results], axis=0)


def kernel(**inputs):
    consts = _consts()
    w = _prep_weights(inputs)
    pool = {**w, **consts}
    xfull = np.asarray(inputs["L0_attn_X"], dtype=np.float32)
    return run_fused(pool, xfull)


# version 08
# speedup vs baseline: 1.2313x; 1.2107x over previous
import contextlib
import os
import traceback
import numpy as np
import ml_dtypes
import concourse.bass as bass
import concourse.mybir as mybir
from concourse.bass_utils import run_bass_kernel_spmd

F32 = mybir.dt.float32
BF16 = mybir.dt.bfloat16
I32 = mybir.dt.int32
AF = mybir.ActivationFunctionType
ALU = mybir.AluOpType
AX = mybir.AxisListType
BF = ml_dtypes.bfloat16

NEG = -1.0e30
T = 2048
D = 1024
NB = 2
N_CORE = 8
ts = bass.ts


ENGS = ("sync", "gpsimd", "vector", "scalar", "tensor")

SELF_SEM = {"tensor": "tS", "vector": "vS", "scalar": "sS", "gpsimd": "gS"}
SELF_WAIT = {"vector", "scalar", "gpsimd"}


class Eng:
    def __init__(self, p, name):
        self.p = p
        self.name = name

    def __getattr__(self, op):
        p, name = self.p, self.name
        ssem = SELF_SEM.get(name)

        def f(*args, **kw):
            if name in SELF_WAIT:
                prior = p.cnt[ssem]
                self.wait(ssem, prior)
            v = p.inc(ssem)
            s = p.sems[ssem]
            if p.debug_lines:
                fr = traceback.extract_stack(limit=2)[0]
                lbl = f"{op}@{fr.lineno}"
                def thunk(e, lbl=lbl, op=op, args=args, kw=kw):
                    with p.nc.named_scope(lbl):
                        getattr(e, op)(*args, **kw).then_inc(s, 1)
                p.q[name].append(thunk)
            else:
                p.q[name].append(
                    lambda e: getattr(e, op)(*args, **kw).then_inc(s, 1))
            return v
        return f

    def dma(self, dst, src, sem):
        p, name = self.p, self.name
        s = p.sem(sem)
        v = p.inc(sem, 16)
        p.q[name].append(lambda e: e.dma_start(dst, src).then_inc(s, 16))
        return v

    def wait(self, sem, v):
        p, name = self.p, self.name
        if v <= p.waited[name].get(sem, 0):
            return
        p.waited[name][sem] = v
        s = p.sems[sem]
        p.q[name].append(lambda e: e.wait_ge(s, v))


class P:
    def __init__(self, nc, es):
        self.nc = nc
        self.es = es
        self.q = {k: [] for k in ENGS}
        self.sems = {}
        self.cnt = {}
        self.waited = {k: {} for k in ENGS}
        self.debug_lines = bool(os.environ.get("MK_DEBUG_LINES"))
        self.sync = Eng(self, "sync")
        self.gps = Eng(self, "gpsimd")
        self.vec = Eng(self, "vector")
        self.sca = Eng(self, "scalar")
        self.ten = Eng(self, "tensor")
        self.engmap = {"sync": self.sync, "gpsimd": self.gps, "vector": self.vec,
                       "scalar": self.sca, "tensor": self.ten}
        self.nbar = 0
        for nm in ("ld", "ldg", "st", "tS", "vS", "sS", "gS", "bar"):
            self.sem(nm)

    def sem(self, nm):
        if nm not in self.sems:
            self.sems[nm] = self.es.enter_context(self.nc.semaphore(nm))
            self.cnt[nm] = 0
        return self.sems[nm]

    def inc(self, sem, n=1):
        self.cnt[sem] += n
        return self.cnt[sem]

    def val(self, sem):
        return self.cnt[sem]

    def barrier(self):
        self.nbar += 1
        tgt = 5 * self.nbar
        s = self.sems["bar"]
        allsems = [k for k in self.sems if k != "bar"]
        for nm in ENGS:
            eng = self.engmap[nm]
            for sm in allsems:
                eng.wait(sm, self.cnt[sm])
            self.q[nm].append(lambda e, s=s: e.nop().then_inc(s, 1))
        for nm in ENGS:
            self.q[nm].append(lambda e, s=s, tgt=tgt: e.wait_ge(s, tgt))

    def sb(self, es, name, shape, dtype):
        self._uid = getattr(self, "_uid", 0) + 1
        return es.enter_context(
            self.nc.sbuf_tensor(f"{name}_u{self._uid}", shape, dtype))

    def ps(self, es, name, shape, dtype=F32):
        self._uid = getattr(self, "_uid", 0) + 1
        return es.enter_context(
            self.nc.psum_tensor(f"{name}_u{self._uid}", shape, dtype))

    def finalize(self):
        with self.nc.Block() as block:
            @block.sync
            def _(e):
                for f in self.q["sync"]:
                    f(e)

            @block.gpsimd
            def _(e):
                for f in self.q["gpsimd"]:
                    f(e)

            @block.vector
            def _(e):
                for f in self.q["vector"]:
                    f(e)

            @block.scalar
            def _(e):
                for f in self.q["scalar"]:
                    f(e)

            @block.tensor
            def _(e):
                for f in self.q["tensor"]:
                    f(e)



def _consts():
    c = {}
    c["identB"] = np.eye(128, dtype=np.float32).astype(BF)
    c["identF"] = np.eye(128, dtype=np.float32)
    i = np.arange(128)
    c["maskC"] = np.where(i[:, None] >= i[None, :], 0.0, NEG).astype(np.float32)
    c["maskA"] = np.where(i[:, None] <= i[None, :], 0.0, NEG).astype(np.float32)
    c["ones64"] = np.ones((64, 128), dtype=np.float32).astype(BF)
    c["ones128"] = np.ones((128, 128), dtype=np.float32).astype(BF)
    g = (i[:, None] // 64 == i[None, :] // 64).astype(np.float32)
    c["G64"] = g.astype(BF)
    p64 = np.zeros((128, 128), dtype=np.float32)
    for blk in range(2):
        o = blk * 64
        for m in range(32):
            p64[o + m + 32, o + m] = -1.0
        for m in range(32, 64):
            p64[o + m - 32, o + m] = 1.0
    c["P64"] = p64.astype(BF)
    selrep = np.zeros((128, 16 * 128), dtype=np.float32)
    selsum = np.zeros((128, 16 * 128), dtype=np.float32)
    for ii in range(16):
        for m in range(128):
            selrep[8 * ii + m // 16, ii * 128 + m] = 1.0
        for k in range(128):
            selsum[k, ii * 128 + 8 * ii + k // 16] = 1.0
    c["selrep"] = selrep.astype(BF)
    c["selsum"] = selsum.astype(BF)
    seln = np.zeros((16, 128), dtype=np.float32)
    for m in range(128):
        seln[m % 16, m] = 1.0
    c["seln"] = seln.astype(BF)
    DH, WIN = 64, 512
    half = np.arange(0, DH, 2, dtype=np.float64)
    inv_freq = 1.0 / (10000.0 ** (half / DH))
    t = np.arange(T, dtype=np.float64)
    freqs = t[:, None] * inv_freq[None, :]
    freqs = np.concatenate([freqs, freqs], axis=-1)
    base = (half + 0.4 * DH) / (1.4 * DH)
    power = (t - T // 2) / float(WIN // 2)
    scale = base[None, :] ** power[:, None]
    scale = np.concatenate([scale, scale], axis=-1)
    cos, sin = np.cos(freqs), np.sin(freqs)

    def fm(x):
        xt = np.ascontiguousarray(x.T.astype(np.float32))
        return np.concatenate([xt, xt], axis=0)
    c["CSq8"] = fm(cos * scale * 8.0)
    c["SSq8"] = fm(sin * scale * 8.0)
    c["CSk"] = fm(cos / scale)
    c["SSk"] = fm(sin / scale)
    return c


def _col_tiles(v, np_dtype=np.float32):
    n = v.shape[0]
    nt = (n + 127) // 128
    out = np.zeros((128, nt), dtype=np_dtype)
    for j in range(nt):
        seg = v[j * 128:(j + 1) * 128]
        out[:len(seg), j] = seg
    return np.ascontiguousarray(out)


def _prep_weights(inp):
    w = {}
    cT = lambda a: np.ascontiguousarray(np.asarray(a).T)
    w["wqT"] = cT(inp["L0_attn_W_q"])
    w["wkT"] = cT(inp["L0_attn_W_k"])
    w["wvT"] = cT(inp["L0_attn_W_v"])
    w["woT"] = cT(inp["L0_attn_W_o"])
    w["lq1T"] = cT(inp["L0_attn_lambda_q1"])
    w["lk1T"] = cT(inp["L0_attn_lambda_k1"])
    w["lq2T"] = cT(inp["L0_attn_lambda_q2"])
    w["lk2T"] = cT(inp["L0_attn_lambda_k2"])
    w["rms"] = np.ascontiguousarray(
        np.asarray(inp["L0_attn_rms_scale"], dtype=np.float32).reshape(128, 1))
    w["wgT"] = cT(inp["L0_ffn_w_gate"])
    w["wuT"] = cT(inp["L0_ffn_w_up"])
    w["wdT"] = cT(inp["L0_ffn_w_down"])
    w["inpT"] = cT(inp["L1_attn_in_proj_w"])
    w["outpT"] = cT(inp["L1_attn_out_proj_w"])
    for sfx, cw, cb, xp, dtw, dtb, alog, dv in (
        ("f", "L1_attn_conv1d_w", "L1_attn_conv1d_bias", "L1_attn_x_proj_w",
         "L1_attn_dt_proj_w", "L1_attn_dt_proj_bias", "L1_attn_A_log", "L1_attn_D"),
        ("b", "L1_attn_conv1d_b_w", "L1_attn_conv1d_b_bias", "L1_attn_x_proj_b_w",
         "L1_attn_dt_proj_b_w", "L1_attn_dt_proj_b_bias", "L1_attn_A_b_log",
         "L1_attn_D_b"),
    ):
        cwa = np.asarray(inp[cw]).reshape(2048, 4)
        w["convw_" + sfx] = np.ascontiguousarray(
            cwa.reshape(16, 128, 4).transpose(1, 0, 2))
        w["convb_" + sfx] = _col_tiles(np.asarray(inp[cb]))
        xpa = np.asarray(inp[xp])
        perm = np.zeros((128, 2048), dtype=np.float32)
        perm[0:16] = xpa[64:80]
        perm[32:48] = xpa[80:96]
        perm[64:128] = xpa[0:64]
        w["xprojT_" + sfx] = cT(perm)
        w["dtpT_" + sfx] = cT(np.asarray(inp[dtw]))
        w["dtb_" + sfx] = _col_tiles(np.asarray(inp[dtb]))
        al = np.asarray(inp[alog])
        ac = np.zeros((128, 256), dtype=np.float32)
        m = np.arange(128)
        for j in range(16):
            for i2 in range(16):
                ac[:, j * 16 + i2] = al[128 * j + 8 * i2 + m // 16, m % 16]
        w["alog_" + sfx] = ac
        w["D_" + sfx] = _col_tiles(np.asarray(inp[dv]))
    w["wqkv"] = np.ascontiguousarray(np.asarray(inp["L1_ffn_wqkv"]))
    w["wo1"] = np.ascontiguousarray(np.asarray(inp["L1_ffn_wo"]))
    w["w1"] = np.ascontiguousarray(np.asarray(inp["L1_ffn_w1"]))
    w["w2"] = np.ascontiguousarray(np.asarray(inp["L1_ffn_w2"]))
    b1 = np.asarray(inp["L1_ffn_b1"])
    w["b1a"] = np.stack([_col_tiles(b1[l, :2730]) for l in range(2)])
    w["b1g"] = np.stack([_col_tiles(b1[l, 2730:]) for l in range(2)])
    w["b2"] = np.stack([_col_tiles(np.asarray(inp["L1_ffn_b2"])[l])
                        for l in range(2)])
    w["lng"] = np.stack([_col_tiles(np.asarray(inp["L1_ffn_ln_g"])[l])
                         for l in range(2)])
    w["lnb"] = np.stack([_col_tiles(np.asarray(inp["L1_ffn_ln_b"])[l])
                         for l in range(2)])
    qs = np.asarray(inp["L1_ffn_q_scale"])
    ks = np.asarray(inp["L1_ffn_k_scale"])
    w["qsc"] = np.ascontiguousarray(
        np.concatenate([qs, qs], axis=1).reshape(2, 128, 1))
    w["ksc"] = np.ascontiguousarray(
        np.concatenate([ks, ks], axis=1).reshape(2, 128, 1))
    return w



def emit_ln(p, es, xsrc_dram, b, xnT, xnT_h=None, identB=None):
    xt = [p.sb(es, f"lnx{i}", [128, D], F32) for i in range(2)]
    sq_v = p.sb(es, "lnsqv", [128, D], BF16)
    sq_s = p.sb(es, "lnsqs", [128, D], BF16)
    st = {nm: p.sb(es, "lnst_" + nm, [128, 1], F32)
          for nm in ("sum", "ssq", "mean", "e2", "var", "std", "rstd")}
    xn_nat = p.sb(es, "lnxn", [128, D], BF16)
    pt_t = [p.ps(es, f"lnpt{i}", [128, 128], BF16) for i in range(2)]
    vdone = [0] * 16
    sdone = [0] * 16
    for tt in range(16):
        buf = xt[tt % 2]
        slot = "lnx" + ("A" if tt % 2 == 0 else "B")
        if tt >= 2:
            p.sync.wait("vS", vdone[tt - 2])
            p.sync.wait("sS", sdone[tt - 2])
        lv = p.sync.dma(buf[:], xsrc_dram[b, ts(tt, 128), :], slot)
        p.vec.wait(slot, lv)
        p.sca.wait(slot, lv)
        p.vec.tensor_scalar(sq_v[:], buf[:], 1.0, 0.0, ALU.mult, ALU.add,
                            accum_out=st["sum"][:])
        sv = p.sca.activation(sq_s[:], buf[:], AF.Square, accum_out=st["ssq"][:])
        sdone[tt] = sv
        p.vec.wait("sS", sv)
        p.vec.tensor_scalar(st["mean"][:], st["sum"][:], 1.0 / D, None, ALU.mult)
        p.vec.tensor_scalar(st["e2"][:], st["ssq"][:], 1.0 / D, None, ALU.mult)
        p.vec.tensor_tensor(st["var"][:], st["mean"][:], st["mean"][:], ALU.mult)
        p.vec.tensor_tensor(st["var"][:], st["e2"][:], st["var"][:], ALU.subtract)
        vv = p.vec.tensor_scalar(st["var"][:], st["var"][:], 1e-5, None, ALU.add)
        p.sca.wait("vS", vv)
        sv = p.sca.activation(st["std"][:], st["var"][:], AF.Sqrt)
        p.vec.wait("sS", sv)
        p.vec.reciprocal(st["rstd"][:], st["std"][:])
        p.vec.wait("tS", p.val("tS"))
        vv = p.vec.tensor_scalar(xn_nat[:], buf[:], st["mean"][:], st["rstd"][:],
                                 ALU.subtract, ALU.mult)
        vdone[tt] = vv
        p.ten.wait("vS", vv)
        for dt in range(8):
            pb = pt_t[dt % 2]
            if dt >= 2:
                p.ten.wait("sS", p.val("sS") - 1)
            tv = p.ten.transpose(pb[:], xn_nat[:, ts(dt, 128)], identB[:])
            p.sca.wait("tS", tv)
            sv = p.sca.copy(xnT[:, dt * T + tt * 128:dt * T + tt * 128 + 128],
                            pb[:])
        if xnT_h is not None:
            p.sync.wait("sS", p.val("sS"))
            src3 = xnT.rearrange("p (a t) -> p a t", a=8)[:, :,
                                                          tt * 128:(tt + 1) * 128]
            p.sync.dma(xnT_h[b, tt].rearrange("a p t -> p a t"), src3, "st")



def build_l0(p, io):
    nc = p.nc
    x, x1 = io["x"], io["x1"]
    xnT_h, attn_h = io["xnT_h"], io["attn_h"]
    es_p = p.es

    identB = p.sb(es_p, "identB", [128, 128], BF16)
    maskC = p.sb(es_p, "maskC", [128, 128], F32)
    rms02 = p.sb(es_p, "rms02", [128, 1], F32)
    lam_bc = p.sb(es_p, "lam_bc", [128, 16], F32)

    p.sync.dma(identB[:], io["identB"], "ld")
    p.sync.dma(maskC[:], io["maskC"], "ld")

    with contextlib.ExitStack() as es0:
        lv4 = [p.sb(es0, f"lv{i}", [64, 16], F32) for i in range(4)]
        ones64 = p.sb(es0, "ones64", [64, 128], BF16)
        prod = p.sb(es0, "prod", [64, 16], BF16)
        pl = p.ps(es0, "pl", [128, 16], F32)
        e1 = p.sb(es0, "e1", [128, 16], F32)
        e2 = p.sb(es0, "e2", [128, 16], F32)
        rmsr = p.sb(es0, "rmsr", [128, 1], F32)
        for i, nm in enumerate(("lq1T", "lk1T", "lq2T", "lk2T")):
            p.sync.dma(lv4[i][:], io[nm], "ld")
        p.sync.dma(rmsr[:], io["rms"], "ld")
        p.gps.dma(ones64[:], io["ones64"], "ldg")
        p.vec.wait("ld", p.val("ld"))
        p.vec.wait("ldg", p.val("ldg"))
        p.vec.tensor_scalar(rms02[:], rmsr[:], 0.2, None, ALU.mult)
        for k in range(2):
            a, bb = (0, 1) if k == 0 else (2, 3)
            vv = p.vec.tensor_tensor(prod[:], lv4[a][:], lv4[bb][:], ALU.mult)
            p.ten.wait("vS", vv)
            tv = p.ten.matmul(pl[:], ones64[:], prod[:], start=True, stop=True)
            p.sca.wait("tS", tv)
            sv = p.sca.activation((e1 if k == 0 else e2)[:], pl[:], AF.Exp)
            p.vec.wait("sS", sv)
        p.vec.tensor_tensor(lam_bc[:], e1[:], e2[:], ALU.subtract)
        p.vec.tensor_scalar(lam_bc[:], lam_bc[:], 0.8, None, ALU.add)
        p.barrier()

    for b in range(NB):
        with contextlib.ExitStack() as es_b:
            xnT = p.sb(es_b, "xnT", [128, 8 * T], BF16)
            acc = p.sb(es_b, "l0acc", [128, 16 * 1024], F32)
            with contextlib.ExitStack() as es_ln:
                emit_ln(p, es_ln, x, b, xnT, xnT_h=xnT_h, identB=identB)
                p.barrier()
            with contextlib.ExitStack() as es_a:
                _l0_attn(p, es_a, io, b, xnT, acc, identB, maskC, rms02, lam_bc)
            p.sync.wait("vS", p.val("vS"))
            p.sync.dma(attn_h[b].rearrange("(a p) f -> p a f", p=128),
                       acc.rearrange("p (a f) -> p a f", a=16), "st")
            p.sync.wait("st", p.val("st"))
            p.barrier()
        with contextlib.ExitStack() as es_f:
            _l0_ffn(p, es_f, io, b, x, x1, xnT_h, attn_h)
            p.barrier()
    p.sync.wait("st", p.val("st"))


def _l0_attn(p, es, io, b, xnT, acc, identB, maskC, rms02, lam_bc):
    GH = 2
    wq_g = p.sb(es, "wq_g", [128, 8 * 128 * GH], BF16)
    wk_g = p.sb(es, "wk_g", [128, 8 * 128 * GH], BF16)
    wv_g = p.sb(es, "wv_g", [128, 8 * 128 * GH], BF16)
    wo_g = p.sb(es, "wo_g", [128, GH * 1024], BF16)
    qT_g = p.sb(es, "qT_g", [128, GH * T], BF16)
    kT_g = p.sb(es, "kT_g", [128, GH * T], BF16)
    v_g = p.sb(es, "v_g", [128, 16 * 128 * GH], BF16)
    ps_s1 = p.ps(es, "ps_s1", [128, 1024], F32)
    ps_s2 = p.ps(es, "ps_s2", [128, 1024], F32)
    ps_o = p.ps(es, "ps_o", [128, 128], F32)

    pt_t = [p.ps(es, f"apt{i}", [128, 128], BF16) for i in range(2)]
    oT_all = p.sb(es, "oT_all", [128, GH * 16 * 128], BF16)
    wo_mm_last = [0]
    E1 = [p.sb(es, f"E1_{i}", [128, T], BF16) for i in range(2)]
    E2 = [p.sb(es, f"E2_{i}", [128, T], BF16) for i in range(2)]
    attn_sb = [p.sb(es, f"attn{i}", [128, T], BF16) for i in range(2)]
    attnT = [p.sb(es, f"attnT{i}", [128, T], BF16) for i in range(2)]
    atd = [0, 0]
    onat = [p.sb(es, f"onat{i}", [128, 128], BF16) for i in range(2)]
    sq128 = p.sb(es, "sq128", [128, 128], BF16)
    st2 = {}
    for nm in ("m", "nm", "rr", "rri"):
        st2[nm] = [p.sb(es, f"at_{nm}{i}", [128, 2], F32) for i in range(2)]
    st2["rp"] = [p.sb(es, f"at_rp{i}", [128, 4], F32) for i in range(2)]
    for nm in ("c2", "ssq", "t0", "s0", "f"):
        st2[nm] = [p.sb(es, f"at_{nm}{i}", [128, 1], F32) for i in range(2)]

    for g in range(16 // GH):
        p.gps.wait("tS", p.val("tS"))
        W = 128 * GH
        for wdst, wsrc in ((wq_g, "wqT"), (wk_g, "wkT"), (wv_g, "wvT")):
            p.gps.dma(wdst.rearrange("p (a w) -> p a w", a=8),
                      io[wsrc].rearrange("(a p) f -> p a f", p=128)[
                          :, :, g * W:(g + 1) * W], "ldg")
        p.gps.dma(wo_g.rearrange("p (a f) -> p a f", a=GH),
                  io["woT"].rearrange("(a p) f -> p a f", p=128)[
                      :, GH * g:GH * (g + 1), :], "ldg")
        lgv = p.val("ldg")
        p.vec.wait("ldg", lgv)
        p.vec.wait("tS", p.val("tS"))
        vv = 0
        for hh in range(GH):
            vv = p.vec.tensor_scalar(wo_g[:, ts(hh, 1024)],
                                     wo_g[:, ts(hh, 1024)], rms02[:, 0:1], None,
                                     ALU.mult)
        p.ten.wait("vS", vv)
        p.ten.wait("ldg", lgv)
        p.ten.wait("sS", p.val("sS"))
        psq = [ps_s1, ps_s2]
        qk_ev = [0, 0]
        ci = 0
        for hh in range(GH):
            for (wsb, dst) in ((wq_g, qT_g), (wk_g, kT_g)):
                for ch in range(4):
                    sl = ci % 2
                    ci += 1
                    p.ten.wait("sS", qk_ev[sl])
                    for dt in range(8):
                        tv = p.ten.matmul(
                            psq[sl][:, 0:512],
                            wsb[:, dt * 128 * GH + hh * 128:
                                dt * 128 * GH + hh * 128 + 128],
                            xnT[:, dt * T + ch * 512:dt * T + (ch + 1) * 512],
                            start=(dt == 0), stop=(dt == 7))
                    p.sca.wait("tS", tv)
                    qk_ev[sl] = p.sca.copy(
                        dst[:, hh * T + ch * 512:hh * T + (ch + 1) * 512],
                        psq[sl][:, 0:512])
        for tt in range(16):
            sl = ci % 2
            ci += 1
            p.ten.wait("sS", qk_ev[sl])
            for dt in range(8):
                tv = p.ten.matmul(psq[sl][:, 0:128 * GH],
                                  xnT[:, dt * T + tt * 128:dt * T + tt * 128 + 128],
                                  wv_g[:, ts(dt, 128 * GH)],
                                  start=(dt == 0), stop=(dt == 7))
            p.sca.wait("tS", tv)
            qk_ev[sl] = p.sca.copy(v_g[:, ts(tt, 128 * GH)],
                                   psq[sl][:, 0:128 * GH])
        p.ten.wait("sS", p.val("sS"))
        iters = [(tt, hh) for tt in range(16) for hh in range(GH)]

        def stage_S(it, mi, kh):
            tt, hh = iters[it]
            par = it % 2
            base = 64 * mi
            pss = ps_s1 if mi == 0 else ps_s2
            kspan = (tt + 1) * 128
            lo = kh * 1024
            hi = min(kspan, lo + 1024)
            p.ten.wait("sS", st["exp" + str(mi)][par] if kh_first(it) != kh
                       else st["exp" + str(mi)][1 - par])
            for c in range(lo, hi, 512):
                cn = min(512, hi - c)
                tv = p.ten.matmul(
                    pss[:, c - lo:c - lo + cn],
                    qT_g[base:base + 64,
                         hh * T + tt * 128:hh * T + tt * 128 + 128],
                    kT_g[base:base + 64, hh * T + c:hh * T + c + cn],
                    start=True, stop=True)
            st["smm" + str(mi)][par] = tv

        def kh_first(it):
            tt, hh = iters[it]
            return tt // 8

        def vec_max(it, mi):
            tt, hh = iters[it]
            par = it % 2
            pss = ps_s1 if mi == 0 else ps_s2
            kspan = (tt + 1) * 128
            dlo = kspan - 128 - kh_first(it) * 1024
            p.vec.wait("tS", st["smm" + str(mi)][par])
            p.vec.tensor_tensor(pss[:, dlo:dlo + 128],
                                pss[:, dlo:dlo + 128], maskC[:], ALU.add)
            p.vec.reduce_max(st2["m"][par][:, mi:mi + 1],
                             pss[:, dlo:dlo + 128], AX.X)
            st["negm" + str(mi)][par] = p.vec.tensor_scalar(
                st2["nm"][par][:, mi:mi + 1], st2["m"][par][:, mi:mi + 1],
                -0.125, None, ALU.mult)

        def sca_exp(it, mi, kh):
            tt, hh = iters[it]
            par = it % 2
            pss = ps_s1 if mi == 0 else ps_s2
            kspan = (tt + 1) * 128
            lo = kh * 1024
            hi = min(kspan, lo + 1024)
            Em = (E1 if mi == 0 else E2)[par]
            p.sca.wait("tS", st["smm" + str(mi)][par])
            p.sca.wait("vS", st["negm" + str(mi)][par])
            if kh == kh_first(it):
                p.sca.wait("vS", st["comb"][par])
            sv = p.sca.activation(Em[:, lo:hi], pss[:, 0:hi - lo], AF.Exp,
                                  bias=st2["nm"][par][:, mi:mi + 1], scale=0.125,
                                  accum_out=st2["rp"][par][:, 2 * mi + kh:
                                                           2 * mi + kh + 1])
            st["exp" + str(mi)][par] = sv

        def vec_combine(it):
            tt, hh = iters[it]
            par = it % 2
            hfull = hh + GH * g
            kspan = (tt + 1) * 128
            p.vec.wait("sS", st["exp1"][par])
            tt_ = iters[it][0]
            if tt_ // 8 == 0:
                p.vec.tensor_copy(st2["rr"][par][:],
                                  st2["rp"][par][:, 0:4:2])
            else:
                p.vec.tensor_tensor(st2["rr"][par][:],
                                    st2["rp"][par][:, 0:4:2],
                                    st2["rp"][par][:, 1:4:2], ALU.add)
            p.vec.reciprocal(st2["rri"][par][:], st2["rr"][par][:])
            p.vec.tensor_scalar(st2["c2"][par][:], st2["rri"][par][:, 1:2],
                                lam_bc[:, hfull:hfull + 1], -1.0, ALU.mult,
                                ALU.mult)
            p.vec.wait("tS", st["trans"][par])
            p.vec.tensor_scalar(attn_sb[par][:, 0:kspan], E1[par][:, 0:kspan],
                                st2["rri"][par][:, 0:1], None, ALU.mult)
            st["comb"][par] = p.vec.scalar_tensor_tensor(
                attn_sb[par][:, 0:kspan], E2[par][:, 0:kspan],
                st2["c2"][par][:, 0:1], attn_sb[par][:, 0:kspan], ALU.mult,
                ALU.add)

        def ten_trans_pv(it):
            tt, hh = iters[it]
            par = it % 2
            p.ten.wait("vS", st["comb"][par])
            p.ten.wait("vS", st["onat"][1 - par])
            for kt in range(tt + 1):
                pb = pt_t[kt % 2]
                if kt % 2 == 0:
                    p.ten.wait("sS", st["tevac"][0])
                else:
                    p.ten.wait("vS", st["tevac"][1])
                tv = p.ten.transpose(pb[:], attn_sb[par][:, ts(kt, 128)],
                                     identB[:])
                if kt % 2 == 0:
                    p.sca.wait("tS", tv)
                    p.sca.wait("tS", st["pv"][1 - par])
                    st["tevac"][0] = p.sca.copy(attnT[par][:, ts(kt, 128)],
                                                pb[:])
                else:
                    p.vec.wait("tS", tv)
                    p.vec.wait("tS", st["pv"][1 - par])
                    st["tevac"][1] = p.vec.tensor_copy(
                        attnT[par][:, ts(kt, 128)], pb[:])
            p.ten.wait("sS", st["tevac"][0])
            p.ten.wait("vS", st["tevac"][1])
            for kt in range(tt + 1):
                tv = p.ten.matmul(ps_o[:], attnT[par][:, ts(kt, 128)],
                                  v_g[:, kt * 128 * GH + hh * 128:
                                      kt * 128 * GH + hh * 128 + 128],
                                  start=(kt == 0), stop=(kt == tt),
                                  skip_group_check=True)
            st["pv"][par] = tv
            p.sca.wait("tS", tv)
            p.sca.wait("vS", st["onat"][par])
            sv = p.sca.activation(sq128[:], ps_o[:], AF.Square,
                                  accum_out=st2["ssq"][par][:])
            p.vec.wait("sS", sv)
            vv = p.vec.tensor_scalar(st2["t0"][par][:], st2["ssq"][par][:],
                                     2.0 ** -40 / 128, 1e-5 * 2.0 ** -40,
                                     ALU.mult, ALU.add)
            p.sca.wait("vS", vv)
            p.sca.activation(st2["s0"][par][:], st2["t0"][par][:], AF.Ln)
            sv = p.sca.activation(st2["f"][par][:], st2["s0"][par][:], AF.Exp,
                                  scale=-0.5)
            p.vec.wait("sS", sv)
            vv = p.vec.tensor_scalar(onat[par][:], ps_o[:],
                                     st2["f"][par][:, 0:1], 2.0 ** -20,
                                     ALU.mult, ALU.mult)
            st["onat"][par] = vv

        def ten_wo(it):
            tt, hh = iters[it]
            par = it % 2
            p.ten.wait("vS", st["onat"][par])
            p.ten.wait("sS", st["tevac"][0])
            tv = p.ten.transpose(pt_t[0][:], onat[par][:], identB[:])
            p.sca.wait("tS", tv)
            p.sca.wait("tS", wo_mm_last[0])
            sv = p.sca.copy(oT_all[:, (hh * 16 + tt) * 128:
                                   (hh * 16 + tt) * 128 + 128], pt_t[0][:])
            st["tevac"][0] = sv

        def wo_phase():
            ps_wo = ps_s1
            p.ten.wait("sS", p.val("sS"))
            vv = 0
            for tt in range(16):
                for half in range(2):
                    for hh in range(GH):
                        tv = p.ten.matmul(
                            ps_wo[:, ts(half, 512)],
                            oT_all[:, (hh * 16 + tt) * 128:
                                   (hh * 16 + tt) * 128 + 128],
                            wo_g[:, hh * 1024 + half * 512:
                                 hh * 1024 + (half + 1) * 512],
                            start=(hh == 0), stop=(hh == GH - 1),
                            skip_group_check=True)
                p.vec.wait("tS", tv)
                for half in range(2):
                    dst = acc[:, tt * 1024 + half * 512:
                              tt * 1024 + (half + 1) * 512]
                    if g == 0:
                        vv = p.vec.tensor_copy(dst, ps_wo[:, ts(half, 512)])
                    else:
                        vv = p.vec.tensor_tensor(dst, ps_wo[:, ts(half, 512)],
                                                 dst, ALU.add)
                p.ten.wait("vS", vv)
            wo_mm_last[0] = p.val("tS")

        st = {nm: [0, 0] for nm in
              ("smm0", "smm1", "negm0", "negm1", "exp0", "exp1", "comb",
               "trans", "tevac", "pv", "onat")}
        nit = len(iters)
        for it in range(nit):
            khf = kh_first(it)
            stage_S(it, 0, khf)
            vec_max(it, 0)
            sca_exp(it, 0, khf)
            stage_S(it, 1, khf)
            vec_max(it, 1)
            sca_exp(it, 1, khf)
            if khf == 1:
                stage_S(it, 0, 0)
                sca_exp(it, 0, 0)
                stage_S(it, 1, 0)
                sca_exp(it, 1, 0)
            if it >= 1:
                ten_trans_pv(it - 1)
                st["trans"][(it - 1) % 2] = st["pv"][(it - 1) % 2]
                ten_wo(it - 1)
            vec_combine(it)
        ten_trans_pv(nit - 1)
        st["trans"][(nit - 1) % 2] = st["pv"][(nit - 1) % 2]
        ten_wo(nit - 1)
        wo_phase()


def _l0_ffn(p, es, io, b, x, x1, xnT_h, attn_h):
    wg_sb = p.sb(es, "wg_sb", [128, 8 * 3072], BF16)
    wu_sb = p.sb(es, "wu_sb", [128, 8 * 3072], BF16)
    wd_sb = p.sb(es, "wd_sb", [128, 24 * 1024], BF16)
    xnt_c = p.sb(es, "xnt_c", [128, 8 * 512], BF16)
    h_full = p.sb(es, "h_full", [128, 24 * 512], BF16)
    ps_g2 = [p.ps(es, f"ps_g{i}", [128, 512], F32) for i in range(2)]
    ps_u2 = [p.ps(es, f"ps_u{i}", [128, 512], F32) for i in range(2)]
    ps_d = [p.ps(es, f"ps_d{i}", [128, 512], F32) for i in range(4)]
    g_sb2 = [p.sb(es, f"g_sb{i}", [128, 512], BF16) for i in range(2)]
    u_sb2 = [p.sb(es, f"u_sb{i}", [128, 512], BF16) for i in range(2)]
    at2 = [p.sb(es, f"at2_{i}", [128, 1024], F32) for i in range(2)]
    x2b = [p.sb(es, f"x2b_{i}", [128, 1024], F32) for i in range(2)]
    o_t = p.sb(es, "o_t", [128, 1024], F32)

    p.gps.dma(wg_sb.rearrange("p (a f) -> p a f", a=8),
              io["wgT"].rearrange("(a p) f -> p a f", p=128), "ldg")
    p.gps.dma(wu_sb.rearrange("p (a f) -> p a f", a=8),
              io["wuT"].rearrange("(a p) f -> p a f", p=128), "ldg")
    p.gps.dma(wd_sb.rearrange("p (a f) -> p a f", a=24),
              io["wdT"].rearrange("(a p) f -> p a f", p=128), "ldg")
    wv_ = p.val("ldg")
    p.ten.wait("ldg", wv_)
    xat_used = [0, 0]
    for tc in range(4):
        p.sync.wait("tS", p.val("tS"))
        p.vec.wait("tS", p.val("tS"))
        for tti in range(4):
            tt = 4 * tc + tti
            lvx = p.sync.dma(
                xnt_c.rearrange("p (a t) -> p a t", a=8)[:, :,
                                                         tti * 128:(tti + 1) * 128],
                xnT_h[b, tt].rearrange("a p t -> p a t"), "ldf")
        p.ten.wait("ldf", lvx)
        guev = [0, 0]
        hfv = [0, 0]
        for ht in range(24):
            sl = ht % 2
            p.ten.wait("sS", guev[sl])
            for dt in range(8):
                p.ten.matmul(ps_g2[sl][:],
                             wg_sb[:, dt * 3072 + ht * 128:
                                   dt * 3072 + ht * 128 + 128],
                             xnt_c[:, ts(dt, 512)],
                             start=(dt == 0), stop=(dt == 7),
                             skip_group_check=True)
            for dt in range(8):
                tv = p.ten.matmul(ps_u2[sl][:],
                                  wu_sb[:, dt * 3072 + ht * 128:
                                        dt * 3072 + ht * 128 + 128],
                                  xnt_c[:, ts(dt, 512)],
                                  start=(dt == 0), stop=(dt == 7),
                                  skip_group_check=True)
            p.sca.wait("tS", tv)
            p.sca.wait("vS", hfv[sl])
            p.sca.activation(g_sb2[sl][:], ps_g2[sl][:], AF.Silu,
                             scale=1.0 / 127.0)
            guev[sl] = p.sca.activation(u_sb2[sl][:], ps_u2[sl][:], AF.Copy,
                                        scale=1.0 / 16129.0)
            p.vec.wait("sS", guev[sl])
            hfv[sl] = p.vec.tensor_tensor(h_full[:, ts(ht, 512)], g_sb2[sl][:],
                                          u_sb2[sl][:], ALU.mult)
        p.ten.wait("vS", p.val("vS"))
        for st4 in range(4):
            sl = st4 % 2
            tt = 4 * tc + st4
            sem = "ldfA" if sl == 0 else "ldfB"
            p.sync.wait("vS", xat_used[sl])
            p.sync.dma(at2[sl][:], attn_h[b, ts(tt, 128), :], sem)
            lv2 = p.sync.dma(x2b[sl][:], x[b, ts(tt, 128), :], sem)
            p.vec.wait("st", p.val("st"))
            p.vec.wait(sem, lv2)
            for half in range(2):
                pd = ps_d[st4]
                for ht in range(24):
                    tv = p.ten.matmul(pd[:], h_full[:, ht * 512 + st4 * 128:
                                                    ht * 512 + st4 * 128 + 128],
                                      wd_sb[:, ht * 1024 + half * 512:
                                            ht * 1024 + (half + 1) * 512],
                                      start=(ht == 0), stop=(ht == 23))
                p.vec.wait("tS", tv)
                dst = o_t[:, half * 512:(half + 1) * 512]
                p.vec.tensor_tensor(dst, pd[:],
                                    x2b[sl][:, half * 512:(half + 1) * 512],
                                    ALU.add)
                vv = p.vec.tensor_tensor(dst,
                                         at2[sl][:, half * 512:(half + 1) * 512],
                                         dst, ALU.add)
                p.ten.wait("vS", vv)
            xat_used[sl] = vv
            p.sync.wait("vS", vv)
            p.sync.wait("st", p.val("st"))
            p.sync.dma(x1[b, ts(tt, 128), :], o_t[:], "st")



CH = 512
NCH = T // CH


def build_mamba(p, io):
    nc = p.nc
    x1 = io["x1"]
    xn1T_h = io["xn1T_h"]
    x_h = io["x_h"]
    sz_h = io["sz_h"]
    maccT = io["maccT"]
    es_p = p.es

    identB = p.sb(es_p, "identB", [128, 128], BF16)
    selrep = p.sb(es_p, "selrep", [128, 2048], BF16)
    selsum = p.sb(es_p, "selsum", [128, 2048], BF16)
    seln = p.sb(es_p, "seln", [16, 128], BF16)
    p.sync.dma(identB[:], io["identB"], "ld")
    p.sync.dma(selrep[:], io["selrep"], "ld")
    p.sync.dma(selsum[:], io["selsum"], "ld")
    p.sync.dma(seln[:], io["seln"], "ld")
    p.ten.wait("ld", p.val("ld"))

    for b in range(NB):
        with contextlib.ExitStack() as es_b:
            xn1T = p.sb(es_b, "xn1T", [128, 8 * T], BF16)
            with contextlib.ExitStack() as es_ln:
                emit_ln(p, es_ln, x1, b, xn1T, xnT_h=None, identB=identB)
            p.sync.wait("sS", p.val("sS"))
            for dt in range(8):
                p.sync.dma(xn1T_h[b, dt], xn1T[:, ts(dt, T)], "st")
            p.barrier()
            _mamba_inproj(p, io, b, xn1T)
            p.barrier()
        for dir_ in ("f", "b"):
            with contextlib.ExitStack() as es_d:
                _mamba_dir(p, es_d, io, b, dir_, selrep, selsum, seln)
                p.barrier()
    p.sync.wait("st", p.val("st"))


def _mamba_inproj(p, io, b, xn1T):
    es = contextlib.ExitStack()
    with es:
        wbuf = [p.sb(es, f"ipw{i}", [128, 8 * 128], BF16) for i in range(2)]
        ob = [p.sb(es, f"ipo{i}", [128, 2048], BF16) for i in range(2)]
        psl = [p.ps(es, f"ps_ip{i}", [128, 2048], F32) for i in range(2)]
        wsem = ["ipwA", "ipwB"]
        stsem = ["ipstA", "ipstB"]
        wv = [0, 0]
        sdone = [0, 0]
        stv = [0, 0]
        mmdone = [0, 0]
        for ft in range(32):
            sl = ft % 2
            ps = psl[sl]
            p.gps.wait("tS", mmdone[sl])
            wv[sl] = p.gps.dma(wbuf[sl].rearrange("p (a w) -> p a w", a=8),
                               io["inpT"].rearrange("(a p) f -> p a f", p=128)[
                                   :, :, ts(ft, 128)], wsem[sl])
            p.ten.wait(wsem[sl], wv[sl])
            p.ten.wait("sS", sdone[sl])
            for ch in range(4):
                for dt in range(8):
                    tv = p.ten.matmul(ps[:, ts(ch, 512)], wbuf[sl][:, ts(dt, 128)],
                                      xn1T[:, dt * T + ch * 512:
                                           dt * T + (ch + 1) * 512],
                                      start=(dt == 0), stop=(dt == 7))
            mmdone[sl] = tv
            p.sca.wait("tS", tv)
            p.sca.wait(stsem[sl], stv[sl])
            if ft < 16:
                sv = p.sca.copy(ob[sl][:], ps[:])
                p.sync.wait("sS", sv)
                stv[sl] = p.sync.dma(io["x_h"][b, ft], ob[sl][:], stsem[sl])
            else:
                sv = p.sca.activation(ob[sl][:], ps[:], AF.Silu)
                p.sync.wait("sS", sv)
                stv[sl] = p.sync.dma(io["sz_h"][b, ft - 16], ob[sl][:], stsem[sl])
            sdone[sl] = sv
        p.sync.wait(stsem[0], stv[0])
        p.sync.wait(stsem[1], stv[1])


def _mamba_dir(p, es, io, b, dir_, selrep, selsum, seln):
    sfx = dir_
    rev = (dir_ == "b")
    xprojT = p.sb(es, "xprojT", [128, 16 * 128], BF16)
    dtpT = p.sb(es, "dtpT", [64, 2048], BF16)
    convw = p.sb(es, "convw", [128, 16 * 4], F32)
    convb = p.sb(es, "convb", [128, 16], F32)
    dtb = p.sb(es, "dtb", [128, 16], F32)
    Dv = p.sb(es, "Dv", [128, 16], F32)
    alog = p.sb(es, "alog", [128, 256], F32)
    Acol = p.sb(es, "Acol", [128, 256], F32)
    outpT = p.sb(es, "outpT", [128, 16 * 1024], BF16)
    p.gps.dma(xprojT.rearrange("p (a w) -> p a w", a=16),
              io["xprojT_" + sfx].rearrange("(a p) f -> p a f", p=128), "ldg")
    p.gps.dma(outpT.rearrange("p (a f) -> p a f", a=16),
              io["outpT"].rearrange("(a p) f -> p a f", p=128), "ldg")
    p.gps.dma(dtpT[:], io["dtpT_" + sfx], "ldg")
    p.sync.dma(convw.rearrange("p (a t) -> p a t", a=16),
               io["convw_" + sfx], "ld")
    p.sync.dma(convb[:], io["convb_" + sfx], "ld")
    p.sync.dma(dtb[:], io["dtb_" + sfx], "ld")
    p.sync.dma(Dv[:], io["D_" + sfx], "ld")
    p.sync.dma(alog[:], io["alog_" + sfx], "ld")
    ldv, lgv = p.val("ld"), p.val("ldg")
    p.sca.wait("ld", ldv)
    sv = p.sca.activation(Acol[:], alog[:], AF.Exp)
    p.vec.wait("sS", sv)
    p.vec.tensor_scalar(Acol[:], Acol[:], -1.0, None, ALU.mult)

    state = p.sb(es, "state", [128, 256], F32)
    p.vec.memset(state[:], 0.0)
    halo = p.sb(es, "halo", [128, 16 * 4], BF16)
    p.vec.memset(halo[:], 0.0)

    W = p.sb(es, "Wc", [128, 16 * (CH + 3)], BF16)
    tmp_r = [p.sb(es, f"tmpr{i}", [128, CH + 3], BF16) for i in range(2)]
    u_ch = p.sb(es, "u_ch", [128, 16 * CH], BF16)
    delta_ch = p.sb(es, "delta_ch", [128, 16 * CH], BF16)
    du_ch = p.sb(es, "du_ch", [128, 16 * CH], BF16)
    sz_ch = p.sb(es, "sz_ch", [128, 16 * CH], BF16)
    yg = p.sb(es, "yg", [128, 16 * CH], BF16)
    xdbl_dt = p.sb(es, "xdbl_dt", [64, CH], BF16)
    B_sb = p.sb(es, "B_sb", [16, CH], BF16)
    C_sb = p.sb(es, "C_sb", [16, CH], BF16)
    B_rep = p.sb(es, "B_rep", [128, CH], BF16)
    C_rep = p.sb(es, "C_rep", [128, CH], BF16)
    ftmp = p.sb(es, "ftmp", [128, CH], F32)
    ftmp2 = p.sb(es, "ftmp2", [128, CH], F32)
    ftmp3 = p.sb(es, "ftmp3", [128, CH], F32)
    dA_sb = [p.sb(es, f"dA{i}", [128, CH], F32) for i in range(2)]
    bb_sb = [p.sb(es, f"bb{i}", [128, CH], F32) for i in range(2)]
    h_sb = [p.sb(es, f"hsb{i}", [128, CH], BF16) for i in range(2)]
    hC = [p.sb(es, f"hC{i}", [128, CH], BF16) for i in range(2)]
    mo_t = p.sb(es, "mo_t", [128, CH], F32)
    ps_a2 = [p.ps(es, f"ps_a{i}", [128, CH], F32) for i in range(2)]
    ps_b2 = [p.ps(es, f"ps_b{i}", [128, CH], F32) for i in range(2)]
    ps_a = ps_a2[0]
    ps_b = ps_b2[0]
    ps_y = p.ps(es, "ps_y", [128, CH], F32)
    ps_op = p.ps(es, "ps_op", [128, CH], F32)

    p.ten.wait("ldg", lgv)
    p.vec.wait("ld", ldv)

    for c in range(NCH):
        t0 = c * CH if not rev else T - CH * (c + 1)
        p.sync.wait("vS", p.val("vS"))
        if not rev:
            for dt in range(16):
                wsl = W[:, dt * (CH + 3):(dt + 1) * (CH + 3)]
                if c == 0:
                    p.vec.memset(wsl[:, 0:3], 0.0)
                else:
                    p.vec.tensor_copy(wsl[:, 0:3], halo[:, ts(dt, 4)][:, 0:3])
            vh = p.val("vS")
            p.sync.wait("vS", vh)
            lv = 0
            for dt in range(16):
                wsl = W[:, dt * (CH + 3):(dt + 1) * (CH + 3)]
                lv = p.sync.dma(wsl[:, 3:CH + 3],
                                io["x_h"][b, dt, :, t0:t0 + CH], "ldx")
            p.vec.wait("ldx", lv)
        else:
            sl_done = [0, 0]
            for dt in range(16):
                sl = dt % 2
                wsl = W[:, dt * (CH + 3):(dt + 1) * (CH + 3)]
                p.sync.wait("vS", sl_done[sl])
                if c == 0:
                    lv = p.sync.dma(tmp_r[sl][:, 0:CH],
                                    io["x_h"][b, dt, :, t0:t0 + CH],
                                    "ldxA" if sl == 0 else "ldxB")
                    p.vec.wait("ldxA" if sl == 0 else "ldxB", lv)
                    p.vec.memset(tmp_r[sl][:, CH:CH + 3], 0.0)
                else:
                    lv = p.sync.dma(tmp_r[sl][:, 0:CH + 3],
                                    io["x_h"][b, dt, :, t0:t0 + CH + 3],
                                    "ldxA" if sl == 0 else "ldxB")
                    p.vec.wait("ldxA" if sl == 0 else "ldxB", lv)
                sl_done[sl] = p.vec.tensor_copy(wsl[:], tmp_r[sl][:, ::-1])
        for dt in range(16):
            wsl = W[:, dt * (CH + 3):(dt + 1) * (CH + 3)]
            usl = u_ch[:, ts(dt, CH)]
            p.vec.tensor_scalar(ftmp[:], wsl[:, 3:CH + 3],
                                convw[:, dt * 4 + 3:dt * 4 + 4], None, ALU.mult)
            for tap in range(3):
                p.vec.scalar_tensor_tensor(
                    ftmp[:], wsl[:, 2 - tap:CH + 2 - tap],
                    convw[:, dt * 4 + 2 - tap:dt * 4 + 3 - tap], ftmp[:],
                    ALU.mult, ALU.add)
            p.vec.tensor_scalar(usl, ftmp[:], convb[:, dt:dt + 1], None, ALU.add)
            p.vec.tensor_copy(halo[:, ts(dt, 4)][:, 0:3], wsl[:, CH:CH + 3])
        p.ten.wait("vS", p.val("vS"))
        p.ten.wait("sS", p.val("sS"))
        for dt in range(16):
            tv = p.ten.matmul(ps_a[:], xprojT[:, ts(dt, 128)],
                              u_ch[:, ts(dt, CH)], start=(dt == 0),
                              stop=(dt == 15))
        p.sca.wait("tS", tv)
        sv = p.sca.copy(xdbl_dt[:], ps_a[64:128, :])
        sv = p.sca.copy(B_sb[:], ps_a[0:16, :])
        sv = p.sca.copy(C_sb[:], ps_a[32:48, :])
        p.ten.wait("sS", sv)
        tv = p.ten.matmul(ps_b[:], seln[:], B_sb[:], start=True, stop=True)
        p.vec.wait("tS", tv)
        vv = p.vec.tensor_copy(B_rep[:], ps_b[:])
        p.ten.wait("vS", vv)
        tv = p.ten.matmul(ps_b[:], seln[:], C_sb[:], start=True, stop=True)
        p.vec.wait("tS", tv)
        vv = p.vec.tensor_copy(C_rep[:], ps_b[:])
        p.ten.wait("vS", vv)
        for dt in range(16):
            tv = p.ten.matmul(ps_b[:], dtpT[:, ts(dt, 128)], xdbl_dt[:],
                              start=True, stop=True)
            p.vec.wait("tS", tv)
            vv = p.vec.tensor_scalar(ftmp[:], ps_b[:], dtb[:, dt:dt + 1], None,
                                     ALU.add)
            p.ten.wait("vS", vv)
            p.vec.tensor_scalar(ftmp[:], ftmp[:], 1e-5, 1.0, ALU.max, ALU.min)
            p.vec.tensor_scalar(ftmp[:], ftmp[:], dtb[:, dt:dt + 1], None,
                                ALU.add)
            p.sca.wait("vS", p.val("vS"))
            sv = p.sca.activation(ftmp2[:], ftmp[:], AF.Abs)
            sv = p.sca.activation(ftmp2[:], ftmp2[:], AF.Exp, scale=-1.0)
            sv = p.sca.activation(ftmp2[:], ftmp2[:], AF.Ln, bias=1.0)
            sv = p.sca.activation(ftmp3[:], ftmp[:], AF.Relu)
            p.vec.wait("sS", sv)
            dsl = delta_ch[:, ts(dt, CH)]
            usl = u_ch[:, ts(dt, CH)]
            p.vec.tensor_tensor(dsl, ftmp3[:], ftmp2[:], ALU.add)
            p.vec.tensor_scalar(usl, usl, -10.0, 10.0, ALU.max, ALU.min)
            p.vec.tensor_tensor(du_ch[:, ts(dt, CH)], dsl, usl, ALU.mult)
        if not rev:
            p.sync.wait("vS", p.val("vS"))
            for dt in range(16):
                lv = p.sync.dma(sz_ch[:, ts(dt, CH)],
                                io["sz_h"][b, dt, :, t0:t0 + CH], "ldsz")
            p.vec.wait("ldsz", lv)
        else:
            sl_done = [0, 0]
            p.sync.wait("vS", p.val("vS"))
            for dt in range(16):
                sl = dt % 2
                p.sync.wait("vS", sl_done[sl])
                sem = "ldszA" if sl == 0 else "ldszB"
                lv = p.sync.dma(tmp_r[sl][:, 0:CH],
                                io["sz_h"][b, dt, :, t0:t0 + CH], sem)
                p.vec.wait(sem, lv)
                sl_done[sl] = p.vec.tensor_copy(sz_ch[:, ts(dt, CH)],
                                                tmp_r[sl][:, CH - 1::-1])
        p.ten.wait("vS", p.val("vS"))
        expv = [0, 0]
        bbv = [0, 0]
        scanv = [0, 0]
        hCv = [0, 0]
        ymmv = [0, 0]
        replv = [0, 0]
        stcv = [0, 0]

        def emit_repl(j, i):
            s = i % 2
            kidx = j * 16 + i
            p.ten.wait("sS", expv[s])
            p.ten.matmul(ps_a2[s][:], selrep[:, ts(i, 128)],
                         delta_ch[:, ts(j, CH)], start=True, stop=True)
            p.ten.wait("vS", bbv[s])
            replv[s] = p.ten.matmul(ps_b2[s][:], selrep[:, ts(i, 128)],
                                    du_ch[:, ts(j, CH)], start=True, stop=True)
            p.sca.wait("tS", replv[s] - 1)
            p.sca.wait("vS", scanv[s])
            expv[s] = p.sca.activation(dA_sb[s][:], ps_a2[s][:], AF.Exp,
                                       scale=Acol[:, kidx:kidx + 1])

        def emit_rest(j, i):
            s = i % 2
            kidx = j * 16 + i
            p.vec.wait("tS", replv[s])
            bbv[s] = p.vec.tensor_tensor(bb_sb[s][:], ps_b2[s][:], B_rep[:],
                                         ALU.mult)
            p.vec.wait("sS", expv[s])
            p.vec.wait("gS", max(stcv[s], hCv[s]))
            scanv[s] = p.vec.tensor_tensor_scan(
                h_sb[s][:], dA_sb[s][:], bb_sb[s][:],
                state[:, kidx:kidx + 1], ALU.mult, ALU.add)
            p.gps.wait("vS", scanv[s])
            stcv[s] = p.gps.tensor_copy(state[:, kidx:kidx + 1],
                                        h_sb[s][:, CH - 1:CH])
            p.gps.wait("tS", ymmv[s])
            hCv[s] = p.gps.tensor_tensor(hC[s][:], h_sb[s][:], C_rep[:],
                                         ALU.mult)
            p.ten.wait("gS", hCv[s])
            ymmv[s] = p.ten.matmul(ps_y[:], selsum[:, ts(i, 128)], hC[s][:],
                                   start=(i == 0), stop=(i == 15),
                                   skip_group_check=True)

        for j in range(16):
            emit_repl(j, 0)
            for i in range(16):
                if i + 1 < 16:
                    emit_repl(j, i + 1)
                emit_rest(j, i)
            p.vec.wait("tS", ymmv[15 % 2])
            p.vec.scalar_tensor_tensor(ftmp[:], u_ch[:, ts(j, CH)],
                                       Dv[:, j:j + 1], ps_y[:], ALU.mult,
                                       ALU.add)
            vv = p.vec.tensor_tensor(yg[:, ts(j, CH)], ftmp[:],
                                     sz_ch[:, ts(j, CH)], ALU.mult)
            p.ten.wait("vS", vv)
        for ft in range(8):
            for j in range(16):
                tv = p.ten.matmul(ps_op[:], outpT[:, j * 1024 + ft * 128:
                                                  j * 1024 + ft * 128 + 128],
                                  yg[:, ts(j, CH)], start=(j == 0),
                                  stop=(j == 15))
            p.vec.wait("tS", tv)
            src_ps = ps_op[:, ::-1] if rev else ps_op[:]
            if dir_ == "f":
                vv = p.vec.tensor_copy(mo_t[:], src_ps)
            else:
                p.sync.wait("mrw", p.val("mrw"))
                lv = p.sync.dma(mo_t[:], io["maccT"][b, ft, :, t0:t0 + CH],
                                "mrw")
                p.vec.wait("mrw", lv)
                vv = p.vec.tensor_tensor(mo_t[:], src_ps, mo_t[:], ALU.add)
            p.ten.wait("vS", vv)
            p.sync.wait("vS", vv)
            sv2 = p.sync.dma(io["maccT"][b, ft, :, t0:t0 + CH], mo_t[:], "mrw")
            p.vec.wait("mrw", sv2)
    p.sync.wait("mrw", p.val("mrw"))



def build_l1ffn(p, io):
    nc = p.nc
    es_p = p.es
    identB = p.sb(es_p, "identB", [128, 128], BF16)
    identF = p.sb(es_p, "identF", [128, 128], F32)
    maskC = p.sb(es_p, "maskC", [128, 128], F32)
    maskA = p.sb(es_p, "maskA", [128, 128], F32)
    G64 = p.sb(es_p, "G64", [128, 128], BF16)
    P64 = p.sb(es_p, "P64", [128, 128], BF16)
    ones128 = p.sb(es_p, "ones128", [128, 128], BF16)
    tab = {}
    for nm in ("CSq8", "SSq8", "CSk", "SSk"):
        tab[nm] = p.sb(es_p, nm, [128, T], BF16)
        p.gps.dma(tab[nm][:], io[nm], "ldtab")
    for nm, t_ in (("identB", identB), ("identF", identF), ("maskC", maskC),
                   ("maskA", maskA), ("G64", G64), ("P64", P64),
                   ("ones128", ones128)):
        p.sync.dma(t_[:], io[nm], "ld")
    for l in range(2):
        for nm in ("qsc", "ksc", "lng", "lnb", "b1a", "b1g", "b2"):
            b1col(p, io, nm, l, 0, 1)
    for e in (p.vec, p.sca, p.ten):
        e.wait("ld", p.val("ld"))
        e.wait("ldtab", p.val("ldtab"))

    for b in range(NB):
        with contextlib.ExitStack() as es_b:
            x_br = p.sb(es_b, "x_br", [128, 8 * T], F32)
            xtmp = p.sb(es_b, "xtmp", [128, T], BF16)
            for dt in range(8):
                lv = p.sync.dma(xtmp[:], io["xn1T_h"][b, dt], "ldf")
                p.vec.wait("ldf", lv)
                vv = p.vec.tensor_copy(x_br[:, ts(dt, T)], xtmp[:])
                p.sync.wait("vS", vv)
            for l in range(2):
                _l1f_attn(p, io, b, l, x_br, identB, maskC, maskA, G64, P64,
                          tab)
                p.barrier()
                _l1f_ffn(p, io, b, l, x_br, ones128)
                p.barrier()
            _l1f_out(p, es_b, io, b, x_br, identF)
            p.barrier()
    p.sync.wait("st", p.val("st"))


def _shift_cast(p, x_br, hb, dt):
    if dt < 4:
        return p.vec.tensor_copy(hb[:, ts(dt, T)], x_br[:, ts(dt, T)])
    p.vec.memset(hb[:, dt * T:dt * T + 1], 0.0)
    return p.vec.tensor_copy(hb[:, dt * T + 1:(dt + 1) * T],
                             x_br[:, dt * T:(dt + 1) * T - 1])


def _l1f_attn(p, io, b, l, x_br, identB, maskC, maskA, G64, P64, tab):
    es = contextlib.ExitStack()
    with es:
        hb = p.sb(es, "hb", [128, 8 * T], BF16)
        o_sb = p.sb(es, "o_sb", [128, 4 * T], BF16)
        wq_h = p.sb(es, "wq_h", [128, 8 * 64], BF16)
        wk_h = p.sb(es, "wk_h", [128, 8 * 64], BF16)
        wv_h = p.sb(es, "wv_h", [128, 8 * 64], BF16)
        wo_sb = p.sb(es, "wo_sb", [128, 4 * 1024], BF16)
        qT = p.sb(es, "qT", [64, T], BF16)
        kT = p.sb(es, "kT", [64, T], BF16)
        qrot = p.sb(es, "qrot", [64, T], BF16)
        krot = p.sb(es, "krot", [64, T], BF16)
        v_sb = p.sb(es, "v_sb", [128, 16 * 64], BF16)
        sq_h = p.sb(es, "sq_h", [64, T], BF16)
        f32a = p.sb(es, "f32a", [64, 512], F32)
        f32b = p.sb(es, "f32b", [64, 512], F32)
        E_sb = p.sb(es, "E_sb", [128, 640], BF16)
        attn_sb = p.sb(es, "attn_sb", [128, 640], BF16)
        attnT = p.sb(es, "attnT", [128, 640], BF16)
        stq = {nm: p.sb(es, "l1_" + nm, [128, 1], F32)
               for nm in ("m", "nm", "r", "ri")}
        ps_s = p.ps(es, "ps_s", [128, 1024], F32)
        ps_qf2 = [p.ps(es, f"ps_qf{i}", [128, 512], F32) for i in range(2)]
        ps_qf = ps_qf2[0]
        ps_q = ps_qf[0:64, :]
        ps_of = p.ps(es, "ps_of", [128, 128], F32)
        ps_o = ps_of[0:64, :]
        ps_wo = p.ps(es, "ps_wo", [128, 512], F32)
        pt_t = [p.ps(es, f"l1pt{i}", [128, 128], BF16) for i in range(2)]

        p.vec.wait("tS", p.val("tS"))
        for dt in range(8):
            vv = _shift_cast(p, x_br, hb, dt)
        p.ten.wait("vS", vv)
        p.gps.dma(wo_sb.rearrange("p (a f) -> p a f", a=4),
                  io["wo1"][l].rearrange("(a p) f -> p a f", p=128), "ldg")
        for h in range(8):
            p.gps.wait("tS", p.val("tS"))
            wq3 = io["wqkv"][l].rearrange("(a p) f -> p a f", p=128)
            for wdst, off in ((wq_h, 0), (wk_h, 512), (wv_h, 1024)):
                p.gps.dma(wdst.rearrange("p (a w) -> p a w", a=8),
                          wq3[:, :, off + h * 64:off + h * 64 + 64], "ldg")
            lgv = p.val("ldg")
            p.ten.wait("ldg", lgv)
            p.ten.wait("sS", p.val("sS"))
            qkvev = [0, 0]
            for ci, (wsb, dst, ch) in enumerate(
                    [(w_, d_, c_) for (w_, d_) in ((wq_h, qT), (wk_h, kT))
                     for c_ in range(4)]):
                sl = ci % 2
                p.ten.wait("sS", qkvev[sl])
                for dt in range(8):
                    tv = p.ten.matmul(ps_qf2[sl][0:64, :], wsb[:, ts(dt, 64)],
                                      hb[:, dt * T + ch * 512:
                                         dt * T + (ch + 1) * 512],
                                      start=(dt == 0), stop=(dt == 7))
                p.sca.wait("tS", tv)
                qkvev[sl] = p.sca.copy(dst[:, ts(ch, 512)], ps_qf2[sl][0:64, :])
            for tt in range(16):
                sl = tt % 2
                p.ten.wait("sS", qkvev[sl])
                for dt in range(8):
                    tv = p.ten.matmul(ps_qf2[sl][:, 0:64],
                                      hb[:, dt * T + tt * 128:
                                         dt * T + tt * 128 + 128],
                                      wv_h[:, ts(dt, 64)],
                                      start=(dt == 0), stop=(dt == 7))
                p.sca.wait("tS", tv)
                qkvev[sl] = p.sca.copy(v_sb[:, ts(tt, 64)], ps_qf2[sl][:, 0:64])
            p.ten.wait("sS", p.val("sS"))
            for (src, dst, scol, cs, ss) in (
                    (qT, qrot, "qsc", "CSq8", "SSq8"),
                    (kT, krot, "ksc", "CSk", "SSk")):
                p.sca.wait("tS", p.val("tS"))
                sv = p.sca.activation(sq_h[:], src[:], AF.Square)
                p.ten.wait("sS", sv)
                for ch in range(4):
                    tv = p.ten.matmul(ps_q[:], G64[0:64, 0:64],
                                      sq_h[:, ts(ch, 512)],
                                      start=True, stop=True)
                    p.vec.wait("tS", tv)
                    p.vec.tensor_scalar(f32a[:], ps_q[:], 1e-24, 3e38,
                                        ALU.max, ALU.min)
                    vv = p.vec.tensor_scalar(f32a[:], f32a[:], 1.0, None,
                                             ALU.mult)
                    p.sca.wait("vS", vv)
                    sv = p.sca.activation(f32a[:], f32a[:], AF.Sqrt)
                    p.vec.wait("sS", sv)
                    p.vec.reciprocal(f32b[:], f32a[:])
                    vv = p.vec.scalar_tensor_tensor(
                        sq_h[:, ts(ch, 512)], src[:, ts(ch, 512)],
                        b1col(p, io, scol, l, 0, 64), f32b[:], ALU.mult,
                        ALU.mult)
                    p.ten.wait("vS", vv)
                    tv = p.ten.matmul(ps_q[:], P64[0:64, 0:64],
                                      sq_h[:, ts(ch, 512)], start=True,
                                      stop=True)
                    p.vec.wait("tS", tv)
                    p.vec.tensor_tensor(f32a[:], sq_h[:, ts(ch, 512)],
                                        tab[cs][0:64, ts(ch, 512)], ALU.mult)
                    p.vec.tensor_tensor(f32b[:], ps_q[:],
                                        tab[ss][0:64, ts(ch, 512)], ALU.mult)
                    vv = p.vec.tensor_tensor(dst[:, ts(ch, 512)], f32a[:],
                                             f32b[:], ALU.add)
                    p.ten.wait("vS", vv)
                p.sca.wait("vS", p.val("vS"))
            for tt in range(16):
                kt0 = max(0, tt - 4)
                nkt = tt + 1 - kt0
                span = nkt * 128
                tv = p.ten.matmul(ps_s[:, 0:min(span, 512)],
                                  qrot[:, ts(tt, 128)],
                                  krot[:, kt0 * 128:kt0 * 128 + min(span, 512)],
                                  start=True, stop=True)
                if span > 512:
                    tv = p.ten.matmul(ps_s[:, 512:span],
                                      qrot[:, ts(tt, 128)],
                                      krot[:, kt0 * 128 + 512:kt0 * 128 + span],
                                      start=True, stop=True)
                p.vec.wait("tS", tv)
                if tt >= 4:
                    p.vec.tensor_tensor(ps_s[:, 0:128], ps_s[:, 0:128],
                                        maskA[:], ALU.add)
                p.vec.tensor_tensor(ps_s[:, span - 128:span],
                                    ps_s[:, span - 128:span], maskC[:], ALU.add)
                p.vec.reduce_max(stq["m"][:], ps_s[:, 0:span], AX.X)
                vv = p.vec.tensor_scalar(stq["nm"][:], stq["m"][:], -1.0, None,
                                         ALU.mult)
                p.sca.wait("vS", vv)
                sv = p.sca.activation(E_sb[:, 0:span], ps_s[:, 0:span], AF.Exp,
                                      bias=stq["nm"][:, 0:1],
                                      accum_out=stq["r"][:])
                p.ten.wait("sS", sv)
                p.vec.wait("sS", sv)
                p.vec.reciprocal(stq["ri"][:], stq["r"][:])
                p.vec.wait("tS", p.val("tS"))
                vv = p.vec.tensor_scalar(attn_sb[:, 0:span], E_sb[:, 0:span],
                                         stq["ri"][:, 0:1], None, ALU.mult)
                p.ten.wait("vS", vv)
                p.ten.wait("sS", p.val("sS"))
                for kt in range(nkt):
                    pb = pt_t[kt % 2]
                    if kt >= 2:
                        p.ten.wait("sS", p.val("sS") - 1)
                    tv = p.ten.transpose(pb[:], attn_sb[:, ts(kt, 128)],
                                         identB[:])
                    p.sca.wait("tS", tv)
                    sv = p.sca.copy(attnT[:, ts(kt, 128)], pb[:])
                p.ten.wait("sS", p.val("sS"))
                for kt in range(nkt):
                    tv = p.ten.matmul(ps_o[:],
                                      v_sb[:, (kt0 + kt) * 64:(kt0 + kt + 1) * 64],
                                      attnT[:, ts(kt, 128)],
                                      start=(kt == 0), stop=(kt == nkt - 1))
                p.sca.wait("tS", tv)
                sv = p.sca.copy(
                    o_sb[(h % 2) * 64:(h % 2) * 64 + 64,
                         (h // 2) * T + tt * 128:(h // 2) * T + tt * 128 + 128],
                    ps_o[:])
                p.ten.wait("sS", sv)
        p.ten.wait("ldg", p.val("ldg"))
        p.ten.wait("sS", p.val("sS"))
        for ft in range(8):
            for ch in range(4):
                for et in range(4):
                    tv = p.ten.matmul(
                        ps_wo[:],
                        wo_sb[:, et * 1024 + ft * 128:et * 1024 + ft * 128 + 128],
                        o_sb[:, et * T + ch * 512:et * T + (ch + 1) * 512],
                        start=(et == 0), stop=(et == 3))
                p.vec.wait("tS", tv)
                dst = x_br[:, ft * T + ch * 512:ft * T + (ch + 1) * 512]
                vv = p.vec.tensor_tensor(dst, ps_wo[:], dst, ALU.add)
                p.ten.wait("vS", vv)


def _l1f_ffn(p, io, b, l, x_br, ones128):
    es = contextlib.ExitStack()
    HID = 2730
    PAIRS = [(j * 128, min(128, HID - j * 128)) for j in range((HID + 127) // 128)]
    with es:
        xn2 = p.sb(es, "xn2", [128, 8 * T], BF16)
        with contextlib.ExitStack() as es2:
            hb2 = p.sb(es2, "hb2", [128, 8 * T], BF16)
            hsq = p.sb(es2, "hsq", [128, 8 * T], BF16)
            m_bc = p.sb(es2, "m_bc", [128, 512], F32)
            r_bc = p.sb(es2, "r_bc", [128, 512], F32)
            fa = p.sb(es2, "fa", [128, 512], F32)
            ps_1 = p.ps(es2, "ps_1", [128, 512], F32)
            ps_2 = p.ps(es2, "ps_2", [128, 512], F32)
            p.vec.wait("tS", p.val("tS"))
            for dt in range(8):
                vv = _shift_cast(p, x_br, hb2, dt)
                p.sca.wait("vS", vv)
                sv = p.sca.activation(hsq[:, ts(dt, T)], hb2[:, ts(dt, T)],
                                      AF.Square)
            p.ten.wait("sS", p.val("sS"))
            p.ten.wait("vS", p.val("vS"))
            for ch in range(4):
                for dt in range(8):
                    p.ten.matmul(ps_1[:], ones128[:],
                                 hb2[:, dt * T + ch * 512:dt * T + (ch + 1) * 512],
                                 start=(dt == 0), stop=(dt == 7),
                                 skip_group_check=True)
                for dt in range(8):
                    tv = p.ten.matmul(ps_2[:], ones128[:],
                                      hsq[:, dt * T + ch * 512:
                                          dt * T + (ch + 1) * 512],
                                      start=(dt == 0), stop=(dt == 7),
                                      skip_group_check=True)
                p.vec.wait("tS", tv)
                p.vec.tensor_scalar(m_bc[:], ps_1[:], 1.0 / D, None, ALU.mult)
                p.vec.tensor_scalar(r_bc[:], ps_2[:], 1.0 / D, None, ALU.mult)
                p.vec.tensor_tensor(fa[:], m_bc[:], m_bc[:], ALU.mult)
                p.vec.tensor_tensor(fa[:], r_bc[:], fa[:], ALU.subtract)
                vv = p.vec.tensor_scalar(fa[:], fa[:], 1e-5, None, ALU.add)
                p.ten.wait("vS", vv)
                p.sca.wait("vS", vv)
                sv = p.sca.activation(fa[:], fa[:], AF.Sqrt)
                p.vec.wait("sS", sv)
                p.vec.reciprocal(r_bc[:], fa[:])
                for dt in range(8):
                    sl = slice(dt * T + ch * 512, dt * T + (ch + 1) * 512)
                    p.vec.tensor_tensor(fa[:], hb2[:, sl], m_bc[:], ALU.subtract)
                    p.vec.tensor_tensor(fa[:], fa[:], r_bc[:], ALU.mult)
                    vv = p.vec.tensor_scalar(
                        xn2[:, sl], fa[:], b1col(p, io, "lng", l, dt, 128),
                        b1col(p, io, "lnb", l, dt, 128), ALU.mult, ALU.add)
        p.barrier()
        with contextlib.ExitStack() as es3:
            hh_sb = p.sb(es3, "hh_sb", [128, len(PAIRS) * 512], BF16)
            w1a = [p.sb(es3, f"w1a{i}", [128, 8 * 128], BF16) for i in range(2)]
            w1g = [p.sb(es3, f"w1g{i}", [128, 8 * 128], BF16) for i in range(2)]
            w2_sb = p.sb(es3, "w2_sb", [128, len(PAIRS) * 1024], BF16)
            ha = p.sb(es3, "ha", [128, 512], BF16)
            hg = p.sb(es3, "hg", [128, 512], BF16)
            ps_a = p.ps(es3, "ps_a", [128, 512], F32)
            ps_g = p.ps(es3, "ps_g", [128, 512], F32)
            ps_w2 = p.ps(es3, "ps_w2", [128, 512], F32)
            for j, (off, sz) in enumerate(PAIRS):
                p.gps.dma(w2_sb[0:sz, ts(j, 1024)],
                          io["w2"][l, off:off + sz, :], "ldg")
            w1sem = ["w1s0", "w1s1"]
            w1lv = [0, 0]
            w1used = [0, 0]
            npair = len(PAIRS)

            w13 = io["w1"][l].rearrange("(a p) f -> p a f", p=128)

            def w1_prefetch(ch, jj):
                sl = jj % 2
                off, sz = PAIRS[jj]
                p.gps.wait("tS", w1used[sl])
                p.gps.dma(w1a[sl].rearrange("p (a w) -> p a w", a=8)[:, :, 0:sz],
                          w13[:, :, off:off + sz], w1sem[sl])
                w1lv[sl] = p.gps.dma(
                    w1g[sl].rearrange("p (a w) -> p a w", a=8)[:, :, 0:sz],
                    w13[:, :, HID + off:HID + off + sz], w1sem[sl])

            evacv = [0, 0]
            for ch in range(4):
                if ch == 0:
                    w1_prefetch(0, 0)
                for j, (off, sz) in enumerate(PAIRS):
                    sl = j % 2
                    nj = j + 1
                    p.ten.wait(w1sem[sl], w1lv[sl])
                    p.ten.wait("sS", p.val("sS"))
                    p.ten.wait("vS", p.val("vS"))
                    for dt in range(8):
                        p.ten.matmul(ps_a[0:sz, :],
                                     w1a[sl][:, ts(dt, 128)][:, 0:sz],
                                     xn2[:, dt * T + ch * 512:
                                         dt * T + (ch + 1) * 512],
                                     start=(dt == 0), stop=(dt == 7),
                                     skip_group_check=True)
                    for dt in range(8):
                        tv = p.ten.matmul(ps_g[0:sz, :],
                                          w1g[sl][:, ts(dt, 128)][:, 0:sz],
                                          xn2[:, dt * T + ch * 512:
                                              dt * T + (ch + 1) * 512],
                                          start=(dt == 0), stop=(dt == 7),
                                          skip_group_check=True)
                    w1used[sl] = tv
                    if nj < npair:
                        w1_prefetch(ch, nj)
                    elif ch < 3:
                        w1_prefetch(ch + 1, 0)
                    p.vec.wait("tS", tv)
                    vv = p.vec.tensor_scalar(ha[0:sz, :], ps_a[0:sz, :],
                                             b1col(p, io, "b1a", l, j, sz),
                                             None, ALU.add)
                    p.sca.wait("tS", tv)
                    sv = p.sca.activation(hg[0:sz, :], ps_g[0:sz, :], AF.Gelu,
                                          bias=b1col(p, io, "b1g", l, j, sz))
                    p.vec.wait("sS", sv)
                    vv = p.vec.tensor_tensor(hh_sb[0:sz, ts(j, 512)],
                                             ha[0:sz, :], hg[0:sz, :], ALU.mult)
                    p.ten.wait("vS", vv)
                p.ten.wait("ldg", p.val("ldg"))
                for ft in range(8):
                    for j, (off, sz) in enumerate(PAIRS):
                        tv = p.ten.matmul(
                            ps_w2[:], w2_sb[0:sz, j * 1024 + ft * 128:
                                            j * 1024 + ft * 128 + 128],
                            hh_sb[0:sz, ts(j, 512)],
                            start=(j == 0), stop=(j == len(PAIRS) - 1))
                    p.vec.wait("tS", tv)
                    dst = x_br[:, ft * T + ch * 512:ft * T + (ch + 1) * 512]
                    vv = p.vec.scalar_tensor_tensor(
                        dst, ps_w2[:], b1col(p, io, "b2", l, ft, 128), dst,
                        ALU.add, ALU.add)
                    p.ten.wait("vS", vv)


def b1col(p, io, name, l, j, sz):
    key = ("_b1", name, l)
    cache = p.__dict__.setdefault("_colcache", {})
    if key not in cache:
        nt = io[name].shape[2]
        t = p.sb(p.es, f"col_{name}{l}", [128, nt], F32)
        p.sync.dma(t[:], io[name][l], "ld")
        cache[key] = t
    return cache[key][0:sz, j:j + 1]


def _l1f_out(p, es_b, io, b, x_br, identF):
    with contextlib.ExitStack() as es:
        mt = p.sb(es, "mt", [128, T], F32)
        out_t = p.sb(es, "out_t", [128, D], F32)
        x1_t = p.sb(es, "x1_t", [128, D], F32)
        ps_f = [p.ps(es, f"opf{i}", [128, 128], F32) for i in range(2)]
        for dt in range(8):
            p.sync.wait("vS", p.val("vS"))
            lv = p.sync.dma(mt[:], io["maccT"][b, dt], "ldf")
            p.vec.wait("ldf", lv)
            vv = p.vec.tensor_tensor(x_br[:, ts(dt, T)], mt[:],
                                     x_br[:, ts(dt, T)], ALU.add)
        p.ten.wait("vS", p.val("vS"))
        for tt in range(16):
            p.sync.wait("st", p.val("st"))
            lv = p.sync.dma(x1_t[:], io["x1"][b, ts(tt, 128), :], "ldx1")
            p.ten.wait("sS", p.val("sS"))
            p.ten.wait("vS", p.val("vS"))
            p.ten.wait("st", p.val("st"))
            for dt in range(8):
                pb = ps_f[dt % 2]
                if dt >= 2:
                    p.ten.wait("sS", p.val("sS") - 1)
                tv = p.ten.transpose(pb[:], x_br[:, dt * T + tt * 128:
                                                 dt * T + tt * 128 + 128],
                                     identF[:])
                p.sca.wait("tS", tv)
                sv = p.sca.copy(out_t[:, ts(dt, 128)], pb[:])
            p.vec.wait("sS", p.val("sS"))
            p.vec.wait("ldx1", lv)
            vv = p.vec.tensor_tensor(out_t[:], x1_t[:], out_t[:], ALU.add)
            p.sync.wait("vS", vv)
            p.sync.dma(io["x2"][b, ts(tt, 128), :], out_t[:], "st")




_L0_NAMES = ["wqT", "wkT", "wvT", "woT", "lq1T", "lk1T", "lq2T", "lk2T", "rms",
             "wgT", "wuT", "wdT", "identB", "maskC", "ones64"]


def _declare(nc, io, cw):
    for nm, arr in cw.items():
        dt = {np.dtype(np.float32): F32, np.dtype(np.int32): I32,
              np.dtype(BF): BF16}[arr.dtype]
        io[nm] = nc.dram_tensor(nm, list(arr.shape), dt, kind="ExternalInput").ap()


def run_l0(pool, xfull):
    cw = {nm: pool[nm] for nm in _L0_NAMES}
    with contextlib.ExitStack() as es:
        nc = bass.Bass("TRN2", target_bir_lowering=False)
        io = {}
        io["x"] = nc.dram_tensor("x", [NB, T, D], F32, kind="ExternalInput").ap()
        io["x1"] = nc.dram_tensor("x1", [NB, T, D], F32,
                                  kind="ExternalOutput").ap()
        io["xnT_h"] = nc.dram_tensor("xnT_h", [NB, 16, 8, 128, 128], BF16,
                                     kind="Internal").ap()
        io["attn_h"] = nc.dram_tensor("attn_h", [NB, T, D], F32,
                                      kind="Internal").ap()
        _declare(nc, io, cw)
        p = P(nc, es)
        build_l0(p, io)
        p.finalize()
        in_maps = []
        for c in range(N_CORE):
            m = dict(cw)
            m["x"] = np.ascontiguousarray(xfull[2 * c:2 * c + 2])
            in_maps.append(m)
        res = run_bass_kernel_spmd(nc, in_maps, core_ids=list(range(N_CORE)))
    return np.concatenate([r["x1"] for r in res.results], axis=0)


_MB_NAMES = (["inpT", "outpT", "identB", "selrep", "selsum", "seln"] +
             [f"{nm}_{s}" for s in ("f", "b")
              for nm in ("xprojT", "dtpT", "convw", "convb", "dtb", "alog", "D")])


def run_mamba(pool, x1full):
    cw = {nm: pool[nm] for nm in _MB_NAMES}
    with contextlib.ExitStack() as es:
        nc = bass.Bass("TRN2", target_bir_lowering=False)
        io = {}
        io["x1"] = nc.dram_tensor("x1", [NB, T, D], F32,
                                  kind="ExternalInput").ap()
        io["xn1T_h"] = nc.dram_tensor("xn1T_h", [NB, 8, 128, T], BF16,
                                      kind="ExternalOutput").ap()
        io["maccT"] = nc.dram_tensor("maccT", [NB, 8, 128, T], F32,
                                     kind="ExternalOutput").ap()
        io["x_h"] = nc.dram_tensor("x_h", [NB, 16, 128, T], BF16,
                                   kind="Internal").ap()
        io["sz_h"] = nc.dram_tensor("sz_h", [NB, 16, 128, T], BF16,
                                    kind="Internal").ap()
        _declare(nc, io, cw)
        p = P(nc, es)
        build_mamba(p, io)
        p.finalize()
        in_maps = []
        for c in range(N_CORE):
            m = dict(cw)
            m["x1"] = np.ascontiguousarray(x1full[2 * c:2 * c + 2])
            in_maps.append(m)
        res = run_bass_kernel_spmd(nc, in_maps, core_ids=list(range(N_CORE)))
    xn1T = np.concatenate([r["xn1T_h"] for r in res.results], axis=0)
    macc = np.concatenate([r["maccT"] for r in res.results], axis=0)
    return xn1T, macc


_L1_NAMES = ["wqkv", "wo1", "w1", "w2", "b1a", "b1g", "b2", "lng", "lnb",
             "qsc", "ksc", "identB", "identF", "maskC", "maskA", "G64", "P64",
             "ones128", "CSq8", "SSq8", "CSk", "SSk"]


def run_l1ffn(pool, x1full, xn1Tfull, maccfull):
    cw = {nm: pool[nm] for nm in _L1_NAMES}
    with contextlib.ExitStack() as es:
        nc = bass.Bass("TRN2", target_bir_lowering=False)
        io = {}
        io["x1"] = nc.dram_tensor("x1", [NB, T, D], F32,
                                  kind="ExternalInput").ap()
        io["xn1T_h"] = nc.dram_tensor("xn1T_h", [NB, 8, 128, T], BF16,
                                      kind="ExternalInput").ap()
        io["maccT"] = nc.dram_tensor("maccT", [NB, 8, 128, T], F32,
                                     kind="ExternalInput").ap()
        io["x2"] = nc.dram_tensor("x2", [NB, T, D], F32,
                                  kind="ExternalOutput").ap()
        _declare(nc, io, cw)
        p = P(nc, es)
        build_l1ffn(p, io)
        p.finalize()
        in_maps = []
        for c in range(N_CORE):
            m = dict(cw)
            m["x1"] = np.ascontiguousarray(x1full[2 * c:2 * c + 2])
            m["xn1T_h"] = np.ascontiguousarray(xn1Tfull[2 * c:2 * c + 2])
            m["maccT"] = np.ascontiguousarray(maccfull[2 * c:2 * c + 2])
            in_maps.append(m)
        res = run_bass_kernel_spmd(nc, in_maps, core_ids=list(range(N_CORE)))
    return np.concatenate([r["x2"] for r in res.results], axis=0)


def run_fused(pool, xfull):
    names = sorted(set(_L0_NAMES + list(_MB_NAMES) + _L1_NAMES))
    cw = {nm: pool[nm] for nm in names}
    with contextlib.ExitStack() as es:
        nc = bass.Bass("TRN2", target_bir_lowering=False)
        io = {}
        io["x"] = nc.dram_tensor("x", [NB, T, D], F32, kind="ExternalInput").ap()
        io["x2"] = nc.dram_tensor("x2", [NB, T, D], F32,
                                  kind="ExternalOutput").ap()
        for nm, shape, dt_ in (
                ("x1", [NB, T, D], F32),
                ("xnT_h", [NB, 16, 8, 128, 128], BF16),
                ("attn_h", [NB, T, D], F32),
                ("xn1T_h", [NB, 8, 128, T], BF16),
                ("maccT", [NB, 8, 128, T], F32),
                ("x_h", [NB, 16, 128, T], BF16),
                ("sz_h", [NB, 16, 128, T], BF16)):
            io[nm] = nc.dram_tensor(nm, shape, dt_, kind="Internal").ap()
        _declare(nc, io, cw)
        p = P(nc, es)
        build_l0(p, io)
        p.barrier()
        build_mamba(p, io)
        p.barrier()
        build_l1ffn(p, io)
        p.finalize()
        in_maps = []
        for c in range(N_CORE):
            m = dict(cw)
            m["x"] = np.ascontiguousarray(xfull[2 * c:2 * c + 2])
            in_maps.append(m)
        res = run_bass_kernel_spmd(nc, in_maps, core_ids=list(range(N_CORE)))
    return np.concatenate([r["x2"] for r in res.results], axis=0)


def kernel(**inputs):
    consts = _consts()
    w = _prep_weights(inputs)
    pool = {**w, **consts}
    xfull = np.asarray(inputs["L0_attn_X"], dtype=np.float32)
    return run_fused(pool, xfull)


# version 09
# speedup vs baseline: 1.3057x; 1.0605x over previous
import contextlib
import os
import traceback
import numpy as np
import ml_dtypes
import concourse.bass as bass
import concourse.mybir as mybir
from concourse.bass_utils import run_bass_kernel_spmd

F32 = mybir.dt.float32
BF16 = mybir.dt.bfloat16
I32 = mybir.dt.int32
AF = mybir.ActivationFunctionType
ALU = mybir.AluOpType
AX = mybir.AxisListType
BF = ml_dtypes.bfloat16

NEG = -1.0e30
T = 2048
D = 1024
NB = 2
N_CORE = 8
ts = bass.ts


ENGS = ("sync", "gpsimd", "vector", "scalar", "tensor")

SELF_SEM = {"tensor": "tS", "vector": "vS", "scalar": "sS", "gpsimd": "gS"}
SELF_WAIT = {"vector", "scalar", "gpsimd"}


class Eng:
    def __init__(self, p, name):
        self.p = p
        self.name = name

    def __getattr__(self, op):
        p, name = self.p, self.name
        ssem = SELF_SEM.get(name)

        def f(*args, **kw):
            if name in SELF_WAIT:
                prior = p.cnt[ssem]
                self.wait(ssem, prior)
            v = p.inc(ssem)
            s = p.sems[ssem]
            if p.debug_lines:
                fr = traceback.extract_stack(limit=2)[0]
                lbl = f"{op}@{fr.lineno}"
                def thunk(e, lbl=lbl, op=op, args=args, kw=kw):
                    with p.nc.named_scope(lbl):
                        getattr(e, op)(*args, **kw).then_inc(s, 1)
                p.q[name].append(thunk)
            else:
                p.q[name].append(
                    lambda e: getattr(e, op)(*args, **kw).then_inc(s, 1))
            return v
        return f

    def dma(self, dst, src, sem):
        p, name = self.p, self.name
        s = p.sem(sem)
        v = p.inc(sem, 16)
        p.q[name].append(lambda e: e.dma_start(dst, src).then_inc(s, 16))
        return v

    def wait(self, sem, v):
        p, name = self.p, self.name
        if v <= p.waited[name].get(sem, 0):
            return
        p.waited[name][sem] = v
        s = p.sems[sem]
        p.q[name].append(lambda e: e.wait_ge(s, v))


class P:
    def __init__(self, nc, es):
        self.nc = nc
        self.es = es
        self.q = {k: [] for k in ENGS}
        self.sems = {}
        self.cnt = {}
        self.waited = {k: {} for k in ENGS}
        self.debug_lines = bool(os.environ.get("MK_DEBUG_LINES"))
        self.sync = Eng(self, "sync")
        self.gps = Eng(self, "gpsimd")
        self.vec = Eng(self, "vector")
        self.sca = Eng(self, "scalar")
        self.ten = Eng(self, "tensor")
        self.engmap = {"sync": self.sync, "gpsimd": self.gps, "vector": self.vec,
                       "scalar": self.sca, "tensor": self.ten}
        self.nbar = 0
        for nm in ("ld", "ldg", "st", "tS", "vS", "sS", "gS", "bar"):
            self.sem(nm)

    def sem(self, nm):
        if nm not in self.sems:
            self.sems[nm] = self.es.enter_context(self.nc.semaphore(nm))
            self.cnt[nm] = 0
        return self.sems[nm]

    def inc(self, sem, n=1):
        self.cnt[sem] += n
        return self.cnt[sem]

    def val(self, sem):
        return self.cnt[sem]

    def barrier(self):
        self.nbar += 1
        tgt = 5 * self.nbar
        s = self.sems["bar"]
        allsems = [k for k in self.sems if k != "bar"]
        for nm in ENGS:
            eng = self.engmap[nm]
            for sm in allsems:
                eng.wait(sm, self.cnt[sm])
            self.q[nm].append(lambda e, s=s: e.nop().then_inc(s, 1))
        for nm in ENGS:
            self.q[nm].append(lambda e, s=s, tgt=tgt: e.wait_ge(s, tgt))

    def sb(self, es, name, shape, dtype):
        self._uid = getattr(self, "_uid", 0) + 1
        return es.enter_context(
            self.nc.sbuf_tensor(f"{name}_u{self._uid}", shape, dtype))

    def ps(self, es, name, shape, dtype=F32):
        self._uid = getattr(self, "_uid", 0) + 1
        return es.enter_context(
            self.nc.psum_tensor(f"{name}_u{self._uid}", shape, dtype))

    def finalize(self):
        with self.nc.Block() as block:
            @block.sync
            def _(e):
                for f in self.q["sync"]:
                    f(e)

            @block.gpsimd
            def _(e):
                for f in self.q["gpsimd"]:
                    f(e)

            @block.vector
            def _(e):
                for f in self.q["vector"]:
                    f(e)

            @block.scalar
            def _(e):
                for f in self.q["scalar"]:
                    f(e)

            @block.tensor
            def _(e):
                for f in self.q["tensor"]:
                    f(e)



def _consts():
    c = {}
    c["identB"] = np.eye(128, dtype=np.float32).astype(BF)
    c["identF"] = np.eye(128, dtype=np.float32)
    i = np.arange(128)
    c["maskC"] = np.where(i[:, None] >= i[None, :], 0.0, NEG).astype(np.float32)
    c["maskA"] = np.where(i[:, None] <= i[None, :], 0.0, NEG).astype(np.float32)
    c["ones64"] = np.ones((64, 128), dtype=np.float32).astype(BF)
    c["ones128"] = np.ones((128, 128), dtype=np.float32).astype(BF)
    g = (i[:, None] // 64 == i[None, :] // 64).astype(np.float32)
    c["G64"] = g.astype(BF)
    p64 = np.zeros((128, 128), dtype=np.float32)
    for blk in range(2):
        o = blk * 64
        for m in range(32):
            p64[o + m + 32, o + m] = -1.0
        for m in range(32, 64):
            p64[o + m - 32, o + m] = 1.0
    c["P64"] = p64.astype(BF)
    selrep = np.zeros((128, 16 * 128), dtype=np.float32)
    selsum = np.zeros((128, 16 * 128), dtype=np.float32)
    for ii in range(16):
        for m in range(128):
            selrep[8 * ii + m // 16, ii * 128 + m] = 1.0
        for k in range(128):
            selsum[k, ii * 128 + 8 * ii + k // 16] = 1.0
    c["selrep"] = selrep.astype(BF)
    c["selsum"] = selsum.astype(BF)
    seln = np.zeros((16, 128), dtype=np.float32)
    for m in range(128):
        seln[m % 16, m] = 1.0
    c["seln"] = seln.astype(BF)
    DH, WIN = 64, 512
    half = np.arange(0, DH, 2, dtype=np.float64)
    inv_freq = 1.0 / (10000.0 ** (half / DH))
    t = np.arange(T, dtype=np.float64)
    freqs = t[:, None] * inv_freq[None, :]
    freqs = np.concatenate([freqs, freqs], axis=-1)
    base = (half + 0.4 * DH) / (1.4 * DH)
    power = (t - T // 2) / float(WIN // 2)
    scale = base[None, :] ** power[:, None]
    scale = np.concatenate([scale, scale], axis=-1)
    cos, sin = np.cos(freqs), np.sin(freqs)

    def fm(x):
        xt = np.ascontiguousarray(x.T.astype(np.float32))
        return np.concatenate([xt, xt], axis=0)
    c["CSq8"] = fm(cos * scale * 8.0)
    c["SSq8"] = fm(sin * scale * 8.0)
    c["CSk"] = fm(cos / scale)
    c["SSk"] = fm(sin / scale)
    return c


def _col_tiles(v, np_dtype=np.float32):
    n = v.shape[0]
    nt = (n + 127) // 128
    out = np.zeros((128, nt), dtype=np_dtype)
    for j in range(nt):
        seg = v[j * 128:(j + 1) * 128]
        out[:len(seg), j] = seg
    return np.ascontiguousarray(out)


def _prep_weights(inp):
    w = {}
    cT = lambda a: np.ascontiguousarray(np.asarray(a).T)
    w["wqT"] = cT(inp["L0_attn_W_q"])
    w["wkT"] = cT(inp["L0_attn_W_k"])
    w["wvT"] = cT(inp["L0_attn_W_v"])
    w["woT"] = cT(inp["L0_attn_W_o"])
    w["lq1T"] = cT(inp["L0_attn_lambda_q1"])
    w["lk1T"] = cT(inp["L0_attn_lambda_k1"])
    w["lq2T"] = cT(inp["L0_attn_lambda_q2"])
    w["lk2T"] = cT(inp["L0_attn_lambda_k2"])
    w["rms"] = np.ascontiguousarray(
        np.asarray(inp["L0_attn_rms_scale"], dtype=np.float32).reshape(128, 1))
    w["wgT"] = cT(inp["L0_ffn_w_gate"])
    w["wuT"] = cT(inp["L0_ffn_w_up"])
    w["wdT"] = cT(inp["L0_ffn_w_down"])
    w["inpT"] = cT(inp["L1_attn_in_proj_w"])
    w["outpT"] = cT(inp["L1_attn_out_proj_w"])
    for sfx, cw, cb, xp, dtw, dtb, alog, dv in (
        ("f", "L1_attn_conv1d_w", "L1_attn_conv1d_bias", "L1_attn_x_proj_w",
         "L1_attn_dt_proj_w", "L1_attn_dt_proj_bias", "L1_attn_A_log", "L1_attn_D"),
        ("b", "L1_attn_conv1d_b_w", "L1_attn_conv1d_b_bias", "L1_attn_x_proj_b_w",
         "L1_attn_dt_proj_b_w", "L1_attn_dt_proj_b_bias", "L1_attn_A_b_log",
         "L1_attn_D_b"),
    ):
        cwa = np.asarray(inp[cw]).reshape(2048, 4)
        w["convw_" + sfx] = np.ascontiguousarray(
            cwa.reshape(16, 128, 4).transpose(1, 0, 2))
        w["convb_" + sfx] = _col_tiles(np.asarray(inp[cb]))
        xpa = np.asarray(inp[xp])
        perm = np.zeros((128, 2048), dtype=np.float32)
        perm[0:16] = xpa[64:80]
        perm[32:48] = xpa[80:96]
        perm[64:128] = xpa[0:64]
        w["xprojT_" + sfx] = cT(perm)
        w["dtpT_" + sfx] = cT(np.asarray(inp[dtw]))
        w["dtb_" + sfx] = _col_tiles(np.asarray(inp[dtb]))
        al = np.asarray(inp[alog])
        ac = np.zeros((128, 256), dtype=np.float32)
        m = np.arange(128)
        for j in range(16):
            for i2 in range(16):
                ac[:, j * 16 + i2] = al[128 * j + 8 * i2 + m // 16, m % 16]
        w["alog_" + sfx] = ac
        w["D_" + sfx] = _col_tiles(np.asarray(inp[dv]))
    w["wqkv"] = np.ascontiguousarray(np.asarray(inp["L1_ffn_wqkv"]))
    w["wo1"] = np.ascontiguousarray(np.asarray(inp["L1_ffn_wo"]))
    w["w1"] = np.ascontiguousarray(np.asarray(inp["L1_ffn_w1"]))
    w["w2"] = np.ascontiguousarray(np.asarray(inp["L1_ffn_w2"]))
    b1 = np.asarray(inp["L1_ffn_b1"])
    w["b1a"] = np.stack([_col_tiles(b1[l, :2730]) for l in range(2)])
    w["b1g"] = np.stack([_col_tiles(b1[l, 2730:]) for l in range(2)])
    w["b2"] = np.stack([_col_tiles(np.asarray(inp["L1_ffn_b2"])[l])
                        for l in range(2)])
    w["lng"] = np.stack([_col_tiles(np.asarray(inp["L1_ffn_ln_g"])[l])
                         for l in range(2)])
    w["lnb"] = np.stack([_col_tiles(np.asarray(inp["L1_ffn_ln_b"])[l])
                         for l in range(2)])
    qs = np.asarray(inp["L1_ffn_q_scale"])
    ks = np.asarray(inp["L1_ffn_k_scale"])
    w["qsc"] = np.ascontiguousarray(
        np.concatenate([qs, qs], axis=1).reshape(2, 128, 1))
    w["ksc"] = np.ascontiguousarray(
        np.concatenate([ks, ks], axis=1).reshape(2, 128, 1))
    return w



def emit_ln(p, es, xsrc_dram, b, xnT, xnT_h=None, identB=None):
    xt = [p.sb(es, f"lnx{i}", [128, D], F32) for i in range(2)]
    sq_v = p.sb(es, "lnsqv", [128, D], BF16)
    sq_s = p.sb(es, "lnsqs", [128, D], BF16)
    st = {nm: p.sb(es, "lnst_" + nm, [128, 1], F32)
          for nm in ("sum", "ssq", "mean", "e2", "var", "std", "rstd")}
    xn_nat = p.sb(es, "lnxn", [128, D], BF16)
    pt_t = [p.ps(es, f"lnpt{i}", [128, 128], BF16) for i in range(2)]
    vdone = [0] * 16
    sdone = [0] * 16
    for tt in range(16):
        buf = xt[tt % 2]
        slot = "lnx" + ("A" if tt % 2 == 0 else "B")
        if tt >= 2:
            p.sync.wait("vS", vdone[tt - 2])
            p.sync.wait("sS", sdone[tt - 2])
        lv = p.sync.dma(buf[:], xsrc_dram[b, ts(tt, 128), :], slot)
        p.vec.wait(slot, lv)
        p.sca.wait(slot, lv)
        p.vec.tensor_scalar(sq_v[:], buf[:], 1.0, 0.0, ALU.mult, ALU.add,
                            accum_out=st["sum"][:])
        sv = p.sca.activation(sq_s[:], buf[:], AF.Square, accum_out=st["ssq"][:])
        sdone[tt] = sv
        p.vec.wait("sS", sv)
        p.vec.tensor_scalar(st["mean"][:], st["sum"][:], 1.0 / D, None, ALU.mult)
        p.vec.tensor_scalar(st["e2"][:], st["ssq"][:], 1.0 / D, None, ALU.mult)
        p.vec.tensor_tensor(st["var"][:], st["mean"][:], st["mean"][:], ALU.mult)
        p.vec.tensor_tensor(st["var"][:], st["e2"][:], st["var"][:], ALU.subtract)
        vv = p.vec.tensor_scalar(st["var"][:], st["var"][:], 1e-5, None, ALU.add)
        p.sca.wait("vS", vv)
        sv = p.sca.activation(st["std"][:], st["var"][:], AF.Sqrt)
        p.vec.wait("sS", sv)
        p.vec.reciprocal(st["rstd"][:], st["std"][:])
        p.vec.wait("tS", p.val("tS"))
        vv = p.vec.tensor_scalar(xn_nat[:], buf[:], st["mean"][:], st["rstd"][:],
                                 ALU.subtract, ALU.mult)
        vdone[tt] = vv
        p.ten.wait("vS", vv)
        for dt in range(8):
            pb = pt_t[dt % 2]
            if dt >= 2:
                p.ten.wait("sS", p.val("sS") - 1)
            tv = p.ten.transpose(pb[:], xn_nat[:, ts(dt, 128)], identB[:])
            p.sca.wait("tS", tv)
            sv = p.sca.copy(xnT[:, dt * T + tt * 128:dt * T + tt * 128 + 128],
                            pb[:])
        if xnT_h is not None:
            p.sync.wait("sS", p.val("sS"))
            src3 = xnT.rearrange("p (a t) -> p a t", a=8)[:, :,
                                                          tt * 128:(tt + 1) * 128]
            p.sync.dma(xnT_h[b, tt].rearrange("a p t -> p a t"), src3, "st")



def build_l0(p, io):
    nc = p.nc
    x, x1 = io["x"], io["x1"]
    xnT_h, attn_h = io["xnT_h"], io["attn_h"]
    es_p = p.es

    identB = p.sb(es_p, "identB", [128, 128], BF16)
    maskC = p.sb(es_p, "maskC", [128, 128], F32)
    rms02 = p.sb(es_p, "rms02", [128, 1], F32)
    lam_bc = p.sb(es_p, "lam_bc", [128, 16], F32)

    p.sync.dma(identB[:], io["identB"], "ld")
    p.sync.dma(maskC[:], io["maskC"], "ld")

    with contextlib.ExitStack() as es0:
        lv4 = [p.sb(es0, f"lv{i}", [64, 16], F32) for i in range(4)]
        ones64 = p.sb(es0, "ones64", [64, 128], BF16)
        prod = p.sb(es0, "prod", [64, 16], BF16)
        pl = p.ps(es0, "pl", [128, 16], F32)
        e1 = p.sb(es0, "e1", [128, 16], F32)
        e2 = p.sb(es0, "e2", [128, 16], F32)
        rmsr = p.sb(es0, "rmsr", [128, 1], F32)
        for i, nm in enumerate(("lq1T", "lk1T", "lq2T", "lk2T")):
            p.sync.dma(lv4[i][:], io[nm], "ld")
        p.sync.dma(rmsr[:], io["rms"], "ld")
        p.gps.dma(ones64[:], io["ones64"], "ldg")
        p.vec.wait("ld", p.val("ld"))
        p.vec.wait("ldg", p.val("ldg"))
        p.vec.tensor_scalar(rms02[:], rmsr[:], 0.2, None, ALU.mult)
        for k in range(2):
            a, bb = (0, 1) if k == 0 else (2, 3)
            vv = p.vec.tensor_tensor(prod[:], lv4[a][:], lv4[bb][:], ALU.mult)
            p.ten.wait("vS", vv)
            tv = p.ten.matmul(pl[:], ones64[:], prod[:], start=True, stop=True)
            p.sca.wait("tS", tv)
            sv = p.sca.activation((e1 if k == 0 else e2)[:], pl[:], AF.Exp)
            p.vec.wait("sS", sv)
        p.vec.tensor_tensor(lam_bc[:], e1[:], e2[:], ALU.subtract)
        p.vec.tensor_scalar(lam_bc[:], lam_bc[:], 0.8, None, ALU.add)
        p.barrier()

    for b in range(NB):
        with contextlib.ExitStack() as es_b:
            xnT = p.sb(es_b, "xnT", [128, 8 * T], BF16)
            acc = p.sb(es_b, "l0acc", [128, 16 * 1024], F32)
            with contextlib.ExitStack() as es_ln:
                emit_ln(p, es_ln, x, b, xnT, xnT_h=xnT_h, identB=identB)
                p.barrier()
            with contextlib.ExitStack() as es_a:
                _l0_attn(p, es_a, io, b, xnT, acc, identB, maskC, rms02, lam_bc)
            p.sync.wait("vS", p.val("vS"))
            p.sync.dma(attn_h[b].rearrange("(a p) f -> p a f", p=128),
                       acc.rearrange("p (a f) -> p a f", a=16), "st")
            p.sync.wait("st", p.val("st"))
            p.barrier()
        with contextlib.ExitStack() as es_f:
            _l0_ffn(p, es_f, io, b, x, x1, xnT_h, attn_h)
            p.barrier()
    p.sync.wait("st", p.val("st"))


def _l0_attn(p, es, io, b, xnT, acc, identB, maskC, rms02, lam_bc):
    GH = 2
    wq_g = p.sb(es, "wq_g", [128, 8 * 128 * GH], BF16)
    wk_g = p.sb(es, "wk_g", [128, 8 * 128 * GH], BF16)
    wv_g = p.sb(es, "wv_g", [128, 8 * 128 * GH], BF16)
    wo_g = p.sb(es, "wo_g", [128, GH * 1024], BF16)
    qT_g = p.sb(es, "qT_g", [128, GH * T], BF16)
    kT_g = p.sb(es, "kT_g", [128, GH * T], BF16)
    v_g = p.sb(es, "v_g", [128, 16 * 128 * GH], BF16)
    ps_s1 = p.ps(es, "ps_s1", [128, 1024], F32)
    ps_s2 = p.ps(es, "ps_s2", [128, 1024], F32)
    ps_o = p.ps(es, "ps_o", [128, 128], F32)

    pt_t = [p.ps(es, f"apt{i}", [128, 128], BF16) for i in range(2)]
    oT_all = p.sb(es, "oT_all", [128, GH * 16 * 128], BF16)
    wo_mm_last = [0]
    E1 = [p.sb(es, f"E1_{i}", [128, T], BF16) for i in range(2)]
    E2 = [p.sb(es, f"E2_{i}", [128, T], BF16) for i in range(2)]
    attn_sb = [p.sb(es, f"attn{i}", [128, T], BF16) for i in range(2)]
    attnT = [p.sb(es, f"attnT{i}", [128, T], BF16) for i in range(2)]
    atd = [0, 0]
    onat = [p.sb(es, f"onat{i}", [128, 128], BF16) for i in range(2)]
    sq128 = p.sb(es, "sq128", [128, 128], BF16)
    st2 = {}
    for nm in ("m", "nm", "rr", "rri"):
        st2[nm] = [p.sb(es, f"at_{nm}{i}", [128, 2], F32) for i in range(2)]
    st2["rp"] = [p.sb(es, f"at_rp{i}", [128, 4], F32) for i in range(2)]
    for nm in ("c2", "ssq", "t0", "s0", "f"):
        st2[nm] = [p.sb(es, f"at_{nm}{i}", [128, 1], F32) for i in range(2)]

    for g in range(16 // GH):
        p.gps.wait("tS", p.val("tS"))
        W = 128 * GH
        for wdst, wsrc in ((wq_g, "wqT"), (wk_g, "wkT"), (wv_g, "wvT")):
            p.gps.dma(wdst.rearrange("p (a w) -> p a w", a=8),
                      io[wsrc].rearrange("(a p) f -> p a f", p=128)[
                          :, :, g * W:(g + 1) * W], "ldg")
        p.gps.dma(wo_g.rearrange("p (a f) -> p a f", a=GH),
                  io["woT"].rearrange("(a p) f -> p a f", p=128)[
                      :, GH * g:GH * (g + 1), :], "ldg")
        lgv = p.val("ldg")
        p.vec.wait("ldg", lgv)
        p.vec.wait("tS", p.val("tS"))
        vv = 0
        for hh in range(GH):
            vv = p.vec.tensor_scalar(wo_g[:, ts(hh, 1024)],
                                     wo_g[:, ts(hh, 1024)], rms02[:, 0:1], None,
                                     ALU.mult)
        p.ten.wait("vS", vv)
        p.ten.wait("ldg", lgv)
        p.ten.wait("sS", p.val("sS"))
        psq = [ps_s1, ps_s2]
        qk_ev = [0, 0]
        ci = 0
        for hh in range(GH):
            for (wsb, dst) in ((wq_g, qT_g), (wk_g, kT_g)):
                for ch in range(4):
                    sl = ci % 2
                    ci += 1
                    p.ten.wait("sS", qk_ev[sl])
                    for dt in range(8):
                        tv = p.ten.matmul(
                            psq[sl][:, 0:512],
                            wsb[:, dt * 128 * GH + hh * 128:
                                dt * 128 * GH + hh * 128 + 128],
                            xnT[:, dt * T + ch * 512:dt * T + (ch + 1) * 512],
                            start=(dt == 0), stop=(dt == 7))
                    p.sca.wait("tS", tv)
                    qk_ev[sl] = p.sca.copy(
                        dst[:, hh * T + ch * 512:hh * T + (ch + 1) * 512],
                        psq[sl][:, 0:512])
        for tt in range(16):
            sl = ci % 2
            ci += 1
            p.ten.wait("sS", qk_ev[sl])
            for dt in range(8):
                tv = p.ten.matmul(psq[sl][:, 0:128 * GH],
                                  xnT[:, dt * T + tt * 128:dt * T + tt * 128 + 128],
                                  wv_g[:, ts(dt, 128 * GH)],
                                  start=(dt == 0), stop=(dt == 7))
            p.sca.wait("tS", tv)
            qk_ev[sl] = p.sca.copy(v_g[:, ts(tt, 128 * GH)],
                                   psq[sl][:, 0:128 * GH])
        p.ten.wait("sS", p.val("sS"))
        iters = [(tt, hh) for tt in range(16) for hh in range(GH)]

        def stage_S(it, mi, kh):
            tt, hh = iters[it]
            par = it % 2
            base = 64 * mi
            pss = ps_s1 if mi == 0 else ps_s2
            kspan = (tt + 1) * 128
            lo = kh * 1024
            hi = min(kspan, lo + 1024)
            p.ten.wait("sS", st["exp" + str(mi)][par] if kh_first(it) != kh
                       else st["exp" + str(mi)][1 - par])
            for c in range(lo, hi, 512):
                cn = min(512, hi - c)
                tv = p.ten.matmul(
                    pss[:, c - lo:c - lo + cn],
                    qT_g[base:base + 64,
                         hh * T + tt * 128:hh * T + tt * 128 + 128],
                    kT_g[base:base + 64, hh * T + c:hh * T + c + cn],
                    start=True, stop=True)
            st["smm" + str(mi)][par] = tv

        def kh_first(it):
            tt, hh = iters[it]
            return tt // 8

        def vec_max(it, mi):
            tt, hh = iters[it]
            par = it % 2
            pss = ps_s1 if mi == 0 else ps_s2
            kspan = (tt + 1) * 128
            dlo = kspan - 128 - kh_first(it) * 1024
            p.vec.wait("tS", st["smm" + str(mi)][par])
            p.vec.tensor_tensor(pss[:, dlo:dlo + 128],
                                pss[:, dlo:dlo + 128], maskC[:], ALU.add)
            p.vec.reduce_max(st2["m"][par][:, mi:mi + 1],
                             pss[:, dlo:dlo + 128], AX.X)
            st["negm" + str(mi)][par] = p.vec.tensor_scalar(
                st2["nm"][par][:, mi:mi + 1], st2["m"][par][:, mi:mi + 1],
                -0.125, None, ALU.mult)

        def sca_exp(it, mi, kh):
            tt, hh = iters[it]
            par = it % 2
            pss = ps_s1 if mi == 0 else ps_s2
            kspan = (tt + 1) * 128
            lo = kh * 1024
            hi = min(kspan, lo + 1024)
            Em = (E1 if mi == 0 else E2)[par]
            p.sca.wait("tS", st["smm" + str(mi)][par])
            p.sca.wait("vS", st["negm" + str(mi)][par])
            if kh == kh_first(it):
                p.sca.wait("vS", st["comb"][par])
            sv = p.sca.activation(Em[:, lo:hi], pss[:, 0:hi - lo], AF.Exp,
                                  bias=st2["nm"][par][:, mi:mi + 1], scale=0.125,
                                  accum_out=st2["rp"][par][:, 2 * mi + kh:
                                                           2 * mi + kh + 1])
            st["exp" + str(mi)][par] = sv

        def vec_combine(it):
            tt, hh = iters[it]
            par = it % 2
            hfull = hh + GH * g
            kspan = (tt + 1) * 128
            p.vec.wait("sS", st["exp1"][par])
            tt_ = iters[it][0]
            if tt_ // 8 == 0:
                p.vec.tensor_copy(st2["rr"][par][:],
                                  st2["rp"][par][:, 0:4:2])
            else:
                p.vec.tensor_tensor(st2["rr"][par][:],
                                    st2["rp"][par][:, 0:4:2],
                                    st2["rp"][par][:, 1:4:2], ALU.add)
            p.vec.reciprocal(st2["rri"][par][:], st2["rr"][par][:])
            p.vec.tensor_scalar(st2["c2"][par][:], st2["rri"][par][:, 1:2],
                                lam_bc[:, hfull:hfull + 1], -1.0, ALU.mult,
                                ALU.mult)
            p.vec.wait("tS", st["trans"][par])
            p.vec.tensor_scalar(attn_sb[par][:, 0:kspan], E1[par][:, 0:kspan],
                                st2["rri"][par][:, 0:1], None, ALU.mult)
            st["comb"][par] = p.vec.scalar_tensor_tensor(
                attn_sb[par][:, 0:kspan], E2[par][:, 0:kspan],
                st2["c2"][par][:, 0:1], attn_sb[par][:, 0:kspan], ALU.mult,
                ALU.add)

        def ten_trans_pv(it):
            tt, hh = iters[it]
            par = it % 2
            p.ten.wait("vS", st["comb"][par])
            p.ten.wait("vS", st["onat"][1 - par])
            for kt in range(tt + 1):
                pb = pt_t[kt % 2]
                if kt % 2 == 0:
                    p.ten.wait("sS", st["tevac"][0])
                else:
                    p.ten.wait("vS", st["tevac"][1])
                tv = p.ten.transpose(pb[:], attn_sb[par][:, ts(kt, 128)],
                                     identB[:])
                if kt % 2 == 0:
                    p.sca.wait("tS", tv)
                    p.sca.wait("tS", st["pv"][1 - par])
                    st["tevac"][0] = p.sca.copy(attnT[par][:, ts(kt, 128)],
                                                pb[:])
                else:
                    p.vec.wait("tS", tv)
                    p.vec.wait("tS", st["pv"][1 - par])
                    st["tevac"][1] = p.vec.tensor_copy(
                        attnT[par][:, ts(kt, 128)], pb[:])
            p.ten.wait("sS", st["tevac"][0])
            p.ten.wait("vS", st["tevac"][1])
            for kt in range(tt + 1):
                tv = p.ten.matmul(ps_o[:], attnT[par][:, ts(kt, 128)],
                                  v_g[:, kt * 128 * GH + hh * 128:
                                      kt * 128 * GH + hh * 128 + 128],
                                  start=(kt == 0), stop=(kt == tt),
                                  skip_group_check=True)
            st["pv"][par] = tv
            p.sca.wait("tS", tv)
            p.sca.wait("vS", st["onat"][par])
            sv = p.sca.activation(sq128[:], ps_o[:], AF.Square,
                                  accum_out=st2["ssq"][par][:])
            p.vec.wait("sS", sv)
            vv = p.vec.tensor_scalar(st2["t0"][par][:], st2["ssq"][par][:],
                                     2.0 ** -40 / 128, 1e-5 * 2.0 ** -40,
                                     ALU.mult, ALU.add)
            p.sca.wait("vS", vv)
            p.sca.activation(st2["s0"][par][:], st2["t0"][par][:], AF.Ln)
            sv = p.sca.activation(st2["f"][par][:], st2["s0"][par][:], AF.Exp,
                                  scale=-0.5)
            p.vec.wait("sS", sv)
            vv = p.vec.tensor_scalar(onat[par][:], ps_o[:],
                                     st2["f"][par][:, 0:1], 2.0 ** -20,
                                     ALU.mult, ALU.mult)
            st["onat"][par] = vv

        def ten_wo(it):
            tt, hh = iters[it]
            par = it % 2
            p.ten.wait("vS", st["onat"][par])
            p.ten.wait("sS", st["tevac"][0])
            tv = p.ten.transpose(pt_t[0][:], onat[par][:], identB[:])
            p.sca.wait("tS", tv)
            p.sca.wait("tS", wo_mm_last[0])
            sv = p.sca.copy(oT_all[:, (hh * 16 + tt) * 128:
                                   (hh * 16 + tt) * 128 + 128], pt_t[0][:])
            st["tevac"][0] = sv

        def wo_phase():
            ps_wo = ps_s1
            p.ten.wait("sS", p.val("sS"))
            vv = 0
            for tt in range(16):
                for half in range(2):
                    for hh in range(GH):
                        tv = p.ten.matmul(
                            ps_wo[:, ts(half, 512)],
                            oT_all[:, (hh * 16 + tt) * 128:
                                   (hh * 16 + tt) * 128 + 128],
                            wo_g[:, hh * 1024 + half * 512:
                                 hh * 1024 + (half + 1) * 512],
                            start=(hh == 0), stop=(hh == GH - 1),
                            skip_group_check=True)
                p.vec.wait("tS", tv)
                for half in range(2):
                    dst = acc[:, tt * 1024 + half * 512:
                              tt * 1024 + (half + 1) * 512]
                    if g == 0:
                        vv = p.vec.tensor_copy(dst, ps_wo[:, ts(half, 512)])
                    else:
                        vv = p.vec.tensor_tensor(dst, ps_wo[:, ts(half, 512)],
                                                 dst, ALU.add)
                p.ten.wait("vS", vv)
            wo_mm_last[0] = p.val("tS")

        st = {nm: [0, 0] for nm in
              ("smm0", "smm1", "negm0", "negm1", "exp0", "exp1", "comb",
               "trans", "tevac", "pv", "onat")}
        nit = len(iters)
        for it in range(nit):
            khf = kh_first(it)
            stage_S(it, 0, khf)
            vec_max(it, 0)
            sca_exp(it, 0, khf)
            stage_S(it, 1, khf)
            vec_max(it, 1)
            sca_exp(it, 1, khf)
            if khf == 1:
                stage_S(it, 0, 0)
                sca_exp(it, 0, 0)
                stage_S(it, 1, 0)
                sca_exp(it, 1, 0)
            if it >= 1:
                ten_trans_pv(it - 1)
                st["trans"][(it - 1) % 2] = st["pv"][(it - 1) % 2]
                ten_wo(it - 1)
            vec_combine(it)
        ten_trans_pv(nit - 1)
        st["trans"][(nit - 1) % 2] = st["pv"][(nit - 1) % 2]
        ten_wo(nit - 1)
        wo_phase()


def _l0_ffn(p, es, io, b, x, x1, xnT_h, attn_h):
    wg_sb = p.sb(es, "wg_sb", [128, 8 * 3072], BF16)
    wu_sb = p.sb(es, "wu_sb", [128, 8 * 3072], BF16)
    wd_sb = p.sb(es, "wd_sb", [128, 24 * 1024], BF16)
    xnt_c = p.sb(es, "xnt_c", [128, 8 * 512], BF16)
    h_full = p.sb(es, "h_full", [128, 24 * 512], BF16)
    ps_g2 = [p.ps(es, f"ps_g{i}", [128, 512], F32) for i in range(2)]
    ps_u2 = [p.ps(es, f"ps_u{i}", [128, 512], F32) for i in range(2)]
    ps_d = [p.ps(es, f"ps_d{i}", [128, 512], F32) for i in range(4)]
    g_sb2 = [p.sb(es, f"g_sb{i}", [128, 512], BF16) for i in range(2)]
    u_sb2 = [p.sb(es, f"u_sb{i}", [128, 512], BF16) for i in range(2)]
    at2 = [p.sb(es, f"at2_{i}", [128, 1024], F32) for i in range(2)]
    x2b = [p.sb(es, f"x2b_{i}", [128, 1024], F32) for i in range(2)]
    o_t = p.sb(es, "o_t", [128, 1024], F32)

    p.gps.dma(wg_sb.rearrange("p (a f) -> p a f", a=8),
              io["wgT"].rearrange("(a p) f -> p a f", p=128), "ldg")
    p.gps.dma(wu_sb.rearrange("p (a f) -> p a f", a=8),
              io["wuT"].rearrange("(a p) f -> p a f", p=128), "ldg")
    p.gps.dma(wd_sb.rearrange("p (a f) -> p a f", a=24),
              io["wdT"].rearrange("(a p) f -> p a f", p=128), "ldg")
    wv_ = p.val("ldg")
    p.ten.wait("ldg", wv_)
    xat_used = [0, 0]
    for tc in range(4):
        p.sync.wait("tS", p.val("tS"))
        p.vec.wait("tS", p.val("tS"))
        for tti in range(4):
            tt = 4 * tc + tti
            lvx = p.sync.dma(
                xnt_c.rearrange("p (a t) -> p a t", a=8)[:, :,
                                                         tti * 128:(tti + 1) * 128],
                xnT_h[b, tt].rearrange("a p t -> p a t"), "ldf")
        p.ten.wait("ldf", lvx)
        guev = [0, 0]
        hfv = [0, 0]
        for ht in range(24):
            sl = ht % 2
            p.ten.wait("sS", guev[sl])
            for dt in range(8):
                p.ten.matmul(ps_g2[sl][:],
                             wg_sb[:, dt * 3072 + ht * 128:
                                   dt * 3072 + ht * 128 + 128],
                             xnt_c[:, ts(dt, 512)],
                             start=(dt == 0), stop=(dt == 7),
                             skip_group_check=True)
            for dt in range(8):
                tv = p.ten.matmul(ps_u2[sl][:],
                                  wu_sb[:, dt * 3072 + ht * 128:
                                        dt * 3072 + ht * 128 + 128],
                                  xnt_c[:, ts(dt, 512)],
                                  start=(dt == 0), stop=(dt == 7),
                                  skip_group_check=True)
            p.sca.wait("tS", tv)
            p.sca.wait("vS", hfv[sl])
            p.sca.activation(g_sb2[sl][:], ps_g2[sl][:], AF.Silu,
                             scale=1.0 / 127.0)
            guev[sl] = p.sca.activation(u_sb2[sl][:], ps_u2[sl][:], AF.Copy,
                                        scale=1.0 / 16129.0)
            p.vec.wait("sS", guev[sl])
            hfv[sl] = p.vec.tensor_tensor(h_full[:, ts(ht, 512)], g_sb2[sl][:],
                                          u_sb2[sl][:], ALU.mult)
        p.ten.wait("vS", p.val("vS"))
        for st4 in range(4):
            sl = st4 % 2
            tt = 4 * tc + st4
            sem = "ldfA" if sl == 0 else "ldfB"
            p.sync.wait("vS", xat_used[sl])
            p.sync.dma(at2[sl][:], attn_h[b, ts(tt, 128), :], sem)
            lv2 = p.sync.dma(x2b[sl][:], x[b, ts(tt, 128), :], sem)
            p.vec.wait("st", p.val("st"))
            p.vec.wait(sem, lv2)
            for half in range(2):
                pd = ps_d[st4]
                for ht in range(24):
                    tv = p.ten.matmul(pd[:], h_full[:, ht * 512 + st4 * 128:
                                                    ht * 512 + st4 * 128 + 128],
                                      wd_sb[:, ht * 1024 + half * 512:
                                            ht * 1024 + (half + 1) * 512],
                                      start=(ht == 0), stop=(ht == 23))
                p.vec.wait("tS", tv)
                dst = o_t[:, half * 512:(half + 1) * 512]
                p.vec.tensor_tensor(dst, pd[:],
                                    x2b[sl][:, half * 512:(half + 1) * 512],
                                    ALU.add)
                vv = p.vec.tensor_tensor(dst,
                                         at2[sl][:, half * 512:(half + 1) * 512],
                                         dst, ALU.add)
                p.ten.wait("vS", vv)
            xat_used[sl] = vv
            p.sync.wait("vS", vv)
            p.sync.wait("st", p.val("st"))
            p.sync.dma(x1[b, ts(tt, 128), :], o_t[:], "st")



CH = 512
NCH = T // CH


def build_mamba(p, io):
    nc = p.nc
    x1 = io["x1"]
    xn1T_h = io["xn1T_h"]
    x_h = io["x_h"]
    sz_h = io["sz_h"]
    maccT = io["maccT"]
    es_p = p.es

    identB = p.sb(es_p, "identB", [128, 128], BF16)
    selrep = p.sb(es_p, "selrep", [128, 2048], BF16)
    selsum = p.sb(es_p, "selsum", [128, 2048], BF16)
    seln = p.sb(es_p, "seln", [16, 128], BF16)
    p.sync.dma(identB[:], io["identB"], "ld")
    p.sync.dma(selrep[:], io["selrep"], "ld")
    p.sync.dma(selsum[:], io["selsum"], "ld")
    p.sync.dma(seln[:], io["seln"], "ld")
    p.ten.wait("ld", p.val("ld"))

    for b in range(NB):
        with contextlib.ExitStack() as es_b:
            xn1T = p.sb(es_b, "xn1T", [128, 8 * T], BF16)
            with contextlib.ExitStack() as es_ln:
                emit_ln(p, es_ln, x1, b, xn1T, xnT_h=None, identB=identB)
            p.sync.wait("sS", p.val("sS"))
            for dt in range(8):
                p.sync.dma(xn1T_h[b, dt], xn1T[:, ts(dt, T)], "st")
            p.barrier()
            _mamba_inproj(p, io, b, xn1T)
            p.barrier()
        for dir_ in ("f", "b"):
            with contextlib.ExitStack() as es_d:
                _mamba_dir(p, es_d, io, b, dir_, selrep, selsum, seln)
                p.barrier()
    p.sync.wait("st", p.val("st"))


def _mamba_inproj(p, io, b, xn1T):
    es = contextlib.ExitStack()
    with es:
        wbuf = [p.sb(es, f"ipw{i}", [128, 8 * 128], BF16) for i in range(2)]
        ob = [p.sb(es, f"ipo{i}", [128, 2048], BF16) for i in range(2)]
        psl = [p.ps(es, f"ps_ip{i}", [128, 2048], F32) for i in range(2)]
        wsem = ["ipwA", "ipwB"]
        stsem = ["ipstA", "ipstB"]
        wv = [0, 0]
        sdone = [0, 0]
        stv = [0, 0]
        mmdone = [0, 0]
        for ft in range(32):
            sl = ft % 2
            ps = psl[sl]
            p.gps.wait("tS", mmdone[sl])
            wv[sl] = p.gps.dma(wbuf[sl].rearrange("p (a w) -> p a w", a=8),
                               io["inpT"].rearrange("(a p) f -> p a f", p=128)[
                                   :, :, ts(ft, 128)], wsem[sl])
            p.ten.wait(wsem[sl], wv[sl])
            p.ten.wait("sS", sdone[sl])
            for ch in range(4):
                for dt in range(8):
                    tv = p.ten.matmul(ps[:, ts(ch, 512)], wbuf[sl][:, ts(dt, 128)],
                                      xn1T[:, dt * T + ch * 512:
                                           dt * T + (ch + 1) * 512],
                                      start=(dt == 0), stop=(dt == 7))
            mmdone[sl] = tv
            p.sca.wait("tS", tv)
            p.sca.wait(stsem[sl], stv[sl])
            if ft < 16:
                sv = p.sca.copy(ob[sl][:], ps[:])
                p.sync.wait("sS", sv)
                stv[sl] = p.sync.dma(io["x_h"][b, ft], ob[sl][:], stsem[sl])
            else:
                sv = p.sca.activation(ob[sl][:], ps[:], AF.Silu)
                p.sync.wait("sS", sv)
                stv[sl] = p.sync.dma(io["sz_h"][b, ft - 16], ob[sl][:], stsem[sl])
            sdone[sl] = sv
        p.sync.wait(stsem[0], stv[0])
        p.sync.wait(stsem[1], stv[1])


def _mamba_dir(p, es, io, b, dir_, selrep, selsum, seln):
    sfx = dir_
    rev = (dir_ == "b")
    xprojT = p.sb(es, "xprojT", [128, 16 * 128], BF16)
    dtpT = p.sb(es, "dtpT", [64, 2048], BF16)
    convw = p.sb(es, "convw", [128, 16 * 4], F32)
    convb = p.sb(es, "convb", [128, 16], F32)
    dtb = p.sb(es, "dtb", [128, 16], F32)
    Dv = p.sb(es, "Dv", [128, 16], F32)
    alog = p.sb(es, "alog", [128, 256], F32)
    Acol = p.sb(es, "Acol", [128, 256], F32)
    outpT = p.sb(es, "outpT", [128, 16 * 1024], BF16)
    p.gps.dma(xprojT.rearrange("p (a w) -> p a w", a=16),
              io["xprojT_" + sfx].rearrange("(a p) f -> p a f", p=128), "ldg")
    p.gps.dma(outpT.rearrange("p (a f) -> p a f", a=16),
              io["outpT"].rearrange("(a p) f -> p a f", p=128), "ldg")
    p.gps.dma(dtpT[:], io["dtpT_" + sfx], "ldg")
    p.sync.dma(convw.rearrange("p (a t) -> p a t", a=16),
               io["convw_" + sfx], "ld")
    p.sync.dma(convb[:], io["convb_" + sfx], "ld")
    p.sync.dma(dtb[:], io["dtb_" + sfx], "ld")
    p.sync.dma(Dv[:], io["D_" + sfx], "ld")
    p.sync.dma(alog[:], io["alog_" + sfx], "ld")
    ldv, lgv = p.val("ld"), p.val("ldg")
    p.sca.wait("ld", ldv)
    sv = p.sca.activation(Acol[:], alog[:], AF.Exp)
    p.vec.wait("sS", sv)
    p.vec.tensor_scalar(Acol[:], Acol[:], -1.0, None, ALU.mult)

    state = p.sb(es, "state", [128, 256], F32)
    p.vec.memset(state[:], 0.0)
    halo = p.sb(es, "halo", [128, 16 * 4], BF16)
    p.vec.memset(halo[:], 0.0)

    W = p.sb(es, "Wc", [128, 16 * (CH + 3)], BF16)
    tmp_r = [p.sb(es, f"tmpr{i}", [128, CH + 3], BF16) for i in range(2)]
    u_ch = p.sb(es, "u_ch", [128, 16 * CH], BF16)
    delta_ch = p.sb(es, "delta_ch", [128, 16 * CH], BF16)
    du_ch = p.sb(es, "du_ch", [128, 16 * CH], BF16)
    sz_ch = p.sb(es, "sz_ch", [128, 16 * CH], BF16)
    yg = p.sb(es, "yg", [128, 16 * CH], BF16)
    xdbl_dt = p.sb(es, "xdbl_dt", [64, CH], BF16)
    B_sb = p.sb(es, "B_sb", [16, CH], BF16)
    C_sb = p.sb(es, "C_sb", [16, CH], BF16)
    B_rep = p.sb(es, "B_rep", [128, CH], BF16)
    C_rep = p.sb(es, "C_rep", [128, CH], BF16)
    ftmp = p.sb(es, "ftmp", [128, CH], F32)
    ftmp2 = p.sb(es, "ftmp2", [128, CH], F32)
    ftmp3 = p.sb(es, "ftmp3", [128, CH], F32)
    dA_sb = [p.sb(es, f"dA{i}", [128, CH], F32) for i in range(2)]
    bb_sb = [p.sb(es, f"bb{i}", [128, CH], F32) for i in range(2)]
    h_sb = [p.sb(es, f"hsb{i}", [128, CH], BF16) for i in range(2)]
    hC = [p.sb(es, f"hC{i}", [128, CH], BF16) for i in range(2)]
    mo_t = p.sb(es, "mo_t", [128, CH], F32)
    ps_a2 = [p.ps(es, f"ps_a{i}", [128, CH], F32) for i in range(2)]
    ps_b2 = [p.ps(es, f"ps_b{i}", [128, CH], F32) for i in range(2)]
    ps_a = ps_a2[0]
    ps_b = ps_b2[0]
    ps_y = p.ps(es, "ps_y", [128, CH], F32)
    ps_op = p.ps(es, "ps_op", [128, CH], F32)

    p.ten.wait("ldg", lgv)
    p.vec.wait("ld", ldv)

    for c in range(NCH):
        t0 = c * CH if not rev else T - CH * (c + 1)
        p.sync.wait("vS", p.val("vS"))
        if not rev:
            for dt in range(16):
                wsl = W[:, dt * (CH + 3):(dt + 1) * (CH + 3)]
                if c == 0:
                    p.vec.memset(wsl[:, 0:3], 0.0)
                else:
                    p.vec.tensor_copy(wsl[:, 0:3], halo[:, ts(dt, 4)][:, 0:3])
            vh = p.val("vS")
            p.sync.wait("vS", vh)
            lv = 0
            for dt in range(16):
                wsl = W[:, dt * (CH + 3):(dt + 1) * (CH + 3)]
                lv = p.sync.dma(wsl[:, 3:CH + 3],
                                io["x_h"][b, dt, :, t0:t0 + CH], "ldx")
            p.vec.wait("ldx", lv)
        else:
            sl_done = [0, 0]
            for dt in range(16):
                sl = dt % 2
                wsl = W[:, dt * (CH + 3):(dt + 1) * (CH + 3)]
                p.sync.wait("vS", sl_done[sl])
                if c == 0:
                    lv = p.sync.dma(tmp_r[sl][:, 0:CH],
                                    io["x_h"][b, dt, :, t0:t0 + CH],
                                    "ldxA" if sl == 0 else "ldxB")
                    p.vec.wait("ldxA" if sl == 0 else "ldxB", lv)
                    p.vec.memset(tmp_r[sl][:, CH:CH + 3], 0.0)
                else:
                    lv = p.sync.dma(tmp_r[sl][:, 0:CH + 3],
                                    io["x_h"][b, dt, :, t0:t0 + CH + 3],
                                    "ldxA" if sl == 0 else "ldxB")
                    p.vec.wait("ldxA" if sl == 0 else "ldxB", lv)
                sl_done[sl] = p.vec.tensor_copy(wsl[:], tmp_r[sl][:, ::-1])
        for dt in range(16):
            wsl = W[:, dt * (CH + 3):(dt + 1) * (CH + 3)]
            usl = u_ch[:, ts(dt, CH)]
            p.vec.tensor_scalar(ftmp[:], wsl[:, 3:CH + 3],
                                convw[:, dt * 4 + 3:dt * 4 + 4], None, ALU.mult)
            for tap in range(3):
                p.vec.scalar_tensor_tensor(
                    ftmp[:], wsl[:, 2 - tap:CH + 2 - tap],
                    convw[:, dt * 4 + 2 - tap:dt * 4 + 3 - tap], ftmp[:],
                    ALU.mult, ALU.add)
            p.vec.tensor_scalar(usl, ftmp[:], convb[:, dt:dt + 1], None, ALU.add)
            p.vec.tensor_copy(halo[:, ts(dt, 4)][:, 0:3], wsl[:, CH:CH + 3])
        p.ten.wait("vS", p.val("vS"))
        p.ten.wait("sS", p.val("sS"))
        for dt in range(16):
            tv = p.ten.matmul(ps_a[:], xprojT[:, ts(dt, 128)],
                              u_ch[:, ts(dt, CH)], start=(dt == 0),
                              stop=(dt == 15))
        p.sca.wait("tS", tv)
        sv = p.sca.copy(xdbl_dt[:], ps_a[64:128, :])
        sv = p.sca.copy(B_sb[:], ps_a[0:16, :])
        sv = p.sca.copy(C_sb[:], ps_a[32:48, :])
        p.ten.wait("sS", sv)
        tv = p.ten.matmul(ps_b[:], seln[:], B_sb[:], start=True, stop=True)
        p.vec.wait("tS", tv)
        vv = p.vec.tensor_copy(B_rep[:], ps_b[:])
        p.ten.wait("vS", vv)
        tv = p.ten.matmul(ps_b[:], seln[:], C_sb[:], start=True, stop=True)
        p.vec.wait("tS", tv)
        vv = p.vec.tensor_copy(C_rep[:], ps_b[:])
        p.ten.wait("vS", vv)
        for dt in range(16):
            tv = p.ten.matmul(ps_b[:], dtpT[:, ts(dt, 128)], xdbl_dt[:],
                              start=True, stop=True)
            p.vec.wait("tS", tv)
            vv = p.vec.tensor_scalar(ftmp[:], ps_b[:], dtb[:, dt:dt + 1], None,
                                     ALU.add)
            p.ten.wait("vS", vv)
            p.vec.tensor_scalar(ftmp[:], ftmp[:], 1e-5, 1.0, ALU.max, ALU.min)
            p.vec.tensor_scalar(ftmp[:], ftmp[:], dtb[:, dt:dt + 1], None,
                                ALU.add)
            p.sca.wait("vS", p.val("vS"))
            sv = p.sca.activation(ftmp2[:], ftmp[:], AF.Abs)
            sv = p.sca.activation(ftmp2[:], ftmp2[:], AF.Exp, scale=-1.0)
            sv = p.sca.activation(ftmp2[:], ftmp2[:], AF.Ln, bias=1.0)
            sv = p.sca.activation(ftmp3[:], ftmp[:], AF.Relu)
            p.vec.wait("sS", sv)
            dsl = delta_ch[:, ts(dt, CH)]
            usl = u_ch[:, ts(dt, CH)]
            p.vec.tensor_tensor(dsl, ftmp3[:], ftmp2[:], ALU.add)
            p.vec.tensor_scalar(usl, usl, -10.0, 10.0, ALU.max, ALU.min)
            p.vec.tensor_tensor(du_ch[:, ts(dt, CH)], dsl, usl, ALU.mult)
        if not rev:
            p.sync.wait("vS", p.val("vS"))
            for dt in range(16):
                lv = p.sync.dma(sz_ch[:, ts(dt, CH)],
                                io["sz_h"][b, dt, :, t0:t0 + CH], "ldsz")
            p.vec.wait("ldsz", lv)
        else:
            sl_done = [0, 0]
            p.sync.wait("vS", p.val("vS"))
            for dt in range(16):
                sl = dt % 2
                p.sync.wait("vS", sl_done[sl])
                sem = "ldszA" if sl == 0 else "ldszB"
                lv = p.sync.dma(tmp_r[sl][:, 0:CH],
                                io["sz_h"][b, dt, :, t0:t0 + CH], sem)
                p.vec.wait(sem, lv)
                sl_done[sl] = p.vec.tensor_copy(sz_ch[:, ts(dt, CH)],
                                                tmp_r[sl][:, CH - 1::-1])
        p.ten.wait("vS", p.val("vS"))
        expv = [0, 0]
        bbv = [0, 0]
        scanv = [0, 0]
        hCv = [0, 0]
        ymmv = [0, 0]
        replv = [0, 0]
        stcv = [0, 0]

        def emit_repl(j, i):
            s = i % 2
            kidx = j * 16 + i
            p.ten.wait("sS", expv[s])
            p.ten.matmul(ps_a2[s][:], selrep[:, ts(i, 128)],
                         delta_ch[:, ts(j, CH)], start=True, stop=True)
            p.ten.wait("vS", bbv[s])
            replv[s] = p.ten.matmul(ps_b2[s][:], selrep[:, ts(i, 128)],
                                    du_ch[:, ts(j, CH)], start=True, stop=True)
            p.sca.wait("tS", replv[s] - 1)
            p.sca.wait("vS", scanv[s])
            expv[s] = p.sca.activation(dA_sb[s][:], ps_a2[s][:], AF.Exp,
                                       scale=Acol[:, kidx:kidx + 1])

        def emit_rest(j, i):
            s = i % 2
            kidx = j * 16 + i
            p.vec.wait("tS", replv[s])
            bbv[s] = p.vec.tensor_tensor(bb_sb[s][:], ps_b2[s][:], B_rep[:],
                                         ALU.mult)
            p.vec.wait("sS", expv[s])
            p.vec.wait("gS", max(stcv[s], hCv[s]))
            scanv[s] = p.vec.tensor_tensor_scan(
                h_sb[s][:], dA_sb[s][:], bb_sb[s][:],
                state[:, kidx:kidx + 1], ALU.mult, ALU.add)
            p.gps.wait("vS", scanv[s])
            stcv[s] = p.gps.tensor_copy(state[:, kidx:kidx + 1],
                                        h_sb[s][:, CH - 1:CH])
            p.gps.wait("tS", ymmv[s])
            hCv[s] = p.gps.tensor_tensor(hC[s][:], h_sb[s][:], C_rep[:],
                                         ALU.mult)
            p.ten.wait("gS", hCv[s])
            ymmv[s] = p.ten.matmul(ps_y[:], selsum[:, ts(i, 128)], hC[s][:],
                                   start=(i == 0), stop=(i == 15),
                                   skip_group_check=True)

        for j in range(16):
            emit_repl(j, 0)
            for i in range(16):
                if i + 1 < 16:
                    emit_repl(j, i + 1)
                emit_rest(j, i)
            p.vec.wait("tS", ymmv[15 % 2])
            p.vec.scalar_tensor_tensor(ftmp[:], u_ch[:, ts(j, CH)],
                                       Dv[:, j:j + 1], ps_y[:], ALU.mult,
                                       ALU.add)
            vv = p.vec.tensor_tensor(yg[:, ts(j, CH)], ftmp[:],
                                     sz_ch[:, ts(j, CH)], ALU.mult)
            p.ten.wait("vS", vv)
        for ft in range(8):
            for j in range(16):
                tv = p.ten.matmul(ps_op[:], outpT[:, j * 1024 + ft * 128:
                                                  j * 1024 + ft * 128 + 128],
                                  yg[:, ts(j, CH)], start=(j == 0),
                                  stop=(j == 15))
            p.vec.wait("tS", tv)
            src_ps = ps_op[:, ::-1] if rev else ps_op[:]
            if dir_ == "f":
                vv = p.vec.tensor_copy(mo_t[:], src_ps)
            else:
                p.sync.wait("mrw", p.val("mrw"))
                lv = p.sync.dma(mo_t[:], io["maccT"][b, ft, :, t0:t0 + CH],
                                "mrw")
                p.vec.wait("mrw", lv)
                vv = p.vec.tensor_tensor(mo_t[:], src_ps, mo_t[:], ALU.add)
            p.ten.wait("vS", vv)
            p.sync.wait("vS", vv)
            sv2 = p.sync.dma(io["maccT"][b, ft, :, t0:t0 + CH], mo_t[:], "mrw")
            p.vec.wait("mrw", sv2)
    p.sync.wait("mrw", p.val("mrw"))



def build_l1ffn(p, io):
    nc = p.nc
    es_p = p.es
    identB = p.sb(es_p, "identB", [128, 128], BF16)
    identF = p.sb(es_p, "identF", [128, 128], F32)
    maskC = p.sb(es_p, "maskC", [128, 128], F32)
    maskA = p.sb(es_p, "maskA", [128, 128], F32)
    G64 = p.sb(es_p, "G64", [128, 128], BF16)
    P64 = p.sb(es_p, "P64", [128, 128], BF16)
    ones128 = p.sb(es_p, "ones128", [128, 128], BF16)
    tab = {}
    for nm in ("CSq8", "SSq8", "CSk", "SSk"):
        tab[nm] = p.sb(es_p, nm, [128, T], BF16)
        p.gps.dma(tab[nm][:], io[nm], "ldtab")
    for nm, t_ in (("identB", identB), ("identF", identF), ("maskC", maskC),
                   ("maskA", maskA), ("G64", G64), ("P64", P64),
                   ("ones128", ones128)):
        p.sync.dma(t_[:], io[nm], "ld")
    for l in range(2):
        for nm in ("qsc", "ksc", "lng", "lnb", "b1a", "b1g", "b2"):
            b1col(p, io, nm, l, 0, 1)
    for e in (p.vec, p.sca, p.ten):
        e.wait("ld", p.val("ld"))
        e.wait("ldtab", p.val("ldtab"))

    for b in range(NB):
        with contextlib.ExitStack() as es_b:
            x_br = p.sb(es_b, "x_br", [128, 8 * T], F32)
            xtmp = p.sb(es_b, "xtmp", [128, T], BF16)
            for dt in range(8):
                lv = p.sync.dma(xtmp[:], io["xn1T_h"][b, dt], "ldf")
                p.vec.wait("ldf", lv)
                vv = p.vec.tensor_copy(x_br[:, ts(dt, T)], xtmp[:])
                p.sync.wait("vS", vv)
            for l in range(2):
                _l1f_attn(p, io, b, l, x_br, identB, maskC, maskA, G64, P64,
                          tab)
                p.barrier()
                _l1f_ffn(p, io, b, l, x_br, ones128)
                p.barrier()
            _l1f_out(p, es_b, io, b, x_br, identF)
            p.barrier()
    p.sync.wait("st", p.val("st"))


def _shift_cast(p, x_br, hb, dt):
    if dt < 4:
        return p.vec.tensor_copy(hb[:, ts(dt, T)], x_br[:, ts(dt, T)])
    p.vec.memset(hb[:, dt * T:dt * T + 1], 0.0)
    return p.vec.tensor_copy(hb[:, dt * T + 1:(dt + 1) * T],
                             x_br[:, dt * T:(dt + 1) * T - 1])


def _l1f_attn(p, io, b, l, x_br, identB, maskC, maskA, G64, P64, tab):
    es = contextlib.ExitStack()
    with es:
        hb = p.sb(es, "hb", [128, 8 * T], BF16)
        o_sb = p.sb(es, "o_sb", [128, 4 * T], BF16)
        wq_h = p.sb(es, "wq_h", [128, 8 * 64], BF16)
        wk_h = p.sb(es, "wk_h", [128, 8 * 64], BF16)
        wv_h = p.sb(es, "wv_h", [128, 8 * 64], BF16)
        wo_sb = p.sb(es, "wo_sb", [128, 4 * 1024], BF16)
        qT = p.sb(es, "qT", [64, T], BF16)
        kT = p.sb(es, "kT", [64, T], BF16)
        qrot = p.sb(es, "qrot", [64, T], BF16)
        krot = p.sb(es, "krot", [64, T], BF16)
        v_sb = p.sb(es, "v_sb", [128, 16 * 64], BF16)
        sq_h = p.sb(es, "sq_h", [64, T], BF16)
        f32a = p.sb(es, "f32a", [64, 1024], F32)
        E_sb = [p.sb(es, f"E_sb{i}", [128, 640], BF16) for i in range(2)]
        attn2 = [p.sb(es, f"attn2_{i}", [128, 640], BF16) for i in range(2)]
        attnT = [p.sb(es, f"l1aT{i}", [128, 640], BF16) for i in range(2)]
        stq = {nm: [p.sb(es, f"l1_{nm}{i}", [128, 1], F32) for i in range(2)]
               for nm in ("m", "nm", "r", "ri")}
        psA = p.ps(es, "psA", [128, 1024], F32)
        psB = p.ps(es, "psB", [128, 1024], F32)
        ps_of = p.ps(es, "ps_of", [128, 128], F32)
        ps_o = ps_of[0:64, :]
        pt_t = [p.ps(es, f"l1pt{i}", [128, 128], BF16) for i in range(2)]

        p.vec.wait("tS", p.val("tS"))
        for dt in range(8):
            vv = _shift_cast(p, x_br, hb, dt)
        p.ten.wait("vS", vv)
        p.gps.dma(wo_sb.rearrange("p (a f) -> p a f", a=4),
                  io["wo1"][l].rearrange("(a p) f -> p a f", p=128), "ldg")
        for h in range(8):
            p.gps.wait("tS", p.val("tS"))
            wq3 = io["wqkv"][l].rearrange("(a p) f -> p a f", p=128)
            for wdst, off in ((wq_h, 0), (wk_h, 512), (wv_h, 1024)):
                p.gps.dma(wdst.rearrange("p (a w) -> p a w", a=8),
                          wq3[:, :, off + h * 64:off + h * 64 + 64], "ldg")
            lgv = p.val("ldg")
            p.ten.wait("ldg", lgv)
            p.ten.wait("sS", p.val("sS"))
            p.ten.wait("vS", p.val("vS"))
            psq = [psA[:, 0:512], psB[:, 0:512]]
            qkvev = [0, 0]
            ci = 0
            for (wsb, dst) in ((wq_h, qT), (wk_h, kT)):
                for ch in range(4):
                    sl = ci % 2
                    ci += 1
                    p.ten.wait("sS", qkvev[sl])
                    for dt in range(8):
                        tv = p.ten.matmul(psq[sl][0:64, :], wsb[:, ts(dt, 64)],
                                          hb[:, dt * T + ch * 512:
                                             dt * T + (ch + 1) * 512],
                                          start=(dt == 0), stop=(dt == 7))
                    p.sca.wait("tS", tv)
                    qkvev[sl] = p.sca.copy(dst[:, ts(ch, 512)], psq[sl][0:64, :])
            for tt in range(16):
                sl = ci % 2
                ci += 1
                p.ten.wait("sS", qkvev[sl])
                for dt in range(8):
                    tv = p.ten.matmul(psq[sl][:, 0:64],
                                      hb[:, dt * T + tt * 128:
                                         dt * T + tt * 128 + 128],
                                      wv_h[:, ts(dt, 64)],
                                      start=(dt == 0), stop=(dt == 7))
                p.sca.wait("tS", tv)
                qkvev[sl] = p.sca.copy(v_sb[:, ts(tt, 64)], psq[sl][:, 0:64])
            p.ten.wait("sS", p.val("sS"))
            for (srcb, dstb, scol, cs, ss) in (
                    (qT, qrot, "qsc", "CSq8", "SSq8"),
                    (kT, krot, "ksc", "CSk", "SSk")):
                p.sca.wait("tS", p.val("tS"))
                p.sca.wait("vS", p.val("vS"))
                sv = p.sca.activation(sq_h[:], srcb[:], AF.Square)
                p.ten.wait("sS", sv)
                for chh in range(2):
                    csl = slice(chh * 1024, (chh + 1) * 1024)
                    for c2 in range(2):
                        tv = p.ten.matmul(
                            psA[0:64, ts(c2, 512)], G64[0:64, 0:64],
                            sq_h[:, chh * 1024 + c2 * 512:
                                 chh * 1024 + (c2 + 1) * 512],
                            start=True, stop=True)
                    p.vec.wait("tS", tv)
                    vv = p.vec.tensor_scalar(f32a[:], psA[0:64, :], 1e-24, 3e38,
                                             ALU.max, ALU.min)
                    p.sca.wait("vS", vv)
                    p.sca.activation(f32a[:], f32a[:], AF.Ln)
                    sv = p.sca.activation(f32a[:], f32a[:], AF.Exp, scale=-0.5)
                    p.vec.wait("sS", sv)
                    vv = p.vec.scalar_tensor_tensor(
                        sq_h[:, csl], srcb[:, csl],
                        b1col(p, io, scol, l, 0, 64), f32a[:], ALU.mult,
                        ALU.mult)
                    p.ten.wait("vS", vv)
                    for c2 in range(2):
                        tv = p.ten.matmul(
                            psB[0:64, ts(c2, 512)], P64[0:64, 0:64],
                            sq_h[:, chh * 1024 + c2 * 512:
                                 chh * 1024 + (c2 + 1) * 512],
                            start=True, stop=True)
                    p.vec.wait("tS", tv)
                    p.vec.tensor_tensor(f32a[:], sq_h[:, csl],
                                        tab[cs][0:64, csl], ALU.mult)
                    p.vec.tensor_tensor(dstb[:, csl], psB[0:64, :],
                                        tab[ss][0:64, csl], ALU.mult)
                    vv = p.vec.tensor_tensor(dstb[:, csl], f32a[:],
                                             dstb[:, csl], ALU.add)
                    p.ten.wait("vS", vv)
            sA = {nm: [0, 0] for nm in
                  ("smm", "negm", "exp", "comb", "trans", "tevac", "pv", "oev")}
            pspar = [psA, psB]

            def l1_S(tt):
                par = tt % 2
                kt0 = max(0, tt - 4)
                span = (tt + 1 - kt0) * 128
                pss = pspar[par]
                p.ten.wait("sS", sA["exp"][par])
                for c in range(0, span, 512):
                    cn = min(512, span - c)
                    tv = p.ten.matmul(pss[:, c:c + cn], qrot[:, ts(tt, 128)],
                                      krot[:, kt0 * 128 + c:kt0 * 128 + c + cn],
                                      start=True, stop=True)
                sA["smm"][par] = tv

            def l1_softmax(tt):
                par = tt % 2
                kt0 = max(0, tt - 4)
                span = (tt + 1 - kt0) * 128
                pss = pspar[par]
                p.vec.wait("tS", sA["smm"][par])
                if tt >= 4:
                    p.vec.tensor_tensor(pss[:, 0:128], pss[:, 0:128], maskA[:],
                                        ALU.add)
                p.vec.tensor_tensor(pss[:, span - 128:span],
                                    pss[:, span - 128:span], maskC[:], ALU.add)
                p.vec.reduce_max(stq["m"][par][:], pss[:, 0:span], AX.X)
                vv = p.vec.tensor_scalar(stq["nm"][par][:], stq["m"][par][:],
                                         -1.0, None, ALU.mult)
                p.sca.wait("vS", vv)
                p.sca.wait("vS", sA["comb"][par])
                sv = p.sca.activation(E_sb[par][:, 0:span], pss[:, 0:span],
                                      AF.Exp, bias=stq["nm"][par][:, 0:1],
                                      accum_out=stq["r"][par][:])
                sA["exp"][par] = sv

            def l1_comb(tt):
                par = tt % 2
                kt0 = max(0, tt - 4)
                span = (tt + 1 - kt0) * 128
                p.vec.wait("sS", sA["exp"][par])
                p.vec.reciprocal(stq["ri"][par][:], stq["r"][par][:])
                p.vec.wait("tS", sA["trans"][par])
                sA["comb"][par] = p.vec.tensor_scalar(
                    attn2[par][:, 0:span], E_sb[par][:, 0:span],
                    stq["ri"][par][:, 0:1], None, ALU.mult)

            def l1_pv(tt):
                par = tt % 2
                kt0 = max(0, tt - 4)
                nkt = tt + 1 - kt0
                p.ten.wait("vS", sA["comb"][par])
                for kt in range(nkt):
                    pb = pt_t[kt % 2]
                    if kt % 2 == 0:
                        p.ten.wait("sS", sA["tevac"][0])
                    else:
                        p.ten.wait("vS", sA["tevac"][1])
                    tv = p.ten.transpose(pb[:], attn2[par][:, ts(kt, 128)],
                                         identB[:])
                    if kt % 2 == 0:
                        p.sca.wait("tS", tv)
                        p.sca.wait("tS", sA["pv"][1 - par])
                        sA["tevac"][0] = p.sca.copy(attnT[par][:, ts(kt, 128)],
                                                    pb[:])
                    else:
                        p.vec.wait("tS", tv)
                        p.vec.wait("tS", sA["pv"][1 - par])
                        sA["tevac"][1] = p.vec.tensor_copy(
                            attnT[par][:, ts(kt, 128)], pb[:])
                p.ten.wait("sS", sA["tevac"][0])
                p.ten.wait("vS", sA["tevac"][1])
                p.ten.wait("sS", sA["oev"][1 - par])
                for kt in range(nkt):
                    tv = p.ten.matmul(ps_o[:],
                                      v_sb[:, (kt0 + kt) * 64:
                                           (kt0 + kt + 1) * 64],
                                      attnT[par][:, ts(kt, 128)],
                                      start=(kt == 0), stop=(kt == nkt - 1),
                                      skip_group_check=True)
                sA["pv"][par] = tv
                p.sca.wait("tS", tv)
                sA["oev"][par] = p.sca.copy(
                    o_sb[(h % 2) * 64:(h % 2) * 64 + 64,
                         (h // 2) * T + tt * 128:(h // 2) * T + tt * 128 + 128],
                    ps_o[:])

            for tt in range(16):
                l1_S(tt)
                l1_softmax(tt)
                if tt >= 1:
                    l1_pv(tt - 1)
                    sA["trans"][(tt - 1) % 2] = sA["pv"][(tt - 1) % 2]
                l1_comb(tt)
            l1_pv(15)
        p.ten.wait("ldg", p.val("ldg"))
        p.ten.wait("sS", p.val("sS"))
        for ft in range(8):
            for ch in range(4):
                for et in range(4):
                    tv = p.ten.matmul(
                        psA[:, 0:512],
                        wo_sb[:, et * 1024 + ft * 128:et * 1024 + ft * 128 + 128],
                        o_sb[:, et * T + ch * 512:et * T + (ch + 1) * 512],
                        start=(et == 0), stop=(et == 3))
                p.vec.wait("tS", tv)
                dst = x_br[:, ft * T + ch * 512:ft * T + (ch + 1) * 512]
                vv = p.vec.tensor_tensor(dst, psA[:, 0:512], dst, ALU.add)
                p.ten.wait("vS", vv)


def _l1f_ffn(p, io, b, l, x_br, ones128):
    es = contextlib.ExitStack()
    HID = 2730
    PAIRS = [(j * 128, min(128, HID - j * 128)) for j in range((HID + 127) // 128)]
    with es:
        xn2 = p.sb(es, "xn2", [128, 8 * T], BF16)
        with contextlib.ExitStack() as es2:
            hb2 = p.sb(es2, "hb2", [128, 8 * T], BF16)
            hsq = p.sb(es2, "hsq", [128, 8 * T], BF16)
            m_bc = p.sb(es2, "m_bc", [128, 512], F32)
            r_bc = p.sb(es2, "r_bc", [128, 512], F32)
            fa = p.sb(es2, "fa", [128, 512], F32)
            ps_1 = p.ps(es2, "ps_1", [128, 512], F32)
            ps_2 = p.ps(es2, "ps_2", [128, 512], F32)
            p.vec.wait("tS", p.val("tS"))
            for dt in range(8):
                vv = _shift_cast(p, x_br, hb2, dt)
                p.sca.wait("vS", vv)
                sv = p.sca.activation(hsq[:, ts(dt, T)], hb2[:, ts(dt, T)],
                                      AF.Square)
            p.ten.wait("sS", p.val("sS"))
            p.ten.wait("vS", p.val("vS"))
            for ch in range(4):
                for dt in range(8):
                    p.ten.matmul(ps_1[:], ones128[:],
                                 hb2[:, dt * T + ch * 512:dt * T + (ch + 1) * 512],
                                 start=(dt == 0), stop=(dt == 7),
                                 skip_group_check=True)
                for dt in range(8):
                    tv = p.ten.matmul(ps_2[:], ones128[:],
                                      hsq[:, dt * T + ch * 512:
                                          dt * T + (ch + 1) * 512],
                                      start=(dt == 0), stop=(dt == 7),
                                      skip_group_check=True)
                p.vec.wait("tS", tv)
                p.vec.tensor_scalar(m_bc[:], ps_1[:], 1.0 / D, None, ALU.mult)
                p.vec.tensor_scalar(r_bc[:], ps_2[:], 1.0 / D, None, ALU.mult)
                p.vec.tensor_tensor(fa[:], m_bc[:], m_bc[:], ALU.mult)
                p.vec.tensor_tensor(fa[:], r_bc[:], fa[:], ALU.subtract)
                vv = p.vec.tensor_scalar(fa[:], fa[:], 1e-5, None, ALU.add)
                p.ten.wait("vS", vv)
                p.sca.wait("vS", vv)
                sv = p.sca.activation(fa[:], fa[:], AF.Sqrt)
                p.vec.wait("sS", sv)
                p.vec.reciprocal(r_bc[:], fa[:])
                for dt in range(8):
                    sl = slice(dt * T + ch * 512, dt * T + (ch + 1) * 512)
                    p.vec.tensor_tensor(fa[:], hb2[:, sl], m_bc[:], ALU.subtract)
                    p.vec.tensor_tensor(fa[:], fa[:], r_bc[:], ALU.mult)
                    vv = p.vec.tensor_scalar(
                        xn2[:, sl], fa[:], b1col(p, io, "lng", l, dt, 128),
                        b1col(p, io, "lnb", l, dt, 128), ALU.mult, ALU.add)
        p.barrier()
        with contextlib.ExitStack() as es3:
            hh_sb = p.sb(es3, "hh_sb", [128, len(PAIRS) * 512], BF16)
            w1a = [p.sb(es3, f"w1a{i}", [128, 8 * 128], BF16) for i in range(2)]
            w1g = [p.sb(es3, f"w1g{i}", [128, 8 * 128], BF16) for i in range(2)]
            w2_sb = p.sb(es3, "w2_sb", [128, len(PAIRS) * 1024], BF16)
            ha = p.sb(es3, "ha", [128, 512], BF16)
            hg = p.sb(es3, "hg", [128, 512], BF16)
            ps_a = p.ps(es3, "ps_a", [128, 512], F32)
            ps_g = p.ps(es3, "ps_g", [128, 512], F32)
            ps_w2 = p.ps(es3, "ps_w2", [128, 512], F32)
            for j, (off, sz) in enumerate(PAIRS):
                p.gps.dma(w2_sb[0:sz, ts(j, 1024)],
                          io["w2"][l, off:off + sz, :], "ldg")
            w1sem = ["w1s0", "w1s1"]
            w1lv = [0, 0]
            w1used = [0, 0]
            npair = len(PAIRS)

            w13 = io["w1"][l].rearrange("(a p) f -> p a f", p=128)

            def w1_prefetch(ch, jj):
                sl = jj % 2
                off, sz = PAIRS[jj]
                p.gps.wait("tS", w1used[sl])
                p.gps.dma(w1a[sl].rearrange("p (a w) -> p a w", a=8)[:, :, 0:sz],
                          w13[:, :, off:off + sz], w1sem[sl])
                w1lv[sl] = p.gps.dma(
                    w1g[sl].rearrange("p (a w) -> p a w", a=8)[:, :, 0:sz],
                    w13[:, :, HID + off:HID + off + sz], w1sem[sl])

            evacv = [0, 0]
            for ch in range(4):
                if ch == 0:
                    w1_prefetch(0, 0)
                for j, (off, sz) in enumerate(PAIRS):
                    sl = j % 2
                    nj = j + 1
                    p.ten.wait(w1sem[sl], w1lv[sl])
                    p.ten.wait("sS", p.val("sS"))
                    p.ten.wait("vS", p.val("vS"))
                    for dt in range(8):
                        p.ten.matmul(ps_a[0:sz, :],
                                     w1a[sl][:, ts(dt, 128)][:, 0:sz],
                                     xn2[:, dt * T + ch * 512:
                                         dt * T + (ch + 1) * 512],
                                     start=(dt == 0), stop=(dt == 7),
                                     skip_group_check=True)
                    for dt in range(8):
                        tv = p.ten.matmul(ps_g[0:sz, :],
                                          w1g[sl][:, ts(dt, 128)][:, 0:sz],
                                          xn2[:, dt * T + ch * 512:
                                              dt * T + (ch + 1) * 512],
                                          start=(dt == 0), stop=(dt == 7),
                                          skip_group_check=True)
                    w1used[sl] = tv
                    if nj < npair:
                        w1_prefetch(ch, nj)
                    elif ch < 3:
                        w1_prefetch(ch + 1, 0)
                    p.vec.wait("tS", tv)
                    vv = p.vec.tensor_scalar(ha[0:sz, :], ps_a[0:sz, :],
                                             b1col(p, io, "b1a", l, j, sz),
                                             None, ALU.add)
                    p.sca.wait("tS", tv)
                    sv = p.sca.activation(hg[0:sz, :], ps_g[0:sz, :], AF.Gelu,
                                          bias=b1col(p, io, "b1g", l, j, sz))
                    p.vec.wait("sS", sv)
                    vv = p.vec.tensor_tensor(hh_sb[0:sz, ts(j, 512)],
                                             ha[0:sz, :], hg[0:sz, :], ALU.mult)
                    p.ten.wait("vS", vv)
                p.ten.wait("ldg", p.val("ldg"))
                for ft in range(8):
                    for j, (off, sz) in enumerate(PAIRS):
                        tv = p.ten.matmul(
                            ps_w2[:], w2_sb[0:sz, j * 1024 + ft * 128:
                                            j * 1024 + ft * 128 + 128],
                            hh_sb[0:sz, ts(j, 512)],
                            start=(j == 0), stop=(j == len(PAIRS) - 1))
                    p.vec.wait("tS", tv)
                    dst = x_br[:, ft * T + ch * 512:ft * T + (ch + 1) * 512]
                    vv = p.vec.scalar_tensor_tensor(
                        dst, ps_w2[:], b1col(p, io, "b2", l, ft, 128), dst,
                        ALU.add, ALU.add)
                    p.ten.wait("vS", vv)


def b1col(p, io, name, l, j, sz):
    key = ("_b1", name, l)
    cache = p.__dict__.setdefault("_colcache", {})
    if key not in cache:
        nt = io[name].shape[2]
        t = p.sb(p.es, f"col_{name}{l}", [128, nt], F32)
        p.sync.dma(t[:], io[name][l], "ld")
        cache[key] = t
    return cache[key][0:sz, j:j + 1]


def _l1f_out(p, es_b, io, b, x_br, identF):
    with contextlib.ExitStack() as es:
        mt = p.sb(es, "mt", [128, T], F32)
        out_t = p.sb(es, "out_t", [128, D], F32)
        x1_t = p.sb(es, "x1_t", [128, D], F32)
        ps_f = [p.ps(es, f"opf{i}", [128, 128], F32) for i in range(2)]
        for dt in range(8):
            p.sync.wait("vS", p.val("vS"))
            lv = p.sync.dma(mt[:], io["maccT"][b, dt], "ldf")
            p.vec.wait("ldf", lv)
            vv = p.vec.tensor_tensor(x_br[:, ts(dt, T)], mt[:],
                                     x_br[:, ts(dt, T)], ALU.add)
        p.ten.wait("vS", p.val("vS"))
        for tt in range(16):
            p.sync.wait("st", p.val("st"))
            lv = p.sync.dma(x1_t[:], io["x1"][b, ts(tt, 128), :], "ldx1")
            p.ten.wait("sS", p.val("sS"))
            p.ten.wait("vS", p.val("vS"))
            p.ten.wait("st", p.val("st"))
            for dt in range(8):
                pb = ps_f[dt % 2]
                if dt >= 2:
                    p.ten.wait("sS", p.val("sS") - 1)
                tv = p.ten.transpose(pb[:], x_br[:, dt * T + tt * 128:
                                                 dt * T + tt * 128 + 128],
                                     identF[:])
                p.sca.wait("tS", tv)
                sv = p.sca.copy(out_t[:, ts(dt, 128)], pb[:])
            p.vec.wait("sS", p.val("sS"))
            p.vec.wait("ldx1", lv)
            vv = p.vec.tensor_tensor(out_t[:], x1_t[:], out_t[:], ALU.add)
            p.sync.wait("vS", vv)
            p.sync.dma(io["x2"][b, ts(tt, 128), :], out_t[:], "st")




_L0_NAMES = ["wqT", "wkT", "wvT", "woT", "lq1T", "lk1T", "lq2T", "lk2T", "rms",
             "wgT", "wuT", "wdT", "identB", "maskC", "ones64"]


def _declare(nc, io, cw):
    for nm, arr in cw.items():
        dt = {np.dtype(np.float32): F32, np.dtype(np.int32): I32,
              np.dtype(BF): BF16}[arr.dtype]
        io[nm] = nc.dram_tensor(nm, list(arr.shape), dt, kind="ExternalInput").ap()


def run_l0(pool, xfull):
    cw = {nm: pool[nm] for nm in _L0_NAMES}
    with contextlib.ExitStack() as es:
        nc = bass.Bass("TRN2", target_bir_lowering=False)
        io = {}
        io["x"] = nc.dram_tensor("x", [NB, T, D], F32, kind="ExternalInput").ap()
        io["x1"] = nc.dram_tensor("x1", [NB, T, D], F32,
                                  kind="ExternalOutput").ap()
        io["xnT_h"] = nc.dram_tensor("xnT_h", [NB, 16, 8, 128, 128], BF16,
                                     kind="Internal").ap()
        io["attn_h"] = nc.dram_tensor("attn_h", [NB, T, D], F32,
                                      kind="Internal").ap()
        _declare(nc, io, cw)
        p = P(nc, es)
        build_l0(p, io)
        p.finalize()
        in_maps = []
        for c in range(N_CORE):
            m = dict(cw)
            m["x"] = np.ascontiguousarray(xfull[2 * c:2 * c + 2])
            in_maps.append(m)
        res = run_bass_kernel_spmd(nc, in_maps, core_ids=list(range(N_CORE)))
    return np.concatenate([r["x1"] for r in res.results], axis=0)


_MB_NAMES = (["inpT", "outpT", "identB", "selrep", "selsum", "seln"] +
             [f"{nm}_{s}" for s in ("f", "b")
              for nm in ("xprojT", "dtpT", "convw", "convb", "dtb", "alog", "D")])


def run_mamba(pool, x1full):
    cw = {nm: pool[nm] for nm in _MB_NAMES}
    with contextlib.ExitStack() as es:
        nc = bass.Bass("TRN2", target_bir_lowering=False)
        io = {}
        io["x1"] = nc.dram_tensor("x1", [NB, T, D], F32,
                                  kind="ExternalInput").ap()
        io["xn1T_h"] = nc.dram_tensor("xn1T_h", [NB, 8, 128, T], BF16,
                                      kind="ExternalOutput").ap()
        io["maccT"] = nc.dram_tensor("maccT", [NB, 8, 128, T], F32,
                                     kind="ExternalOutput").ap()
        io["x_h"] = nc.dram_tensor("x_h", [NB, 16, 128, T], BF16,
                                   kind="Internal").ap()
        io["sz_h"] = nc.dram_tensor("sz_h", [NB, 16, 128, T], BF16,
                                    kind="Internal").ap()
        _declare(nc, io, cw)
        p = P(nc, es)
        build_mamba(p, io)
        p.finalize()
        in_maps = []
        for c in range(N_CORE):
            m = dict(cw)
            m["x1"] = np.ascontiguousarray(x1full[2 * c:2 * c + 2])
            in_maps.append(m)
        res = run_bass_kernel_spmd(nc, in_maps, core_ids=list(range(N_CORE)))
    xn1T = np.concatenate([r["xn1T_h"] for r in res.results], axis=0)
    macc = np.concatenate([r["maccT"] for r in res.results], axis=0)
    return xn1T, macc


_L1_NAMES = ["wqkv", "wo1", "w1", "w2", "b1a", "b1g", "b2", "lng", "lnb",
             "qsc", "ksc", "identB", "identF", "maskC", "maskA", "G64", "P64",
             "ones128", "CSq8", "SSq8", "CSk", "SSk"]


def run_l1ffn(pool, x1full, xn1Tfull, maccfull):
    cw = {nm: pool[nm] for nm in _L1_NAMES}
    with contextlib.ExitStack() as es:
        nc = bass.Bass("TRN2", target_bir_lowering=False)
        io = {}
        io["x1"] = nc.dram_tensor("x1", [NB, T, D], F32,
                                  kind="ExternalInput").ap()
        io["xn1T_h"] = nc.dram_tensor("xn1T_h", [NB, 8, 128, T], BF16,
                                      kind="ExternalInput").ap()
        io["maccT"] = nc.dram_tensor("maccT", [NB, 8, 128, T], F32,
                                     kind="ExternalInput").ap()
        io["x2"] = nc.dram_tensor("x2", [NB, T, D], F32,
                                  kind="ExternalOutput").ap()
        _declare(nc, io, cw)
        p = P(nc, es)
        build_l1ffn(p, io)
        p.finalize()
        in_maps = []
        for c in range(N_CORE):
            m = dict(cw)
            m["x1"] = np.ascontiguousarray(x1full[2 * c:2 * c + 2])
            m["xn1T_h"] = np.ascontiguousarray(xn1Tfull[2 * c:2 * c + 2])
            m["maccT"] = np.ascontiguousarray(maccfull[2 * c:2 * c + 2])
            in_maps.append(m)
        res = run_bass_kernel_spmd(nc, in_maps, core_ids=list(range(N_CORE)))
    return np.concatenate([r["x2"] for r in res.results], axis=0)


def run_fused(pool, xfull):
    names = sorted(set(_L0_NAMES + list(_MB_NAMES) + _L1_NAMES))
    cw = {nm: pool[nm] for nm in names}
    with contextlib.ExitStack() as es:
        nc = bass.Bass("TRN2", target_bir_lowering=False)
        io = {}
        io["x"] = nc.dram_tensor("x", [NB, T, D], F32, kind="ExternalInput").ap()
        io["x2"] = nc.dram_tensor("x2", [NB, T, D], F32,
                                  kind="ExternalOutput").ap()
        for nm, shape, dt_ in (
                ("x1", [NB, T, D], F32),
                ("xnT_h", [NB, 16, 8, 128, 128], BF16),
                ("attn_h", [NB, T, D], F32),
                ("xn1T_h", [NB, 8, 128, T], BF16),
                ("maccT", [NB, 8, 128, T], F32),
                ("x_h", [NB, 16, 128, T], BF16),
                ("sz_h", [NB, 16, 128, T], BF16)):
            io[nm] = nc.dram_tensor(nm, shape, dt_, kind="Internal").ap()
        _declare(nc, io, cw)
        p = P(nc, es)
        build_l0(p, io)
        p.barrier()
        build_mamba(p, io)
        p.barrier()
        build_l1ffn(p, io)
        p.finalize()
        in_maps = []
        for c in range(N_CORE):
            m = dict(cw)
            m["x"] = np.ascontiguousarray(xfull[2 * c:2 * c + 2])
            in_maps.append(m)
        res = run_bass_kernel_spmd(nc, in_maps, core_ids=list(range(N_CORE)))
    return np.concatenate([r["x2"] for r in res.results], axis=0)


def kernel(**inputs):
    consts = _consts()
    w = _prep_weights(inputs)
    pool = {**w, **consts}
    xfull = np.asarray(inputs["L0_attn_X"], dtype=np.float32)
    return run_fused(pool, xfull)


# version 12 (final)
# speedup vs baseline: 1.3196x; 1.0106x over previous
import contextlib
import os
import traceback
import numpy as np
import ml_dtypes
import concourse.bass as bass
import concourse.mybir as mybir
from concourse.bass_utils import run_bass_kernel_spmd

F32 = mybir.dt.float32
BF16 = mybir.dt.bfloat16
I32 = mybir.dt.int32
AF = mybir.ActivationFunctionType
ALU = mybir.AluOpType
AX = mybir.AxisListType
BF = ml_dtypes.bfloat16

NEG = -1.0e30
T = 2048
D = 1024
NB = 2
N_CORE = 8
ts = bass.ts


ENGS = ("sync", "gpsimd", "vector", "scalar", "tensor")

SELF_SEM = {"tensor": "tS", "vector": "vS", "scalar": "sS", "gpsimd": "gS"}
SELF_WAIT = {"vector", "scalar", "gpsimd"}


class Eng:
    def __init__(self, p, name):
        self.p = p
        self.name = name

    def __getattr__(self, op):
        p, name = self.p, self.name
        ssem = SELF_SEM.get(name)

        def f(*args, **kw):
            if name in SELF_WAIT:
                prior = p.cnt[ssem]
                self.wait(ssem, prior)
            v = p.inc(ssem)
            s = p.sems[ssem]
            if p.debug_lines:
                fr = traceback.extract_stack(limit=2)[0]
                lbl = f"{op}@{fr.lineno}"
                def thunk(e, lbl=lbl, op=op, args=args, kw=kw):
                    with p.nc.named_scope(lbl):
                        getattr(e, op)(*args, **kw).then_inc(s, 1)
                p.q[name].append(thunk)
            else:
                p.q[name].append(
                    lambda e: getattr(e, op)(*args, **kw).then_inc(s, 1))
            return v
        return f

    def dma(self, dst, src, sem):
        p, name = self.p, self.name
        s = p.sem(sem)
        v = p.inc(sem, 16)
        p.q[name].append(lambda e: e.dma_start(dst, src).then_inc(s, 16))
        return v

    def wait(self, sem, v):
        p, name = self.p, self.name
        if v <= p.waited[name].get(sem, 0):
            return
        p.waited[name][sem] = v
        s = p.sems[sem]
        p.q[name].append(lambda e: e.wait_ge(s, v))


class P:
    def __init__(self, nc, es):
        self.nc = nc
        self.es = es
        self.q = {k: [] for k in ENGS}
        self.sems = {}
        self.cnt = {}
        self.waited = {k: {} for k in ENGS}
        self.debug_lines = bool(os.environ.get("MK_DEBUG_LINES"))
        self.sync = Eng(self, "sync")
        self.gps = Eng(self, "gpsimd")
        self.vec = Eng(self, "vector")
        self.sca = Eng(self, "scalar")
        self.ten = Eng(self, "tensor")
        self.engmap = {"sync": self.sync, "gpsimd": self.gps, "vector": self.vec,
                       "scalar": self.sca, "tensor": self.ten}
        self.nbar = 0
        for nm in ("ld", "ldg", "st", "tS", "vS", "sS", "gS", "bar"):
            self.sem(nm)

    def sem(self, nm):
        if nm not in self.sems:
            self.sems[nm] = self.es.enter_context(self.nc.semaphore(nm))
            self.cnt[nm] = 0
        return self.sems[nm]

    def inc(self, sem, n=1):
        self.cnt[sem] += n
        return self.cnt[sem]

    def val(self, sem):
        return self.cnt[sem]

    def barrier(self):
        self.nbar += 1
        tgt = 5 * self.nbar
        s = self.sems["bar"]
        allsems = [k for k in self.sems if k != "bar"]
        for nm in ENGS:
            eng = self.engmap[nm]
            for sm in allsems:
                eng.wait(sm, self.cnt[sm])
            self.q[nm].append(lambda e, s=s: e.nop().then_inc(s, 1))
        for nm in ENGS:
            self.q[nm].append(lambda e, s=s, tgt=tgt: e.wait_ge(s, tgt))

    def sb(self, es, name, shape, dtype):
        self._uid = getattr(self, "_uid", 0) + 1
        return es.enter_context(
            self.nc.sbuf_tensor(f"{name}_u{self._uid}", shape, dtype))

    def ps(self, es, name, shape, dtype=F32):
        self._uid = getattr(self, "_uid", 0) + 1
        return es.enter_context(
            self.nc.psum_tensor(f"{name}_u{self._uid}", shape, dtype))

    def finalize(self):
        with self.nc.Block() as block:
            @block.sync
            def _(e):
                for f in self.q["sync"]:
                    f(e)

            @block.gpsimd
            def _(e):
                for f in self.q["gpsimd"]:
                    f(e)

            @block.vector
            def _(e):
                for f in self.q["vector"]:
                    f(e)

            @block.scalar
            def _(e):
                for f in self.q["scalar"]:
                    f(e)

            @block.tensor
            def _(e):
                for f in self.q["tensor"]:
                    f(e)



def _consts():
    c = {}
    c["identB"] = np.eye(128, dtype=np.float32).astype(BF)
    c["identF"] = np.eye(128, dtype=np.float32)
    i = np.arange(128)
    c["maskC"] = np.where(i[:, None] >= i[None, :], 0.0, NEG).astype(np.float32)
    c["maskA"] = np.where(i[:, None] <= i[None, :], 0.0, NEG).astype(np.float32)
    c["ones64"] = np.ones((64, 128), dtype=np.float32).astype(BF)
    c["ones128"] = np.ones((128, 128), dtype=np.float32).astype(BF)
    g = (i[:, None] // 64 == i[None, :] // 64).astype(np.float32)
    c["G64"] = g.astype(BF)
    p64 = np.zeros((128, 128), dtype=np.float32)
    for blk in range(2):
        o = blk * 64
        for m in range(32):
            p64[o + m + 32, o + m] = -1.0
        for m in range(32, 64):
            p64[o + m - 32, o + m] = 1.0
    c["P64"] = p64.astype(BF)
    selrep = np.zeros((128, 16 * 128), dtype=np.float32)
    selsum = np.zeros((128, 16 * 128), dtype=np.float32)
    for ii in range(16):
        for m in range(128):
            selrep[8 * ii + m // 16, ii * 128 + m] = 1.0
        for k in range(128):
            selsum[k, ii * 128 + 8 * ii + k // 16] = 1.0
    c["selrep"] = selrep.astype(BF)
    c["selsum"] = selsum.astype(BF)
    seln = np.zeros((16, 128), dtype=np.float32)
    for m in range(128):
        seln[m % 16, m] = 1.0
    c["seln"] = seln.astype(BF)
    DH, WIN = 64, 512
    half = np.arange(0, DH, 2, dtype=np.float64)
    inv_freq = 1.0 / (10000.0 ** (half / DH))
    t = np.arange(T, dtype=np.float64)
    freqs = t[:, None] * inv_freq[None, :]
    freqs = np.concatenate([freqs, freqs], axis=-1)
    base = (half + 0.4 * DH) / (1.4 * DH)
    power = (t - T // 2) / float(WIN // 2)
    scale = base[None, :] ** power[:, None]
    scale = np.concatenate([scale, scale], axis=-1)
    cos, sin = np.cos(freqs), np.sin(freqs)

    def fm(x):
        xt = np.ascontiguousarray(x.T.astype(np.float32))
        return np.concatenate([xt, xt], axis=0)
    c["CSq8"] = fm(cos * scale * 8.0)
    c["SSq8"] = fm(sin * scale * 8.0)
    c["CSk"] = fm(cos / scale)
    c["SSk"] = fm(sin / scale)
    return c


def _col_tiles(v, np_dtype=np.float32):
    n = v.shape[0]
    nt = (n + 127) // 128
    out = np.zeros((128, nt), dtype=np_dtype)
    for j in range(nt):
        seg = v[j * 128:(j + 1) * 128]
        out[:len(seg), j] = seg
    return np.ascontiguousarray(out)


def _prep_weights(inp):
    w = {}
    cT = lambda a: np.ascontiguousarray(np.asarray(a).T)
    w["wqT"] = cT(inp["L0_attn_W_q"])
    w["wkT"] = cT(inp["L0_attn_W_k"])
    w["wvT"] = cT(inp["L0_attn_W_v"])
    w["woT"] = cT(inp["L0_attn_W_o"])
    w["lq1T"] = cT(inp["L0_attn_lambda_q1"])
    w["lk1T"] = cT(inp["L0_attn_lambda_k1"])
    w["lq2T"] = cT(inp["L0_attn_lambda_q2"])
    w["lk2T"] = cT(inp["L0_attn_lambda_k2"])
    w["rms"] = np.ascontiguousarray(
        np.asarray(inp["L0_attn_rms_scale"], dtype=np.float32).reshape(128, 1))
    w["wgT"] = cT(inp["L0_ffn_w_gate"])
    w["wuT"] = cT(inp["L0_ffn_w_up"])
    w["wdT"] = cT(inp["L0_ffn_w_down"])
    w["inpT"] = cT(inp["L1_attn_in_proj_w"])
    w["outpT"] = cT(inp["L1_attn_out_proj_w"])
    for sfx, cw, cb, xp, dtw, dtb, alog, dv in (
        ("f", "L1_attn_conv1d_w", "L1_attn_conv1d_bias", "L1_attn_x_proj_w",
         "L1_attn_dt_proj_w", "L1_attn_dt_proj_bias", "L1_attn_A_log", "L1_attn_D"),
        ("b", "L1_attn_conv1d_b_w", "L1_attn_conv1d_b_bias", "L1_attn_x_proj_b_w",
         "L1_attn_dt_proj_b_w", "L1_attn_dt_proj_b_bias", "L1_attn_A_b_log",
         "L1_attn_D_b"),
    ):
        cwa = np.asarray(inp[cw]).reshape(2048, 4)
        w["convw_" + sfx] = np.ascontiguousarray(
            cwa.reshape(16, 128, 4).transpose(1, 0, 2))
        w["convb_" + sfx] = _col_tiles(np.asarray(inp[cb]))
        xpa = np.asarray(inp[xp])
        perm = np.zeros((128, 2048), dtype=np.float32)
        perm[0:16] = xpa[64:80]
        perm[32:48] = xpa[80:96]
        perm[64:128] = xpa[0:64]
        w["xprojT_" + sfx] = cT(perm)
        w["dtpT_" + sfx] = cT(np.asarray(inp[dtw]))
        w["dtb_" + sfx] = _col_tiles(np.asarray(inp[dtb]))
        al = np.asarray(inp[alog])
        ac = np.zeros((128, 256), dtype=np.float32)
        m = np.arange(128)
        for j in range(16):
            for i2 in range(16):
                ac[:, j * 16 + i2] = al[128 * j + 8 * i2 + m // 16, m % 16]
        w["alog_" + sfx] = ac
        w["D_" + sfx] = _col_tiles(np.asarray(inp[dv]))
    w["wqkv"] = np.ascontiguousarray(np.asarray(inp["L1_ffn_wqkv"]))
    w["wo1"] = np.ascontiguousarray(np.asarray(inp["L1_ffn_wo"]))
    w["w1"] = np.ascontiguousarray(np.asarray(inp["L1_ffn_w1"]))
    w["w2"] = np.ascontiguousarray(np.asarray(inp["L1_ffn_w2"]))
    b1 = np.asarray(inp["L1_ffn_b1"])
    w["b1a"] = np.stack([_col_tiles(b1[l, :2730]) for l in range(2)])
    w["b1g"] = np.stack([_col_tiles(b1[l, 2730:]) for l in range(2)])
    w["b2"] = np.stack([_col_tiles(np.asarray(inp["L1_ffn_b2"])[l])
                        for l in range(2)])
    w["lng"] = np.stack([_col_tiles(np.asarray(inp["L1_ffn_ln_g"])[l])
                         for l in range(2)])
    w["lnb"] = np.stack([_col_tiles(np.asarray(inp["L1_ffn_ln_b"])[l])
                         for l in range(2)])
    qs = np.asarray(inp["L1_ffn_q_scale"])
    ks = np.asarray(inp["L1_ffn_k_scale"])
    w["qsc"] = np.ascontiguousarray(
        np.concatenate([qs, qs], axis=1).reshape(2, 128, 1))
    w["ksc"] = np.ascontiguousarray(
        np.concatenate([ks, ks], axis=1).reshape(2, 128, 1))
    return w



def emit_ln(p, es, xsrc_dram, b, xnT, xnT_h=None, identB=None):
    xt = [p.sb(es, f"lnx{i}", [128, D], F32) for i in range(2)]
    sq_v = p.sb(es, "lnsqv", [128, D], BF16)
    sq_s = p.sb(es, "lnsqs", [128, D], BF16)
    st = {nm: p.sb(es, "lnst_" + nm, [128, 1], F32)
          for nm in ("sum", "ssq", "mean", "e2", "var", "std", "rstd")}
    xn_nat = p.sb(es, "lnxn", [128, D], BF16)
    pt_t = [p.ps(es, f"lnpt{i}", [128, 128], BF16) for i in range(2)]
    vdone = [0] * 16
    sdone = [0] * 16
    for tt in range(16):
        buf = xt[tt % 2]
        slot = "lnx" + ("A" if tt % 2 == 0 else "B")
        if tt >= 2:
            p.sync.wait("vS", vdone[tt - 2])
            p.sync.wait("sS", sdone[tt - 2])
        lv = p.sync.dma(buf[:], xsrc_dram[b, ts(tt, 128), :], slot)
        p.vec.wait(slot, lv)
        p.sca.wait(slot, lv)
        p.vec.tensor_scalar(sq_v[:], buf[:], 1.0, 0.0, ALU.mult, ALU.add,
                            accum_out=st["sum"][:])
        sv = p.sca.activation(sq_s[:], buf[:], AF.Square, accum_out=st["ssq"][:])
        sdone[tt] = sv
        p.vec.wait("sS", sv)
        p.vec.tensor_scalar(st["mean"][:], st["sum"][:], 1.0 / D, None, ALU.mult)
        p.vec.tensor_scalar(st["e2"][:], st["ssq"][:], 1.0 / D, None, ALU.mult)
        p.vec.tensor_tensor(st["var"][:], st["mean"][:], st["mean"][:], ALU.mult)
        p.vec.tensor_tensor(st["var"][:], st["e2"][:], st["var"][:], ALU.subtract)
        vv = p.vec.tensor_scalar(st["var"][:], st["var"][:], 1e-5, None, ALU.add)
        p.sca.wait("vS", vv)
        sv = p.sca.activation(st["std"][:], st["var"][:], AF.Sqrt)
        p.vec.wait("sS", sv)
        p.vec.reciprocal(st["rstd"][:], st["std"][:])
        p.vec.wait("tS", p.val("tS"))
        vv = p.vec.tensor_scalar(xn_nat[:], buf[:], st["mean"][:], st["rstd"][:],
                                 ALU.subtract, ALU.mult)
        vdone[tt] = vv
        p.ten.wait("vS", vv)
        for dt in range(8):
            pb = pt_t[dt % 2]
            if dt >= 2:
                p.ten.wait("sS", p.val("sS") - 1)
            tv = p.ten.transpose(pb[:], xn_nat[:, ts(dt, 128)], identB[:])
            p.sca.wait("tS", tv)
            sv = p.sca.copy(xnT[:, dt * T + tt * 128:dt * T + tt * 128 + 128],
                            pb[:])
        if xnT_h is not None:
            p.sync.wait("sS", p.val("sS"))
            src3 = xnT.rearrange("p (a t) -> p a t", a=8)[:, :,
                                                          tt * 128:(tt + 1) * 128]
            p.sync.dma(xnT_h[b, tt].rearrange("a p t -> p a t"), src3, "st")



def build_l0(p, io):
    nc = p.nc
    x, x1 = io["x"], io["x1"]
    xnT_h, attn_h = io["xnT_h"], io["attn_h"]
    es_p = p.es

    identB = p.sb(es_p, "identB", [128, 128], BF16)
    maskC = p.sb(es_p, "maskC", [128, 128], F32)
    rms02 = p.sb(es_p, "rms02", [128, 1], F32)
    lam_bc = p.sb(es_p, "lam_bc", [128, 16], F32)

    p.sync.dma(identB[:], io["identB"], "ld")
    p.sync.dma(maskC[:], io["maskC"], "ld")

    with contextlib.ExitStack() as es0:
        lv4 = [p.sb(es0, f"lv{i}", [64, 16], F32) for i in range(4)]
        ones64 = p.sb(es0, "ones64", [64, 128], BF16)
        prod = p.sb(es0, "prod", [64, 16], BF16)
        pl = p.ps(es0, "pl", [128, 16], F32)
        e1 = p.sb(es0, "e1", [128, 16], F32)
        e2 = p.sb(es0, "e2", [128, 16], F32)
        rmsr = p.sb(es0, "rmsr", [128, 1], F32)
        for i, nm in enumerate(("lq1T", "lk1T", "lq2T", "lk2T")):
            p.sync.dma(lv4[i][:], io[nm], "ld")
        p.sync.dma(rmsr[:], io["rms"], "ld")
        p.gps.dma(ones64[:], io["ones64"], "ldg")
        p.vec.wait("ld", p.val("ld"))
        p.vec.wait("ldg", p.val("ldg"))
        p.vec.tensor_scalar(rms02[:], rmsr[:], 0.2, None, ALU.mult)
        for k in range(2):
            a, bb = (0, 1) if k == 0 else (2, 3)
            vv = p.vec.tensor_tensor(prod[:], lv4[a][:], lv4[bb][:], ALU.mult)
            p.ten.wait("vS", vv)
            tv = p.ten.matmul(pl[:], ones64[:], prod[:], start=True, stop=True)
            p.sca.wait("tS", tv)
            sv = p.sca.activation((e1 if k == 0 else e2)[:], pl[:], AF.Exp)
            p.vec.wait("sS", sv)
        p.vec.tensor_tensor(lam_bc[:], e1[:], e2[:], ALU.subtract)
        p.vec.tensor_scalar(lam_bc[:], lam_bc[:], 0.8, None, ALU.add)
        p.barrier()

    for b in range(NB):
        with contextlib.ExitStack() as es_b:
            xnT = p.sb(es_b, "xnT", [128, 8 * T], BF16)
            acc = p.sb(es_b, "l0acc", [128, 16 * 1024], F32)
            with contextlib.ExitStack() as es_ln:
                emit_ln(p, es_ln, x, b, xnT, xnT_h=xnT_h, identB=identB)
                p.barrier()
            with contextlib.ExitStack() as es_a:
                _l0_attn(p, es_a, io, b, xnT, acc, identB, maskC, rms02, lam_bc)
            p.sync.wait("vS", p.val("vS"))
            p.sync.dma(attn_h[b].rearrange("(a p) f -> p a f", p=128),
                       acc.rearrange("p (a f) -> p a f", a=16), "st")
            p.sync.wait("st", p.val("st"))
            p.barrier()
        with contextlib.ExitStack() as es_f:
            _l0_ffn(p, es_f, io, b, x, x1, xnT_h, attn_h)
            p.barrier()
    p.sync.wait("st", p.val("st"))


def _l0_attn(p, es, io, b, xnT, acc, identB, maskC, rms02, lam_bc):
    GH = 2
    wq_g = p.sb(es, "wq_g", [128, 8 * 128 * GH], BF16)
    wk_g = p.sb(es, "wk_g", [128, 8 * 128 * GH], BF16)
    wv_g = p.sb(es, "wv_g", [128, 8 * 128 * GH], BF16)
    wo_g = p.sb(es, "wo_g", [128, GH * 1024], BF16)
    qT_g = p.sb(es, "qT_g", [128, GH * T], BF16)
    kT_g = p.sb(es, "kT_g", [128, GH * T], BF16)
    v_g = p.sb(es, "v_g", [128, 16 * 128 * GH], BF16)
    ps_s1 = p.ps(es, "ps_s1", [128, 1024], F32)
    ps_s2 = p.ps(es, "ps_s2", [128, 1024], F32)
    ps_o = p.ps(es, "ps_o", [128, 128], F32)

    pt_t = [p.ps(es, f"apt{i}", [128, 128], BF16) for i in range(3)]
    oT_all = p.sb(es, "oT_all", [128, GH * 16 * 128], BF16)
    wo_mm_last = [0]
    E1 = [p.sb(es, f"E1_{i}", [128, T], BF16) for i in range(2)]
    E2 = [p.sb(es, f"E2_{i}", [128, T], BF16) for i in range(2)]
    attn_sb = [p.sb(es, f"attn{i}", [128, T], BF16) for i in range(2)]
    attnT = [p.sb(es, f"attnT{i}", [128, T], BF16) for i in range(2)]
    atd = [0, 0]
    onat = [p.sb(es, f"onat{i}", [128, 128], BF16) for i in range(2)]
    sq128 = p.sb(es, "sq128", [128, 128], BF16)
    st2 = {}
    for nm in ("m", "nm", "rr", "rri"):
        st2[nm] = [p.sb(es, f"at_{nm}{i}", [128, 2], F32) for i in range(2)]
    st2["rp"] = [p.sb(es, f"at_rp{i}", [128, 4], F32) for i in range(2)]
    for nm in ("c2", "ssq", "t0", "s0", "f"):
        st2[nm] = [p.sb(es, f"at_{nm}{i}", [128, 1], F32) for i in range(2)]

    for g in range(16 // GH):
        p.gps.wait("tS", p.val("tS"))
        W = 128 * GH
        for wdst, wsrc in ((wq_g, "wqT"), (wk_g, "wkT"), (wv_g, "wvT")):
            p.gps.dma(wdst.rearrange("p (a w) -> p a w", a=8),
                      io[wsrc].rearrange("(a p) f -> p a f", p=128)[
                          :, :, g * W:(g + 1) * W], "ldg")
        p.gps.dma(wo_g.rearrange("p (a f) -> p a f", a=GH),
                  io["woT"].rearrange("(a p) f -> p a f", p=128)[
                      :, GH * g:GH * (g + 1), :], "ldg")
        lgv = p.val("ldg")
        p.vec.wait("ldg", lgv)
        p.vec.wait("tS", p.val("tS"))
        vv = 0
        for hh in range(GH):
            vv = p.vec.tensor_scalar(wo_g[:, ts(hh, 1024)],
                                     wo_g[:, ts(hh, 1024)], rms02[:, 0:1], None,
                                     ALU.mult)
        p.ten.wait("vS", vv)
        p.ten.wait("ldg", lgv)
        p.ten.wait("sS", p.val("sS"))
        psq = [ps_s1, ps_s2]
        qk_ev = [0, 0]
        ci = 0
        for hh in range(GH):
            for (wsb, dst) in ((wq_g, qT_g), (wk_g, kT_g)):
                for ch in range(4):
                    sl = ci % 2
                    ci += 1
                    p.ten.wait("sS", qk_ev[sl])
                    for dt in range(8):
                        tv = p.ten.matmul(
                            psq[sl][:, 0:512],
                            wsb[:, dt * 128 * GH + hh * 128:
                                dt * 128 * GH + hh * 128 + 128],
                            xnT[:, dt * T + ch * 512:dt * T + (ch + 1) * 512],
                            start=(dt == 0), stop=(dt == 7))
                    p.sca.wait("tS", tv)
                    qk_ev[sl] = p.sca.copy(
                        dst[:, hh * T + ch * 512:hh * T + (ch + 1) * 512],
                        psq[sl][:, 0:512])
        for tt in range(16):
            sl = ci % 2
            ci += 1
            p.ten.wait("sS", qk_ev[sl])
            for dt in range(8):
                tv = p.ten.matmul(psq[sl][:, 0:128 * GH],
                                  xnT[:, dt * T + tt * 128:dt * T + tt * 128 + 128],
                                  wv_g[:, ts(dt, 128 * GH)],
                                  start=(dt == 0), stop=(dt == 7))
            p.sca.wait("tS", tv)
            qk_ev[sl] = p.sca.copy(v_g[:, ts(tt, 128 * GH)],
                                   psq[sl][:, 0:128 * GH])
        p.ten.wait("sS", p.val("sS"))
        iters = [(tt, hh) for tt in range(16) for hh in range(GH)]

        def stage_S(it, mi, kh):
            tt, hh = iters[it]
            par = it % 2
            base = 64 * mi
            pss = ps_s1 if mi == 0 else ps_s2
            kspan = (tt + 1) * 128
            lo = kh * 1024
            hi = min(kspan, lo + 1024)
            p.ten.wait("sS", st["exp" + str(mi)][par] if kh_first(it) != kh
                       else st["exp" + str(mi)][1 - par])
            for c in range(lo, hi, 512):
                cn = min(512, hi - c)
                tv = p.ten.matmul(
                    pss[:, c - lo:c - lo + cn],
                    qT_g[base:base + 64,
                         hh * T + tt * 128:hh * T + tt * 128 + 128],
                    kT_g[base:base + 64, hh * T + c:hh * T + c + cn],
                    start=True, stop=True)
            st["smm" + str(mi)][par] = tv

        def kh_first(it):
            tt, hh = iters[it]
            return tt // 8

        def vec_max(it, mi):
            tt, hh = iters[it]
            par = it % 2
            pss = ps_s1 if mi == 0 else ps_s2
            kspan = (tt + 1) * 128
            dlo = kspan - 128 - kh_first(it) * 1024
            p.vec.wait("tS", st["smm" + str(mi)][par])
            p.vec.tensor_tensor(pss[:, dlo:dlo + 128],
                                pss[:, dlo:dlo + 128], maskC[:], ALU.add)
            p.vec.reduce_max(st2["m"][par][:, mi:mi + 1],
                             pss[:, dlo:dlo + 128], AX.X)
            st["negm" + str(mi)][par] = p.vec.tensor_scalar(
                st2["nm"][par][:, mi:mi + 1], st2["m"][par][:, mi:mi + 1],
                -0.125, None, ALU.mult)

        def sca_exp(it, mi, kh):
            tt, hh = iters[it]
            par = it % 2
            pss = ps_s1 if mi == 0 else ps_s2
            kspan = (tt + 1) * 128
            lo = kh * 1024
            hi = min(kspan, lo + 1024)
            Em = (E1 if mi == 0 else E2)[par]
            p.sca.wait("tS", st["smm" + str(mi)][par])
            p.sca.wait("vS", st["negm" + str(mi)][par])
            if kh == kh_first(it):
                p.sca.wait("vS", st["comb"][par])
            sv = p.sca.activation(Em[:, lo:hi], pss[:, 0:hi - lo], AF.Exp,
                                  bias=st2["nm"][par][:, mi:mi + 1], scale=0.125,
                                  accum_out=st2["rp"][par][:, 2 * mi + kh:
                                                           2 * mi + kh + 1])
            st["exp" + str(mi)][par] = sv

        def vec_combine(it):
            tt, hh = iters[it]
            par = it % 2
            hfull = hh + GH * g
            kspan = (tt + 1) * 128
            p.vec.wait("sS", st["exp1"][par])
            tt_ = iters[it][0]
            if tt_ // 8 == 0:
                p.vec.tensor_copy(st2["rr"][par][:],
                                  st2["rp"][par][:, 0:4:2])
            else:
                p.vec.tensor_tensor(st2["rr"][par][:],
                                    st2["rp"][par][:, 0:4:2],
                                    st2["rp"][par][:, 1:4:2], ALU.add)
            p.vec.reciprocal(st2["rri"][par][:], st2["rr"][par][:])
            p.vec.tensor_scalar(st2["c2"][par][:], st2["rri"][par][:, 1:2],
                                lam_bc[:, hfull:hfull + 1], -1.0, ALU.mult,
                                ALU.mult)
            p.vec.wait("tS", st["trans"][par])
            p.vec.tensor_scalar(attn_sb[par][:, 0:kspan], E1[par][:, 0:kspan],
                                st2["rri"][par][:, 0:1], None, ALU.mult)
            st["comb"][par] = p.vec.scalar_tensor_tensor(
                attn_sb[par][:, 0:kspan], E2[par][:, 0:kspan],
                st2["c2"][par][:, 0:1], attn_sb[par][:, 0:kspan], ALU.mult,
                ALU.add)

        def ten_trans_pv(it):
            tt, hh = iters[it]
            par = it % 2
            p.ten.wait("vS", st["comb"][par])
            p.ten.wait("vS", st["onat"][1 - par])
            for kt in range(tt + 1):
                pb = pt_t[kt % 3]
                sem3, val3 = st["tev3"][kt % 3]
                p.ten.wait(sem3, val3)
                tv = p.ten.transpose(pb[:], attn_sb[par][:, ts(kt, 128)],
                                     identB[:])
                if kt % 2 == 0:
                    p.sca.wait("tS", tv)
                    p.sca.wait("tS", st["pv"][1 - par])
                    st["tevac"][0] = p.sca.copy(attnT[par][:, ts(kt, 128)],
                                                pb[:])
                    st["tev3"][kt % 3] = ("sS", st["tevac"][0])
                else:
                    p.vec.wait("tS", tv)
                    p.vec.wait("tS", st["pv"][1 - par])
                    st["tevac"][1] = p.vec.tensor_copy(
                        attnT[par][:, ts(kt, 128)], pb[:])
                    st["tev3"][kt % 3] = ("vS", st["tevac"][1])
            p.ten.wait("sS", st["tevac"][0])
            p.ten.wait("vS", st["tevac"][1])
            for kt in range(tt + 1):
                tv = p.ten.matmul(ps_o[:], attnT[par][:, ts(kt, 128)],
                                  v_g[:, kt * 128 * GH + hh * 128:
                                      kt * 128 * GH + hh * 128 + 128],
                                  start=(kt == 0), stop=(kt == tt),
                                  skip_group_check=True)
            st["pv"][par] = tv
            p.sca.wait("tS", tv)
            p.sca.wait("vS", st["onat"][par])
            sv = p.sca.activation(sq128[:], ps_o[:], AF.Square,
                                  accum_out=st2["ssq"][par][:])
            p.vec.wait("sS", sv)
            vv = p.vec.tensor_scalar(st2["t0"][par][:], st2["ssq"][par][:],
                                     2.0 ** -40 / 128, 1e-5 * 2.0 ** -40,
                                     ALU.mult, ALU.add)
            p.sca.wait("vS", vv)
            p.sca.activation(st2["s0"][par][:], st2["t0"][par][:], AF.Ln)
            sv = p.sca.activation(st2["f"][par][:], st2["s0"][par][:], AF.Exp,
                                  scale=-0.5)
            p.vec.wait("sS", sv)
            vv = p.vec.tensor_scalar(onat[par][:], ps_o[:],
                                     st2["f"][par][:, 0:1], 2.0 ** -20,
                                     ALU.mult, ALU.mult)
            st["onat"][par] = vv

        def ten_wo(it):
            tt, hh = iters[it]
            par = it % 2
            p.ten.wait("vS", st["onat"][par])
            sem3, val3 = st["tev3"][0]
            p.ten.wait(sem3, val3)
            tv = p.ten.transpose(pt_t[0][:], onat[par][:], identB[:])
            p.sca.wait("tS", tv)
            p.sca.wait("tS", wo_mm_last[0])
            sv = p.sca.copy(oT_all[:, (hh * 16 + tt) * 128:
                                   (hh * 16 + tt) * 128 + 128], pt_t[0][:])
            st["tevac"][0] = sv
            st["tev3"][0] = ("sS", sv)

        def wo_phase():
            ps_wo = ps_s1
            p.ten.wait("sS", p.val("sS"))
            vv = 0
            for tt in range(16):
                for half in range(2):
                    for hh in range(GH):
                        tv = p.ten.matmul(
                            ps_wo[:, ts(half, 512)],
                            oT_all[:, (hh * 16 + tt) * 128:
                                   (hh * 16 + tt) * 128 + 128],
                            wo_g[:, hh * 1024 + half * 512:
                                 hh * 1024 + (half + 1) * 512],
                            start=(hh == 0), stop=(hh == GH - 1),
                            skip_group_check=True)
                p.vec.wait("tS", tv)
                for half in range(2):
                    dst = acc[:, tt * 1024 + half * 512:
                              tt * 1024 + (half + 1) * 512]
                    if g == 0:
                        vv = p.vec.tensor_copy(dst, ps_wo[:, ts(half, 512)])
                    else:
                        vv = p.vec.tensor_tensor(dst, ps_wo[:, ts(half, 512)],
                                                 dst, ALU.add)
                p.ten.wait("vS", vv)
            wo_mm_last[0] = p.val("tS")

        st = {nm: [0, 0] for nm in
              ("smm0", "smm1", "negm0", "negm1", "exp0", "exp1", "comb",
               "trans", "tevac", "pv", "onat")}
        st["tev3"] = [("sS", 0), ("sS", 0), ("sS", 0)]
        nit = len(iters)
        for it in range(nit):
            khf = kh_first(it)
            stage_S(it, 0, khf)
            vec_max(it, 0)
            sca_exp(it, 0, khf)
            stage_S(it, 1, khf)
            vec_max(it, 1)
            sca_exp(it, 1, khf)
            if khf == 1:
                stage_S(it, 0, 0)
                sca_exp(it, 0, 0)
                stage_S(it, 1, 0)
                sca_exp(it, 1, 0)
            if it >= 1:
                ten_trans_pv(it - 1)
                st["trans"][(it - 1) % 2] = st["pv"][(it - 1) % 2]
                ten_wo(it - 1)
            vec_combine(it)
        ten_trans_pv(nit - 1)
        st["trans"][(nit - 1) % 2] = st["pv"][(nit - 1) % 2]
        ten_wo(nit - 1)
        wo_phase()


def _l0_ffn(p, es, io, b, x, x1, xnT_h, attn_h):
    wg_sb = p.sb(es, "wg_sb", [128, 8 * 3072], BF16)
    wu_sb = p.sb(es, "wu_sb", [128, 8 * 3072], BF16)
    wd_sb = p.sb(es, "wd_sb", [128, 24 * 1024], BF16)
    xnt_c = p.sb(es, "xnt_c", [128, 8 * 512], BF16)
    h_full = p.sb(es, "h_full", [128, 24 * 512], BF16)
    ps_g2 = [p.ps(es, f"ps_g{i}", [128, 512], F32) for i in range(2)]
    ps_u2 = [p.ps(es, f"ps_u{i}", [128, 512], F32) for i in range(2)]
    ps_d = [p.ps(es, f"ps_d{i}", [128, 512], F32) for i in range(4)]
    g_sb2 = [p.sb(es, f"g_sb{i}", [128, 512], BF16) for i in range(2)]
    u_sb2 = [p.sb(es, f"u_sb{i}", [128, 512], BF16) for i in range(2)]
    at2 = [p.sb(es, f"at2_{i}", [128, 1024], F32) for i in range(2)]
    x2b = [p.sb(es, f"x2b_{i}", [128, 1024], F32) for i in range(2)]
    o_t = p.sb(es, "o_t", [128, 1024], F32)

    p.gps.dma(wg_sb.rearrange("p (a f) -> p a f", a=8),
              io["wgT"].rearrange("(a p) f -> p a f", p=128), "ldg")
    p.gps.dma(wu_sb.rearrange("p (a f) -> p a f", a=8),
              io["wuT"].rearrange("(a p) f -> p a f", p=128), "ldg")
    p.gps.dma(wd_sb.rearrange("p (a f) -> p a f", a=24),
              io["wdT"].rearrange("(a p) f -> p a f", p=128), "ldg")
    wv_ = p.val("ldg")
    p.ten.wait("ldg", wv_)
    xat_used = [0, 0]
    for tc in range(4):
        p.sync.wait("tS", p.val("tS"))
        p.vec.wait("tS", p.val("tS"))
        for tti in range(4):
            tt = 4 * tc + tti
            lvx = p.sync.dma(
                xnt_c.rearrange("p (a t) -> p a t", a=8)[:, :,
                                                         tti * 128:(tti + 1) * 128],
                xnT_h[b, tt].rearrange("a p t -> p a t"), "ldf")
        p.ten.wait("ldf", lvx)
        guev = [0, 0]
        hfv = [0, 0]
        for ht in range(24):
            sl = ht % 2
            p.ten.wait("sS", guev[sl])
            for dt in range(8):
                p.ten.matmul(ps_g2[sl][:],
                             wg_sb[:, dt * 3072 + ht * 128:
                                   dt * 3072 + ht * 128 + 128],
                             xnt_c[:, ts(dt, 512)],
                             start=(dt == 0), stop=(dt == 7),
                             skip_group_check=True)
            for dt in range(8):
                tv = p.ten.matmul(ps_u2[sl][:],
                                  wu_sb[:, dt * 3072 + ht * 128:
                                        dt * 3072 + ht * 128 + 128],
                                  xnt_c[:, ts(dt, 512)],
                                  start=(dt == 0), stop=(dt == 7),
                                  skip_group_check=True)
            p.sca.wait("tS", tv)
            p.sca.wait("vS", hfv[sl])
            p.sca.activation(g_sb2[sl][:], ps_g2[sl][:], AF.Silu,
                             scale=1.0 / 127.0)
            guev[sl] = p.sca.activation(u_sb2[sl][:], ps_u2[sl][:], AF.Copy,
                                        scale=1.0 / 16129.0)
            p.vec.wait("sS", guev[sl])
            hfv[sl] = p.vec.tensor_tensor(h_full[:, ts(ht, 512)], g_sb2[sl][:],
                                          u_sb2[sl][:], ALU.mult)
        p.ten.wait("vS", p.val("vS"))
        for st4 in range(4):
            sl = st4 % 2
            tt = 4 * tc + st4
            sem = "ldfA" if sl == 0 else "ldfB"
            p.sync.wait("vS", xat_used[sl])
            p.sync.dma(at2[sl][:], attn_h[b, ts(tt, 128), :], sem)
            lv2 = p.sync.dma(x2b[sl][:], x[b, ts(tt, 128), :], sem)
            p.vec.wait("st", p.val("st"))
            p.vec.wait(sem, lv2)
            for half in range(2):
                pd = ps_d[st4]
                for ht in range(24):
                    tv = p.ten.matmul(pd[:], h_full[:, ht * 512 + st4 * 128:
                                                    ht * 512 + st4 * 128 + 128],
                                      wd_sb[:, ht * 1024 + half * 512:
                                            ht * 1024 + (half + 1) * 512],
                                      start=(ht == 0), stop=(ht == 23))
                p.vec.wait("tS", tv)
                dst = o_t[:, half * 512:(half + 1) * 512]
                p.vec.tensor_tensor(dst, pd[:],
                                    x2b[sl][:, half * 512:(half + 1) * 512],
                                    ALU.add)
                vv = p.vec.tensor_tensor(dst,
                                         at2[sl][:, half * 512:(half + 1) * 512],
                                         dst, ALU.add)
                p.ten.wait("vS", vv)
            xat_used[sl] = vv
            p.sync.wait("vS", vv)
            p.sync.wait("st", p.val("st"))
            p.sync.dma(x1[b, ts(tt, 128), :], o_t[:], "st")



CH = 512
NCH = T // CH


def build_mamba(p, io):
    nc = p.nc
    x1 = io["x1"]
    xn1T_h = io["xn1T_h"]
    x_h = io["x_h"]
    sz_h = io["sz_h"]
    maccT = io["maccT"]
    es_p = p.es

    identB = p.sb(es_p, "identB", [128, 128], BF16)
    selrep = p.sb(es_p, "selrep", [128, 2048], BF16)
    selsum = p.sb(es_p, "selsum", [128, 2048], BF16)
    seln = p.sb(es_p, "seln", [16, 128], BF16)
    p.sync.dma(identB[:], io["identB"], "ld")
    p.sync.dma(selrep[:], io["selrep"], "ld")
    p.sync.dma(selsum[:], io["selsum"], "ld")
    p.sync.dma(seln[:], io["seln"], "ld")
    p.ten.wait("ld", p.val("ld"))

    for b in range(NB):
        with contextlib.ExitStack() as es_b:
            xn1T = p.sb(es_b, "xn1T", [128, 8 * T], BF16)
            with contextlib.ExitStack() as es_ln:
                emit_ln(p, es_ln, x1, b, xn1T, xnT_h=None, identB=identB)
            p.sync.wait("sS", p.val("sS"))
            for dt in range(8):
                p.sync.dma(xn1T_h[b, dt], xn1T[:, ts(dt, T)], "st")
            p.barrier()
            _mamba_inproj(p, io, b, xn1T)
            p.barrier()
        for dir_ in ("f", "b"):
            with contextlib.ExitStack() as es_d:
                _mamba_dir(p, es_d, io, b, dir_, selrep, selsum, seln)
                p.barrier()
    p.sync.wait("st", p.val("st"))


def _mamba_inproj(p, io, b, xn1T):
    es = contextlib.ExitStack()
    with es:
        wbuf = [p.sb(es, f"ipw{i}", [128, 8 * 128], BF16) for i in range(2)]
        ob = [p.sb(es, f"ipo{i}", [128, 2048], BF16) for i in range(2)]
        psl = [p.ps(es, f"ps_ip{i}", [128, 2048], F32) for i in range(2)]
        wsem = ["ipwA", "ipwB"]
        stsem = ["ipstA", "ipstB"]
        wv = [0, 0]
        sdone = [0, 0]
        stv = [0, 0]
        mmdone = [0, 0]
        for ft in range(32):
            sl = ft % 2
            ps = psl[sl]
            p.gps.wait("tS", mmdone[sl])
            wv[sl] = p.gps.dma(wbuf[sl].rearrange("p (a w) -> p a w", a=8),
                               io["inpT"].rearrange("(a p) f -> p a f", p=128)[
                                   :, :, ts(ft, 128)], wsem[sl])
            p.ten.wait(wsem[sl], wv[sl])
            p.ten.wait("sS", sdone[sl])
            for ch in range(4):
                for dt in range(8):
                    tv = p.ten.matmul(ps[:, ts(ch, 512)], wbuf[sl][:, ts(dt, 128)],
                                      xn1T[:, dt * T + ch * 512:
                                           dt * T + (ch + 1) * 512],
                                      start=(dt == 0), stop=(dt == 7))
            mmdone[sl] = tv
            p.sca.wait("tS", tv)
            p.sca.wait(stsem[sl], stv[sl])
            if ft < 16:
                sv = p.sca.copy(ob[sl][:], ps[:])
                p.sync.wait("sS", sv)
                stv[sl] = p.sync.dma(io["x_h"][b, ft], ob[sl][:], stsem[sl])
            else:
                sv = p.sca.activation(ob[sl][:], ps[:], AF.Silu)
                p.sync.wait("sS", sv)
                stv[sl] = p.sync.dma(io["sz_h"][b, ft - 16], ob[sl][:], stsem[sl])
            sdone[sl] = sv
        p.sync.wait(stsem[0], stv[0])
        p.sync.wait(stsem[1], stv[1])


def _mamba_dir(p, es, io, b, dir_, selrep, selsum, seln):
    sfx = dir_
    rev = (dir_ == "b")
    xprojT = p.sb(es, "xprojT", [128, 16 * 128], BF16)
    dtpT = p.sb(es, "dtpT", [64, 2048], BF16)
    convw = p.sb(es, "convw", [128, 16 * 4], F32)
    convb = p.sb(es, "convb", [128, 16], F32)
    dtb = p.sb(es, "dtb", [128, 16], F32)
    Dv = p.sb(es, "Dv", [128, 16], F32)
    alog = p.sb(es, "alog", [128, 256], F32)
    Acol = p.sb(es, "Acol", [128, 256], F32)
    outpT = p.sb(es, "outpT", [128, 16 * 1024], BF16)
    p.gps.dma(xprojT.rearrange("p (a w) -> p a w", a=16),
              io["xprojT_" + sfx].rearrange("(a p) f -> p a f", p=128), "ldg")
    p.gps.dma(outpT.rearrange("p (a f) -> p a f", a=16),
              io["outpT"].rearrange("(a p) f -> p a f", p=128), "ldg")
    p.gps.dma(dtpT[:], io["dtpT_" + sfx], "ldg")
    p.sync.dma(convw.rearrange("p (a t) -> p a t", a=16),
               io["convw_" + sfx], "ld")
    p.sync.dma(convb[:], io["convb_" + sfx], "ld")
    p.sync.dma(dtb[:], io["dtb_" + sfx], "ld")
    p.sync.dma(Dv[:], io["D_" + sfx], "ld")
    p.sync.dma(alog[:], io["alog_" + sfx], "ld")
    ldv, lgv = p.val("ld"), p.val("ldg")
    p.sca.wait("ld", ldv)
    sv = p.sca.activation(Acol[:], alog[:], AF.Exp)
    p.vec.wait("sS", sv)
    p.vec.tensor_scalar(Acol[:], Acol[:], -1.0, None, ALU.mult)

    state = p.sb(es, "state", [128, 256], F32)
    p.vec.memset(state[:], 0.0)
    halo = p.sb(es, "halo", [128, 16 * 4], BF16)
    p.vec.memset(halo[:], 0.0)

    W = p.sb(es, "Wc", [128, 16 * (CH + 3)], BF16)
    tmp_r = [p.sb(es, f"tmpr{i}", [128, CH + 3], BF16) for i in range(2)]
    u_ch = p.sb(es, "u_ch", [128, 16 * CH], BF16)
    delta_ch = p.sb(es, "delta_ch", [128, 16 * CH], BF16)
    du_ch = p.sb(es, "du_ch", [128, 16 * CH], BF16)
    sz_ch = p.sb(es, "sz_ch", [128, 16 * CH], BF16)
    yg = p.sb(es, "yg", [128, 16 * CH], BF16)
    xdbl_dt = p.sb(es, "xdbl_dt", [64, CH], BF16)
    B_sb = p.sb(es, "B_sb", [16, CH], BF16)
    C_sb = p.sb(es, "C_sb", [16, CH], BF16)
    B_rep = p.sb(es, "B_rep", [128, CH], BF16)
    C_rep = p.sb(es, "C_rep", [128, CH], BF16)
    ftmp = p.sb(es, "ftmp", [128, CH], F32)
    ftmp2 = p.sb(es, "ftmp2", [128, CH], F32)
    ftmp3 = p.sb(es, "ftmp3", [128, CH], F32)
    dA_sb = [p.sb(es, f"dA{i}", [128, CH], F32) for i in range(2)]
    bb_sb = [p.sb(es, f"bb{i}", [128, CH], F32) for i in range(2)]
    h_sb = [p.sb(es, f"hsb{i}", [128, CH], BF16) for i in range(2)]
    hC = [p.sb(es, f"hC{i}", [128, CH], BF16) for i in range(2)]
    mo_t = p.sb(es, "mo_t", [128, CH], F32)
    ps_a2 = [p.ps(es, f"ps_a{i}", [128, CH], F32) for i in range(2)]
    ps_b2 = [p.ps(es, f"ps_b{i}", [128, CH], F32) for i in range(2)]
    ps_a = ps_a2[0]
    ps_b = ps_b2[0]
    ps_y = p.ps(es, "ps_y", [128, CH], F32)
    ps_op = p.ps(es, "ps_op", [128, CH], F32)

    p.ten.wait("ldg", lgv)
    p.vec.wait("ld", ldv)

    for c in range(NCH):
        t0 = c * CH if not rev else T - CH * (c + 1)
        p.sync.wait("vS", p.val("vS"))
        if not rev:
            for dt in range(16):
                wsl = W[:, dt * (CH + 3):(dt + 1) * (CH + 3)]
                if c == 0:
                    p.vec.memset(wsl[:, 0:3], 0.0)
                else:
                    p.vec.tensor_copy(wsl[:, 0:3], halo[:, ts(dt, 4)][:, 0:3])
            vh = p.val("vS")
            p.sync.wait("vS", vh)
            lv = 0
            for dt in range(16):
                wsl = W[:, dt * (CH + 3):(dt + 1) * (CH + 3)]
                lv = p.sync.dma(wsl[:, 3:CH + 3],
                                io["x_h"][b, dt, :, t0:t0 + CH], "ldx")
            p.vec.wait("ldx", lv)
        else:
            sl_done = [0, 0]
            for dt in range(16):
                sl = dt % 2
                wsl = W[:, dt * (CH + 3):(dt + 1) * (CH + 3)]
                p.sync.wait("vS", sl_done[sl])
                if c == 0:
                    lv = p.sync.dma(tmp_r[sl][:, 0:CH],
                                    io["x_h"][b, dt, :, t0:t0 + CH],
                                    "ldxA" if sl == 0 else "ldxB")
                    p.vec.wait("ldxA" if sl == 0 else "ldxB", lv)
                    p.vec.memset(tmp_r[sl][:, CH:CH + 3], 0.0)
                else:
                    lv = p.sync.dma(tmp_r[sl][:, 0:CH + 3],
                                    io["x_h"][b, dt, :, t0:t0 + CH + 3],
                                    "ldxA" if sl == 0 else "ldxB")
                    p.vec.wait("ldxA" if sl == 0 else "ldxB", lv)
                sl_done[sl] = p.vec.tensor_copy(wsl[:], tmp_r[sl][:, ::-1])
        for dt in range(16):
            wsl = W[:, dt * (CH + 3):(dt + 1) * (CH + 3)]
            usl = u_ch[:, ts(dt, CH)]
            p.vec.tensor_scalar(ftmp[:], wsl[:, 3:CH + 3],
                                convw[:, dt * 4 + 3:dt * 4 + 4], None, ALU.mult)
            for tap in range(3):
                p.vec.scalar_tensor_tensor(
                    ftmp[:], wsl[:, 2 - tap:CH + 2 - tap],
                    convw[:, dt * 4 + 2 - tap:dt * 4 + 3 - tap], ftmp[:],
                    ALU.mult, ALU.add)
            p.vec.tensor_scalar(usl, ftmp[:], convb[:, dt:dt + 1], None, ALU.add)
            p.vec.tensor_copy(halo[:, ts(dt, 4)][:, 0:3], wsl[:, CH:CH + 3])
        p.ten.wait("vS", p.val("vS"))
        p.ten.wait("sS", p.val("sS"))
        for dt in range(16):
            tv = p.ten.matmul(ps_a[:], xprojT[:, ts(dt, 128)],
                              u_ch[:, ts(dt, CH)], start=(dt == 0),
                              stop=(dt == 15))
        p.sca.wait("tS", tv)
        sv = p.sca.copy(xdbl_dt[:], ps_a[64:128, :])
        sv = p.sca.copy(B_sb[:], ps_a[0:16, :])
        sv = p.sca.copy(C_sb[:], ps_a[32:48, :])
        p.ten.wait("sS", sv)
        tv = p.ten.matmul(ps_b[:], seln[:], B_sb[:], start=True, stop=True)
        p.vec.wait("tS", tv)
        vv = p.vec.tensor_copy(B_rep[:], ps_b[:])
        p.ten.wait("vS", vv)
        tv = p.ten.matmul(ps_b[:], seln[:], C_sb[:], start=True, stop=True)
        p.vec.wait("tS", tv)
        vv = p.vec.tensor_copy(C_rep[:], ps_b[:])
        p.ten.wait("vS", vv)
        dtbv = [0, 0]
        for dt in range(17):
            if dt < 16:
                s_ = dt % 2
                p.ten.wait("vS", dtbv[s_])
                p.ten.matmul(ps_b2[s_][:], dtpT[:, ts(dt, 128)], xdbl_dt[:],
                             start=True, stop=True)
            if dt == 0:
                continue
            d0 = dt - 1
            s_ = d0 % 2
            p.vec.wait("tS", p.val("tS") - (1 if dt < 16 else 0))
            vv = p.vec.tensor_scalar(ftmp[:], ps_b2[s_][:], dtb[:, d0:d0 + 1],
                                     None, ALU.add)
            dtbv[s_] = vv
            p.vec.tensor_scalar(ftmp[:], ftmp[:], 1e-5, 1.0, ALU.max, ALU.min)
            p.vec.tensor_scalar(ftmp[:], ftmp[:], dtb[:, d0:d0 + 1], None,
                                ALU.add)
            p.sca.wait("vS", p.val("vS"))
            sv = p.sca.activation(ftmp2[:], ftmp[:], AF.Abs)
            sv = p.sca.activation(ftmp2[:], ftmp2[:], AF.Exp, scale=-1.0)
            sv = p.sca.activation(ftmp2[:], ftmp2[:], AF.Ln, bias=1.0)
            sv = p.sca.activation(ftmp3[:], ftmp[:], AF.Relu)
            p.vec.wait("sS", sv)
            dsl = delta_ch[:, ts(d0, CH)]
            usl = u_ch[:, ts(d0, CH)]
            p.vec.tensor_tensor(dsl, ftmp3[:], ftmp2[:], ALU.add)
            p.vec.tensor_scalar(usl, usl, -10.0, 10.0, ALU.max, ALU.min)
            p.vec.tensor_tensor(du_ch[:, ts(d0, CH)], dsl, usl, ALU.mult)
        if not rev:
            p.sync.wait("vS", p.val("vS"))
            for dt in range(16):
                lv = p.sync.dma(sz_ch[:, ts(dt, CH)],
                                io["sz_h"][b, dt, :, t0:t0 + CH], "ldsz")
            p.vec.wait("ldsz", lv)
        else:
            sl_done = [0, 0]
            p.sync.wait("vS", p.val("vS"))
            for dt in range(16):
                sl = dt % 2
                p.sync.wait("vS", sl_done[sl])
                sem = "ldszA" if sl == 0 else "ldszB"
                lv = p.sync.dma(tmp_r[sl][:, 0:CH],
                                io["sz_h"][b, dt, :, t0:t0 + CH], sem)
                p.vec.wait(sem, lv)
                sl_done[sl] = p.vec.tensor_copy(sz_ch[:, ts(dt, CH)],
                                                tmp_r[sl][:, CH - 1::-1])
        p.ten.wait("vS", p.val("vS"))
        expv = [0, 0]
        bbv = [0, 0]
        scanv = [0, 0]
        hCv = [0, 0]
        ymmv = [0, 0]
        replv = [0, 0]
        stcv = [0, 0]

        def emit_repl(j, i):
            s = i % 2
            kidx = j * 16 + i
            p.ten.wait("sS", expv[s])
            p.ten.matmul(ps_a2[s][:], selrep[:, ts(i, 128)],
                         delta_ch[:, ts(j, CH)], start=True, stop=True)
            p.ten.wait("vS", bbv[s])
            replv[s] = p.ten.matmul(ps_b2[s][:], selrep[:, ts(i, 128)],
                                    du_ch[:, ts(j, CH)], start=True, stop=True)
            p.sca.wait("tS", replv[s] - 1)
            p.sca.wait("vS", scanv[s])
            expv[s] = p.sca.activation(dA_sb[s][:], ps_a2[s][:], AF.Exp,
                                       scale=Acol[:, kidx:kidx + 1])

        def emit_rest(j, i):
            s = i % 2
            kidx = j * 16 + i
            p.vec.wait("tS", replv[s])
            bbv[s] = p.vec.tensor_tensor(bb_sb[s][:], ps_b2[s][:], B_rep[:],
                                         ALU.mult)
            p.vec.wait("sS", expv[s])
            p.vec.wait("gS", max(stcv[s], hCv[s]))
            scanv[s] = p.vec.tensor_tensor_scan(
                h_sb[s][:], dA_sb[s][:], bb_sb[s][:],
                state[:, kidx:kidx + 1], ALU.mult, ALU.add)
            p.gps.wait("vS", scanv[s])
            stcv[s] = p.gps.tensor_copy(state[:, kidx:kidx + 1],
                                        h_sb[s][:, CH - 1:CH])
            p.gps.wait("tS", ymmv[s])
            hCv[s] = p.gps.tensor_tensor(hC[s][:], h_sb[s][:], C_rep[:],
                                         ALU.mult)
            p.ten.wait("gS", hCv[s])
            ymmv[s] = p.ten.matmul(ps_y[:], selsum[:, ts(i, 128)], hC[s][:],
                                   start=(i == 0), stop=(i == 15),
                                   skip_group_check=True)

        for j in range(16):
            emit_repl(j, 0)
            for i in range(16):
                if i + 1 < 16:
                    emit_repl(j, i + 1)
                emit_rest(j, i)
            p.vec.wait("tS", ymmv[15 % 2])
            p.vec.scalar_tensor_tensor(ftmp[:], u_ch[:, ts(j, CH)],
                                       Dv[:, j:j + 1], ps_y[:], ALU.mult,
                                       ALU.add)
            vv = p.vec.tensor_tensor(yg[:, ts(j, CH)], ftmp[:],
                                     sz_ch[:, ts(j, CH)], ALU.mult)
            p.ten.wait("vS", vv)
        for ft in range(8):
            for j in range(16):
                tv = p.ten.matmul(ps_op[:], outpT[:, j * 1024 + ft * 128:
                                                  j * 1024 + ft * 128 + 128],
                                  yg[:, ts(j, CH)], start=(j == 0),
                                  stop=(j == 15))
            p.vec.wait("tS", tv)
            src_ps = ps_op[:, ::-1] if rev else ps_op[:]
            if dir_ == "f":
                vv = p.vec.tensor_copy(mo_t[:], src_ps)
            else:
                p.sync.wait("mrw", p.val("mrw"))
                lv = p.sync.dma(mo_t[:], io["maccT"][b, ft, :, t0:t0 + CH],
                                "mrw")
                p.vec.wait("mrw", lv)
                vv = p.vec.tensor_tensor(mo_t[:], src_ps, mo_t[:], ALU.add)
            p.ten.wait("vS", vv)
            p.sync.wait("vS", vv)
            sv2 = p.sync.dma(io["maccT"][b, ft, :, t0:t0 + CH], mo_t[:], "mrw")
            p.vec.wait("mrw", sv2)
    p.sync.wait("mrw", p.val("mrw"))



def build_l1ffn(p, io):
    nc = p.nc
    es_p = p.es
    identB = p.sb(es_p, "identB", [128, 128], BF16)
    identF = p.sb(es_p, "identF", [128, 128], F32)
    maskC = p.sb(es_p, "maskC", [128, 128], F32)
    maskA = p.sb(es_p, "maskA", [128, 128], F32)
    G64 = p.sb(es_p, "G64", [128, 128], BF16)
    P64 = p.sb(es_p, "P64", [128, 128], BF16)
    ones128 = p.sb(es_p, "ones128", [128, 128], BF16)
    tab = {}
    for nm in ("CSq8", "SSq8", "CSk", "SSk"):
        tab[nm] = p.sb(es_p, nm, [128, T], BF16)
        p.gps.dma(tab[nm][:], io[nm], "ldtab")
    for nm, t_ in (("identB", identB), ("identF", identF), ("maskC", maskC),
                   ("maskA", maskA), ("G64", G64), ("P64", P64),
                   ("ones128", ones128)):
        p.sync.dma(t_[:], io[nm], "ld")
    for l in range(2):
        for nm in ("qsc", "ksc", "lng", "lnb", "b1a", "b1g", "b2"):
            b1col(p, io, nm, l, 0, 1)
    for e in (p.vec, p.sca, p.ten):
        e.wait("ld", p.val("ld"))
        e.wait("ldtab", p.val("ldtab"))

    for b in range(NB):
        with contextlib.ExitStack() as es_b:
            x_br = p.sb(es_b, "x_br", [128, 8 * T], F32)
            xtmp = p.sb(es_b, "xtmp", [128, T], BF16)
            for dt in range(8):
                lv = p.sync.dma(xtmp[:], io["xn1T_h"][b, dt], "ldf")
                p.vec.wait("ldf", lv)
                vv = p.vec.tensor_copy(x_br[:, ts(dt, T)], xtmp[:])
                p.sync.wait("vS", vv)
            for l in range(2):
                _l1f_attn(p, io, b, l, x_br, identB, maskC, maskA, G64, P64,
                          tab)
                p.barrier()
                _l1f_ffn(p, io, b, l, x_br, ones128)
                p.barrier()
            _l1f_out(p, es_b, io, b, x_br, identF)
            p.barrier()
    p.sync.wait("st", p.val("st"))


def _shift_cast(p, x_br, hb, dt):
    if dt < 4:
        return p.vec.tensor_copy(hb[:, ts(dt, T)], x_br[:, ts(dt, T)])
    p.vec.memset(hb[:, dt * T:dt * T + 1], 0.0)
    return p.vec.tensor_copy(hb[:, dt * T + 1:(dt + 1) * T],
                             x_br[:, dt * T:(dt + 1) * T - 1])


def _l1f_attn(p, io, b, l, x_br, identB, maskC, maskA, G64, P64, tab):
    es = contextlib.ExitStack()
    with es:
        hb = p.sb(es, "hb", [128, 8 * T], BF16)
        o_sb = p.sb(es, "o_sb", [128, 4 * T], BF16)
        wq_h = p.sb(es, "wq_h", [128, 8 * 64], BF16)
        wk_h = p.sb(es, "wk_h", [128, 8 * 64], BF16)
        wv_h = p.sb(es, "wv_h", [128, 8 * 64], BF16)
        wo_sb = p.sb(es, "wo_sb", [128, 4 * 1024], BF16)
        qT = p.sb(es, "qT", [64, T], BF16)
        kT = p.sb(es, "kT", [64, T], BF16)
        qrot = p.sb(es, "qrot", [64, T], BF16)
        krot = p.sb(es, "krot", [64, T], BF16)
        v_sb = p.sb(es, "v_sb", [128, 16 * 64], BF16)
        sq_h = p.sb(es, "sq_h", [64, T], BF16)
        f32a = p.sb(es, "f32a", [64, 1024], F32)
        E_sb = [p.sb(es, f"E_sb{i}", [128, 640], BF16) for i in range(2)]
        attn2 = [p.sb(es, f"attn2_{i}", [128, 640], BF16) for i in range(2)]
        attnT = [p.sb(es, f"l1aT{i}", [128, 640], BF16) for i in range(2)]
        stq = {nm: [p.sb(es, f"l1_{nm}{i}", [128, 1], F32) for i in range(2)]
               for nm in ("m", "nm", "r", "ri")}
        psA = p.ps(es, "psA", [128, 1024], F32)
        psB = p.ps(es, "psB", [128, 1024], F32)
        ps_of = p.ps(es, "ps_of", [128, 128], F32)
        ps_o = ps_of[0:64, :]
        pt_t = [p.ps(es, f"l1pt{i}", [128, 128], BF16) for i in range(2)]

        p.vec.wait("tS", p.val("tS"))
        for dt in range(8):
            vv = _shift_cast(p, x_br, hb, dt)
        p.ten.wait("vS", vv)
        p.gps.dma(wo_sb.rearrange("p (a f) -> p a f", a=4),
                  io["wo1"][l].rearrange("(a p) f -> p a f", p=128), "ldg")
        for h in range(8):
            p.gps.wait("tS", p.val("tS"))
            wq3 = io["wqkv"][l].rearrange("(a p) f -> p a f", p=128)
            for wdst, off in ((wq_h, 0), (wk_h, 512), (wv_h, 1024)):
                p.gps.dma(wdst.rearrange("p (a w) -> p a w", a=8),
                          wq3[:, :, off + h * 64:off + h * 64 + 64], "ldg")
            lgv = p.val("ldg")
            p.ten.wait("ldg", lgv)
            p.ten.wait("sS", p.val("sS"))
            p.ten.wait("vS", p.val("vS"))
            psq = [psA[:, 0:512], psB[:, 0:512]]
            qkvev = [0, 0]
            ci = 0
            for (wsb, dst) in ((wq_h, qT), (wk_h, kT)):
                for ch in range(4):
                    sl = ci % 2
                    ci += 1
                    p.ten.wait("sS", qkvev[sl])
                    for dt in range(8):
                        tv = p.ten.matmul(psq[sl][0:64, :], wsb[:, ts(dt, 64)],
                                          hb[:, dt * T + ch * 512:
                                             dt * T + (ch + 1) * 512],
                                          start=(dt == 0), stop=(dt == 7))
                    p.sca.wait("tS", tv)
                    qkvev[sl] = p.sca.copy(dst[:, ts(ch, 512)], psq[sl][0:64, :])
            for tt in range(16):
                sl = ci % 2
                ci += 1
                p.ten.wait("sS", qkvev[sl])
                for dt in range(8):
                    tv = p.ten.matmul(psq[sl][:, 0:64],
                                      hb[:, dt * T + tt * 128:
                                         dt * T + tt * 128 + 128],
                                      wv_h[:, ts(dt, 64)],
                                      start=(dt == 0), stop=(dt == 7))
                p.sca.wait("tS", tv)
                qkvev[sl] = p.sca.copy(v_sb[:, ts(tt, 64)], psq[sl][:, 0:64])
            p.ten.wait("sS", p.val("sS"))
            for (srcb, dstb, scol, cs, ss) in (
                    (qT, qrot, "qsc", "CSq8", "SSq8"),
                    (kT, krot, "ksc", "CSk", "SSk")):
                p.sca.wait("tS", p.val("tS"))
                p.sca.wait("vS", p.val("vS"))
                sv = p.sca.activation(sq_h[:], srcb[:], AF.Square)
                p.ten.wait("sS", sv)
                for chh in range(2):
                    csl = slice(chh * 1024, (chh + 1) * 1024)
                    for c2 in range(2):
                        tv = p.ten.matmul(
                            psA[0:64, ts(c2, 512)], G64[0:64, 0:64],
                            sq_h[:, chh * 1024 + c2 * 512:
                                 chh * 1024 + (c2 + 1) * 512],
                            start=True, stop=True)
                    p.vec.wait("tS", tv)
                    vv = p.vec.tensor_scalar(f32a[:], psA[0:64, :], 1e-24, 3e38,
                                             ALU.max, ALU.min)
                    p.sca.wait("vS", vv)
                    p.sca.activation(f32a[:], f32a[:], AF.Ln)
                    sv = p.sca.activation(f32a[:], f32a[:], AF.Exp, scale=-0.5)
                    p.vec.wait("sS", sv)
                    vv = p.vec.scalar_tensor_tensor(
                        sq_h[:, csl], srcb[:, csl],
                        b1col(p, io, scol, l, 0, 64), f32a[:], ALU.mult,
                        ALU.mult)
                    p.ten.wait("vS", vv)
                    for c2 in range(2):
                        tv = p.ten.matmul(
                            psB[0:64, ts(c2, 512)], P64[0:64, 0:64],
                            sq_h[:, chh * 1024 + c2 * 512:
                                 chh * 1024 + (c2 + 1) * 512],
                            start=True, stop=True)
                    p.vec.wait("tS", tv)
                    p.vec.tensor_tensor(f32a[:], sq_h[:, csl],
                                        tab[cs][0:64, csl], ALU.mult)
                    p.vec.tensor_tensor(dstb[:, csl], psB[0:64, :],
                                        tab[ss][0:64, csl], ALU.mult)
                    vv = p.vec.tensor_tensor(dstb[:, csl], f32a[:],
                                             dstb[:, csl], ALU.add)
                    p.ten.wait("vS", vv)
            sA = {nm: [0, 0] for nm in
                  ("smm", "negm", "exp", "comb", "trans", "tevac", "pv", "oev")}
            pspar = [psA, psB]

            def l1_S(tt):
                par = tt % 2
                kt0 = max(0, tt - 4)
                span = (tt + 1 - kt0) * 128
                pss = pspar[par]
                p.ten.wait("sS", sA["exp"][par])
                for c in range(0, span, 512):
                    cn = min(512, span - c)
                    tv = p.ten.matmul(pss[:, c:c + cn], qrot[:, ts(tt, 128)],
                                      krot[:, kt0 * 128 + c:kt0 * 128 + c + cn],
                                      start=True, stop=True)
                sA["smm"][par] = tv

            def l1_softmax(tt):
                par = tt % 2
                kt0 = max(0, tt - 4)
                span = (tt + 1 - kt0) * 128
                pss = pspar[par]
                p.vec.wait("tS", sA["smm"][par])
                if tt >= 4:
                    p.vec.tensor_tensor(pss[:, 0:128], pss[:, 0:128], maskA[:],
                                        ALU.add)
                p.vec.tensor_tensor(pss[:, span - 128:span],
                                    pss[:, span - 128:span], maskC[:], ALU.add)
                p.vec.reduce_max(stq["m"][par][:], pss[:, 0:span], AX.X)
                vv = p.vec.tensor_scalar(stq["nm"][par][:], stq["m"][par][:],
                                         -1.0, None, ALU.mult)
                p.sca.wait("vS", vv)
                p.sca.wait("vS", sA["comb"][par])
                sv = p.sca.activation(E_sb[par][:, 0:span], pss[:, 0:span],
                                      AF.Exp, bias=stq["nm"][par][:, 0:1],
                                      accum_out=stq["r"][par][:])
                sA["exp"][par] = sv

            def l1_comb(tt):
                par = tt % 2
                kt0 = max(0, tt - 4)
                span = (tt + 1 - kt0) * 128
                p.vec.wait("sS", sA["exp"][par])
                p.vec.reciprocal(stq["ri"][par][:], stq["r"][par][:])
                p.vec.wait("tS", sA["trans"][par])
                sA["comb"][par] = p.vec.tensor_scalar(
                    attn2[par][:, 0:span], E_sb[par][:, 0:span],
                    stq["ri"][par][:, 0:1], None, ALU.mult)

            def l1_pv(tt):
                par = tt % 2
                kt0 = max(0, tt - 4)
                nkt = tt + 1 - kt0
                p.ten.wait("vS", sA["comb"][par])
                for kt in range(nkt):
                    pb = pt_t[kt % 2]
                    if kt % 2 == 0:
                        p.ten.wait("sS", sA["tevac"][0])
                    else:
                        p.ten.wait("vS", sA["tevac"][1])
                    tv = p.ten.transpose(pb[:], attn2[par][:, ts(kt, 128)],
                                         identB[:])
                    if kt % 2 == 0:
                        p.sca.wait("tS", tv)
                        p.sca.wait("tS", sA["pv"][1 - par])
                        sA["tevac"][0] = p.sca.copy(attnT[par][:, ts(kt, 128)],
                                                    pb[:])
                    else:
                        p.vec.wait("tS", tv)
                        p.vec.wait("tS", sA["pv"][1 - par])
                        sA["tevac"][1] = p.vec.tensor_copy(
                            attnT[par][:, ts(kt, 128)], pb[:])
                p.ten.wait("sS", sA["tevac"][0])
                p.ten.wait("vS", sA["tevac"][1])
                p.ten.wait("sS", sA["oev"][1 - par])
                for kt in range(nkt):
                    tv = p.ten.matmul(ps_o[:],
                                      v_sb[:, (kt0 + kt) * 64:
                                           (kt0 + kt + 1) * 64],
                                      attnT[par][:, ts(kt, 128)],
                                      start=(kt == 0), stop=(kt == nkt - 1),
                                      skip_group_check=True)
                sA["pv"][par] = tv
                p.sca.wait("tS", tv)
                sA["oev"][par] = p.sca.copy(
                    o_sb[(h % 2) * 64:(h % 2) * 64 + 64,
                         (h // 2) * T + tt * 128:(h // 2) * T + tt * 128 + 128],
                    ps_o[:])

            for tt in range(16):
                l1_S(tt)
                l1_softmax(tt)
                if tt >= 1:
                    l1_pv(tt - 1)
                    sA["trans"][(tt - 1) % 2] = sA["pv"][(tt - 1) % 2]
                l1_comb(tt)
            l1_pv(15)
        p.ten.wait("ldg", p.val("ldg"))
        p.ten.wait("sS", p.val("sS"))
        for ft in range(8):
            for ch in range(4):
                for et in range(4):
                    tv = p.ten.matmul(
                        psA[:, 0:512],
                        wo_sb[:, et * 1024 + ft * 128:et * 1024 + ft * 128 + 128],
                        o_sb[:, et * T + ch * 512:et * T + (ch + 1) * 512],
                        start=(et == 0), stop=(et == 3))
                p.vec.wait("tS", tv)
                dst = x_br[:, ft * T + ch * 512:ft * T + (ch + 1) * 512]
                vv = p.vec.tensor_tensor(dst, psA[:, 0:512], dst, ALU.add)
                p.ten.wait("vS", vv)


def _l1f_ffn(p, io, b, l, x_br, ones128):
    es = contextlib.ExitStack()
    HID = 2730
    PAIRS = [(j * 128, min(128, HID - j * 128)) for j in range((HID + 127) // 128)]
    with es:
        xn2 = p.sb(es, "xn2", [128, 8 * T], BF16)
        with contextlib.ExitStack() as es2:
            hb2 = p.sb(es2, "hb2", [128, 8 * T], BF16)
            hsq = p.sb(es2, "hsq", [128, 8 * T], BF16)
            m_bc = p.sb(es2, "m_bc", [128, 512], F32)
            r_bc = p.sb(es2, "r_bc", [128, 512], F32)
            fa = p.sb(es2, "fa", [128, 512], F32)
            ps_1 = p.ps(es2, "ps_1", [128, 512], F32)
            ps_2 = p.ps(es2, "ps_2", [128, 512], F32)
            p.vec.wait("tS", p.val("tS"))
            for dt in range(8):
                vv = _shift_cast(p, x_br, hb2, dt)
                p.sca.wait("vS", vv)
                sv = p.sca.activation(hsq[:, ts(dt, T)], hb2[:, ts(dt, T)],
                                      AF.Square)
            p.ten.wait("sS", p.val("sS"))
            p.ten.wait("vS", p.val("vS"))
            for ch in range(4):
                for dt in range(8):
                    p.ten.matmul(ps_1[:], ones128[:],
                                 hb2[:, dt * T + ch * 512:dt * T + (ch + 1) * 512],
                                 start=(dt == 0), stop=(dt == 7),
                                 skip_group_check=True)
                for dt in range(8):
                    tv = p.ten.matmul(ps_2[:], ones128[:],
                                      hsq[:, dt * T + ch * 512:
                                          dt * T + (ch + 1) * 512],
                                      start=(dt == 0), stop=(dt == 7),
                                      skip_group_check=True)
                p.vec.wait("tS", tv)
                p.vec.tensor_scalar(m_bc[:], ps_1[:], 1.0 / D, None, ALU.mult)
                p.vec.tensor_scalar(r_bc[:], ps_2[:], 1.0 / D, None, ALU.mult)
                p.vec.tensor_tensor(fa[:], m_bc[:], m_bc[:], ALU.mult)
                p.vec.tensor_tensor(fa[:], r_bc[:], fa[:], ALU.subtract)
                vv = p.vec.tensor_scalar(fa[:], fa[:], 1e-5, None, ALU.add)
                p.ten.wait("vS", vv)
                p.sca.wait("vS", vv)
                sv = p.sca.activation(fa[:], fa[:], AF.Sqrt)
                p.vec.wait("sS", sv)
                p.vec.reciprocal(r_bc[:], fa[:])
                for dt in range(8):
                    sl = slice(dt * T + ch * 512, dt * T + (ch + 1) * 512)
                    p.vec.tensor_tensor(fa[:], hb2[:, sl], m_bc[:], ALU.subtract)
                    p.vec.tensor_tensor(fa[:], fa[:], r_bc[:], ALU.mult)
                    vv = p.vec.tensor_scalar(
                        xn2[:, sl], fa[:], b1col(p, io, "lng", l, dt, 128),
                        b1col(p, io, "lnb", l, dt, 128), ALU.mult, ALU.add)
        p.barrier()
        with contextlib.ExitStack() as es3:
            hh_sb = p.sb(es3, "hh_sb", [128, len(PAIRS) * 512], BF16)
            w1a = [p.sb(es3, f"w1a{i}", [128, 8 * 128], BF16) for i in range(2)]
            w1g = [p.sb(es3, f"w1g{i}", [128, 8 * 128], BF16) for i in range(2)]
            w2_sb = p.sb(es3, "w2_sb", [128, len(PAIRS) * 1024], BF16)
            ha = p.sb(es3, "ha", [128, 512], BF16)
            hg = p.sb(es3, "hg", [128, 512], BF16)
            ps_a = p.ps(es3, "ps_a", [128, 512], F32)
            ps_g = p.ps(es3, "ps_g", [128, 512], F32)
            ps_w2 = p.ps(es3, "ps_w2", [128, 512], F32)
            for j, (off, sz) in enumerate(PAIRS):
                p.gps.dma(w2_sb[0:sz, ts(j, 1024)],
                          io["w2"][l, off:off + sz, :], "ldg")
            w1sem = ["w1s0", "w1s1"]
            w1lv = [0, 0]
            w1used = [0, 0]
            npair = len(PAIRS)

            w13 = io["w1"][l].rearrange("(a p) f -> p a f", p=128)

            def w1_prefetch(ch, jj):
                sl = jj % 2
                off, sz = PAIRS[jj]
                p.gps.wait("tS", w1used[sl])
                p.gps.dma(w1a[sl].rearrange("p (a w) -> p a w", a=8)[:, :, 0:sz],
                          w13[:, :, off:off + sz], w1sem[sl])
                w1lv[sl] = p.gps.dma(
                    w1g[sl].rearrange("p (a w) -> p a w", a=8)[:, :, 0:sz],
                    w13[:, :, HID + off:HID + off + sz], w1sem[sl])

            evacv = [0, 0]
            for ch in range(4):
                if ch == 0:
                    w1_prefetch(0, 0)
                for j, (off, sz) in enumerate(PAIRS):
                    sl = j % 2
                    nj = j + 1
                    p.ten.wait(w1sem[sl], w1lv[sl])
                    p.ten.wait("sS", p.val("sS"))
                    p.ten.wait("vS", p.val("vS"))
                    for dt in range(8):
                        p.ten.matmul(ps_a[0:sz, :],
                                     w1a[sl][:, ts(dt, 128)][:, 0:sz],
                                     xn2[:, dt * T + ch * 512:
                                         dt * T + (ch + 1) * 512],
                                     start=(dt == 0), stop=(dt == 7),
                                     skip_group_check=True)
                    for dt in range(8):
                        tv = p.ten.matmul(ps_g[0:sz, :],
                                          w1g[sl][:, ts(dt, 128)][:, 0:sz],
                                          xn2[:, dt * T + ch * 512:
                                              dt * T + (ch + 1) * 512],
                                          start=(dt == 0), stop=(dt == 7),
                                          skip_group_check=True)
                    w1used[sl] = tv
                    if nj < npair:
                        w1_prefetch(ch, nj)
                    elif ch < 3:
                        w1_prefetch(ch + 1, 0)
                    p.vec.wait("tS", tv)
                    vv = p.vec.tensor_scalar(ha[0:sz, :], ps_a[0:sz, :],
                                             b1col(p, io, "b1a", l, j, sz),
                                             None, ALU.add)
                    p.sca.wait("tS", tv)
                    sv = p.sca.activation(hg[0:sz, :], ps_g[0:sz, :], AF.Gelu,
                                          bias=b1col(p, io, "b1g", l, j, sz))
                    p.vec.wait("sS", sv)
                    vv = p.vec.tensor_tensor(hh_sb[0:sz, ts(j, 512)],
                                             ha[0:sz, :], hg[0:sz, :], ALU.mult)
                    p.ten.wait("vS", vv)
                p.ten.wait("ldg", p.val("ldg"))
                for ft in range(8):
                    for j, (off, sz) in enumerate(PAIRS):
                        tv = p.ten.matmul(
                            ps_w2[:], w2_sb[0:sz, j * 1024 + ft * 128:
                                            j * 1024 + ft * 128 + 128],
                            hh_sb[0:sz, ts(j, 512)],
                            start=(j == 0), stop=(j == len(PAIRS) - 1))
                    p.vec.wait("tS", tv)
                    dst = x_br[:, ft * T + ch * 512:ft * T + (ch + 1) * 512]
                    vv = p.vec.scalar_tensor_tensor(
                        dst, ps_w2[:], b1col(p, io, "b2", l, ft, 128), dst,
                        ALU.add, ALU.add)
                    p.ten.wait("vS", vv)


def b1col(p, io, name, l, j, sz):
    key = ("_b1", name, l)
    cache = p.__dict__.setdefault("_colcache", {})
    if key not in cache:
        nt = io[name].shape[2]
        t = p.sb(p.es, f"col_{name}{l}", [128, nt], F32)
        p.sync.dma(t[:], io[name][l], "ld")
        cache[key] = t
    return cache[key][0:sz, j:j + 1]


def _l1f_out(p, es_b, io, b, x_br, identF):
    with contextlib.ExitStack() as es:
        mt = p.sb(es, "mt", [128, T], F32)
        out_t = p.sb(es, "out_t", [128, D], F32)
        x1_t = p.sb(es, "x1_t", [128, D], F32)
        ps_f = [p.ps(es, f"opf{i}", [128, 128], F32) for i in range(2)]
        for dt in range(8):
            p.sync.wait("vS", p.val("vS"))
            lv = p.sync.dma(mt[:], io["maccT"][b, dt], "ldf")
            p.vec.wait("ldf", lv)
            vv = p.vec.tensor_tensor(x_br[:, ts(dt, T)], mt[:],
                                     x_br[:, ts(dt, T)], ALU.add)
        p.ten.wait("vS", p.val("vS"))
        for tt in range(16):
            p.sync.wait("st", p.val("st"))
            lv = p.sync.dma(x1_t[:], io["x1"][b, ts(tt, 128), :], "ldx1")
            p.ten.wait("sS", p.val("sS"))
            p.ten.wait("vS", p.val("vS"))
            p.ten.wait("st", p.val("st"))
            for dt in range(8):
                pb = ps_f[dt % 2]
                if dt >= 2:
                    p.ten.wait("sS", p.val("sS") - 1)
                tv = p.ten.transpose(pb[:], x_br[:, dt * T + tt * 128:
                                                 dt * T + tt * 128 + 128],
                                     identF[:])
                p.sca.wait("tS", tv)
                sv = p.sca.copy(out_t[:, ts(dt, 128)], pb[:])
            p.vec.wait("sS", p.val("sS"))
            p.vec.wait("ldx1", lv)
            vv = p.vec.tensor_tensor(out_t[:], x1_t[:], out_t[:], ALU.add)
            p.sync.wait("vS", vv)
            p.sync.dma(io["x2"][b, ts(tt, 128), :], out_t[:], "st")




_L0_NAMES = ["wqT", "wkT", "wvT", "woT", "lq1T", "lk1T", "lq2T", "lk2T", "rms",
             "wgT", "wuT", "wdT", "identB", "maskC", "ones64"]


def _declare(nc, io, cw):
    for nm, arr in cw.items():
        dt = {np.dtype(np.float32): F32, np.dtype(np.int32): I32,
              np.dtype(BF): BF16}[arr.dtype]
        io[nm] = nc.dram_tensor(nm, list(arr.shape), dt, kind="ExternalInput").ap()


def run_l0(pool, xfull):
    cw = {nm: pool[nm] for nm in _L0_NAMES}
    with contextlib.ExitStack() as es:
        nc = bass.Bass("TRN2", target_bir_lowering=False)
        io = {}
        io["x"] = nc.dram_tensor("x", [NB, T, D], F32, kind="ExternalInput").ap()
        io["x1"] = nc.dram_tensor("x1", [NB, T, D], F32,
                                  kind="ExternalOutput").ap()
        io["xnT_h"] = nc.dram_tensor("xnT_h", [NB, 16, 8, 128, 128], BF16,
                                     kind="Internal").ap()
        io["attn_h"] = nc.dram_tensor("attn_h", [NB, T, D], F32,
                                      kind="Internal").ap()
        _declare(nc, io, cw)
        p = P(nc, es)
        build_l0(p, io)
        p.finalize()
        in_maps = []
        for c in range(N_CORE):
            m = dict(cw)
            m["x"] = np.ascontiguousarray(xfull[2 * c:2 * c + 2])
            in_maps.append(m)
        res = run_bass_kernel_spmd(nc, in_maps, core_ids=list(range(N_CORE)))
    return np.concatenate([r["x1"] for r in res.results], axis=0)


_MB_NAMES = (["inpT", "outpT", "identB", "selrep", "selsum", "seln"] +
             [f"{nm}_{s}" for s in ("f", "b")
              for nm in ("xprojT", "dtpT", "convw", "convb", "dtb", "alog", "D")])


def run_mamba(pool, x1full):
    cw = {nm: pool[nm] for nm in _MB_NAMES}
    with contextlib.ExitStack() as es:
        nc = bass.Bass("TRN2", target_bir_lowering=False)
        io = {}
        io["x1"] = nc.dram_tensor("x1", [NB, T, D], F32,
                                  kind="ExternalInput").ap()
        io["xn1T_h"] = nc.dram_tensor("xn1T_h", [NB, 8, 128, T], BF16,
                                      kind="ExternalOutput").ap()
        io["maccT"] = nc.dram_tensor("maccT", [NB, 8, 128, T], F32,
                                     kind="ExternalOutput").ap()
        io["x_h"] = nc.dram_tensor("x_h", [NB, 16, 128, T], BF16,
                                   kind="Internal").ap()
        io["sz_h"] = nc.dram_tensor("sz_h", [NB, 16, 128, T], BF16,
                                    kind="Internal").ap()
        _declare(nc, io, cw)
        p = P(nc, es)
        build_mamba(p, io)
        p.finalize()
        in_maps = []
        for c in range(N_CORE):
            m = dict(cw)
            m["x1"] = np.ascontiguousarray(x1full[2 * c:2 * c + 2])
            in_maps.append(m)
        res = run_bass_kernel_spmd(nc, in_maps, core_ids=list(range(N_CORE)))
    xn1T = np.concatenate([r["xn1T_h"] for r in res.results], axis=0)
    macc = np.concatenate([r["maccT"] for r in res.results], axis=0)
    return xn1T, macc


_L1_NAMES = ["wqkv", "wo1", "w1", "w2", "b1a", "b1g", "b2", "lng", "lnb",
             "qsc", "ksc", "identB", "identF", "maskC", "maskA", "G64", "P64",
             "ones128", "CSq8", "SSq8", "CSk", "SSk"]


def run_l1ffn(pool, x1full, xn1Tfull, maccfull):
    cw = {nm: pool[nm] for nm in _L1_NAMES}
    with contextlib.ExitStack() as es:
        nc = bass.Bass("TRN2", target_bir_lowering=False)
        io = {}
        io["x1"] = nc.dram_tensor("x1", [NB, T, D], F32,
                                  kind="ExternalInput").ap()
        io["xn1T_h"] = nc.dram_tensor("xn1T_h", [NB, 8, 128, T], BF16,
                                      kind="ExternalInput").ap()
        io["maccT"] = nc.dram_tensor("maccT", [NB, 8, 128, T], F32,
                                     kind="ExternalInput").ap()
        io["x2"] = nc.dram_tensor("x2", [NB, T, D], F32,
                                  kind="ExternalOutput").ap()
        _declare(nc, io, cw)
        p = P(nc, es)
        build_l1ffn(p, io)
        p.finalize()
        in_maps = []
        for c in range(N_CORE):
            m = dict(cw)
            m["x1"] = np.ascontiguousarray(x1full[2 * c:2 * c + 2])
            m["xn1T_h"] = np.ascontiguousarray(xn1Tfull[2 * c:2 * c + 2])
            m["maccT"] = np.ascontiguousarray(maccfull[2 * c:2 * c + 2])
            in_maps.append(m)
        res = run_bass_kernel_spmd(nc, in_maps, core_ids=list(range(N_CORE)))
    return np.concatenate([r["x2"] for r in res.results], axis=0)


def run_fused(pool, xfull):
    names = sorted(set(_L0_NAMES + list(_MB_NAMES) + _L1_NAMES))
    cw = {nm: pool[nm] for nm in names}
    with contextlib.ExitStack() as es:
        nc = bass.Bass("TRN2", target_bir_lowering=False)
        io = {}
        io["x"] = nc.dram_tensor("x", [NB, T, D], F32, kind="ExternalInput").ap()
        io["x2"] = nc.dram_tensor("x2", [NB, T, D], F32,
                                  kind="ExternalOutput").ap()
        for nm, shape, dt_ in (
                ("x1", [NB, T, D], F32),
                ("xnT_h", [NB, 16, 8, 128, 128], BF16),
                ("attn_h", [NB, T, D], F32),
                ("xn1T_h", [NB, 8, 128, T], BF16),
                ("maccT", [NB, 8, 128, T], F32),
                ("x_h", [NB, 16, 128, T], BF16),
                ("sz_h", [NB, 16, 128, T], BF16)):
            io[nm] = nc.dram_tensor(nm, shape, dt_, kind="Internal").ap()
        _declare(nc, io, cw)
        p = P(nc, es)
        build_l0(p, io)
        p.barrier()
        build_mamba(p, io)
        p.barrier()
        build_l1ffn(p, io)
        p.finalize()
        in_maps = []
        for c in range(N_CORE):
            m = dict(cw)
            m["x"] = np.ascontiguousarray(xfull[2 * c:2 * c + 2])
            in_maps.append(m)
        res = run_bass_kernel_spmd(nc, in_maps, core_ids=list(range(N_CORE)))
    return np.concatenate([r["x2"] for r in res.results], axis=0)


def kernel(**inputs):
    consts = _consts()
    w = _prep_weights(inputs)
    pool = {**w, **consts}
    xfull = np.asarray(inputs["L0_attn_X"], dtype=np.float32)
    return run_fused(pool, xfull)
